# version 01
# baseline (speedup 1.0000x reference)
import contextlib
import numpy as np
import ml_dtypes
import concourse.bass as bass
import concourse.mybir as mybir
from concourse.bass_utils import run_bass_kernel_spmd

F32 = mybir.dt.float32
BF16 = mybir.dt.bfloat16
U32 = mybir.dt.uint32
AF = mybir.ActivationFunctionType
ALU = mybir.AluOpType
AX = mybir.AxisListType.X
BF = ml_dtypes.bfloat16
ds = bass.ds

S, D = 4096, 1024
NT, NG = 32, 8
TOK, NTO, NGO = 2048, 16, 4
HD, HOWN, NPAIR = 64, 8, 4
H1, FDIM = 16, 64
W0 = 4096
NEXP, EH, G1 = 8, 2048, 256
EPS = 1e-5
NEGBIG = -30000.0

DEV_TAPS = ()
PHASE_LIMIT = 99


class Build:
    def __init__(self):
        nc = bass.Bass("TRN2", target_bir_lowering=False)
        self.nc = nc
        self.cnt = {}
        self.sems = {}
        for name in ("dy", "dw", "tm", "sc", "vc", "gp", "cc",
                     "px0", "px1", "pt0", "pt1", "sag", "pa0", "pa1",
                     "p3x", "p4x", "p4w", "p5x", "p5w", "p5r", "p5k",
                     "p6x", "p6g", "p6s", "p6l0", "p6l1", "p6w",
                     "p6o0", "p6o1", "p6o2", "p6o3", "pcp", "p0w", "pnb"):
            self.sems[name] = nc.alloc_semaphore(name)
            self.cnt[name] = 0
        self.dma_owner = {}
        self.vdirty = set()
        self.sdirty = set()
        self.last = {}
        self.taps = {}

    def inc(self, inst, name, k=1):
        inst.then_inc(self.sems[name], k)
        self.cnt[name] += k
        return self.cnt[name]

    def mm(self, *a, **kw):
        inst = self.nc.tensor.matmul(*a, **kw)
        self.last["tm"] = inst
        return inst

    def tr(self, *a, **kw):
        inst = self.nc.tensor.transpose(*a, **kw)
        self.last["tm"] = inst
        return inst

    def incl(self, name):
        return self.inc(self.last[name], name)

    def n(self, name):
        return self.cnt[name]

    def wt(self, eng, name, val=None):
        if val is None:
            val = self.cnt[name]
        if val > 0:
            eng.wait_ge(self.sems[name], val)

    def ldy(self, out, in_, sem="dy"):
        self.dma_owner[sem] = "sync"
        return self.inc(self.nc.sync.dma_start(out=out, in_=in_), sem, 16)

    def ldw(self, out, in_, accum=False, sem="dw"):
        kw = {"accum_op": ALU.add} if accum else {}
        self.dma_owner[sem] = "gpsimd"
        return self.inc(self.nc.gpsimd.dma_start(out=out, in_=in_, **kw), sem, 16)

    def V(self, emit, reads=(), writes=()):
        rn = {ap.tensor.name for ap in reads}
        if rn & self.vdirty:
            self.nc.vector.drain()
            self.vdirty = set()
        inst = emit(self.nc.vector)
        self.vdirty |= {ap.tensor.name for ap in writes}
        self.last["vc"] = inst
        return inst

    def SC(self, emit, reads=(), writes=()):
        rn = {ap.tensor.name for ap in reads}
        if rn & self.sdirty:
            self.nc.scalar.drain()
            self.sdirty = set()
        inst = emit(self.nc.scalar)
        self.sdirty |= {ap.tensor.name for ap in writes}
        self.last["sc"] = inst
        return inst

    def finish(self):
        for sem, owner in self.dma_owner.items():
            eng = self.nc.sync if owner == "sync" else self.nc.gpsimd
            self.wt(eng, sem)
        self.nc.all_engine_barrier()
        self.vdirty = set()
        self.sdirty = set()

    def tap(self, name, shape, dtype=F32):
        if name in DEV_TAPS:
            t = self.nc.dram_tensor("tap_" + name, list(shape), dtype,
                                    kind="ExternalOutput")
            self.taps[name] = t.ap()
            return self.taps[name]
        return None


def _ln_tile(b, out_sb, x_ap, W, sc):
    nc = b.nc
    b.V(lambda v: v.reduce_sum(sc["ssum"][:, 0:1], x_ap, AX),
        reads=(x_ap,), writes=(sc["ssum"][:, 0:1],))
    b.SC(lambda s: s.activation(sc["sqscr"][:, 0:W], x_ap, AF.Square,
                                accum_out=sc["sqs"][:, 0:1]),
         reads=(x_ap,), writes=(sc["sqscr"][:, 0:W], sc["sqs"][:, 0:1]))
    b.incl("sc")
    b.wt(nc.vector, "sc")
    b.V(lambda v: v.tensor_scalar(sc["mean"][:, 0:1], sc["ssum"][:, 0:1],
                                  1.0 / W, None, ALU.mult),
        reads=(sc["ssum"][:, 0:1],), writes=(sc["mean"][:, 0:1],))
    b.V(lambda v: v.tensor_scalar(sc["msq"][:, 0:1], sc["sqs"][:, 0:1],
                                  1.0 / W, None, ALU.mult),
        reads=(sc["sqs"][:, 0:1],), writes=(sc["msq"][:, 0:1],))
    b.V(lambda v: v.tensor_mul(sc["mm"][:, 0:1], sc["mean"][:, 0:1],
                               sc["mean"][:, 0:1]),
        reads=(sc["mean"][:, 0:1],), writes=(sc["mm"][:, 0:1],))
    b.V(lambda v: v.tensor_sub(sc["var"][:, 0:1], sc["msq"][:, 0:1],
                               sc["mm"][:, 0:1]),
        reads=(sc["msq"][:, 0:1], sc["mm"][:, 0:1]), writes=(sc["var"][:, 0:1],))
    b.V(lambda v: v.tensor_scalar(sc["vare"][:, 0:1], sc["var"][:, 0:1],
                                  EPS, None, ALU.add),
        reads=(sc["var"][:, 0:1],), writes=(sc["vare"][:, 0:1],))
    b.V(lambda v: v.reciprocal(sc["rv"][:, 0:1], sc["vare"][:, 0:1]),
        reads=(sc["vare"][:, 0:1],), writes=(sc["rv"][:, 0:1],))
    b.incl("vc")
    b.wt(nc.scalar, "vc")
    b.SC(lambda s: s.activation(sc["rstd"][:, 0:1], sc["rv"][:, 0:1], AF.Sqrt),
         reads=(sc["rv"][:, 0:1],), writes=(sc["rstd"][:, 0:1],))
    b.incl("sc")
    b.wt(nc.vector, "sc")
    b.V(lambda v: v.scalar_tensor_tensor(sc["nmean"][:, 0:1], sc["mean"][:, 0:1],
                                         -1.0, sc["rstd"][:, 0:1],
                                         ALU.mult, ALU.mult),
        reads=(sc["mean"][:, 0:1],), writes=(sc["nmean"][:, 0:1],))
    b.V(lambda v: v.tensor_scalar(out_sb, x_ap, sc["rstd"][:, 0:1],
                                  sc["nmean"][:, 0:1], ALU.mult, ALU.add),
        reads=(x_ap, sc["nmean"][:, 0:1]), writes=(out_sb,))
    return b.n("vc"), b.n("sc")


def _make_consts():
    idb = np.eye(128, dtype=BF)
    idf = np.eye(128, dtype=np.float32)
    tri = np.where(np.arange(128)[None, :] > np.arange(128)[:, None],
                   np.float32(NEGBIG), np.float32(0.0)).astype(BF)
    ones_b = np.ones((128, 128), dtype=BF)
    ones_f = np.ones((1, 128), dtype=np.float32)
    sel2 = np.zeros((128, 2), dtype=BF)
    sel2[0:64, 0] = 1
    sel2[64:128, 1] = 1
    inv = (1.0 / (np.arange(S, dtype=np.float64) + 1.0)).astype(BF)[None, :]
    triinv = np.zeros((NG, 4, 128, 512), dtype=BF)
    for g in range(NG):
        qg = 512 * g + np.arange(512)
        for c in range(4):
            kg = 128 * (4 * g + c) + np.arange(128)
            m = (kg[:, None] <= qg[None, :]).astype(np.float64)
            triinv[g, c] = (m / (qg[None, :] + 1.0)).astype(BF)
    return idb, idf, tri, ones_b, ones_f, sel2, inv, triinv


def _strided3(ap_base, pstep, a, b_, c, d):
    return bass.AP(ap_base.tensor, ap_base.offset, [[pstep, 128], [a, b_], [c, d]])


def build_nc():
    b = Build()
    nc = b.nc
    st = contextlib.ExitStack()

    def inp(name, shape, dtype=F32):
        return nc.dram_tensor(name, list(shape), dtype, kind="ExternalInput").ap()

    x_seq = inp("x_seq", [S, D])
    x_own = inp("x_own", [TOK, D])
    tok_base = inp("tok_base", [1, 1], U32)
    qkw_own = inp("qkw_own", [D, 2 * 512])
    vw_own = inp("vw_own", [D, 512])
    qb_pp = inp("qb_pp", [128, 4])
    kb_pp = inp("kb_pp", [128, 4])
    cprojw = inp("cprojw", [D, D])
    gain_r = inp("gain_r", [64, 8])
    gain_s = inp("gain_s", [64, 8])
    gain_c = inp("gain_c", [64, 8])
    w1 = inp("w1", [D, W0])
    b1_pp = inp("b1_pp", [128, 32])
    w2 = inp("w2", [W0, D])
    b2_row = inp("b2_row", [1, D])
    wq1 = inp("wq1", [D, D])
    wk1 = inp("wk1", [D, D])
    wv1 = inp("wv1", [D, D])
    wo1 = inp("wo1", [D, D])
    bo_row = inp("bo_row", [1, D])
    wf_rep = inp("wf_rep", [128, FDIM])
    bf_row = inp("bf_row", [1, FDIM])
    bf_pp = inp("bf_pp", [64, 1])
    wg1 = inp("wg1", [D, G1])
    bg1_pp = inp("bg1_pp", [128, 2])
    wg2 = inp("wg2", [G1, NEXP])
    w_in = inp("w_in", [NEXP, D, EH])
    w_out = inp("w_out", [NEXP, EH, D])

    out = nc.dram_tensor("out", [TOK, D], F32, kind="ExternalOutput").ap()

    idb_h, idf_h, tri_h, onesb_h, onesf_h, sel2_h, inv_h, triinv_h = _make_consts()
    c_idb = nc.inline_tensor(idb_h, name="c_idb").ap()
    c_idf = nc.inline_tensor(idf_h, name="c_idf").ap()
    c_tri = nc.inline_tensor(tri_h, name="c_tri").ap()
    c_onesb = nc.inline_tensor(onesb_h, name="c_onesb").ap()
    c_onesf = nc.inline_tensor(onesf_h, name="c_onesf").ap()
    c_sel2 = nc.inline_tensor(sel2_h, name="c_sel2").ap()
    c_inv = nc.inline_tensor(inv_h, name="c_inv").ap()
    c_triinv = nc.inline_tensor(triinv_h, name="c_triinv").ap()

    ag_in = nc.dram_tensor("ag_in", [512, S], BF16)
    ag_out = nc.dram_tensor("ag_out", [2, 512, S], BF16)
    ar_in = nc.dram_tensor("ar_in", [64, H1, 65], F32)
    ar_out = nc.dram_tensor("ar_out", [64, H1, 65], F32)
    stream_a = nc.dram_tensor("stream_a", [TOK, D], F32)
    stream_b = nc.dram_tensor("stream_b", [TOK, D], F32)
    stream_c = nc.dram_tensor("stream_c", [TOK, D], F32)
    lnx6 = nc.dram_tensor("lnx6", [128, NTO * 1024], BF16)
    oacc_d = nc.dram_tensor("oacc_d", [TOK, D], F32)

    def sb(name, shape, dtype):
        return st.enter_context(nc.sbuf_tensor(name, list(shape), dtype))

    idb = sb("idb", [128, 128], BF16)
    idf = sb("idf", [128, 128], F32)
    tri = sb("tri", [128, 128], BF16)
    onesb = sb("onesb", [128, 128], BF16)
    onesf = sb("onesf", [1, 128], F32)
    sel2 = sb("sel2", [128, 2], BF16)
    negb_rep = sb("negb_rep", [128, 8], F32)
    grep = sb("grep", [64, 8], F32)
    srep = sb("srep", [64, 8], F32)
    craw = sb("craw", [64, 8], F32)
    cnrep = sb("cnrep", [64, 8], F32)
    qbp = sb("qbp", [128, 4], F32)
    kbp = sb("kbp", [128, 4], F32)
    b1p = sb("b1p", [128, 32], F32)
    wfr = sb("wfr", [128, FDIM], BF16)
    bfr = sb("bfr", [1, FDIM], BF16)
    bfp = sb("bfp", [64, 1], F32)
    bg1p = sb("bg1p", [128, 2], F32)
    tokb = sb("tokb", [1, 1], U32)
    lnsc = {k: sb("lnsc_" + k, [128, 1], F32)
            for k in ("ssum", "sqs", "mean", "msq", "mm", "var", "vare", "rv",
                      "rstd", "nmean")}
    lnsc["sqscr"] = sb("lnsc_sqscr", [128, 1024], BF16)
    q2all = sb("q2all", [2, 32], F32)
    k2all = sb("k2all", [2, 32], F32)
    nb2 = sb("nb2", [2, 4], F32)
    nb2b = sb("nb2b", [2, 4], F32)
    nbr1 = sb("nbr1", [1, 4], F32)
    tiny = sb("tiny", [2, 8], F32)

    b.ldy(idb[:, :], c_idb)
    b.ldy(idf[:, :], c_idf)
    b.ldy(tri[:, :], c_tri)
    b.ldy(onesb[:, :], c_onesb)
    b.ldy(onesf[:, :], c_onesf)
    b.ldy(sel2[:, :], c_sel2)
    b.ldy(grep[:, :], gain_r)
    b.ldy(srep[:, :], gain_s)
    b.ldy(craw[:, :], gain_c)
    b.ldy(qbp[:, :], qb_pp)
    b.ldy(kbp[:, :], kb_pp)
    b.ldy(b1p[:, :], b1_pp)
    b.ldy(bfp[:, :], bf_pp)
    b.ldy(bg1p[:, :], bg1_pp)
    b.ldy(tokb[:, :], tok_base)
    b.ldw(wfr[:, :], wf_rep)
    b.ldw(bfr[:, :], bf_row)
    setup_dy = b.n("dy")
    b.wt(nc.vector, "dy", setup_dy)
    b.V(lambda v: v.tensor_scalar(cnrep[:, :], craw[:, :], -1.0, None, ALU.mult),
        reads=(craw[:, 0:1],), writes=(cnrep[:, 0:1],))
    b.incl("vc")

    p01 = contextlib.ExitStack()

    def sb1(name, shape, dtype):
        return p01.enter_context(nc.sbuf_tensor(name, list(shape), dtype))

    invp = sb1("invp", [1, S], BF16)
    carryT = sb1("carryT", [1, NG * 512], BF16)
    qfm = sb1("qfm", [128, 4 * S], BF16)
    kfm = sb1("kfm", [128, 4 * S], BF16)
    vext = sb1("vext", [128, NT * 520], BF16)
    probs = sb1("probs", [128, 4 * 512], BF16)
    triinv_s = sb1("triinv_s", [128, 2 * 2048], BF16)
    recip_sb = sb1("recip_sb", [1, 512], F32)
    pvs_sb = sb1("pvs_sb", [128, 512], F32)
    vfms = sb1("vfms", [128, 512], BF16)
    skv = sb1("skv", [128, 512], F32)
    e1 = sb1("e1", [128, 512], F32)
    e2 = sb1("e2", [128, 512], F32)
    outpair = sb1("outpair", [128, 512], BF16)
    gs_sb = sb1("gs_sb", [64, 8], F32)
    cacc = [sb1("cacc0", [64, 8], F32), sb1("cacc1", [64, 8], F32)]
    ct_sb = sb1("ct_sb", [8, 64], F32)

    p0t = contextlib.ExitStack()

    def sb0(name, shape, dtype):
        return p0t.enter_context(nc.sbuf_tensor(name, list(shape), dtype))

    qkwb = sb0("qkwb", [128, 8 * 1024], BF16)
    vwb = sb0("vwb", [128, 8 * 512], BF16)

    xa = sb0("xa", [128, 2 * 1024], F32)
    lnt = sb0("lnt", [128, 2 * 1024], F32)
    lnxfm = sb0("lnxfm", [128, 2 * 4096], BF16)
    sqscr2 = sb0("sqscr2", [128, 512], BF16)

    b.ldy(invp[:, :], c_inv)
    b.ldw(qkwb[:, :].rearrange("p (c n) -> p c n", c=8),
          qkw_own.rearrange("(c p) n -> p c n", p=128), sem="p0w")
    b.ldw(vwb[:, :].rearrange("p (c n) -> p c n", c=8),
          vw_own.rearrange("(c p) n -> p c n", p=128), sem="p0w")
    dw_w0 = b.n("p0w")
    b.V(lambda v: v.memset(carryT[0:1, 0:512], 0.0), writes=(carryT[0:1, 0:1],))
    vxv = vext[:, :].rearrange("p (t h o) -> p t h o", t=NT, h=HOWN)
    b.V(lambda v: v.memset(vxv[:, :, :, 64:65], 1.0), writes=(vext[:, 0:1],))
    vready0 = b.incl("vc")

    with (nc.psum_tensor("trps", [128, 1024], F32) as trps,
          nc.psum_tensor("mmA", [128, 512], F32) as mmA,
          nc.psum_tensor("mmB", [128, 512], F32) as mmB,
          nc.psum_tensor("vps", [128, 512], F32) as vps,
          nc.psum_tensor("q2ps", [2, 512], F32) as q2ps,
          nc.psum_tensor("gsps", [65, 8], F32) as gsps,
          nc.psum_tensor("ctps", [8, 64], F32) as ctps):
        lnv, lnsct, trv = {}, {}, {}
        last_sel2_tm = 0
        b.wt(nc.tensor, "dy", setup_dy)
        for t in range(NT):
            par = t % 2
            gpar = (t // 4) % 2
            if t >= 2:
                b.wt(nc.sync, "vc", trv[t - 2])
                b.wt(nc.sync, "sc", lnsct[t - 2])
            ldn = b.ldy(xa[:, 1024 * par:1024 * par + 1024],
                        x_seq[128 * t:128 * t + 128, :], sem=f"px{par}")
            b.wt(nc.vector, f"px{par}", ldn)
            b.wt(nc.scalar, f"px{par}", ldn)
            xap = xa[:, 1024 * par:1024 * par + 1024]
            lnap = lnt[:, 1024 * par:1024 * par + 1024]
            _v, _s = _ln_tile(b, lnap, xap, D, lnsc)
            lnv[t] = b.incl("vc")
            lnsct[t] = _s
            b.wt(nc.tensor, "vc", lnv[t])
            if t >= 1:
                b.wt(nc.tensor, "vc", trv[t - 1])
            for c in range(8):
                b.tr(trps[:, 128 * c:128 * c + 128],
                                    lnap[:, 128 * c:128 * c + 128], idf[:, :])
            tmn = b.incl("tm")
            b.wt(nc.vector, "tm", tmn)
            dst = lnxfm[:, 4096 * gpar + 128 * (t % 4):]
            dstv = bass.AP(dst.tensor, dst.offset,
                           [[dst.ap[0][0], 128], [512, 8], [1, 128]])
            srcv = trps[:, :].rearrange("p (c n) -> p c n", c=8)
            b.V(lambda v, dv=dstv, sv=srcv: v.tensor_copy(dv, sv),
                writes=(lnxfm[:, 0:1],))
            trv[t] = b.incl("vc")

            if t % 4 == 3:
                gf = t // 4
                gpar2 = gf % 2
                rhs_base = 4096 * gpar2
                b.wt(nc.tensor, "vc", trv[t])
                if gf == 0:
                    b.wt(nc.tensor, "p0w", dw_w0)
                    b.wt(nc.tensor, "vc", vready0)
                for which, wofs, dst_fm, biasp, scl in (
                        ("q", 0, qfm, qbp, 0.125), ("k", 512, kfm, kbp, None)):
                    for pc in range(4):
                        ps = mmA if pc % 2 == 0 else mmB
                        b.wt(nc.tensor, "vc")
                        for k in range(8):
                            b.mm(
                                ps[:, :],
                                qkwb[:, 1024 * k + wofs + 128 * pc:
                                     1024 * k + wofs + 128 * pc + 128],
                                lnxfm[:, rhs_base + 512 * k:
                                      rhs_base + 512 * k + 512],
                                start=(k == 0), stop=(k == 7))
                        tmn = b.incl("tm")
                        b.wt(nc.vector, "tm", tmn)
                        dstap = dst_fm[:, 4096 * pc + 512 * gf:
                                       4096 * pc + 512 * gf + 512]
                        if scl is None:
                            b.V(lambda v, o=dstap, p=ps, bp=biasp, c=pc:
                                v.tensor_scalar(o, p[:, :], bp[:, c:c + 1],
                                                None, ALU.add),
                                writes=(dst_fm[:, 0:1],))
                        else:
                            b.V(lambda v, o=dstap, p=ps, bp=biasp, c=pc, s_=scl:
                                v.tensor_scalar(o, p[:, :], bp[:, c:c + 1],
                                                s_, ALU.add, ALU.mult),
                                writes=(dst_fm[:, 0:1],))
                        vn = b.incl("vc")
                        b.wt(nc.scalar, "vc", vn)
                        b.wt(nc.scalar, "tm", last_sel2_tm)
                        b.SC(lambda s, i=dstap: s.activation(
                            sqscr2[:, 0:512], i, AF.Square),
                            writes=(sqscr2[:, 0:1],))
                        scn = b.incl("sc")
                        b.wt(nc.tensor, "sc", scn)
                        b.mm(q2ps[:, :], sel2[:, :],
                                         sqscr2[:, 0:512], start=True, stop=True)
                        last_sel2_tm = tmn = b.incl("tm")
                        b.wt(nc.vector, "tm", tmn)
                        dcol = (q2all if which == "q" else k2all)
                        b.V(lambda v, d=dcol, col=8 * pc + gf: v.reduce_max(
                            d[:, col:col + 1], q2ps[:, :], AX),
                            writes=(dcol[:, 0:1],))
                        b.incl("vc")
                for tt in range(4 * gf, 4 * gf + 4):
                    b.wt(nc.tensor, "vc")
                    for k in range(8):
                        b.mm(
                            vps[:, :],
                            lnxfm[:, rhs_base + 512 * k + 128 * (tt % 4):
                                  rhs_base + 512 * k + 128 * (tt % 4) + 128],
                            vwb[:, 512 * k:512 * k + 512],
                            start=(k == 0), stop=(k == 7))
                    tmn = b.incl("tm")
                    b.wt(nc.vector, "tm", tmn)
                    dst = vext[:, 520 * tt:]
                    dstv = bass.AP(dst.tensor, dst.offset,
                                   [[dst.ap[0][0], 128], [65, 8], [1, 64]])
                    srcv = vps[:, :].rearrange("p (h o) -> p h o", h=8)
                    b.V(lambda v, dv=dstv, sv=srcv: v.tensor_copy(dv, sv),
                        writes=(vext[:, 0:1],))
                    vpse = b.incl("vc")
                b.wt(nc.tensor, "vc", vpse)
                for lh in range(8):
                    for ti in range(4):
                        tt2 = 4 * gf + ti
                        b.mm(gsps[:, lh:lh + 1],
                             vext[:, 520 * tt2 + 65 * lh:
                                  520 * tt2 + 65 * lh + 65],
                             onesb[:, 0:1],
                             start=(ti == 0), stop=(ti == 3))
                tmn = b.incl("tm")
                b.wt(nc.vector, "tm", tmn)
                b.V(lambda v: v.tensor_copy(gs_sb[:, :], gsps[0:64, :]),
                    writes=(gs_sb[:, 0:1],))
                vn = b.incl("vc")
                src_c = cacc[gf % 2]
                dst_c = cacc[1 - gf % 2]
                if gf > 0:
                    b.wt(nc.tensor, "vc", vn)
                    b.tr(ctps[:, :], src_c[:, 0:8],
                                        idf[0:64, 0:64])
                    tmn = b.incl("tm")
                    b.wt(nc.vector, "tm", tmn)
                    b.wt(nc.vector, "dw")
                    b.V(lambda v: v.tensor_copy(ct_sb[:, :], ctps[0:8, 0:64]),
                        writes=(ct_sb[0:1, 0:1],))
                    vn2 = b.incl("vc")
                    b.wt(nc.gpsimd, "vc", vn2)
                    b.ldw(carryT[0:1, 512 * gf:512 * gf + 512], ct_sb[:, :])
                    b.V(lambda v, d=dst_c, s_=src_c: v.tensor_add(
                        d[:, :], s_[:, :], gs_sb[:, :]),
                        reads=(gs_sb[:, 0:1],), writes=(dst_c[:, 0:1],))
                else:
                    b.V(lambda v, d=dst_c: v.tensor_copy(d[:, :], gs_sb[:, :]),
                        reads=(gs_sb[:, 0:1],), writes=(dst_c[:, 0:1],))
                b.incl("vc")

        for c in range(4):
            for d_, a_ in ((q2all, 0), (k2all, 1)):
                src = bass.AP(d_, 8 * c, [[32, 2], [1, 8]])
                b.V(lambda v, s_=src, aa=a_, cc=c: v.reduce_max(
                    tiny[:, 4 * aa + cc:4 * aa + cc + 1], s_, AX),
                    reads=(d_[:, 0:1],), writes=(tiny[:, 0:1],))
        b.V(lambda v: v.tensor_mul(nb2[:, :], tiny[:, 0:4], tiny[:, 4:8]),
            reads=(tiny[:, 0:1],), writes=(nb2[:, 0:1],))
        vn = b.incl("vc")
        b.wt(nc.scalar, "vc", vn)
        b.SC(lambda s: s.activation(nb2b[:, :], nb2[:, :], AF.Sqrt),
             reads=(nb2[:, 0:1],), writes=(nb2b[:, 0:1],))
        scn = b.incl("sc")
        b.wt(nc.vector, "sc", scn)
        b.V(lambda v: v.tensor_scalar(nb2[:, :], nb2b[:, :], -1.0, None,
                                      ALU.mult),
            reads=(nb2b[:, 0:1],), writes=(nb2[:, 0:1],))
        vn = b.incl("vc")
        b.wt(nc.sync, "vc", vn)
        nbl = b.ldy(nbr1[:, :], nb2[1:2, :], sem="pnb")
        b.wt(nc.tensor, "pnb", nbl)
        b.wt(nc.tensor, "vc", vn)
        b.mm(mmA[:, 0:4], onesf[0:1, :], nb2[0:1, :],
                         start=True, stop=True)
        b.mm(mmA[:, 4:8], onesf[0:1, :], nbr1[0:1, :],
                         start=True, stop=True)
        tmn = b.incl("tm")
        b.wt(nc.vector, "tm", tmn)
        dstv = bass.AP(negb_rep, 0, [[8, 128], [1, 2], [2, 4]])
        srcv = mmA[:, 0:8].rearrange("p (r c) -> p r c", r=2)
        b.V(lambda v: v.tensor_copy(dstv, srcv), writes=(negb_rep[:, 0:1],))
        negb_v = b.incl("vc")

        for nm, t_, w in (("p0_qfm", qfm, 4 * S), ("p0_kfm", kfm, 4 * S),
                          ("p0_vext", vext, NT * 520)):
            tp = b.tap(nm, [128, w], BF16)
            if tp is not None:
                b.wt(nc.sync, "vc")
                b.ldy(tp, t_[:, :])
        tp = b.tap("p0_negb", [128, 8])
        if tp is not None:
            b.wt(nc.sync, "vc")
            b.ldy(tp, negb_rep[:, :])
        tp = b.tap("p0_carryT", [1, NG * 512], BF16)
        if tp is not None:
            b.wt(nc.sync, "dw")
            b.ldy(tp, carryT[:, :])

    p0t.close()
    if PHASE_LIMIT < 1:
        p01.close()
        b.finish()
        st.close()
        return b

    with (nc.psum_tensor("sA0", [128, 512], F32) as sA0,
          nc.psum_tensor("sA1", [128, 512], F32) as sA1,
          nc.psum_tensor("sB0", [128, 512], F32) as sB0,
          nc.psum_tensor("sB1", [128, 512], F32) as sB1,
          nc.psum_tensor("pvA", [65, 512], F32) as pvA,
          nc.psum_tensor("pvB", [65, 512], F32) as pvB,
          nc.psum_tensor("cmps", [64, 512], F32) as cmps,
          nc.psum_tensor("vtr", [64, 512], BF16) as vtr):
        sbanks = [(sA0, sB0), (sA1, sB1)]
        b.wt(nc.tensor, "dw")
        trild = {0: b.ldy(triinv_s[:, 0:2048].rearrange("p (c q) -> p c q", c=4),
                          c_triinv[0].rearrange("c k q -> k c q"),
                          sem="pt0")}
        prev_ep_v = negb_v
        prev_ep_tm = 0
        last_ag = 0
        for g in range(NG):
            nj = 4 * g + 4
            gpar = g % 2
            if g + 1 < NG:
                b.wt(nc.sync, "tm", prev_ep_tm)
                trild[g + 1] = b.ldy(
                    triinv_s[:, 2048 * (1 - gpar):2048 * (1 - gpar) + 2048]
                    .rearrange("p (c q) -> p c q", c=4),
                    c_triinv[g + 1].rearrange("c k q -> k c q"),
                    sem=f"pt{(g + 1) % 2}")
            for pc in range(NPAIR):
                lhA, lhB = 2 * pc, 2 * pc + 1
                exp_cnt, pv_cnt, s_cnt = {}, {}, {}
                for j in range(nj):
                    c2 = j - 4 * g
                    qoff = 128 * c2 if c2 > 0 else 0
                    N = 512 - qoff
                    sA, sB = sbanks[j % 2]
                    if j >= 2:
                        b.wt(nc.tensor, "sc", exp_cnt[j - 2])
                    elif j == 0:
                        b.wt(nc.tensor, "sc")
                        b.wt(nc.tensor, "vc", prev_ep_v)
                        if g == 0 and pc == 0:
                            b.wt(nc.tensor, "pt0", trild[0])
                    diag = c2 >= 0
                    for (sps, r0) in ((sA, 0), (sB, 64)):
                        b.mm(
                            sps[:, qoff:512],
                            kfm[:, 4096 * pc + 128 * j:
                                4096 * pc + 128 * j + 128][r0:r0 + 64, :],
                            qfm[:, 4096 * pc + 512 * g + qoff:
                                4096 * pc + 512 * g + 512][r0:r0 + 64, :],
                            start=True, stop=not diag, tile_position=(r0, 0))
                        if diag:
                            b.mm(
                                sps[:, qoff:qoff + 128], tri[:, :], idb[:, :],
                                start=False, stop=True, tile_position=(0, 0))
                    s_cnt[j] = b.incl("tm")
                    b.wt(nc.scalar, "tm", s_cnt[j])
                    if j >= 2:
                        b.wt(nc.scalar, "tm", pv_cnt[j - 2])
                    for (sps, lh, po) in ((sA, lhA, 0), (sB, lhB, 512)):
                        b.SC(lambda s, ss=sps, l=lh, p=po + 1024 * (j % 2),
                             q=qoff, n=N:
                             s.activation(probs[:, p + q:p + q + n],
                                          ss[:, q:512], AF.Exp,
                                          bias=negb_rep[:, l:l + 1]),
                             writes=(probs[:, 0:1],))
                    exp_cnt[j] = b.incl("sc")
                    b.wt(nc.tensor, "sc", exp_cnt[j])
                    for (pv, lh, po) in ((pvA, lhA, 0), (pvB, lhB, 512)):
                        b.mm(
                            pv[:, qoff:512],
                            vext[:, 520 * j + 65 * lh:520 * j + 65 * lh + 65],
                            probs[:, po + 1024 * (j % 2) + qoff:
                                  po + 1024 * (j % 2) + 512],
                            start=(j == 0), stop=(j == nj - 1))
                    pv_cnt[j] = b.incl("tm")
                sweep_tm = pv_cnt[nj - 1]

                b.wt(nc.vector, "sag", last_ag)
                for (pv, lh, r0) in ((pvA, lhA, 0), (pvB, lhB, 64)):
                    b.wt(nc.tensor, "tm", sweep_tm)
                    if r0 != 0:
                        b.wt(nc.tensor, "vc")
                    for c2 in range(4):
                        jj = 4 * g + c2
                        b.tr(
                            vtr[:, 128 * c2:128 * c2 + 128],
                            vext[:, 520 * jj + 65 * lh:520 * jj + 65 * lh + 64],
                            idb[:, :])
                    tmn = b.incl("tm")
                    b.wt(nc.vector, "tm", tmn)
                    b.V(lambda v: v.tensor_copy(vfms[0:64, 0:512],
                                                vtr[0:64, :]),
                        writes=(vfms[0:1, 0:1],))
                    b.V(lambda v, p=pv: v.reciprocal(recip_sb[0:1, :],
                                                     p[64:65, :]),
                        writes=(recip_sb[0:1, 0:1],))
                    vn = b.incl("vc")
                    b.wt(nc.tensor, "vc", vn)
                    b.mm(cmps[0:64, :], onesf[0:1, 0:64],
                         recip_sb[0:1, :], start=True, stop=True)
                    rtm = b.incl("tm")
                    b.wt(nc.scalar, "tm", sweep_tm)
                    b.wt(nc.scalar, "vc", prev_ep_v)
                    b.SC(lambda s, p=pv: s.copy(pvs_sb[0:64, :], p[0:64, :]),
                         writes=(pvs_sb[0:1, 0:1],))
                    scn = b.incl("sc")
                    b.wt(nc.vector, "tm", rtm)
                    b.wt(nc.vector, "sc", scn)
                    b.V(lambda v: v.tensor_mul(e1[0:64, :], pvs_sb[0:64, :],
                                               cmps[0:64, :]),
                        reads=(pvs_sb[0:1, 0:1],), writes=(e1[0:1, 0:1],))
                    b.V(lambda v, l=lh: v.tensor_scalar(
                        skv[0:64, :], vfms[0:64, 0:512], srep[0:64, l:l + 1],
                        None, ALU.mult),
                        reads=(vfms[0:1, 0:1],), writes=(skv[0:1, 0:1],))
                    b.V(lambda v, l=lh: v.scalar_tensor_tensor(
                        e2[0:64, :], e1[0:64, :], grep[0:64, l:l + 1],
                        skv[0:64, :], ALU.mult, ALU.add),
                        reads=(e1[0:1, 0:1], skv[0:1, 0:1]),
                        writes=(e2[0:1, 0:1],))
                    e1v = b.incl("vc")
                    b.wt(nc.tensor, "vc", e1v)
                    if g > 0:
                        b.wt(nc.tensor, f"pt{g % 2}", trild[g])
                        b.mm(
                            cmps[:, :],
                            carryT[0:1, 512 * g + 64 * lh:
                                   512 * g + 64 * lh + 64],
                            invp[0:1, 512 * g:512 * g + 512],
                            start=True, stop=False)
                    for c2 in range(4):
                        jj = 4 * g + c2
                        b.mm(
                            cmps[:, :],
                            vext[:, 520 * jj + 65 * lh:520 * jj + 65 * lh + 64],
                            triinv_s[:, 2048 * gpar + 512 * c2:
                                     2048 * gpar + 512 * c2 + 512],
                            start=(c2 == 0 and g == 0), stop=(c2 == 3))
                    tmn = b.incl("tm")
                    b.wt(nc.vector, "tm", tmn)
                    b.V(lambda v, l=lh, r=r0: v.scalar_tensor_tensor(
                        outpair[r:r + 64, :], cmps[0:64, :],
                        cnrep[0:64, l:l + 1], e2[0:64, :], ALU.mult, ALU.add),
                        reads=(e2[0:1, 0:1],), writes=(outpair[0:1, 0:1],))
                    prev_ep_v = b.incl("vc")
                    prev_ep_tm = b.n("tm")
                b.wt(nc.sync, "vc", prev_ep_v)
                last_ag = b.ldy(
                    ag_in.ap()[128 * pc:128 * pc + 128, 512 * g:512 * g + 512],
                    outpair[:, :], sem="sag")

        tp = b.tap("p1_ag_in", [512, S], BF16)
        if tp is not None:
            b.wt(nc.gpsimd, "sag")
            b.ldw(tp, ag_in.ap())

    p01.close()
    b.finish()
    if PHASE_LIMIT < 2:
        st.close()
        return b

    b.wt(nc.gpsimd, "sag")
    cci = nc.gpsimd.collective_compute(
        "AllGather", ALU.bypass,
        replica_groups=[[0, 1], [2, 3], [4, 5], [6, 7]],
        ins=[ag_in.ap().opt()], outs=[ag_out.ap().opt()])
    b.inc(cci, "cc", 1)

    p3 = contextlib.ExitStack()

    def sb3(name, shape, dtype):
        return p3.enter_context(nc.sbuf_tensor(name, list(shape), dtype))

    cpw = sb3("cpw", [128, 8 * 1024], BF16)
    afm = sb3("afm", [128, 2 * 4096], BF16)
    xg3 = sb3("xg3", [128, 4 * 1024], F32)
    ysb3 = sb3("ysb3", [128, 1024], F32)
    ln3 = sb3("ln3", [128, 1024], F32)
    b.ldw(cpw[:, :].rearrange("p (c n) -> p c n", c=8),
          cprojw.rearrange("(c p) n -> p c n", p=128), sem="pcp")
    dw3 = b.n("pcp")
    tokv = nc.sync.value_load(tokb[0:1, 0:1])
    b.wt(nc.sync, "cc", b.n("cc"))
    with nc.psum_tensor("yp3", [128, 1024], F32) as yp3:
        grp_tm = {}
        for go in range(NGO):
            gpar = go % 2
            if go >= 2:
                b.wt(nc.sync, "tm", grp_tm[go - 2])
            src_ap = bass.AP(ag_out.ap().tensor, tokv + 512 * go,
                             [[S, 128], [512 * S, 2], [128 * S, 4], [1, 512]])
            dsta = afm[:, 4096 * gpar:4096 * gpar + 4096]
            dst_ap = bass.AP(dsta.tensor, dsta.offset,
                             [[dsta.ap[0][0], 128], [2048, 2], [512, 4],
                              [1, 512]])
            b.ldy(dst_ap, src_ap, sem=f"pa{gpar}")
            if go >= 1:
                b.wt(nc.sync, "vc")
                b.wt(nc.sync, "sc")
            for tt in range(4):
                b.ldy(xg3[:, 1024 * tt:1024 * tt + 1024],
                      x_own[512 * go + 128 * tt:512 * go + 128 * tt + 128, :],
                      sem="p3x")
            ldn = b.n("p3x")
            for tt in range(4):
                b.wt(nc.tensor, f"pa{gpar}")
                if go == 0 and tt == 0:
                    b.wt(nc.tensor, "pcp", dw3)
                b.wt(nc.tensor, "vc")
                for n2 in range(2):
                    for k in range(8):
                        b.mm(
                            yp3[:, 512 * n2:512 * n2 + 512],
                            afm[:, 4096 * gpar + 512 * k + 128 * tt:
                                4096 * gpar + 512 * k + 128 * tt + 128],
                            cpw[:, 1024 * k + 512 * n2:
                                1024 * k + 512 * n2 + 512],
                            start=(k == 0), stop=(k == 7))
                grp_tm[go] = tmn = b.incl("tm")
                b.wt(nc.vector, "tm", tmn)
                b.wt(nc.vector, "p3x", ldn)
                b.V(lambda v: v.tensor_copy(ysb3[:, :], yp3[:, :]),
                    writes=(ysb3[:, 0:1],))
                vn = b.incl("vc")
                b.wt(nc.scalar, "vc", vn)
                _ln_tile(b, ln3[:, :], ysb3[:, :], D, lnsc)
                xt = xg3[:, 1024 * tt:1024 * tt + 1024]
                b.V(lambda v, x_=xt: v.tensor_add(ysb3[:, :], ln3[:, :], x_),
                    reads=(ln3[:, 0:1], ysb3[:, 0:1]), writes=(ysb3[:, 0:1],))
                vn = b.incl("vc")
                b.wt(nc.sync, "vc", vn)
                b.ldy(stream_a.ap()[512 * go + 128 * tt:
                                    512 * go + 128 * tt + 128, :], ysb3[:, :])
    p3.close()
    b.finish()
    tp = b.tap("p3_stream", [TOK, D])
    if tp is not None:
        b.ldw(tp, stream_a.ap())
        b.wt(nc.gpsimd, "dw")
    if PHASE_LIMIT < 3:
        st.close()
        return b

    p4 = contextlib.ExitStack()

    def sb4(name, shape, dtype):
        return p4.enter_context(nc.sbuf_tensor(name, list(shape), dtype))

    w1b = sb4("w1b", [128, 8 * W0], BF16)
    w2b = sb4("w2b", [128, 32 * 1024], BF16)
    lnxf4 = sb4("lnxf4", [128, 8 * 512], BF16)
    hfm = sb4("hfm", [128, 32 * 512], BF16)
    b2r = sb4("b2r", [1, D], BF16)
    xg4 = sb4("xg4", [128, 4 * 1024], F32)
    ln4 = sb4("ln4", [128, 1024], F32)
    ysb4 = sb4("ysb4", [128, 1024], F32)
    b.ldw(b2r[:, :], b2_row, sem="p4w")
    for half in range(2):
        b.ldw(w1b[:, 4 * W0 * half:4 * W0 * half + 4 * W0]
              .rearrange("p (c n) -> p c n", c=4),
              w1.rearrange("(c p) n -> p c n", p=128)[:, 4 * half:4 * half + 4, :],
              sem="p4w")
        b.ldw(w2b[:, 16384 * half:16384 * half + 16384]
              .rearrange("p (c n) -> p c n", c=16),
              w2.rearrange("(c p) n -> p c n", p=128)[:, 16 * half:16 * half + 16, :],
              sem="p4w")
    dw4 = b.n("p4w")
    with (nc.psum_tensor("trp4", [128, 1024], F32) as trp4,
          nc.psum_tensor("hp4a", [128, 512], F32) as hp4a,
          nc.psum_tensor("hp4b", [128, 512], F32) as hp4b,
          nc.psum_tensor("yp4", [128, 1024], F32) as yp4):
        for go in range(NGO):
            if go >= 1:
                b.wt(nc.sync, "vc")
                b.wt(nc.sync, "sc")
            for tt in range(4):
                b.ldy(xg4[:, 1024 * tt:1024 * tt + 1024],
                      stream_a.ap()[512 * go + 128 * tt:
                                    512 * go + 128 * tt + 128, :], sem="p4x")
            ldn = b.n("p4x")
            b.wt(nc.vector, "p4x", ldn)
            b.wt(nc.scalar, "p4x", ldn)
            for tt in range(4):
                xap = xg4[:, 1024 * tt:1024 * tt + 1024]
                _ln_tile(b, ln4[:, :], xap, D, lnsc)
                lnn = b.incl("vc")
                b.wt(nc.tensor, "vc", lnn)
                for c in range(8):
                    b.tr(trp4[:, 128 * c:128 * c + 128],
                                        ln4[:, 128 * c:128 * c + 128],
                                        idf[:, :])
                tmn = b.incl("tm")
                b.wt(nc.vector, "tm", tmn)
                dst = lnxf4[:, 128 * tt:]
                dstv = bass.AP(dst.tensor, dst.offset,
                               [[dst.ap[0][0], 128], [512, 8], [1, 128]])
                srcv = trp4[:, :].rearrange("p (c n) -> p c n", c=8)
                b.V(lambda v, dv=dstv, sv=srcv: v.tensor_copy(dv, sv),
                    writes=(lnxf4[:, 0:1],))
                b.incl("vc")
            lnxv = b.n("vc")
            b.wt(nc.tensor, "vc", lnxv)
            if go == 0:
                b.wt(nc.tensor, "p4w", dw4)
            hv = {}
            for hc in range(32):
                ps = hp4a if hc % 2 == 0 else hp4b
                if hc >= 2:
                    b.wt(nc.tensor, "sc", hv[hc - 2])
                for k in range(8):
                    b.mm(
                        ps[:, :],
                        w1b[:, W0 * k + 128 * hc:W0 * k + 128 * hc + 128],
                        lnxf4[:, 512 * k:512 * k + 512],
                        start=(k == 0), stop=(k == 7))
                tmn = b.incl("tm")
                b.wt(nc.scalar, "tm", tmn)
                b.SC(lambda s, p=ps, h=hc: s.activation(
                    hfm[:, 512 * h:512 * h + 512], p[:, :], AF.Relu,
                    bias=b1p[:, h:h + 1]),
                    writes=(hfm[:, 0:1],))
                hv[hc] = b.incl("sc")
            hdone = hv[31]
            for tt in range(4):
                b.wt(nc.tensor, "sc", hdone)
                b.wt(nc.tensor, "vc")
                for n2 in range(2):
                    for hc in range(32):
                        b.mm(
                            yp4[:, 512 * n2:512 * n2 + 512],
                            hfm[:, 512 * hc + 128 * tt:
                                512 * hc + 128 * tt + 128],
                            w2b[:, 1024 * hc + 512 * n2:
                                1024 * hc + 512 * n2 + 512],
                            start=(hc == 0), stop=False)
                    b.mm(
                        yp4[:, 512 * n2:512 * n2 + 512], onesb[0:1, :],
                        b2r[0:1, 512 * n2:512 * n2 + 512],
                        start=False, stop=True)
                tmn = b.incl("tm")
                b.wt(nc.vector, "tm", tmn)
                b.wt(nc.scalar, "tm", tmn)
                b.V(lambda v: v.tensor_copy(ysb4[:, :], yp4[:, :]),
                    writes=(ysb4[:, 0:1],))
                vn = b.incl("vc")
                b.wt(nc.scalar, "vc", vn)
                _ln_tile(b, ln4[:, :], ysb4[:, :], D, lnsc)
                xt = xg4[:, 1024 * tt:1024 * tt + 1024]
                b.V(lambda v, x_=xt: v.tensor_add(ysb4[:, :], ln4[:, :], x_),
                    reads=(ln4[:, 0:1], ysb4[:, 0:1]), writes=(ysb4[:, 0:1],))
                vn = b.incl("vc")
                b.wt(nc.sync, "vc", vn)
                b.ldy(stream_b.ap()[512 * go + 128 * tt:
                                    512 * go + 128 * tt + 128, :], ysb4[:, :])
    p4.close()
    b.finish()
    tp = b.tap("p4_stream", [TOK, D])
    if tp is not None:
        b.ldw(tp, stream_b.ap())
        b.wt(nc.gpsimd, "dw")
    if PHASE_LIMIT < 4:
        st.close()
        return b

    p5 = contextlib.ExitStack()

    def sb5(name, shape, dtype):
        return p5.enter_context(nc.sbuf_tensor(name, list(shape), dtype))

    lnx_all = sb5("lnx_all", [128, NTO * 1024], BF16)
    wkb = sb5("wkb", [128, 8 * 1024], BF16)
    wvb = sb5("wvb", [128, 8 * 1024], BF16)
    wqb = sb5("wqb", [128, 8 * 1024], BF16)
    wob = sb5("wob", [128, 8 * 1024], BF16)
    kfm1 = sb5("kfm1", [128, 8 * 512], BF16)
    kft = sb5("kft", [128, 4 * 1024], BF16)
    vx1 = sb5("vx1", [128, 4 * 1040], BF16)
    qfsb = sb5("qfsb", [64, 16 * 512], BF16)
    kvf = sb5("kvf", [128, 520], F32)
    kvb = sb5("kvb", [64, 16 * 65], BF16)
    atm = sb5("atm", [128, 4 * 1024], F32)
    a1fm = sb5("a1fm", [128, 8 * 512], BF16)
    xg5 = sb5("xg5", [128, 4 * 1024], F32)
    ln5 = sb5("ln5", [128, 1024], F32)
    ysb5 = sb5("ysb5", [128, 1024], F32)
    bor = sb5("bor", [1, D], BF16)
    nrec = sb5("nrec", [128, 16], F32)
    nmax = sb5("nmax", [128, 16], F32)
    kvacc = [sb5("kvacc0", [128, 520], F32), sb5("kvacc1", [128, 520], F32)]
    b.ldw(bor[:, :], bo_row, sem="p5w")
    b.ldw(wkb[:, :].rearrange("p (c n) -> p c n", c=8),
          wk1.rearrange("(c p) n -> p c n", p=128), sem="p5w")
    b.ldw(wvb[:, :].rearrange("p (c n) -> p c n", c=8),
          wv1.rearrange("(c p) n -> p c n", p=128), sem="p5w")
    b.ldw(wqb[:, :].rearrange("p (c n) -> p c n", c=8),
          wq1.rearrange("(c p) n -> p c n", p=128), sem="p5w")
    b.ldw(wob[:, :].rearrange("p (c n) -> p c n", c=8),
          wo1.rearrange("(c p) n -> p c n", p=128), sem="p5w")
    dw5 = b.n("p5w")
    vx1v = vx1[:, :].rearrange("p (t h o) -> p t h o", t=4, h=H1)
    b.V(lambda v: v.memset(vx1v[:, :, :, 64:65], 1.0), writes=(vx1[:, 0:1],))
    memv = b.incl("vc")
    with (nc.psum_tensor("trp5", [128, 1024], F32) as trp5,
          nc.psum_tensor("pp5a", [128, 512], F32) as pp5a,
          nc.psum_tensor("pp5b", [128, 512], F32) as pp5b,
          nc.psum_tensor("kv0", [128, 260], F32) as kv0,
          nc.psum_tensor("kv1", [128, 260], F32) as kv1,
          nc.psum_tensor("kv2", [128, 260], F32) as kv2,
          nc.psum_tensor("kv3", [128, 260], F32) as kv3):
        kvps = [kv0, kv1, kv2, kv3]
        for go in range(NGO):
            if go >= 1:
                b.wt(nc.sync, "vc")
                b.wt(nc.sync, "sc")
            for tt in range(4):
                b.ldy(xg5[:, 1024 * tt:1024 * tt + 1024],
                      stream_b.ap()[512 * go + 128 * tt:
                                    512 * go + 128 * tt + 128, :], sem="p5x")
            ldn = b.n("p5x")
            b.wt(nc.vector, "p5x", ldn)
            b.wt(nc.scalar, "p5x", ldn)
            if go == 0:
                b.wt(nc.tensor, "vc", memv)
            for tt in range(4):
                xap = xg5[:, 1024 * tt:1024 * tt + 1024]
                _ln_tile(b, ln5[:, :], xap, D, lnsc)
                lnn = b.incl("vc")
                b.wt(nc.tensor, "vc", lnn)
                for c in range(8):
                    b.tr(trp5[:, 128 * c:128 * c + 128],
                                        ln5[:, 128 * c:128 * c + 128],
                                        idf[:, :])
                tmn = b.incl("tm")
                b.wt(nc.vector, "tm", tmn)
                dst = lnx_all[:, 4096 * go + 128 * tt:]
                dstv = bass.AP(dst.tensor, dst.offset,
                               [[dst.ap[0][0], 128], [512, 8], [1, 128]])
                srcv = trp5[:, :].rearrange("p (c n) -> p c n", c=8)
                b.V(lambda v, dv=dstv, sv=srcv: v.tensor_copy(dv, sv),
                    writes=(lnx_all[:, 0:1],))
                b.incl("vc")
            lnxv = b.n("vc")
            base5 = 4096 * go
            b.wt(nc.tensor, "vc", lnxv)
            if go == 0:
                b.wt(nc.tensor, "p5w", dw5)
            for c in range(8):
                ps = pp5a if c % 2 == 0 else pp5b
                b.wt(nc.tensor, "vc")
                for k in range(8):
                    b.mm(
                        ps[:, :],
                        wkb[:, 1024 * k + 128 * c:1024 * k + 128 * c + 128],
                        lnx_all[:, base5 + 512 * k:base5 + 512 * k + 512],
                        start=(k == 0), stop=(k == 7))
                tmn = b.incl("tm")
                b.wt(nc.vector, "tm", tmn)
                b.V(lambda v, p=ps, c_=c: v.tensor_copy(
                    kfm1[:, 512 * c_:512 * c_ + 512], p[:, :]),
                    writes=(kfm1[:, 0:1],))
                b.incl("vc")
            for tt in range(4):
                for n2 in range(2):
                    ps = pp5a if n2 == 0 else pp5b
                    b.wt(nc.tensor, "vc")
                    for k in range(8):
                        b.mm(
                            ps[:, :],
                            lnx_all[:, base5 + 512 * k + 128 * tt:
                                    base5 + 512 * k + 128 * tt + 128],
                            wvb[:, 1024 * k + 512 * n2:
                                1024 * k + 512 * n2 + 512],
                            start=(k == 0), stop=(k == 7))
                    tmn = b.incl("tm")
                    b.wt(nc.vector, "tm", tmn)
                    dst = vx1[:, 1040 * tt + 65 * 8 * n2:]
                    dstv = bass.AP(dst.tensor, dst.offset,
                                   [[dst.ap[0][0], 128], [65, 8], [1, 64]])
                    srcv = ps[:, :].rearrange("p (h o) -> p h o", h=8)
                    b.V(lambda v, dv=dstv, sv=srcv: v.tensor_copy(dv, sv),
                        writes=(vx1[:, 0:1],))
                    b.incl("vc")
            vdone = b.n("vc")
            for tt in range(4):
                b.wt(nc.tensor, "vc", vdone)
                b.wt(nc.tensor, "sc")
                for h in range(H1):
                    ps = pp5a if h < 8 else pp5b
                    hc, r0 = h // 2, 64 * (h % 2)
                    b.mm(
                        ps[:, 64 * (h % 8):64 * (h % 8) + 64],
                        kfm1[:, 512 * hc + 128 * tt:
                             512 * hc + 128 * tt + 128][r0:r0 + 64, :],
                        wfr[r0:r0 + 64, :], start=True, stop=False,
                        tile_position=(r0, 0))
                    b.mm(
                        ps[:, 64 * (h % 8):64 * (h % 8) + 64],
                        onesb[0:1, :], bfr[0:1, :], start=False, stop=True,
                        tile_position=(0, 0))
                tmn = b.incl("tm")
                b.wt(nc.scalar, "tm", tmn)
                for (ps, off) in ((pp5a, 0), (pp5b, 512)):
                    b.SC(lambda s, p=ps, o=off, t_=tt: s.activation(
                        kft[:, 1024 * t_ + o:1024 * t_ + o + 512], p[:, :],
                        AF.Relu),
                        writes=(kft[:, 0:1],))
                b.incl("sc")
            kfdone = b.n("sc")
            b.wt(nc.tensor, "sc", kfdone)
            b.wt(nc.tensor, "vc")
            for pr in range(8):
                tn = kvps[pr // 4]
                coff = 65 * (pr % 4)
                for hh2 in range(2):
                    h = 2 * pr + hh2
                    r0 = 64 * hh2
                    for ti in range(4):
                        b.mm(
                            tn[r0:r0 + 64, coff:coff + 65],
                            kft[:, 1024 * ti + 64 * h:1024 * ti + 64 * h + 64],
                            vx1[:, 1040 * ti + 65 * h:1040 * ti + 65 * h + 65],
                            start=(ti == 0), stop=(ti == 3),
                            tile_position=(0, r0))
            kvtm = b.incl("tm")
            b.wt(nc.vector, "tm", kvtm)
            dstk = kvacc[go % 2]
            srck = kvacc[(go + 1) % 2]
            if go == 0:
                for i in range(2):
                    b.V(lambda v, i_=i, d=dstk: v.tensor_copy(
                        d[:, 260 * i_:260 * i_ + 260], kvps[i_][:, 0:260]),
                        writes=(dstk[:, 0:1],))
            else:
                for i in range(2):
                    b.V(lambda v, i_=i, d=dstk, s_=srck: v.tensor_add(
                        d[:, 260 * i_:260 * i_ + 260],
                        s_[:, 260 * i_:260 * i_ + 260], kvps[i_][:, 0:260]),
                        reads=(srck[:, 0:1],), writes=(dstk[:, 0:1],))
            b.incl("vc")
        vn = b.n("vc")
        b.wt(nc.sync, "vc", vn)
        kvv = kvacc[1][:, :].rearrange("p (pr o) -> p pr o", pr=8)
        arv = ar_in.ap().rearrange("f (pr two) o -> two f pr o", two=2)
        b.ldy(arv[0], kvv[0:64, :, :], sem="p5r")
        b.ldy(arv[1], kvv[64:128, :, :], sem="p5r")
        b.wt(nc.gpsimd, "p5r")
        cci = nc.gpsimd.collective_compute(
            "AllReduce", ALU.add,
            replica_groups=[[0, 1], [2, 3], [4, 5], [6, 7]],
            ins=[ar_in.ap().opt()], outs=[ar_out.ap().opt()])
        b.inc(cci, "cc", 1)
        b.wt(nc.gpsimd, "cc", b.n("cc"))
        b.ldw(kvb[:, :].rearrange("p (h o) -> p h o", h=H1), ar_out.ap(),
              sem="p5k")
        kvldn = b.n("p5k")

        for go in range(NGO):
            b.wt(nc.sync, "vc")
            b.wt(nc.sync, "sc")
            for tt in range(4):
                b.ldy(xg5[:, 1024 * tt:1024 * tt + 1024],
                      stream_b.ap()[512 * go + 128 * tt:
                                    512 * go + 128 * tt + 128, :], sem="p5x")
            ldn = b.n("p5x")
            base5 = 4096 * go
            for c in range(8):
                ps = pp5a if c % 2 == 0 else pp5b
                b.wt(nc.tensor, "vc")
                if go == 0 and c == 0:
                    b.wt(nc.tensor, "p5k", kvldn)
                    b.wt(nc.tensor, "sc")
                for k in range(8):
                    b.mm(
                        ps[:, :],
                        wqb[:, 1024 * k + 128 * c:1024 * k + 128 * c + 128],
                        lnx_all[:, base5 + 512 * k:base5 + 512 * k + 512],
                        start=(k == 0), stop=(k == 7))
                tmn = b.incl("tm")
                b.wt(nc.vector, "tm", tmn)
                b.V(lambda v, p=ps, c_=c: v.tensor_copy(
                    kfm1[:, 512 * c_:512 * c_ + 512], p[:, :]),
                    writes=(kfm1[:, 0:1],))
                b.incl("vc")
            qv = b.n("vc")
            b.wt(nc.tensor, "vc", qv)
            for h in range(H1):
                hc, r0 = h // 2, 64 * (h % 2)
                ps = pp5a if h % 2 == 0 else pp5b
                b.wt(nc.tensor, "sc")
                b.mm(
                    ps[0:64, :], wfr[r0:r0 + 64, :],
                    kfm1[:, 512 * hc:512 * hc + 512][r0:r0 + 64, :],
                    start=True, stop=True, tile_position=(r0, 0))
                tmn = b.incl("tm")
                b.wt(nc.scalar, "tm", tmn)
                b.SC(lambda s, p=ps, h_=h: s.activation(
                    qfsb[0:64, 512 * h_:512 * h_ + 512], p[0:64, :], AF.Relu,
                    bias=bfp[0:64, 0:1]),
                    writes=(qfsb[0:1, 0:1],))
                b.incl("sc")
            qfdone = b.n("sc")
            b.wt(nc.tensor, "sc", qfdone)
            for tt in range(4):
                b.wt(nc.tensor, "vc")
                for h in range(H1):
                    b.mm(
                        kvps[h // 4][:, 65 * (h % 4):65 * (h % 4) + 65],
                        qfsb[0:64, 512 * h + 128 * tt:512 * h + 128 * tt + 128],
                        kvb[0:64, 65 * h:65 * h + 65],
                        start=True, stop=True)
                tmn = b.incl("tm")
                b.wt(nc.vector, "tm", tmn)
                for i in range(4):
                    pv_ = kvps[i]
                    src = bass.AP(pv_, 64, [[260, 128], [65, 4]])
                    b.V(lambda v, s_=src, i_=i: v.tensor_scalar(
                        nmax[:, 4 * i_:4 * i_ + 4], s_, 1e-6, None, ALU.max),
                        writes=(nmax[:, 0:1],))
                b.V(lambda v: v.reciprocal(nrec[:, :], nmax[:, :]),
                    reads=(nmax[:, 0:1],), writes=(nrec[:, 0:1],))
                for h in range(H1):
                    pv_ = kvps[h // 4]
                    b.V(lambda v, p=pv_, h_=h, t_=tt: v.tensor_scalar(
                        atm[:, 1024 * t_ + 64 * h_:1024 * t_ + 64 * h_ + 64],
                        p[:, 65 * (h_ % 4):65 * (h_ % 4) + 64],
                        nrec[:, h_:h_ + 1], None, ALU.mult),
                        reads=(nrec[:, 0:1],), writes=(atm[:, 0:1],))
                b.incl("vc")
            for tt in range(4):
                b.wt(nc.tensor, "vc")
                for c in range(8):
                    b.tr(trp5[:, 128 * c:128 * c + 128],
                         atm[:, 1024 * tt + 128 * c:1024 * tt + 128 * c + 128],
                         idf[:, :])
                tmn = b.incl("tm")
                b.wt(nc.vector, "tm", tmn)
                dst = a1fm[:, 128 * tt:]
                dstv = bass.AP(dst.tensor, dst.offset,
                               [[dst.ap[0][0], 128], [512, 8], [1, 128]])
                srcv = trp5[:, :].rearrange("p (c n) -> p c n", c=8)
                b.V(lambda v, dv=dstv, sv=srcv: v.tensor_copy(dv, sv),
                    writes=(a1fm[:, 0:1],))
                b.incl("vc")
            a1v = b.n("vc")
            b.wt(nc.tensor, "vc", a1v)
            b.wt(nc.vector, "p5x", ldn)
            b.wt(nc.scalar, "p5x", ldn)
            for tt in range(4):
                b.wt(nc.tensor, "vc")
                for n2 in range(2):
                    for k in range(8):
                        b.mm(
                            trp5[:, 512 * n2:512 * n2 + 512],
                            a1fm[:, 512 * k + 128 * tt:
                                 512 * k + 128 * tt + 128],
                            wob[:, 1024 * k + 512 * n2:
                                1024 * k + 512 * n2 + 512],
                            start=(k == 0), stop=False)
                    b.mm(
                        trp5[:, 512 * n2:512 * n2 + 512], onesb[0:1, :],
                        bor[0:1, 512 * n2:512 * n2 + 512],
                        start=False, stop=True)
                tmn = b.incl("tm")
                b.wt(nc.vector, "tm", tmn)
                b.wt(nc.scalar, "tm", tmn)
                b.V(lambda v: v.tensor_copy(ysb5[:, :], trp5[:, :]),
                    writes=(ysb5[:, 0:1],))
                vn = b.incl("vc")
                b.wt(nc.scalar, "vc", vn)
                _ln_tile(b, ln5[:, :], ysb5[:, :], D, lnsc)
                xt = xg5[:, 1024 * tt:1024 * tt + 1024]
                b.V(lambda v, x_=xt: v.tensor_add(ysb5[:, :], ln5[:, :], x_),
                    reads=(ln5[:, 0:1], ysb5[:, 0:1]), writes=(ysb5[:, 0:1],))
                vn = b.incl("vc")
                b.wt(nc.sync, "vc", vn)
                b.ldy(stream_c.ap()[512 * go + 128 * tt:
                                    512 * go + 128 * tt + 128, :], ysb5[:, :])
    p5.close()
    b.finish()
    tp = b.tap("p5_stream", [TOK, D])
    if tp is not None:
        b.ldw(tp, stream_c.ap())
        b.wt(nc.gpsimd, "dw")
    if PHASE_LIMIT < 5:
        st.close()
        return b

    p6 = contextlib.ExitStack()

    def sb6(name, shape, dtype):
        return p6.enter_context(nc.sbuf_tensor(name, list(shape), dtype))

    lnf32 = sb6("lnf32", [128, 8 * 512], F32)
    lnxb6 = sb6("lnxb6", [128, 8 * 512], BF16)
    wg1f = sb6("wg1f", [128, 8 * G1], F32)
    wg2f = sb6("wg2f", [128, 2 * NEXP], F32)
    gsb = sb6("gsb", [128, 1024], F32)
    gates = sb6("gates", [128, NTO * NEXP], F32)
    gm1 = sb6("gm1", [128, 1], F32)
    gm2 = sb6("gm2", [128, 1], F32)
    gtmp = sb6("gtmp", [128, NEXP], F32)
    gtmp2 = sb6("gtmp2", [128, NEXP], F32)
    gsum = sb6("gsum", [128, 1], F32)
    gprob = sb6("gprob", [128, NEXP], F32)
    winb = sb6("winb", [128, 8 * EH], BF16)
    woutb = sb6("woutb", [128, 16 * 1024], BF16)
    hfm6 = sb6("hfm6", [128, 16 * 512], BF16)
    lnxg = sb6("lnxg", [128, 2 * 8 * 512], BF16)
    otmp = sb6("otmp", [128, 4 * 512], F32)
    xg6 = sb6("xg6", [128, 4 * 1024], F32)
    ln6 = sb6("ln6", [128, 1024], F32)
    ysb6 = sb6("ysb6", [128, 1024], F32)
    b.ldy(wg1f[:, :].rearrange("p (c n) -> p c n", c=8),
          wg1.rearrange("(c p) n -> p c n", p=128), sem="p6g")
    b.ldy(wg2f[:, :].rearrange("p (c n) -> p c n", c=2),
          wg2.rearrange("(c p) n -> p c n", p=128), sem="p6g")
    with (nc.psum_tensor("trp6", [128, 1024], F32) as trp6,
          nc.psum_tensor("gp6", [128, 1024], F32) as gp6,
          nc.psum_tensor("hp6a", [128, 512], F32) as hp6a,
          nc.psum_tensor("hp6b", [128, 512], F32) as hp6b,
          nc.psum_tensor("op6a", [128, 512], F32) as op6a,
          nc.psum_tensor("op6b", [128, 512], F32) as op6b):
        for go in range(NGO):
            b.wt(nc.sync, "vc")
            b.wt(nc.sync, "sc")
            for tt in range(4):
                b.ldy(xg6[:, 1024 * tt:1024 * tt + 1024],
                      stream_c.ap()[512 * go + 128 * tt:
                                    512 * go + 128 * tt + 128, :], sem="p6x")
            ldn = b.n("p6x")
            b.wt(nc.vector, "p6x", ldn)
            b.wt(nc.scalar, "p6x", ldn)
            for tt in range(4):
                xap = xg6[:, 1024 * tt:1024 * tt + 1024]
                _ln_tile(b, ln6[:, :], xap, D, lnsc)
                lnn = b.incl("vc")
                b.wt(nc.tensor, "vc", lnn)
                if go > 0 or tt > 0:
                    b.wt(nc.tensor, "sc")
                for c in range(8):
                    b.tr(trp6[:, 128 * c:128 * c + 128],
                                        ln6[:, 128 * c:128 * c + 128],
                                        idf[:, :])
                tmn = b.incl("tm")
                b.wt(nc.vector, "tm", tmn)
                b.wt(nc.scalar, "tm", tmn)
                dst = lnxb6[:, 128 * tt:]
                dstv = bass.AP(dst.tensor, dst.offset,
                               [[dst.ap[0][0], 128], [512, 8], [1, 128]])
                srcv = trp6[:, :].rearrange("p (c n) -> p c n", c=8)
                b.V(lambda v, dv=dstv, sv=srcv: v.tensor_copy(dv, sv),
                    writes=(lnxb6[:, 0:1],))
                evv = b.incl("vc")
                b.wt(nc.scalar, "vc", evv)
                dstf = lnf32[:, 128 * tt:]
                dstfv = bass.AP(dstf.tensor, dstf.offset,
                                [[dstf.ap[0][0], 128], [512, 8], [1, 128]])
                b.SC(lambda s, dv=dstfv, sv=srcv: s.copy(dv, sv),
                     writes=(lnf32[:, 0:1],))
                b.incl("sc")
            lnxv, lnxs = b.n("vc"), b.n("sc")
            b.wt(nc.sync, "vc", lnxv)
            b.ldy(lnx6.ap()[:, 4096 * go:4096 * go + 4096], lnxb6[:, :],
                  sem="p6s")
            b.wt(nc.tensor, "sc", lnxs)
            b.wt(nc.tensor, "p6g", b.n("p6g"))
            b.wt(nc.tensor, "vc")
            for m in range(2):
                for k in range(8):
                    b.mm(
                        gp6[:, 512 * m:512 * m + 512],
                        wg1f[:, G1 * k + 128 * m:G1 * k + 128 * m + 128],
                        lnf32[:, 512 * k:512 * k + 512],
                        start=(k == 0), stop=(k == 7))
            tmn = b.incl("tm")
            b.wt(nc.scalar, "tm", tmn)
            for m in range(2):
                b.SC(lambda s, m_=m: s.activation(
                    gsb[:, 512 * m_:512 * m_ + 512],
                    gp6[:, 512 * m_:512 * m_ + 512], AF.Gelu,
                    bias=bg1p[:, m_:m_ + 1]),
                    writes=(gsb[:, 0:1],))
            gsn = b.incl("sc")
            b.wt(nc.tensor, "sc", gsn)
            for tt in range(4):
                b.wt(nc.tensor, "vc")
                for m in range(2):
                    b.mm(
                        gp6[:, 0:NEXP],
                        gsb[:, 512 * m + 128 * tt:512 * m + 128 * tt + 128],
                        wg2f[:, NEXP * m:NEXP * m + NEXP],
                        start=(m == 0), stop=(m == 1))
                tmn = b.incl("tm")
                b.wt(nc.vector, "tm", tmn)
                b.V(lambda v: v.reduce_max(gm1[:, 0:1], gp6[:, 0:NEXP], AX,
                                           negate=True),
                    writes=(gm1[:, 0:1],))
                vn = b.incl("vc")
                b.wt(nc.scalar, "vc", vn)
                b.SC(lambda s: s.activation(gprob[:, :], gp6[:, 0:NEXP],
                                            AF.Exp, bias=gm1[:, 0:1],
                                            accum_out=gsum[:, 0:1]),
                     writes=(gprob[:, 0:1], gsum[:, 0:1]))
                scn = b.incl("sc")
                b.wt(nc.vector, "sc", scn)
                b.V(lambda v: v.reciprocal(gm2[:, 0:1], gsum[:, 0:1]),
                    writes=(gm2[:, 0:1],))
                b.V(lambda v: v.tensor_scalar(gtmp[:, :], gprob[:, :],
                                              gm2[:, 0:1], None, ALU.mult),
                    reads=(gm2[:, 0:1],), writes=(gtmp[:, 0:1],))
                b.V(lambda v: v.reduce_max(gm1[:, 0:1], gtmp[:, :], AX),
                    reads=(gtmp[:, 0:1],), writes=(gm1[:, 0:1],))
                b.V(lambda v: v.tensor_scalar(gtmp2[:, :], gtmp[:, :],
                                              gm1[:, 0:1], None, ALU.is_lt),
                    reads=(gm1[:, 0:1],), writes=(gtmp2[:, 0:1],))
                b.V(lambda v: v.tensor_mul(gprob[:, :], gtmp2[:, :],
                                           gtmp[:, :]),
                    reads=(gtmp2[:, 0:1],), writes=(gprob[:, 0:1],))
                b.V(lambda v: v.reduce_max(gm2[:, 0:1], gprob[:, :], AX),
                    reads=(gprob[:, 0:1],), writes=(gm2[:, 0:1],))
                b.V(lambda v: v.tensor_scalar(gtmp2[:, :], gtmp[:, :],
                                              gm2[:, 0:1], None, ALU.is_ge),
                    reads=(gm2[:, 0:1],), writes=(gtmp2[:, 0:1],))
                b.V(lambda v, t_=tt, g_=go: v.tensor_mul(
                    gates[:, NEXP * (4 * g_ + t_):NEXP * (4 * g_ + t_) + NEXP],
                    gtmp2[:, :], gtmp[:, :]),
                    reads=(gtmp2[:, 0:1],), writes=(gates[:, 0:1],))
                b.incl("vc")
        gatesv = b.n("vc")
        lnx6_dy = b.n("p6s")
        tp = b.tap("p6_gates", [128, NTO * NEXP])
        if tp is not None:
            b.wt(nc.sync, "vc", gatesv)
            b.ldy(tp, gates[:, :])

        otmp_dw = {}
        for e in range(NEXP if PHASE_LIMIT >= 6 else 0):
            b.wt(nc.gpsimd, "tm")
            b.ldw(winb[:, :].rearrange("p (c n) -> p c n", c=8),
                  w_in[e].rearrange("(c p) n -> p c n", p=128), sem="p6w")
            for half in range(2):
                b.ldw(woutb[:, 8192 * half:8192 * half + 8192]
                      .rearrange("p (c n) -> p c n", c=8),
                      w_out[e].rearrange("(c p) n -> p c n",
                                         p=128)[:, 8 * half:8 * half + 8, :],
                      sem="p6w")
            dwe = b.n("p6w")
            for go in range(NGO):
                gpar = go % 2
                b.wt(nc.sync, "tm")
                if e == 0 and go == 0:
                    b.wt(nc.sync, "p6s", lnx6_dy)
                b.ldy(lnxg[:, 4096 * gpar:4096 * gpar + 4096],
                      lnx6.ap()[:, 4096 * go:4096 * go + 4096],
                      sem=f"p6l{gpar}")
                ldn = b.n(f"p6l{gpar}")
                b.wt(nc.tensor, f"p6l{gpar}", ldn)
                if go == 0:
                    b.wt(nc.tensor, "p6w", dwe)
                hv = {}
                for hc in range(16):
                    ps = hp6a if hc % 2 == 0 else hp6b
                    if hc >= 2:
                        b.wt(nc.tensor, "sc", hv[hc - 2])
                    for k in range(8):
                        b.mm(
                            ps[:, :],
                            winb[:, EH * k + 128 * hc:EH * k + 128 * hc + 128],
                            lnxg[:, 4096 * gpar + 512 * k:
                                 4096 * gpar + 512 * k + 512],
                            start=(k == 0), stop=(k == 7))
                    tmn = b.incl("tm")
                    b.wt(nc.scalar, "tm", tmn)
                    b.SC(lambda s, p=ps, h=hc: s.activation(
                        hfm6[:, 512 * h:512 * h + 512], p[:, :], AF.Gelu),
                        writes=(hfm6[:, 0:1],))
                    hv[hc] = b.incl("sc")
                hdone = hv[15]
                b.wt(nc.tensor, "sc", hdone)
                if e == 0 and go == 0:
                    b.wt(nc.tensor, "vc", gatesv)
                for tt in range(4):
                    for n2 in range(2):
                        ps = op6a if n2 == 0 else op6b
                        b.wt(nc.tensor, "vc")
                        for hc in range(16):
                            b.mm(
                                ps[:, :],
                                hfm6[:, 512 * hc + 128 * tt:
                                     512 * hc + 128 * tt + 128],
                                woutb[:, 1024 * hc + 512 * n2:
                                      1024 * hc + 512 * n2 + 512],
                                start=(hc == 0), stop=(hc == 15))
                        tmn = b.incl("tm")
                        gt = 4 * go + tt
                        slot = (2 * gt + n2) % 4
                        b.wt(nc.vector, "tm", tmn)
                        if slot in otmp_dw:
                            b.wt(nc.vector, f"p6o{slot}", otmp_dw[slot])
                        b.V(lambda v, p=ps, sl=slot, g_=gt, e_=e:
                            v.tensor_scalar(
                                otmp[:, 512 * sl:512 * sl + 512], p[:, :],
                                gates[:, NEXP * g_ + e_:NEXP * g_ + e_ + 1],
                                None, ALU.mult),
                            writes=(otmp[:, 0:1],))
                        vn = b.incl("vc")
                        b.wt(nc.gpsimd, "vc", vn)
                        otmp_dw[slot] = b.ldw(
                            oacc_d.ap()[128 * gt:128 * gt + 128,
                                        512 * n2:512 * n2 + 512],
                            otmp[:, 512 * slot:512 * slot + 512],
                            accum=(e > 0), sem=f"p6o{slot}")
        for sl in range(4):
            b.wt(nc.sync, f"p6o{sl}")
        tp = b.tap("p6_oacc", [TOK, D])
        if tp is not None:
            b.ldw(tp, oacc_d.ap())
        for tt in range(NTO if PHASE_LIMIT >= 7 else 0):
            b.wt(nc.sync, "vc")
            b.wt(nc.sync, "sc")
            par = tt % 2
            b.ldy(xg6[:, 2048 * par:2048 * par + 1024],
                  stream_c.ap()[128 * tt:128 * tt + 128, :], sem="p6x")
            b.ldy(xg6[:, 2048 * par + 1024:2048 * par + 2048],
                  oacc_d.ap()[128 * tt:128 * tt + 128, :], sem="p6x")
            ldn = b.n("p6x")
            b.wt(nc.vector, "p6x", ldn)
            b.wt(nc.scalar, "p6x", ldn)
            oap = xg6[:, 2048 * par + 1024:2048 * par + 2048]
            _ln_tile(b, ln6[:, :], oap, D, lnsc)
            xt = xg6[:, 2048 * par:2048 * par + 1024]
            b.V(lambda v, x_=xt: v.tensor_add(ysb6[:, :], ln6[:, :], x_),
                reads=(ln6[:, 0:1],), writes=(ysb6[:, 0:1],))
            vn = b.incl("vc")
            b.wt(nc.sync, "vc", vn)
            b.ldy(out[128 * tt:128 * tt + 128, :], ysb6[:, :])
    p6.close()
    b.wt(nc.sync, "dy")
    b.wt(nc.gpsimd, "dw")
    st.close()
    return b


_NC_CACHE = {}


def _get_nc():
    key = (PHASE_LIMIT, tuple(DEV_TAPS))
    if key not in _NC_CACHE:
        _NC_CACHE[key] = build_nc()
    return _NC_CACHE[key]


def kernel(**inputs):
    f = np.float32
    x = np.ascontiguousarray(inputs["L0_attn_x"], dtype=f)
    qk_w = np.asarray(inputs["L0_attn_qk_w"], dtype=f)
    qk_b = np.asarray(inputs["L0_attn_qk_b"], dtype=f)
    v_w = np.asarray(inputs["L0_attn_v_w"], dtype=f)
    gains = [np.asarray(inputs["L0_attn_resid_gain"], dtype=f).reshape(16),
             np.asarray(inputs["L0_attn_skip_gain"], dtype=f).reshape(16),
             np.asarray(inputs["L0_attn_centre_gain"], dtype=f).reshape(16)]
    in_maps = []
    for core in range(8):
        p, hh = core // 2, core % 2
        hs = slice(512 * hh, 512 * hh + 512)
        m = {
            "x_seq": np.ascontiguousarray(x[p]),
            "x_own": np.ascontiguousarray(x[p, 2048 * hh:2048 * hh + 2048]),
            "tok_base": np.array([[2048 * hh]], dtype=np.uint32),
            "qkw_own": np.ascontiguousarray(
                np.concatenate([qk_w[:, 0:1024][:, hs],
                                qk_w[:, 1024:2048][:, hs]], axis=1)),
            "vw_own": np.ascontiguousarray(v_w[:, hs]),
            "qb_pp": np.ascontiguousarray(qk_b[0:1024][hs].reshape(4, 128).T),
            "kb_pp": np.ascontiguousarray(qk_b[1024:2048][hs].reshape(4, 128).T),
            "cprojw": np.ascontiguousarray(inputs["L0_attn_cproj_w"], dtype=f),
            "gain_r": np.ascontiguousarray(
                np.broadcast_to(gains[0][8 * hh:8 * hh + 8][None, :], (64, 8))),
            "gain_s": np.ascontiguousarray(
                np.broadcast_to(gains[1][8 * hh:8 * hh + 8][None, :], (64, 8))),
            "gain_c": np.ascontiguousarray(
                np.broadcast_to(gains[2][8 * hh:8 * hh + 8][None, :], (64, 8))),
            "w1": np.ascontiguousarray(inputs["L0_ffn_w1"], dtype=f),
            "b1_pp": np.ascontiguousarray(
                np.asarray(inputs["L0_ffn_b1"], dtype=f).reshape(32, 128).T),
            "w2": np.ascontiguousarray(inputs["L0_ffn_w2"], dtype=f),
            "b2_row": np.ascontiguousarray(
                np.asarray(inputs["L0_ffn_b2"], dtype=f)[None, :]),
            "wq1": np.ascontiguousarray(inputs["L1_attn_Wq"], dtype=f),
            "wk1": np.ascontiguousarray(inputs["L1_attn_Wk"], dtype=f),
            "wv1": np.ascontiguousarray(inputs["L1_attn_Wv"], dtype=f),
            "wo1": np.ascontiguousarray(inputs["L1_attn_Wo"], dtype=f),
            "bo_row": np.ascontiguousarray(
                np.asarray(inputs["L1_attn_bo"], dtype=f)[None, :]),
            "wf_rep": np.ascontiguousarray(
                np.concatenate([np.asarray(inputs["L1_attn_Wf"], dtype=f)] * 2,
                               axis=0)),
            "bf_row": np.ascontiguousarray(
                np.asarray(inputs["L1_attn_bf"], dtype=f)[None, :]),
            "bf_pp": np.ascontiguousarray(
                np.asarray(inputs["L1_attn_bf"], dtype=f)[:, None]),
            "wg1": np.ascontiguousarray(inputs["L1_ffn_wg1"], dtype=f),
            "bg1_pp": np.ascontiguousarray(
                np.asarray(inputs["L1_ffn_bg1"], dtype=f).reshape(2, 128).T),
            "wg2": np.ascontiguousarray(inputs["L1_ffn_wg2"], dtype=f),
            "w_in": np.ascontiguousarray(inputs["L1_ffn_w_in"], dtype=f),
            "w_out": np.ascontiguousarray(inputs["L1_ffn_w_out"], dtype=f),
        }
        in_maps.append(m)

    b = _get_nc()
    res = run_bass_kernel_spmd(b.nc, in_maps, core_ids=list(range(8)))
    if DEV_TAPS:
        kernel.last_results = res.results
    outp = np.empty((4, S, D), dtype=np.float32)
    for core in range(8):
        p, hh = core // 2, core % 2
        outp[p, 2048 * hh:2048 * hh + 2048] = res.results[core]["out"]
    return outp


# version 02
# speedup vs baseline: 1.1170x; 1.1170x over previous
import contextlib
import numpy as np
import ml_dtypes
import concourse.bass as bass
import concourse.mybir as mybir
from concourse.bass_utils import run_bass_kernel_spmd

F32 = mybir.dt.float32
BF16 = mybir.dt.bfloat16
U32 = mybir.dt.uint32
AF = mybir.ActivationFunctionType
ALU = mybir.AluOpType
AX = mybir.AxisListType.X
BF = ml_dtypes.bfloat16
ds = bass.ds

S, D = 4096, 1024
NT, NG = 32, 8
TOK, NTO, NGO = 2048, 16, 4
HD, HOWN, NPAIR = 64, 8, 4
H1, FDIM = 16, 64
W0 = 4096
NEXP, EH, G1 = 8, 2048, 256
EPS = 1e-5
NEGBIG = -30000.0

DEV_TAPS = ()
PHASE_LIMIT = 99


class Build:
    def __init__(self):
        nc = bass.Bass("TRN2", target_bir_lowering=False)
        self.nc = nc
        self.cnt = {}
        self.sems = {}
        for name in ("dy", "dw", "tm", "sc", "vc", "gp", "cc",
                     "px0", "px1", "pt0", "pt1", "sag", "pa0", "pa1",
                     "p3x", "p4x", "p4w", "p5x", "p5w", "p5r", "p5k",
                     "p6x", "p6g", "p6s", "p6l0", "p6l1", "p6w",
                     "p6o0", "p6o1", "p6o2", "p6o3", "pcp", "p0w", "pnb", "p6w0", "p6w1"):
            self.sems[name] = nc.alloc_semaphore(name)
            self.cnt[name] = 0
        self.dma_owner = {}
        self.vdirty = set()
        self.sdirty = set()
        self.last = {}
        self.taps = {}

    def inc(self, inst, name, k=1):
        inst.then_inc(self.sems[name], k)
        self.cnt[name] += k
        return self.cnt[name]

    def mm(self, *a, **kw):
        inst = self.nc.tensor.matmul(*a, **kw)
        self.last["tm"] = inst
        return inst

    def tr(self, *a, **kw):
        inst = self.nc.tensor.transpose(*a, **kw)
        self.last["tm"] = inst
        return inst

    def incl(self, name):
        return self.inc(self.last[name], name)

    def n(self, name):
        return self.cnt[name]

    def wt(self, eng, name, val=None):
        if val is None:
            val = self.cnt[name]
        if val > 0:
            eng.wait_ge(self.sems[name], val)

    def ldy(self, out, in_, sem="dy"):
        self.dma_owner[sem] = "sync"
        return self.inc(self.nc.sync.dma_start(out=out, in_=in_), sem, 16)

    def ldw(self, out, in_, accum=False, sem="dw"):
        kw = {"accum_op": ALU.add} if accum else {}
        self.dma_owner[sem] = "gpsimd"
        return self.inc(self.nc.gpsimd.dma_start(out=out, in_=in_, **kw), sem, 16)

    def V(self, emit, reads=(), writes=()):
        rn = {ap.tensor.name for ap in reads}
        if rn & self.vdirty:
            self.nc.vector.drain()
            self.vdirty = set()
        inst = emit(self.nc.vector)
        self.vdirty |= {ap.tensor.name for ap in writes}
        self.last["vc"] = inst
        return inst

    def SC(self, emit, reads=(), writes=()):
        rn = {ap.tensor.name for ap in reads}
        if rn & self.sdirty:
            self.nc.scalar.drain()
            self.sdirty = set()
        inst = emit(self.nc.scalar)
        self.sdirty |= {ap.tensor.name for ap in writes}
        self.last["sc"] = inst
        return inst

    def finish(self):
        for sem, owner in self.dma_owner.items():
            eng = self.nc.sync if owner == "sync" else self.nc.gpsimd
            self.wt(eng, sem)
        self.nc.all_engine_barrier()
        self.vdirty = set()
        self.sdirty = set()

    def tap(self, name, shape, dtype=F32):
        if name in DEV_TAPS:
            t = self.nc.dram_tensor("tap_" + name, list(shape), dtype,
                                    kind="ExternalOutput")
            self.taps[name] = t.ap()
            return self.taps[name]
        return None


def _ln_tile(b, out_sb, x_ap, W, sc):
    nc = b.nc
    b.V(lambda v: v.reduce_sum(sc["ssum"][:, 0:1], x_ap, AX),
        reads=(x_ap,), writes=(sc["ssum"][:, 0:1],))
    b.SC(lambda s: s.activation(sc["sqscr"][:, 0:W], x_ap, AF.Square,
                                accum_out=sc["sqs"][:, 0:1]),
         reads=(x_ap,), writes=(sc["sqscr"][:, 0:W], sc["sqs"][:, 0:1]))
    b.incl("sc")
    b.wt(nc.vector, "sc")
    b.V(lambda v: v.tensor_scalar(sc["mean"][:, 0:1], sc["ssum"][:, 0:1],
                                  1.0 / W, None, ALU.mult),
        reads=(sc["ssum"][:, 0:1],), writes=(sc["mean"][:, 0:1],))
    b.V(lambda v: v.tensor_scalar(sc["msq"][:, 0:1], sc["sqs"][:, 0:1],
                                  1.0 / W, None, ALU.mult),
        reads=(sc["sqs"][:, 0:1],), writes=(sc["msq"][:, 0:1],))
    b.V(lambda v: v.tensor_mul(sc["mm"][:, 0:1], sc["mean"][:, 0:1],
                               sc["mean"][:, 0:1]),
        reads=(sc["mean"][:, 0:1],), writes=(sc["mm"][:, 0:1],))
    b.V(lambda v: v.tensor_sub(sc["var"][:, 0:1], sc["msq"][:, 0:1],
                               sc["mm"][:, 0:1]),
        reads=(sc["msq"][:, 0:1], sc["mm"][:, 0:1]), writes=(sc["var"][:, 0:1],))
    b.V(lambda v: v.tensor_scalar(sc["vare"][:, 0:1], sc["var"][:, 0:1],
                                  EPS, None, ALU.add),
        reads=(sc["var"][:, 0:1],), writes=(sc["vare"][:, 0:1],))
    b.V(lambda v: v.reciprocal(sc["rv"][:, 0:1], sc["vare"][:, 0:1]),
        reads=(sc["vare"][:, 0:1],), writes=(sc["rv"][:, 0:1],))
    b.incl("vc")
    b.wt(nc.scalar, "vc")
    b.SC(lambda s: s.activation(sc["rstd"][:, 0:1], sc["rv"][:, 0:1], AF.Sqrt),
         reads=(sc["rv"][:, 0:1],), writes=(sc["rstd"][:, 0:1],))
    b.incl("sc")
    b.wt(nc.vector, "sc")
    b.V(lambda v: v.scalar_tensor_tensor(sc["nmean"][:, 0:1], sc["mean"][:, 0:1],
                                         -1.0, sc["rstd"][:, 0:1],
                                         ALU.mult, ALU.mult),
        reads=(sc["mean"][:, 0:1],), writes=(sc["nmean"][:, 0:1],))
    b.V(lambda v: v.tensor_scalar(out_sb, x_ap, sc["rstd"][:, 0:1],
                                  sc["nmean"][:, 0:1], ALU.mult, ALU.add),
        reads=(x_ap, sc["nmean"][:, 0:1]), writes=(out_sb,))
    return b.n("vc"), b.n("sc")


def _make_consts():
    idb = np.eye(128, dtype=BF)
    idf = np.eye(128, dtype=np.float32)
    tri = np.where(np.arange(128)[None, :] > np.arange(128)[:, None],
                   np.float32(NEGBIG), np.float32(0.0)).astype(BF)
    ones_b = np.ones((128, 128), dtype=BF)
    ones_f = np.ones((1, 128), dtype=np.float32)
    sel2 = np.zeros((128, 2), dtype=BF)
    sel2[0:64, 0] = 1
    sel2[64:128, 1] = 1
    inv = (1.0 / (np.arange(S, dtype=np.float64) + 1.0)).astype(BF)[None, :]
    triinv = np.zeros((NG, 4, 128, 512), dtype=BF)
    for g in range(NG):
        qg = 512 * g + np.arange(512)
        for c in range(4):
            kg = 128 * (4 * g + c) + np.arange(128)
            m = (kg[:, None] <= qg[None, :]).astype(np.float64)
            triinv[g, c] = (m / (qg[None, :] + 1.0)).astype(BF)
    return idb, idf, tri, ones_b, ones_f, sel2, inv, triinv


def _strided3(ap_base, pstep, a, b_, c, d):
    return bass.AP(ap_base.tensor, ap_base.offset, [[pstep, 128], [a, b_], [c, d]])


def build_nc():
    b = Build()
    nc = b.nc
    st = contextlib.ExitStack()

    def inp(name, shape, dtype=F32):
        return nc.dram_tensor(name, list(shape), dtype, kind="ExternalInput").ap()

    x_seq = inp("x_seq", [S, D])
    x_own = inp("x_own", [TOK, D])
    tok_base = inp("tok_base", [1, 1], U32)
    qkw_own = inp("qkw_own", [D, 2 * 512])
    vw_own = inp("vw_own", [D, 512])
    qb_pp = inp("qb_pp", [128, 4])
    kb_pp = inp("kb_pp", [128, 4])
    cprojw = inp("cprojw", [D, D])
    gain_r = inp("gain_r", [64, 8])
    gain_s = inp("gain_s", [64, 8])
    gain_c = inp("gain_c", [64, 8])
    w1 = inp("w1", [D, W0])
    b1_pp = inp("b1_pp", [128, 32])
    w2 = inp("w2", [W0, D])
    b2_row = inp("b2_row", [1, D])
    wq1 = inp("wq1", [D, D])
    wk1 = inp("wk1", [D, D])
    wv1 = inp("wv1", [D, D])
    wo1 = inp("wo1", [D, D])
    bo_row = inp("bo_row", [1, D])
    wf_rep = inp("wf_rep", [128, FDIM])
    bf_row = inp("bf_row", [1, FDIM])
    bf_pp = inp("bf_pp", [64, 1])
    wg1 = inp("wg1", [D, G1])
    bg1_pp = inp("bg1_pp", [128, 2])
    wg2 = inp("wg2", [G1, NEXP])
    w_in = inp("w_in", [NEXP, D, EH])
    w_out = inp("w_out", [NEXP, EH, D])

    out = nc.dram_tensor("out", [TOK, D], F32, kind="ExternalOutput").ap()

    idb_h, idf_h, tri_h, onesb_h, onesf_h, sel2_h, inv_h, triinv_h = _make_consts()
    c_idb = nc.inline_tensor(idb_h, name="c_idb").ap()
    c_idf = nc.inline_tensor(idf_h, name="c_idf").ap()
    c_tri = nc.inline_tensor(tri_h, name="c_tri").ap()
    c_onesb = nc.inline_tensor(onesb_h, name="c_onesb").ap()
    c_onesf = nc.inline_tensor(onesf_h, name="c_onesf").ap()
    c_sel2 = nc.inline_tensor(sel2_h, name="c_sel2").ap()
    c_inv = nc.inline_tensor(inv_h, name="c_inv").ap()
    c_triinv = nc.inline_tensor(triinv_h, name="c_triinv").ap()

    ag_in = nc.dram_tensor("ag_in", [512, S], BF16)
    ag_out = nc.dram_tensor("ag_out", [2, 512, S], BF16)
    ar_in = nc.dram_tensor("ar_in", [64, H1, 65], F32)
    ar_out = nc.dram_tensor("ar_out", [64, H1, 65], F32)
    stream_a = nc.dram_tensor("stream_a", [TOK, D], F32)
    stream_b = nc.dram_tensor("stream_b", [TOK, D], F32)
    stream_c = nc.dram_tensor("stream_c", [TOK, D], F32)
    lnx6 = nc.dram_tensor("lnx6", [128, NTO * 1024], BF16)
    oacc_d = nc.dram_tensor("oacc_d", [TOK, D], F32)

    def sb(name, shape, dtype):
        return st.enter_context(nc.sbuf_tensor(name, list(shape), dtype))

    idb = sb("idb", [128, 128], BF16)
    idf = sb("idf", [128, 128], F32)
    tri = sb("tri", [128, 128], BF16)
    onesb = sb("onesb", [128, 128], BF16)
    onesf = sb("onesf", [1, 128], F32)
    sel2 = sb("sel2", [128, 2], BF16)
    negb_rep = sb("negb_rep", [128, 8], F32)
    grep = sb("grep", [64, 8], F32)
    srep = sb("srep", [64, 8], F32)
    craw = sb("craw", [64, 8], F32)
    cnrep = sb("cnrep", [64, 8], F32)
    qbp = sb("qbp", [128, 4], F32)
    kbp = sb("kbp", [128, 4], F32)
    b1p = sb("b1p", [128, 32], F32)
    wfr = sb("wfr", [128, FDIM], BF16)
    bfr = sb("bfr", [1, FDIM], BF16)
    bfp = sb("bfp", [64, 1], F32)
    bg1p = sb("bg1p", [128, 2], F32)
    tokb = sb("tokb", [1, 1], U32)
    lnsc = {k: sb("lnsc_" + k, [128, 1], F32)
            for k in ("ssum", "sqs", "mean", "msq", "mm", "var", "vare", "rv",
                      "rstd", "nmean")}
    lnsc["sqscr"] = sb("lnsc_sqscr", [128, 1024], BF16)
    q2all = sb("q2all", [2, 32], F32)
    k2all = sb("k2all", [2, 32], F32)
    nb2 = sb("nb2", [2, 4], F32)
    nb2b = sb("nb2b", [2, 4], F32)
    nbr1 = sb("nbr1", [1, 4], F32)
    tiny = sb("tiny", [2, 8], F32)

    b.ldy(idb[:, :], c_idb)
    b.ldy(idf[:, :], c_idf)
    b.ldy(tri[:, :], c_tri)
    b.ldy(onesb[:, :], c_onesb)
    b.ldy(onesf[:, :], c_onesf)
    b.ldy(sel2[:, :], c_sel2)
    b.ldy(grep[:, :], gain_r)
    b.ldy(srep[:, :], gain_s)
    b.ldy(craw[:, :], gain_c)
    b.ldy(qbp[:, :], qb_pp)
    b.ldy(kbp[:, :], kb_pp)
    b.ldy(b1p[:, :], b1_pp)
    b.ldy(bfp[:, :], bf_pp)
    b.ldy(bg1p[:, :], bg1_pp)
    b.ldy(tokb[:, :], tok_base)
    b.ldw(wfr[:, :], wf_rep)
    b.ldw(bfr[:, :], bf_row)
    setup_dy = b.n("dy")
    b.wt(nc.vector, "dy", setup_dy)
    b.V(lambda v: v.tensor_scalar(cnrep[:, :], craw[:, :], -1.0, None, ALU.mult),
        reads=(craw[:, 0:1],), writes=(cnrep[:, 0:1],))
    b.incl("vc")

    _sid = {}
    _sid["ph0"] = nc.enter_named_scope("ph0", False)[0]
    p01 = contextlib.ExitStack()

    def sb1(name, shape, dtype):
        return p01.enter_context(nc.sbuf_tensor(name, list(shape), dtype))

    invp = sb1("invp", [1, S], BF16)
    carryT = sb1("carryT", [1, NG * 512], BF16)
    qfm = sb1("qfm", [128, 4 * S], BF16)
    kfm = sb1("kfm", [128, 4 * S], BF16)
    vext = sb1("vext", [128, NT * 520], BF16)
    probs = sb1("probs", [128, 4 * 512], BF16)
    triinv_s = sb1("triinv_s", [128, 2 * 2048], BF16)
    recip_sb = sb1("recip_sb", [1, 512], F32)
    pvs_sb = sb1("pvs_sb", [128, 512], F32)
    vfms = sb1("vfms", [128, 512], BF16)
    skv = sb1("skv", [128, 512], F32)
    e1 = sb1("e1", [128, 512], F32)
    e2 = sb1("e2", [128, 512], F32)
    outpair = sb1("outpair", [128, 512], BF16)
    gs_sb = sb1("gs_sb", [64, 8], F32)
    cacc = [sb1("cacc0", [64, 8], F32), sb1("cacc1", [64, 8], F32)]
    ct_sb = sb1("ct_sb", [8, 64], F32)

    p0t = contextlib.ExitStack()

    def sb0(name, shape, dtype):
        return p0t.enter_context(nc.sbuf_tensor(name, list(shape), dtype))

    qkwb = sb0("qkwb", [128, 8 * 1024], BF16)
    vwb = sb0("vwb", [128, 8 * 512], BF16)

    xa = sb0("xa", [128, 2 * 1024], F32)
    lnt = sb0("lnt", [128, 2 * 1024], F32)
    lnxfm = sb0("lnxfm", [128, 2 * 4096], BF16)
    sqscr2 = sb0("sqscr2", [128, 512], BF16)

    b.ldy(invp[:, :], c_inv)
    b.ldw(qkwb[:, :].rearrange("p (c n) -> p c n", c=8),
          qkw_own.rearrange("(c p) n -> p c n", p=128), sem="p0w")
    b.ldw(vwb[:, :].rearrange("p (c n) -> p c n", c=8),
          vw_own.rearrange("(c p) n -> p c n", p=128), sem="p0w")
    dw_w0 = b.n("p0w")
    b.V(lambda v: v.memset(carryT[0:1, 0:512], 0.0), writes=(carryT[0:1, 0:1],))
    vxv = vext[:, :].rearrange("p (t h o) -> p t h o", t=NT, h=HOWN)
    b.V(lambda v: v.memset(vxv[:, :, :, 64:65], 1.0), writes=(vext[:, 0:1],))
    vready0 = b.incl("vc")

    with (nc.psum_tensor("trps", [128, 1024], F32) as trps,
          nc.psum_tensor("mmA", [128, 512], F32) as mmA,
          nc.psum_tensor("mmB", [128, 512], F32) as mmB,
          nc.psum_tensor("vps", [128, 512], F32) as vps,
          nc.psum_tensor("q2ps", [2, 512], F32) as q2ps,
          nc.psum_tensor("gsps", [65, 8], F32) as gsps,
          nc.psum_tensor("ctps", [8, 64], F32) as ctps):
        lnv, lnsct, trv = {}, {}, {}
        last_sel2_tm = 0
        b.wt(nc.tensor, "dy", setup_dy)
        for t in range(NT):
            par = t % 2
            gpar = (t // 4) % 2
            if t >= 2:
                b.wt(nc.sync, "vc", trv[t - 2])
                b.wt(nc.sync, "sc", lnsct[t - 2])
            ldn = b.ldy(xa[:, 1024 * par:1024 * par + 1024],
                        x_seq[128 * t:128 * t + 128, :], sem=f"px{par}")
            b.wt(nc.vector, f"px{par}", ldn)
            b.wt(nc.scalar, f"px{par}", ldn)
            xap = xa[:, 1024 * par:1024 * par + 1024]
            lnap = lnt[:, 1024 * par:1024 * par + 1024]
            _v, _s = _ln_tile(b, lnap, xap, D, lnsc)
            lnv[t] = b.incl("vc")
            lnsct[t] = _s
            b.wt(nc.tensor, "vc", lnv[t])
            if t >= 1:
                b.wt(nc.tensor, "vc", trv[t - 1])
            for c in range(8):
                b.tr(trps[:, 128 * c:128 * c + 128],
                                    lnap[:, 128 * c:128 * c + 128], idf[:, :])
            tmn = b.incl("tm")
            b.wt(nc.vector, "tm", tmn)
            dst = lnxfm[:, 4096 * gpar + 128 * (t % 4):]
            dstv = bass.AP(dst.tensor, dst.offset,
                           [[dst.ap[0][0], 128], [512, 8], [1, 128]])
            srcv = trps[:, :].rearrange("p (c n) -> p c n", c=8)
            b.V(lambda v, dv=dstv, sv=srcv: v.tensor_copy(dv, sv),
                writes=(lnxfm[:, 0:1],))
            trv[t] = b.incl("vc")

            if t % 4 == 3:
                gf = t // 4
                gpar2 = gf % 2
                rhs_base = 4096 * gpar2
                b.wt(nc.tensor, "vc", trv[t])
                if gf == 0:
                    b.wt(nc.tensor, "p0w", dw_w0)
                    b.wt(nc.tensor, "vc", vready0)
                for which, wofs, dst_fm, biasp, scl in (
                        ("q", 0, qfm, qbp, 0.125), ("k", 512, kfm, kbp, None)):
                    for pc in range(4):
                        ps = mmA if pc % 2 == 0 else mmB
                        b.wt(nc.tensor, "vc")
                        for k in range(8):
                            b.mm(
                                ps[:, :],
                                qkwb[:, 1024 * k + wofs + 128 * pc:
                                     1024 * k + wofs + 128 * pc + 128],
                                lnxfm[:, rhs_base + 512 * k:
                                      rhs_base + 512 * k + 512],
                                start=(k == 0), stop=(k == 7))
                        tmn = b.incl("tm")
                        b.wt(nc.vector, "tm", tmn)
                        dstap = dst_fm[:, 4096 * pc + 512 * gf:
                                       4096 * pc + 512 * gf + 512]
                        if scl is None:
                            b.V(lambda v, o=dstap, p=ps, bp=biasp, c=pc:
                                v.tensor_scalar(o, p[:, :], bp[:, c:c + 1],
                                                None, ALU.add),
                                writes=(dst_fm[:, 0:1],))
                        else:
                            b.V(lambda v, o=dstap, p=ps, bp=biasp, c=pc, s_=scl:
                                v.tensor_scalar(o, p[:, :], bp[:, c:c + 1],
                                                s_, ALU.add, ALU.mult),
                                writes=(dst_fm[:, 0:1],))
                        vn = b.incl("vc")
                        b.wt(nc.scalar, "vc", vn)
                        b.wt(nc.scalar, "tm", last_sel2_tm)
                        b.SC(lambda s, i=dstap: s.activation(
                            sqscr2[:, 0:512], i, AF.Square),
                            writes=(sqscr2[:, 0:1],))
                        scn = b.incl("sc")
                        b.wt(nc.tensor, "sc", scn)
                        b.mm(q2ps[:, :], sel2[:, :],
                                         sqscr2[:, 0:512], start=True, stop=True)
                        last_sel2_tm = tmn = b.incl("tm")
                        b.wt(nc.vector, "tm", tmn)
                        dcol = (q2all if which == "q" else k2all)
                        b.V(lambda v, d=dcol, col=8 * pc + gf: v.reduce_max(
                            d[:, col:col + 1], q2ps[:, :], AX),
                            writes=(dcol[:, 0:1],))
                        b.incl("vc")
                for tt in range(4 * gf, 4 * gf + 4):
                    b.wt(nc.tensor, "vc")
                    for k in range(8):
                        b.mm(
                            vps[:, :],
                            lnxfm[:, rhs_base + 512 * k + 128 * (tt % 4):
                                  rhs_base + 512 * k + 128 * (tt % 4) + 128],
                            vwb[:, 512 * k:512 * k + 512],
                            start=(k == 0), stop=(k == 7))
                    tmn = b.incl("tm")
                    b.wt(nc.vector, "tm", tmn)
                    dst = vext[:, 520 * tt:]
                    dstv = bass.AP(dst.tensor, dst.offset,
                                   [[dst.ap[0][0], 128], [65, 8], [1, 64]])
                    srcv = vps[:, :].rearrange("p (h o) -> p h o", h=8)
                    b.V(lambda v, dv=dstv, sv=srcv: v.tensor_copy(dv, sv),
                        writes=(vext[:, 0:1],))
                    vpse = b.incl("vc")
                b.wt(nc.tensor, "vc", vpse)
                for lh in range(8):
                    for ti in range(4):
                        tt2 = 4 * gf + ti
                        b.mm(gsps[:, lh:lh + 1],
                             vext[:, 520 * tt2 + 65 * lh:
                                  520 * tt2 + 65 * lh + 65],
                             onesb[:, 0:1],
                             start=(ti == 0), stop=(ti == 3))
                tmn = b.incl("tm")
                b.wt(nc.vector, "tm", tmn)
                b.V(lambda v: v.tensor_copy(gs_sb[:, :], gsps[0:64, :]),
                    writes=(gs_sb[:, 0:1],))
                vn = b.incl("vc")
                src_c = cacc[gf % 2]
                dst_c = cacc[1 - gf % 2]
                if gf > 0:
                    b.wt(nc.tensor, "vc", vn)
                    b.tr(ctps[:, :], src_c[:, 0:8],
                                        idf[0:64, 0:64])
                    tmn = b.incl("tm")
                    b.wt(nc.vector, "tm", tmn)
                    b.wt(nc.vector, "dw")
                    b.V(lambda v: v.tensor_copy(ct_sb[:, :], ctps[0:8, 0:64]),
                        writes=(ct_sb[0:1, 0:1],))
                    vn2 = b.incl("vc")
                    b.wt(nc.gpsimd, "vc", vn2)
                    b.ldw(carryT[0:1, 512 * gf:512 * gf + 512], ct_sb[:, :])
                    b.V(lambda v, d=dst_c, s_=src_c: v.tensor_add(
                        d[:, :], s_[:, :], gs_sb[:, :]),
                        reads=(gs_sb[:, 0:1],), writes=(dst_c[:, 0:1],))
                else:
                    b.V(lambda v, d=dst_c: v.tensor_copy(d[:, :], gs_sb[:, :]),
                        reads=(gs_sb[:, 0:1],), writes=(dst_c[:, 0:1],))
                b.incl("vc")

        for c in range(4):
            for d_, a_ in ((q2all, 0), (k2all, 1)):
                src = bass.AP(d_, 8 * c, [[32, 2], [1, 8]])
                b.V(lambda v, s_=src, aa=a_, cc=c: v.reduce_max(
                    tiny[:, 4 * aa + cc:4 * aa + cc + 1], s_, AX),
                    reads=(d_[:, 0:1],), writes=(tiny[:, 0:1],))
        b.V(lambda v: v.tensor_mul(nb2[:, :], tiny[:, 0:4], tiny[:, 4:8]),
            reads=(tiny[:, 0:1],), writes=(nb2[:, 0:1],))
        vn = b.incl("vc")
        b.wt(nc.scalar, "vc", vn)
        b.SC(lambda s: s.activation(nb2b[:, :], nb2[:, :], AF.Sqrt),
             reads=(nb2[:, 0:1],), writes=(nb2b[:, 0:1],))
        scn = b.incl("sc")
        b.wt(nc.vector, "sc", scn)
        b.V(lambda v: v.tensor_scalar(nb2[:, :], nb2b[:, :], -1.0, None,
                                      ALU.mult),
            reads=(nb2b[:, 0:1],), writes=(nb2[:, 0:1],))
        vn = b.incl("vc")
        b.wt(nc.sync, "vc", vn)
        nbl = b.ldy(nbr1[:, :], nb2[1:2, :], sem="pnb")
        b.wt(nc.tensor, "pnb", nbl)
        b.wt(nc.tensor, "vc", vn)
        b.mm(mmA[:, 0:4], onesf[0:1, :], nb2[0:1, :],
                         start=True, stop=True)
        b.mm(mmA[:, 4:8], onesf[0:1, :], nbr1[0:1, :],
                         start=True, stop=True)
        tmn = b.incl("tm")
        b.wt(nc.vector, "tm", tmn)
        dstv = bass.AP(negb_rep, 0, [[8, 128], [1, 2], [2, 4]])
        srcv = mmA[:, 0:8].rearrange("p (r c) -> p r c", r=2)
        b.V(lambda v: v.tensor_copy(dstv, srcv), writes=(negb_rep[:, 0:1],))
        negb_v = b.incl("vc")

        for nm, t_, w in (("p0_qfm", qfm, 4 * S), ("p0_kfm", kfm, 4 * S),
                          ("p0_vext", vext, NT * 520)):
            tp = b.tap(nm, [128, w], BF16)
            if tp is not None:
                b.wt(nc.sync, "vc")
                b.ldy(tp, t_[:, :])
        tp = b.tap("p0_negb", [128, 8])
        if tp is not None:
            b.wt(nc.sync, "vc")
            b.ldy(tp, negb_rep[:, :])
        tp = b.tap("p0_carryT", [1, NG * 512], BF16)
        if tp is not None:
            b.wt(nc.sync, "dw")
            b.ldy(tp, carryT[:, :])

    p0t.close()
    if PHASE_LIMIT < 1:
        p01.close()
        b.finish()
        st.close()
        return b

    nc.leave_named_scope("ph0", _sid["ph0"], False)
    _sid["ph1"] = nc.enter_named_scope("ph1", False)[0]
    with (nc.psum_tensor("sA0", [128, 512], F32) as sA0,
          nc.psum_tensor("sA1", [128, 512], F32) as sA1,
          nc.psum_tensor("sB0", [128, 512], F32) as sB0,
          nc.psum_tensor("sB1", [128, 512], F32) as sB1,
          nc.psum_tensor("pvA", [65, 512], F32) as pvA,
          nc.psum_tensor("pvB", [65, 512], F32) as pvB,
          nc.psum_tensor("cmps", [64, 512], F32) as cmps,
          nc.psum_tensor("vtr", [64, 512], BF16) as vtr):
        sbanks = [(sA0, sB0), (sA1, sB1)]
        b.wt(nc.tensor, "dw")
        trild = {0: b.ldy(triinv_s[:, 0:2048].rearrange("p (c q) -> p c q", c=4),
                          c_triinv[0].rearrange("c k q -> k c q"),
                          sem="pt0")}
        prev_ep_v = negb_v
        prev_ep_tm = 0
        last_ag = 0
        for g in range(NG):
            nj = 4 * g + 4
            gpar = g % 2
            if g + 1 < NG:
                b.wt(nc.sync, "tm", prev_ep_tm)
                trild[g + 1] = b.ldy(
                    triinv_s[:, 2048 * (1 - gpar):2048 * (1 - gpar) + 2048]
                    .rearrange("p (c q) -> p c q", c=4),
                    c_triinv[g + 1].rearrange("c k q -> k c q"),
                    sem=f"pt{(g + 1) % 2}")
            for pc in range(NPAIR):
                lhA, lhB = 2 * pc, 2 * pc + 1
                exp_cnt, pv_cnt, s_cnt = {}, {}, {}
                for j in range(nj):
                    c2 = j - 4 * g
                    qoff = 128 * c2 if c2 > 0 else 0
                    N = 512 - qoff
                    sA, sB = sbanks[j % 2]
                    if j >= 2:
                        b.wt(nc.tensor, "sc", exp_cnt[j - 2])
                    elif j == 0:
                        b.wt(nc.tensor, "sc")
                        b.wt(nc.tensor, "vc", prev_ep_v)
                        if g == 0 and pc == 0:
                            b.wt(nc.tensor, "pt0", trild[0])
                    diag = c2 >= 0
                    for (sps, r0) in ((sA, 0), (sB, 64)):
                        b.mm(
                            sps[:, qoff:512],
                            kfm[:, 4096 * pc + 128 * j:
                                4096 * pc + 128 * j + 128][r0:r0 + 64, :],
                            qfm[:, 4096 * pc + 512 * g + qoff:
                                4096 * pc + 512 * g + 512][r0:r0 + 64, :],
                            start=True, stop=not diag, tile_position=(r0, 0))
                        if diag:
                            b.mm(
                                sps[:, qoff:qoff + 128], tri[:, :], idb[:, :],
                                start=False, stop=True, tile_position=(0, 0))
                    s_cnt[j] = b.incl("tm")
                    b.wt(nc.scalar, "tm", s_cnt[j])
                    if j >= 2:
                        b.wt(nc.scalar, "tm", pv_cnt[j - 2])
                    for (sps, lh, po) in ((sA, lhA, 0), (sB, lhB, 512)):
                        b.SC(lambda s, ss=sps, l=lh, p=po + 1024 * (j % 2),
                             q=qoff, n=N:
                             s.activation(probs[:, p + q:p + q + n],
                                          ss[:, q:512], AF.Exp,
                                          bias=negb_rep[:, l:l + 1]),
                             writes=(probs[:, 0:1],))
                    exp_cnt[j] = b.incl("sc")
                    b.wt(nc.tensor, "sc", exp_cnt[j])
                    for (pv, lh, po) in ((pvA, lhA, 0), (pvB, lhB, 512)):
                        b.mm(
                            pv[:, qoff:512],
                            vext[:, 520 * j + 65 * lh:520 * j + 65 * lh + 65],
                            probs[:, po + 1024 * (j % 2) + qoff:
                                  po + 1024 * (j % 2) + 512],
                            start=(j == 0), stop=(j == nj - 1))
                    pv_cnt[j] = b.incl("tm")
                sweep_tm = pv_cnt[nj - 1]

                b.wt(nc.vector, "sag", last_ag)
                for (pv, lh, r0) in ((pvA, lhA, 0), (pvB, lhB, 64)):
                    b.wt(nc.tensor, "tm", sweep_tm)
                    if r0 != 0:
                        b.wt(nc.tensor, "vc")
                    for c2 in range(4):
                        jj = 4 * g + c2
                        b.tr(
                            vtr[:, 128 * c2:128 * c2 + 128],
                            vext[:, 520 * jj + 65 * lh:520 * jj + 65 * lh + 64],
                            idb[:, :])
                    tmn = b.incl("tm")
                    b.wt(nc.vector, "tm", tmn)
                    b.V(lambda v: v.tensor_copy(vfms[0:64, 0:512],
                                                vtr[0:64, :]),
                        writes=(vfms[0:1, 0:1],))
                    b.V(lambda v, p=pv: v.reciprocal(recip_sb[0:1, :],
                                                     p[64:65, :]),
                        writes=(recip_sb[0:1, 0:1],))
                    vn = b.incl("vc")
                    b.wt(nc.tensor, "vc", vn)
                    b.mm(cmps[0:64, :], onesf[0:1, 0:64],
                         recip_sb[0:1, :], start=True, stop=True)
                    rtm = b.incl("tm")
                    b.wt(nc.scalar, "tm", sweep_tm)
                    b.wt(nc.scalar, "vc", prev_ep_v)
                    b.SC(lambda s, p=pv: s.copy(pvs_sb[0:64, :], p[0:64, :]),
                         writes=(pvs_sb[0:1, 0:1],))
                    scn = b.incl("sc")
                    b.wt(nc.vector, "tm", rtm)
                    b.wt(nc.vector, "sc", scn)
                    b.V(lambda v: v.tensor_mul(e1[0:64, :], pvs_sb[0:64, :],
                                               cmps[0:64, :]),
                        reads=(pvs_sb[0:1, 0:1],), writes=(e1[0:1, 0:1],))
                    b.V(lambda v, l=lh: v.tensor_scalar(
                        skv[0:64, :], vfms[0:64, 0:512], srep[0:64, l:l + 1],
                        None, ALU.mult),
                        reads=(vfms[0:1, 0:1],), writes=(skv[0:1, 0:1],))
                    b.V(lambda v, l=lh: v.scalar_tensor_tensor(
                        e2[0:64, :], e1[0:64, :], grep[0:64, l:l + 1],
                        skv[0:64, :], ALU.mult, ALU.add),
                        reads=(e1[0:1, 0:1], skv[0:1, 0:1]),
                        writes=(e2[0:1, 0:1],))
                    e1v = b.incl("vc")
                    b.wt(nc.tensor, "vc", e1v)
                    if g > 0:
                        b.wt(nc.tensor, f"pt{g % 2}", trild[g])
                        b.mm(
                            cmps[:, :],
                            carryT[0:1, 512 * g + 64 * lh:
                                   512 * g + 64 * lh + 64],
                            invp[0:1, 512 * g:512 * g + 512],
                            start=True, stop=False)
                    for c2 in range(4):
                        jj = 4 * g + c2
                        b.mm(
                            cmps[:, :],
                            vext[:, 520 * jj + 65 * lh:520 * jj + 65 * lh + 64],
                            triinv_s[:, 2048 * gpar + 512 * c2:
                                     2048 * gpar + 512 * c2 + 512],
                            start=(c2 == 0 and g == 0), stop=(c2 == 3))
                    tmn = b.incl("tm")
                    b.wt(nc.vector, "tm", tmn)
                    b.V(lambda v, l=lh, r=r0: v.scalar_tensor_tensor(
                        outpair[r:r + 64, :], cmps[0:64, :],
                        cnrep[0:64, l:l + 1], e2[0:64, :], ALU.mult, ALU.add),
                        reads=(e2[0:1, 0:1],), writes=(outpair[0:1, 0:1],))
                    prev_ep_v = b.incl("vc")
                    prev_ep_tm = b.n("tm")
                b.wt(nc.sync, "vc", prev_ep_v)
                last_ag = b.ldy(
                    ag_in.ap()[128 * pc:128 * pc + 128, 512 * g:512 * g + 512],
                    outpair[:, :], sem="sag")

        tp = b.tap("p1_ag_in", [512, S], BF16)
        if tp is not None:
            b.wt(nc.gpsimd, "sag")
            b.ldw(tp, ag_in.ap())

    p01.close()
    b.finish()
    if PHASE_LIMIT < 2:
        st.close()
        return b

    nc.leave_named_scope("ph1", _sid["ph1"], False)
    _sid["ph3"] = nc.enter_named_scope("ph3", False)[0]
    b.wt(nc.gpsimd, "sag")
    cci = nc.gpsimd.collective_compute(
        "AllGather", ALU.bypass,
        replica_groups=[[0, 1], [2, 3], [4, 5], [6, 7]],
        ins=[ag_in.ap().opt()], outs=[ag_out.ap().opt()])
    b.inc(cci, "cc", 1)

    w4s = contextlib.ExitStack()
    w1b = w4s.enter_context(nc.sbuf_tensor("w1b", [128, 8 * W0], BF16))
    w2b = w4s.enter_context(nc.sbuf_tensor("w2b", [128, 32 * 1024], BF16))
    b2r = w4s.enter_context(nc.sbuf_tensor("b2r", [1, D], BF16))
    b.ldw(b2r[:, :], b2_row, sem="p4w")
    for half in range(2):
        b.ldw(w1b[:, 4 * W0 * half:4 * W0 * half + 4 * W0]
              .rearrange("p (c n) -> p c n", c=4),
              w1.rearrange("(c p) n -> p c n", p=128)[:, 4 * half:4 * half + 4, :],
              sem="p4w")
        b.ldw(w2b[:, 16384 * half:16384 * half + 16384]
              .rearrange("p (c n) -> p c n", c=16),
              w2.rearrange("(c p) n -> p c n", p=128)[:, 16 * half:16 * half + 16, :],
              sem="p4w")
    dw4 = b.n("p4w")

    p3 = contextlib.ExitStack()

    def sb3(name, shape, dtype):
        return p3.enter_context(nc.sbuf_tensor(name, list(shape), dtype))

    cpw = sb3("cpw", [128, 8 * 1024], BF16)
    afm = sb3("afm", [128, 2 * 4096], BF16)
    xg3 = sb3("xg3", [128, 4 * 1024], F32)
    ysb3 = sb3("ysb3", [128, 1024], F32)
    ln3 = sb3("ln3", [128, 1024], F32)
    b.ldw(cpw[:, :].rearrange("p (c n) -> p c n", c=8),
          cprojw.rearrange("(c p) n -> p c n", p=128), sem="pcp")
    dw3 = b.n("pcp")
    tokv = nc.sync.value_load(tokb[0:1, 0:1])
    b.wt(nc.sync, "cc", b.n("cc"))
    with nc.psum_tensor("yp3", [128, 1024], F32) as yp3:
        grp_tm = {}
        for go in range(NGO):
            gpar = go % 2
            if go >= 2:
                b.wt(nc.sync, "tm", grp_tm[go - 2])
            src_ap = bass.AP(ag_out.ap().tensor, tokv + 512 * go,
                             [[S, 128], [512 * S, 2], [128 * S, 4], [1, 512]])
            dsta = afm[:, 4096 * gpar:4096 * gpar + 4096]
            dst_ap = bass.AP(dsta.tensor, dsta.offset,
                             [[dsta.ap[0][0], 128], [2048, 2], [512, 4],
                              [1, 512]])
            b.ldy(dst_ap, src_ap, sem=f"pa{gpar}")
            if go >= 1:
                b.wt(nc.sync, "vc")
                b.wt(nc.sync, "sc")
            for tt in range(4):
                b.ldy(xg3[:, 1024 * tt:1024 * tt + 1024],
                      x_own[512 * go + 128 * tt:512 * go + 128 * tt + 128, :],
                      sem="p3x")
            ldn = b.n("p3x")
            for tt in range(4):
                b.wt(nc.tensor, f"pa{gpar}")
                if go == 0 and tt == 0:
                    b.wt(nc.tensor, "pcp", dw3)
                b.wt(nc.tensor, "vc")
                for k in range(8):
                    for n2 in range(2):
                        b.mm(
                            yp3[:, 512 * n2:512 * n2 + 512],
                            afm[:, 4096 * gpar + 512 * k + 128 * tt:
                                4096 * gpar + 512 * k + 128 * tt + 128],
                            cpw[:, 1024 * k + 512 * n2:
                                1024 * k + 512 * n2 + 512],
                            start=(k == 0), stop=(k == 7))
                grp_tm[go] = tmn = b.incl("tm")
                b.wt(nc.vector, "tm", tmn)
                b.wt(nc.vector, "p3x", ldn)
                b.V(lambda v: v.tensor_copy(ysb3[:, :], yp3[:, :]),
                    writes=(ysb3[:, 0:1],))
                vn = b.incl("vc")
                b.wt(nc.scalar, "vc", vn)
                _ln_tile(b, ln3[:, :], ysb3[:, :], D, lnsc)
                xt = xg3[:, 1024 * tt:1024 * tt + 1024]
                b.V(lambda v, x_=xt: v.tensor_add(ysb3[:, :], ln3[:, :], x_),
                    reads=(ln3[:, 0:1], ysb3[:, 0:1]), writes=(ysb3[:, 0:1],))
                vn = b.incl("vc")
                b.wt(nc.sync, "vc", vn)
                b.ldy(stream_a.ap()[512 * go + 128 * tt:
                                    512 * go + 128 * tt + 128, :], ysb3[:, :])
    p3.close()
    b.finish()
    tp = b.tap("p3_stream", [TOK, D])
    if tp is not None:
        b.ldw(tp, stream_a.ap())
        b.wt(nc.gpsimd, "dw")
    if PHASE_LIMIT < 3:
        st.close()
        return b

    nc.leave_named_scope("ph3", _sid["ph3"], False)
    _sid["ph4"] = nc.enter_named_scope("ph4", False)[0]
    p4 = contextlib.ExitStack()

    def sb4(name, shape, dtype):
        return p4.enter_context(nc.sbuf_tensor(name, list(shape), dtype))

    lnxf4 = sb4("lnxf4", [128, 8 * 512], BF16)
    hfm = sb4("hfm", [128, 32 * 512], BF16)
    xg4 = sb4("xg4", [128, 4 * 1024], F32)
    ln4 = sb4("ln4", [128, 1024], F32)
    ysb4 = sb4("ysb4", [128, 1024], F32)
    with (nc.psum_tensor("trp4", [128, 1024], F32) as trp4,
          nc.psum_tensor("hp4a", [128, 512], F32) as hp4a,
          nc.psum_tensor("hp4b", [128, 512], F32) as hp4b,
          nc.psum_tensor("yp4", [128, 1024], F32) as yp4):
        for go in range(NGO):
            if go >= 1:
                b.wt(nc.sync, "vc")
                b.wt(nc.sync, "sc")
            for tt in range(4):
                b.ldy(xg4[:, 1024 * tt:1024 * tt + 1024],
                      stream_a.ap()[512 * go + 128 * tt:
                                    512 * go + 128 * tt + 128, :], sem="p4x")
            ldn = b.n("p4x")
            b.wt(nc.vector, "p4x", ldn)
            b.wt(nc.scalar, "p4x", ldn)
            for tt in range(4):
                xap = xg4[:, 1024 * tt:1024 * tt + 1024]
                _ln_tile(b, ln4[:, :], xap, D, lnsc)
                lnn = b.incl("vc")
                b.wt(nc.tensor, "vc", lnn)
                for c in range(8):
                    b.tr(trp4[:, 128 * c:128 * c + 128],
                                        ln4[:, 128 * c:128 * c + 128],
                                        idf[:, :])
                tmn = b.incl("tm")
                b.wt(nc.vector, "tm", tmn)
                dst = lnxf4[:, 128 * tt:]
                dstv = bass.AP(dst.tensor, dst.offset,
                               [[dst.ap[0][0], 128], [512, 8], [1, 128]])
                srcv = trp4[:, :].rearrange("p (c n) -> p c n", c=8)
                b.V(lambda v, dv=dstv, sv=srcv: v.tensor_copy(dv, sv),
                    writes=(lnxf4[:, 0:1],))
                b.incl("vc")
            lnxv = b.n("vc")
            b.wt(nc.tensor, "vc", lnxv)
            if go == 0:
                b.wt(nc.tensor, "p4w", dw4)
            hv = {}
            for hc in range(32):
                ps = hp4a if hc % 2 == 0 else hp4b
                if hc >= 2:
                    b.wt(nc.tensor, "sc", hv[hc - 2])
                for k in range(8):
                    b.mm(
                        ps[:, :],
                        w1b[:, W0 * k + 128 * hc:W0 * k + 128 * hc + 128],
                        lnxf4[:, 512 * k:512 * k + 512],
                        start=(k == 0), stop=(k == 7))
                tmn = b.incl("tm")
                b.wt(nc.scalar, "tm", tmn)
                b.SC(lambda s, p=ps, h=hc: s.activation(
                    hfm[:, 512 * h:512 * h + 512], p[:, :], AF.Relu,
                    bias=b1p[:, h:h + 1]),
                    writes=(hfm[:, 0:1],))
                hv[hc] = b.incl("sc")
            hdone = hv[31]
            for tt in range(4):
                b.wt(nc.tensor, "sc", hdone)
                b.wt(nc.tensor, "vc")
                for hc in range(32):
                    for n2 in range(2):
                        b.mm(
                            yp4[:, 512 * n2:512 * n2 + 512],
                            hfm[:, 512 * hc + 128 * tt:
                                512 * hc + 128 * tt + 128],
                            w2b[:, 1024 * hc + 512 * n2:
                                1024 * hc + 512 * n2 + 512],
                            start=(hc == 0), stop=False)
                for n2 in range(2):
                    b.mm(
                        yp4[:, 512 * n2:512 * n2 + 512], onesb[0:1, :],
                        b2r[0:1, 512 * n2:512 * n2 + 512],
                        start=False, stop=True)
                tmn = b.incl("tm")
                b.wt(nc.vector, "tm", tmn)
                b.wt(nc.scalar, "tm", tmn)
                b.V(lambda v: v.tensor_copy(ysb4[:, :], yp4[:, :]),
                    writes=(ysb4[:, 0:1],))
                vn = b.incl("vc")
                b.wt(nc.scalar, "vc", vn)
                _ln_tile(b, ln4[:, :], ysb4[:, :], D, lnsc)
                xt = xg4[:, 1024 * tt:1024 * tt + 1024]
                b.V(lambda v, x_=xt: v.tensor_add(ysb4[:, :], ln4[:, :], x_),
                    reads=(ln4[:, 0:1], ysb4[:, 0:1]), writes=(ysb4[:, 0:1],))
                vn = b.incl("vc")
                b.wt(nc.sync, "vc", vn)
                b.ldy(stream_b.ap()[512 * go + 128 * tt:
                                    512 * go + 128 * tt + 128, :], ysb4[:, :])
    p4.close()
    w4s.close()
    b.finish()
    tp = b.tap("p4_stream", [TOK, D])
    if tp is not None:
        b.ldw(tp, stream_b.ap())
        b.wt(nc.gpsimd, "dw")
    if PHASE_LIMIT < 4:
        st.close()
        return b

    nc.leave_named_scope("ph4", _sid["ph4"], False)
    _sid["ph5"] = nc.enter_named_scope("ph5", False)[0]
    p5 = contextlib.ExitStack()

    def sb5(name, shape, dtype):
        return p5.enter_context(nc.sbuf_tensor(name, list(shape), dtype))

    lnx_all = sb5("lnx_all", [128, NTO * 1024], BF16)
    wkb = sb5("wkb", [128, 8 * 1024], BF16)
    wvb = sb5("wvb", [128, 8 * 1024], BF16)
    wqb = sb5("wqb", [128, 8 * 1024], BF16)
    wob = sb5("wob", [128, 8 * 1024], BF16)
    kfm1 = sb5("kfm1", [128, 8 * 512], BF16)
    kft = sb5("kft", [128, 4 * 1024], BF16)
    vx1 = sb5("vx1", [128, 4 * 1040], BF16)
    qfsb = sb5("qfsb", [64, 16 * 512], BF16)
    kvf = sb5("kvf", [128, 520], F32)
    kvb = sb5("kvb", [64, 16 * 65], BF16)
    atm = sb5("atm", [128, 4 * 1024], F32)
    a1fm = sb5("a1fm", [128, 8 * 512], BF16)
    xg5 = sb5("xg5", [128, 4 * 1024], F32)
    ln5 = sb5("ln5", [128, 1024], F32)
    ysb5 = sb5("ysb5", [128, 1024], F32)
    bor = sb5("bor", [1, D], BF16)
    nrec = sb5("nrec", [128, 16], F32)
    nmax = sb5("nmax", [128, 16], F32)
    kvacc = [sb5("kvacc0", [128, 520], F32), sb5("kvacc1", [128, 520], F32)]
    b.ldw(bor[:, :], bo_row, sem="p5w")
    b.ldw(wkb[:, :].rearrange("p (c n) -> p c n", c=8),
          wk1.rearrange("(c p) n -> p c n", p=128), sem="p5w")
    b.ldw(wvb[:, :].rearrange("p (c n) -> p c n", c=8),
          wv1.rearrange("(c p) n -> p c n", p=128), sem="p5w")
    b.ldw(wqb[:, :].rearrange("p (c n) -> p c n", c=8),
          wq1.rearrange("(c p) n -> p c n", p=128), sem="p5w")
    b.ldw(wob[:, :].rearrange("p (c n) -> p c n", c=8),
          wo1.rearrange("(c p) n -> p c n", p=128), sem="p5w")
    dw5 = b.n("p5w")
    vx1v = vx1[:, :].rearrange("p (t h o) -> p t h o", t=4, h=H1)
    b.V(lambda v: v.memset(vx1v[:, :, :, 64:65], 1.0), writes=(vx1[:, 0:1],))
    memv = b.incl("vc")
    with (nc.psum_tensor("trp5", [128, 1024], F32) as trp5,
          nc.psum_tensor("pp5a", [128, 512], F32) as pp5a,
          nc.psum_tensor("pp5b", [128, 512], F32) as pp5b,
          nc.psum_tensor("kv0", [128, 260], F32) as kv0,
          nc.psum_tensor("kv1", [128, 260], F32) as kv1,
          nc.psum_tensor("kv2", [128, 260], F32) as kv2,
          nc.psum_tensor("kv3", [128, 260], F32) as kv3):
        kvps = [kv0, kv1, kv2, kv3]
        for go in range(NGO):
            if go >= 1:
                b.wt(nc.sync, "vc")
                b.wt(nc.sync, "sc")
            for tt in range(4):
                b.ldy(xg5[:, 1024 * tt:1024 * tt + 1024],
                      stream_b.ap()[512 * go + 128 * tt:
                                    512 * go + 128 * tt + 128, :], sem="p5x")
            ldn = b.n("p5x")
            b.wt(nc.vector, "p5x", ldn)
            b.wt(nc.scalar, "p5x", ldn)
            if go == 0:
                b.wt(nc.tensor, "vc", memv)
            for tt in range(4):
                xap = xg5[:, 1024 * tt:1024 * tt + 1024]
                _ln_tile(b, ln5[:, :], xap, D, lnsc)
                lnn = b.incl("vc")
                b.wt(nc.tensor, "vc", lnn)
                for c in range(8):
                    b.tr(trp5[:, 128 * c:128 * c + 128],
                                        ln5[:, 128 * c:128 * c + 128],
                                        idf[:, :])
                tmn = b.incl("tm")
                b.wt(nc.vector, "tm", tmn)
                dst = lnx_all[:, 4096 * go + 128 * tt:]
                dstv = bass.AP(dst.tensor, dst.offset,
                               [[dst.ap[0][0], 128], [512, 8], [1, 128]])
                srcv = trp5[:, :].rearrange("p (c n) -> p c n", c=8)
                b.V(lambda v, dv=dstv, sv=srcv: v.tensor_copy(dv, sv),
                    writes=(lnx_all[:, 0:1],))
                b.incl("vc")
            lnxv = b.n("vc")
            base5 = 4096 * go
            b.wt(nc.tensor, "vc", lnxv)
            if go == 0:
                b.wt(nc.tensor, "p5w", dw5)
            for c in range(8):
                ps = pp5a if c % 2 == 0 else pp5b
                b.wt(nc.tensor, "vc")
                for k in range(8):
                    b.mm(
                        ps[:, :],
                        wkb[:, 1024 * k + 128 * c:1024 * k + 128 * c + 128],
                        lnx_all[:, base5 + 512 * k:base5 + 512 * k + 512],
                        start=(k == 0), stop=(k == 7))
                tmn = b.incl("tm")
                b.wt(nc.vector, "tm", tmn)
                b.V(lambda v, p=ps, c_=c: v.tensor_copy(
                    kfm1[:, 512 * c_:512 * c_ + 512], p[:, :]),
                    writes=(kfm1[:, 0:1],))
                b.incl("vc")
            for tt in range(4):
                b.wt(nc.tensor, "vc")
                for k in range(8):
                    for n2 in range(2):
                        ps = pp5a if n2 == 0 else pp5b
                        b.mm(
                            ps[:, :],
                            lnx_all[:, base5 + 512 * k + 128 * tt:
                                    base5 + 512 * k + 128 * tt + 128],
                            wvb[:, 1024 * k + 512 * n2:
                                1024 * k + 512 * n2 + 512],
                            start=(k == 0), stop=(k == 7))
                tmn = b.incl("tm")
                b.wt(nc.vector, "tm", tmn)
                for n2 in range(2):
                    ps = pp5a if n2 == 0 else pp5b
                    dst = vx1[:, 1040 * tt + 65 * 8 * n2:]
                    dstv = bass.AP(dst.tensor, dst.offset,
                                   [[dst.ap[0][0], 128], [65, 8], [1, 64]])
                    srcv = ps[:, :].rearrange("p (h o) -> p h o", h=8)
                    b.V(lambda v, dv=dstv, sv=srcv: v.tensor_copy(dv, sv),
                        writes=(vx1[:, 0:1],))
                b.incl("vc")
            vdone = b.n("vc")
            for tt in range(4):
                b.wt(nc.tensor, "vc", vdone)
                b.wt(nc.tensor, "sc")
                for h in range(H1):
                    ps = pp5a if h < 8 else pp5b
                    hc, r0 = h // 2, 64 * (h % 2)
                    b.mm(
                        ps[:, 64 * (h % 8):64 * (h % 8) + 64],
                        kfm1[:, 512 * hc + 128 * tt:
                             512 * hc + 128 * tt + 128][r0:r0 + 64, :],
                        wfr[r0:r0 + 64, :], start=True, stop=False,
                        tile_position=(r0, 0))
                    b.mm(
                        ps[:, 64 * (h % 8):64 * (h % 8) + 64],
                        onesb[0:1, :], bfr[0:1, :], start=False, stop=True,
                        tile_position=(0, 0))
                tmn = b.incl("tm")
                b.wt(nc.scalar, "tm", tmn)
                for (ps, off) in ((pp5a, 0), (pp5b, 512)):
                    b.SC(lambda s, p=ps, o=off, t_=tt: s.activation(
                        kft[:, 1024 * t_ + o:1024 * t_ + o + 512], p[:, :],
                        AF.Relu),
                        writes=(kft[:, 0:1],))
                b.incl("sc")
            kfdone = b.n("sc")
            b.wt(nc.tensor, "sc", kfdone)
            b.wt(nc.tensor, "vc")
            for pr in range(8):
                tn = kvps[pr // 4]
                coff = 65 * (pr % 4)
                for hh2 in range(2):
                    h = 2 * pr + hh2
                    r0 = 64 * hh2
                    for ti in range(4):
                        b.mm(
                            tn[r0:r0 + 64, coff:coff + 65],
                            kft[:, 1024 * ti + 64 * h:1024 * ti + 64 * h + 64],
                            vx1[:, 1040 * ti + 65 * h:1040 * ti + 65 * h + 65],
                            start=(ti == 0), stop=(ti == 3),
                            tile_position=(0, r0))
            kvtm = b.incl("tm")
            b.wt(nc.vector, "tm", kvtm)
            dstk = kvacc[go % 2]
            srck = kvacc[(go + 1) % 2]
            if go == 0:
                for i in range(2):
                    b.V(lambda v, i_=i, d=dstk: v.tensor_copy(
                        d[:, 260 * i_:260 * i_ + 260], kvps[i_][:, 0:260]),
                        writes=(dstk[:, 0:1],))
            else:
                for i in range(2):
                    b.V(lambda v, i_=i, d=dstk, s_=srck: v.tensor_add(
                        d[:, 260 * i_:260 * i_ + 260],
                        s_[:, 260 * i_:260 * i_ + 260], kvps[i_][:, 0:260]),
                        reads=(srck[:, 0:1],), writes=(dstk[:, 0:1],))
            b.incl("vc")
        vn = b.n("vc")
        b.wt(nc.sync, "vc", vn)
        kvv = kvacc[1][:, :].rearrange("p (pr o) -> p pr o", pr=8)
        arv = ar_in.ap().rearrange("f (pr two) o -> two f pr o", two=2)
        b.ldy(arv[0], kvv[0:64, :, :], sem="p5r")
        b.ldy(arv[1], kvv[64:128, :, :], sem="p5r")
        b.wt(nc.gpsimd, "p5r")
        cci = nc.gpsimd.collective_compute(
            "AllReduce", ALU.add,
            replica_groups=[[0, 1], [2, 3], [4, 5], [6, 7]],
            ins=[ar_in.ap().opt()], outs=[ar_out.ap().opt()])
        b.inc(cci, "cc", 1)
        b.wt(nc.gpsimd, "cc", b.n("cc"))
        b.ldw(kvb[:, :].rearrange("p (h o) -> p h o", h=H1), ar_out.ap(),
              sem="p5k")
        kvldn = b.n("p5k")

        for go in range(NGO):
            b.wt(nc.sync, "vc")
            b.wt(nc.sync, "sc")
            for tt in range(4):
                b.ldy(xg5[:, 1024 * tt:1024 * tt + 1024],
                      stream_b.ap()[512 * go + 128 * tt:
                                    512 * go + 128 * tt + 128, :], sem="p5x")
            ldn = b.n("p5x")
            base5 = 4096 * go
            for c in range(8):
                ps = pp5a if c % 2 == 0 else pp5b
                b.wt(nc.tensor, "vc")
                if go == 0 and c == 0:
                    b.wt(nc.tensor, "p5k", kvldn)
                    b.wt(nc.tensor, "sc")
                for k in range(8):
                    b.mm(
                        ps[:, :],
                        wqb[:, 1024 * k + 128 * c:1024 * k + 128 * c + 128],
                        lnx_all[:, base5 + 512 * k:base5 + 512 * k + 512],
                        start=(k == 0), stop=(k == 7))
                tmn = b.incl("tm")
                b.wt(nc.vector, "tm", tmn)
                b.V(lambda v, p=ps, c_=c: v.tensor_copy(
                    kfm1[:, 512 * c_:512 * c_ + 512], p[:, :]),
                    writes=(kfm1[:, 0:1],))
                b.incl("vc")
            qv = b.n("vc")
            b.wt(nc.tensor, "vc", qv)
            for h in range(H1):
                hc, r0 = h // 2, 64 * (h % 2)
                ps = pp5a if h % 2 == 0 else pp5b
                b.wt(nc.tensor, "sc")
                b.mm(
                    ps[0:64, :], wfr[r0:r0 + 64, :],
                    kfm1[:, 512 * hc:512 * hc + 512][r0:r0 + 64, :],
                    start=True, stop=True, tile_position=(r0, 0))
                tmn = b.incl("tm")
                b.wt(nc.scalar, "tm", tmn)
                b.SC(lambda s, p=ps, h_=h: s.activation(
                    qfsb[0:64, 512 * h_:512 * h_ + 512], p[0:64, :], AF.Relu,
                    bias=bfp[0:64, 0:1]),
                    writes=(qfsb[0:1, 0:1],))
                b.incl("sc")
            qfdone = b.n("sc")
            b.wt(nc.tensor, "sc", qfdone)
            for tt in range(4):
                b.wt(nc.tensor, "vc")
                for h in range(H1):
                    b.mm(
                        kvps[h // 4][:, 65 * (h % 4):65 * (h % 4) + 65],
                        qfsb[0:64, 512 * h + 128 * tt:512 * h + 128 * tt + 128],
                        kvb[0:64, 65 * h:65 * h + 65],
                        start=True, stop=True)
                tmn = b.incl("tm")
                b.wt(nc.vector, "tm", tmn)
                for i in range(4):
                    pv_ = kvps[i]
                    src = bass.AP(pv_, 64, [[260, 128], [65, 4]])
                    b.V(lambda v, s_=src, i_=i: v.tensor_scalar(
                        nmax[:, 4 * i_:4 * i_ + 4], s_, 1e-6, None, ALU.max),
                        writes=(nmax[:, 0:1],))
                b.V(lambda v: v.reciprocal(nrec[:, :], nmax[:, :]),
                    reads=(nmax[:, 0:1],), writes=(nrec[:, 0:1],))
                for h in range(H1):
                    pv_ = kvps[h // 4]
                    b.V(lambda v, p=pv_, h_=h, t_=tt: v.tensor_scalar(
                        atm[:, 1024 * t_ + 64 * h_:1024 * t_ + 64 * h_ + 64],
                        p[:, 65 * (h_ % 4):65 * (h_ % 4) + 64],
                        nrec[:, h_:h_ + 1], None, ALU.mult),
                        reads=(nrec[:, 0:1],), writes=(atm[:, 0:1],))
                b.incl("vc")
            for tt in range(4):
                b.wt(nc.tensor, "vc")
                for c in range(8):
                    b.tr(trp5[:, 128 * c:128 * c + 128],
                         atm[:, 1024 * tt + 128 * c:1024 * tt + 128 * c + 128],
                         idf[:, :])
                tmn = b.incl("tm")
                b.wt(nc.vector, "tm", tmn)
                dst = a1fm[:, 128 * tt:]
                dstv = bass.AP(dst.tensor, dst.offset,
                               [[dst.ap[0][0], 128], [512, 8], [1, 128]])
                srcv = trp5[:, :].rearrange("p (c n) -> p c n", c=8)
                b.V(lambda v, dv=dstv, sv=srcv: v.tensor_copy(dv, sv),
                    writes=(a1fm[:, 0:1],))
                b.incl("vc")
            a1v = b.n("vc")
            b.wt(nc.tensor, "vc", a1v)
            b.wt(nc.vector, "p5x", ldn)
            b.wt(nc.scalar, "p5x", ldn)
            for tt in range(4):
                b.wt(nc.tensor, "vc")
                for k in range(8):
                    for n2 in range(2):
                        b.mm(
                            trp5[:, 512 * n2:512 * n2 + 512],
                            a1fm[:, 512 * k + 128 * tt:
                                 512 * k + 128 * tt + 128],
                            wob[:, 1024 * k + 512 * n2:
                                1024 * k + 512 * n2 + 512],
                            start=(k == 0), stop=False)
                for n2 in range(2):
                    b.mm(
                        trp5[:, 512 * n2:512 * n2 + 512], onesb[0:1, :],
                        bor[0:1, 512 * n2:512 * n2 + 512],
                        start=False, stop=True)
                tmn = b.incl("tm")
                b.wt(nc.vector, "tm", tmn)
                b.wt(nc.scalar, "tm", tmn)
                b.V(lambda v: v.tensor_copy(ysb5[:, :], trp5[:, :]),
                    writes=(ysb5[:, 0:1],))
                vn = b.incl("vc")
                b.wt(nc.scalar, "vc", vn)
                _ln_tile(b, ln5[:, :], ysb5[:, :], D, lnsc)
                xt = xg5[:, 1024 * tt:1024 * tt + 1024]
                b.V(lambda v, x_=xt: v.tensor_add(ysb5[:, :], ln5[:, :], x_),
                    reads=(ln5[:, 0:1], ysb5[:, 0:1]), writes=(ysb5[:, 0:1],))
                vn = b.incl("vc")
                b.wt(nc.sync, "vc", vn)
                b.ldy(stream_c.ap()[512 * go + 128 * tt:
                                    512 * go + 128 * tt + 128, :], ysb5[:, :])
    p5.close()
    b.finish()
    tp = b.tap("p5_stream", [TOK, D])
    if tp is not None:
        b.ldw(tp, stream_c.ap())
        b.wt(nc.gpsimd, "dw")
    if PHASE_LIMIT < 5:
        st.close()
        return b

    nc.leave_named_scope("ph5", _sid["ph5"], False)
    _sid["ph6"] = nc.enter_named_scope("ph6", False)[0]
    p6 = contextlib.ExitStack()

    def sb6(name, shape, dtype):
        return p6.enter_context(nc.sbuf_tensor(name, list(shape), dtype))

    gates = sb6("gates", [128, NTO * NEXP], F32)
    xg6 = sb6("xg6", [128, 4 * 1024], F32)
    ln6 = sb6("ln6", [128, 1024], F32)
    ysb6 = sb6("ysb6", [128, 1024], F32)

    p6a = contextlib.ExitStack()

    def sb6a(name, shape, dtype):
        return p6a.enter_context(nc.sbuf_tensor(name, list(shape), dtype))

    lnf32 = sb6a("lnf32", [128, 8 * 512], F32)
    lnxb6 = sb6a("lnxb6", [128, 8 * 512], BF16)
    wg1f = sb6a("wg1f", [128, 8 * G1], F32)
    wg2f = sb6a("wg2f", [128, 2 * NEXP], F32)
    gsb = sb6a("gsb", [128, 1024], F32)
    gm1 = sb6a("gm1", [128, 1], F32)
    gm2 = sb6a("gm2", [128, 1], F32)
    gtmp = sb6a("gtmp", [128, NEXP], F32)
    gtmp2 = sb6a("gtmp2", [128, NEXP], F32)
    gsum = sb6a("gsum", [128, 1], F32)
    gprob = sb6a("gprob", [128, NEXP], F32)
    b.ldy(wg1f[:, :].rearrange("p (c n) -> p c n", c=8),
          wg1.rearrange("(c p) n -> p c n", p=128), sem="p6g")
    b.ldy(wg2f[:, :].rearrange("p (c n) -> p c n", c=2),
          wg2.rearrange("(c p) n -> p c n", p=128), sem="p6g")
    with (nc.psum_tensor("trp6", [128, 1024], F32) as trp6,
          nc.psum_tensor("gp6", [128, 1024], F32) as gp6):
        for go in range(NGO):
            b.wt(nc.sync, "vc")
            b.wt(nc.sync, "sc")
            for tt in range(4):
                b.ldy(xg6[:, 1024 * tt:1024 * tt + 1024],
                      stream_c.ap()[512 * go + 128 * tt:
                                    512 * go + 128 * tt + 128, :], sem="p6x")
            ldn = b.n("p6x")
            b.wt(nc.vector, "p6x", ldn)
            b.wt(nc.scalar, "p6x", ldn)
            for tt in range(4):
                xap = xg6[:, 1024 * tt:1024 * tt + 1024]
                _ln_tile(b, ln6[:, :], xap, D, lnsc)
                lnn = b.incl("vc")
                b.wt(nc.tensor, "vc", lnn)
                if go > 0 or tt > 0:
                    b.wt(nc.tensor, "sc")
                for c in range(8):
                    b.tr(trp6[:, 128 * c:128 * c + 128],
                         ln6[:, 128 * c:128 * c + 128], idf[:, :])
                tmn = b.incl("tm")
                b.wt(nc.vector, "tm", tmn)
                b.wt(nc.scalar, "tm", tmn)
                dst = lnxb6[:, 128 * tt:]
                dstv = bass.AP(dst.tensor, dst.offset,
                               [[dst.ap[0][0], 128], [512, 8], [1, 128]])
                srcv = trp6[:, :].rearrange("p (c n) -> p c n", c=8)
                b.V(lambda v, dv=dstv, sv=srcv: v.tensor_copy(dv, sv),
                    writes=(lnxb6[:, 0:1],))
                evv = b.incl("vc")
                b.wt(nc.scalar, "vc", evv)
                dstf = lnf32[:, 128 * tt:]
                dstfv = bass.AP(dstf.tensor, dstf.offset,
                                [[dstf.ap[0][0], 128], [512, 8], [1, 128]])
                b.SC(lambda s, dv=dstfv, sv=srcv: s.copy(dv, sv),
                     writes=(lnf32[:, 0:1],))
                b.incl("sc")
            lnxv, lnxs = b.n("vc"), b.n("sc")
            b.wt(nc.sync, "vc", lnxv)
            b.ldy(lnx6.ap()[:, 4096 * go:4096 * go + 4096], lnxb6[:, :],
                  sem="p6s")
            b.wt(nc.tensor, "sc", lnxs)
            b.wt(nc.tensor, "p6g", b.n("p6g"))
            b.wt(nc.tensor, "vc")
            for m in range(2):
                for k in range(8):
                    b.mm(
                        gp6[:, 512 * m:512 * m + 512],
                        wg1f[:, G1 * k + 128 * m:G1 * k + 128 * m + 128],
                        lnf32[:, 512 * k:512 * k + 512],
                        start=(k == 0), stop=(k == 7))
            tmn = b.incl("tm")
            b.wt(nc.scalar, "tm", tmn)
            for m in range(2):
                b.SC(lambda s, m_=m: s.activation(
                    gsb[:, 512 * m_:512 * m_ + 512],
                    gp6[:, 512 * m_:512 * m_ + 512], AF.Gelu,
                    bias=bg1p[:, m_:m_ + 1]),
                    writes=(gsb[:, 0:1],))
            gsn = b.incl("sc")
            b.wt(nc.tensor, "sc", gsn)
            for tt in range(4):
                b.wt(nc.tensor, "vc")
                for m in range(2):
                    b.mm(
                        gp6[:, 0:NEXP],
                        gsb[:, 512 * m + 128 * tt:512 * m + 128 * tt + 128],
                        wg2f[:, NEXP * m:NEXP * m + NEXP],
                        start=(m == 0), stop=(m == 1))
                tmn = b.incl("tm")
                b.wt(nc.vector, "tm", tmn)
                b.V(lambda v: v.reduce_max(gm1[:, 0:1], gp6[:, 0:NEXP], AX,
                                           negate=True),
                    writes=(gm1[:, 0:1],))
                vn = b.incl("vc")
                b.wt(nc.scalar, "vc", vn)
                b.SC(lambda s: s.activation(gprob[:, :], gp6[:, 0:NEXP],
                                            AF.Exp, bias=gm1[:, 0:1],
                                            accum_out=gsum[:, 0:1]),
                     writes=(gprob[:, 0:1], gsum[:, 0:1]))
                scn = b.incl("sc")
                b.wt(nc.vector, "sc", scn)
                b.V(lambda v: v.reciprocal(gm2[:, 0:1], gsum[:, 0:1]),
                    writes=(gm2[:, 0:1],))
                b.V(lambda v: v.tensor_scalar(gtmp[:, :], gprob[:, :],
                                              gm2[:, 0:1], None, ALU.mult),
                    reads=(gm2[:, 0:1],), writes=(gtmp[:, 0:1],))
                b.V(lambda v: v.reduce_max(gm1[:, 0:1], gtmp[:, :], AX),
                    reads=(gtmp[:, 0:1],), writes=(gm1[:, 0:1],))
                b.V(lambda v: v.tensor_scalar(gtmp2[:, :], gtmp[:, :],
                                              gm1[:, 0:1], None, ALU.is_lt),
                    reads=(gm1[:, 0:1],), writes=(gtmp2[:, 0:1],))
                b.V(lambda v: v.tensor_mul(gprob[:, :], gtmp2[:, :],
                                           gtmp[:, :]),
                    reads=(gtmp2[:, 0:1],), writes=(gprob[:, 0:1],))
                b.V(lambda v: v.reduce_max(gm2[:, 0:1], gprob[:, :], AX),
                    reads=(gprob[:, 0:1],), writes=(gm2[:, 0:1],))
                b.V(lambda v: v.tensor_scalar(gtmp2[:, :], gtmp[:, :],
                                              gm2[:, 0:1], None, ALU.is_ge),
                    reads=(gm2[:, 0:1],), writes=(gtmp2[:, 0:1],))
                b.V(lambda v, t_=tt, g_=go: v.tensor_mul(
                    gates[:, NEXP * (4 * g_ + t_):NEXP * (4 * g_ + t_) + NEXP],
                    gtmp2[:, :], gtmp[:, :]),
                    reads=(gtmp2[:, 0:1],), writes=(gates[:, 0:1],))
                b.incl("vc")
    gatesv = b.n("vc")
    lnx6_dy = b.n("p6s")
    tp = b.tap("p6_gates", [128, NTO * NEXP])
    if tp is not None:
        b.wt(nc.sync, "vc", gatesv)
        b.ldy(tp, gates[:, :])
    b.wt(nc.sync, "vc", gatesv)
    b.wt(nc.gpsimd, "vc", gatesv)
    p6a.close()

    p6b = contextlib.ExitStack()

    def sb6b(name, shape, dtype):
        return p6b.enter_context(nc.sbuf_tensor(name, list(shape), dtype))

    winb = [sb6b(f"winb{i}", [128, 8 * EH], BF16) for i in range(2)]
    woutb = [sb6b(f"woutb{i}", [128, 16 * 1024], BF16) for i in range(2)]
    hfm6 = sb6b("hfm6", [128, 16 * 512], BF16)
    lnxg = sb6b("lnxg", [128, 2 * 8 * 512], BF16)
    otmp = sb6b("otmp", [128, 4 * 512], F32)

    def load_w(e):
        sl = e % 2
        b.ldw(winb[sl][:, :].rearrange("p (c n) -> p c n", c=8),
              w_in[e].rearrange("(c p) n -> p c n", p=128), sem=f"p6w{sl}")
        for half in range(2):
            b.ldw(woutb[sl][:, 8192 * half:8192 * half + 8192]
                  .rearrange("p (c n) -> p c n", c=8),
                  w_out[e].rearrange("(c p) n -> p c n",
                                     p=128)[:, 8 * half:8 * half + 8, :],
                  sem=f"p6w{sl}")
        return b.n(f"p6w{sl}")

    def load_lnx(idx):
        go, sl = idx % 4, idx % 2
        b.ldy(lnxg[:, 4096 * sl:4096 * sl + 4096],
              lnx6.ap()[:, 4096 * go:4096 * go + 4096], sem=f"p6l{sl}")
        return b.n(f"p6l{sl}")

    with (nc.psum_tensor("hp6a", [128, 512], F32) as hp6a,
          nc.psum_tensor("hp6b", [128, 512], F32) as hp6b,
          nc.psum_tensor("op6a", [128, 512], F32) as op6a,
          nc.psum_tensor("op6b", [128, 512], F32) as op6b):
        otmp_dw = {}
        wldn, lnld = {}, {}
        h_tm = {}
        exp_end_tm = {}
        if PHASE_LIMIT >= 6:
            wldn[0] = load_w(0)
            b.wt(nc.sync, "p6s", lnx6_dy)
            lnld[0] = load_lnx(0)
            lnld[1] = load_lnx(1)
        for e in range(NEXP if PHASE_LIMIT >= 6 else 0):
            if e + 1 < NEXP:
                if e >= 1:
                    b.wt(nc.gpsimd, "tm", exp_end_tm[e - 1])
                wldn[e + 1] = load_w(e + 1)
            for go in range(NGO):
                idx = 4 * e + go
                b.wt(nc.tensor, f"p6l{idx % 2}", lnld[idx])
                if go == 0:
                    b.wt(nc.tensor, f"p6w{e % 2}", wldn[e])
                hv = {}
                for hc in range(16):
                    ps = hp6a if hc % 2 == 0 else hp6b
                    if hc >= 2:
                        b.wt(nc.tensor, "sc", hv[hc - 2])
                    for k in range(8):
                        b.mm(
                            ps[:, :],
                            winb[e % 2][:, EH * k + 128 * hc:
                                        EH * k + 128 * hc + 128],
                            lnxg[:, 4096 * (idx % 2) + 512 * k:
                                 4096 * (idx % 2) + 512 * k + 512],
                            start=(k == 0), stop=(k == 7))
                    tmn = b.incl("tm")
                    b.wt(nc.scalar, "tm", tmn)
                    b.SC(lambda s, p=ps, h=hc: s.activation(
                        hfm6[:, 512 * h:512 * h + 512], p[:, :], AF.Gelu),
                        writes=(hfm6[:, 0:1],))
                    hv[hc] = b.incl("sc")
                h_tm[idx] = b.n("tm")
                if idx + 2 < 4 * NEXP:
                    b.wt(nc.sync, "tm", h_tm[idx])
                    lnld[idx + 2] = load_lnx(idx + 2)
                hdone = hv[15]
                b.wt(nc.tensor, "sc", hdone)
                if e == 0 and go == 0:
                    b.wt(nc.tensor, "vc", gatesv)
                for tt in range(4):
                    b.wt(nc.tensor, "vc")
                    for hc in range(16):
                        for n2 in range(2):
                            ps = op6a if n2 == 0 else op6b
                            b.mm(
                                ps[:, :],
                                hfm6[:, 512 * hc + 128 * tt:
                                     512 * hc + 128 * tt + 128],
                                woutb[e % 2][:, 1024 * hc + 512 * n2:
                                             1024 * hc + 512 * n2 + 512],
                                start=(hc == 0), stop=(hc == 15))
                    tmn = b.incl("tm")
                    gt = 4 * go + tt
                    b.wt(nc.vector, "tm", tmn)
                    for n2 in range(2):
                        ps = op6a if n2 == 0 else op6b
                        slot = (2 * gt + n2) % 4
                        if slot in otmp_dw:
                            b.wt(nc.vector, f"p6o{slot}", otmp_dw[slot])
                        b.V(lambda v, p=ps, sl=slot, g_=gt, e_=e:
                            v.tensor_scalar(
                                otmp[:, 512 * sl:512 * sl + 512], p[:, :],
                                gates[:, NEXP * g_ + e_:NEXP * g_ + e_ + 1],
                                None, ALU.mult),
                            writes=(otmp[:, 0:1],))
                        vn = b.incl("vc")
                        b.wt(nc.gpsimd, "vc", vn)
                        otmp_dw[slot] = b.ldw(
                            oacc_d.ap()[128 * gt:128 * gt + 128,
                                        512 * n2:512 * n2 + 512],
                            otmp[:, 512 * slot:512 * slot + 512],
                            accum=(e > 0), sem=f"p6o{slot}")
            exp_end_tm[e] = b.n("tm")
    b.wt(nc.vector, "tm")
    b.wt(nc.scalar, "tm")
    p6b.close()

    for sl in range(4):
        b.wt(nc.sync, f"p6o{sl}")
    tp = b.tap("p6_oacc", [TOK, D])
    if tp is not None:
        b.ldw(tp, oacc_d.ap())
    for tt in range(NTO if PHASE_LIMIT >= 7 else 0):
        b.wt(nc.sync, "vc")
        b.wt(nc.sync, "sc")
        par = tt % 2
        b.ldy(xg6[:, 2048 * par:2048 * par + 1024],
              stream_c.ap()[128 * tt:128 * tt + 128, :], sem="p6x")
        b.ldy(xg6[:, 2048 * par + 1024:2048 * par + 2048],
              oacc_d.ap()[128 * tt:128 * tt + 128, :], sem="p6x")
        ldn = b.n("p6x")
        b.wt(nc.vector, "p6x", ldn)
        b.wt(nc.scalar, "p6x", ldn)
        oap = xg6[:, 2048 * par + 1024:2048 * par + 2048]
        _ln_tile(b, ln6[:, :], oap, D, lnsc)
        xt = xg6[:, 2048 * par:2048 * par + 1024]
        b.V(lambda v, x_=xt: v.tensor_add(ysb6[:, :], ln6[:, :], x_),
            reads=(ln6[:, 0:1],), writes=(ysb6[:, 0:1],))
        vn = b.incl("vc")
        b.wt(nc.sync, "vc", vn)
        b.ldy(out[128 * tt:128 * tt + 128, :], ysb6[:, :])
    p6.close()
    b.wt(nc.sync, "dy")
    b.wt(nc.gpsimd, "dw")
    nc.leave_named_scope("ph6", _sid["ph6"], False)
    st.close()
    return b


_NC_CACHE = {}


def _get_nc():
    key = (PHASE_LIMIT, tuple(DEV_TAPS))
    if key not in _NC_CACHE:
        _NC_CACHE[key] = build_nc()
    return _NC_CACHE[key]


def kernel(**inputs):
    f = np.float32
    x = np.ascontiguousarray(inputs["L0_attn_x"], dtype=f)
    qk_w = np.asarray(inputs["L0_attn_qk_w"], dtype=f)
    qk_b = np.asarray(inputs["L0_attn_qk_b"], dtype=f)
    v_w = np.asarray(inputs["L0_attn_v_w"], dtype=f)
    gains = [np.asarray(inputs["L0_attn_resid_gain"], dtype=f).reshape(16),
             np.asarray(inputs["L0_attn_skip_gain"], dtype=f).reshape(16),
             np.asarray(inputs["L0_attn_centre_gain"], dtype=f).reshape(16)]
    in_maps = []
    for core in range(8):
        p, hh = core // 2, core % 2
        hs = slice(512 * hh, 512 * hh + 512)
        m = {
            "x_seq": np.ascontiguousarray(x[p]),
            "x_own": np.ascontiguousarray(x[p, 2048 * hh:2048 * hh + 2048]),
            "tok_base": np.array([[2048 * hh]], dtype=np.uint32),
            "qkw_own": np.ascontiguousarray(
                np.concatenate([qk_w[:, 0:1024][:, hs],
                                qk_w[:, 1024:2048][:, hs]], axis=1)),
            "vw_own": np.ascontiguousarray(v_w[:, hs]),
            "qb_pp": np.ascontiguousarray(qk_b[0:1024][hs].reshape(4, 128).T),
            "kb_pp": np.ascontiguousarray(qk_b[1024:2048][hs].reshape(4, 128).T),
            "cprojw": np.ascontiguousarray(inputs["L0_attn_cproj_w"], dtype=f),
            "gain_r": np.ascontiguousarray(
                np.broadcast_to(gains[0][8 * hh:8 * hh + 8][None, :], (64, 8))),
            "gain_s": np.ascontiguousarray(
                np.broadcast_to(gains[1][8 * hh:8 * hh + 8][None, :], (64, 8))),
            "gain_c": np.ascontiguousarray(
                np.broadcast_to(gains[2][8 * hh:8 * hh + 8][None, :], (64, 8))),
            "w1": np.ascontiguousarray(inputs["L0_ffn_w1"], dtype=f),
            "b1_pp": np.ascontiguousarray(
                np.asarray(inputs["L0_ffn_b1"], dtype=f).reshape(32, 128).T),
            "w2": np.ascontiguousarray(inputs["L0_ffn_w2"], dtype=f),
            "b2_row": np.ascontiguousarray(
                np.asarray(inputs["L0_ffn_b2"], dtype=f)[None, :]),
            "wq1": np.ascontiguousarray(inputs["L1_attn_Wq"], dtype=f),
            "wk1": np.ascontiguousarray(inputs["L1_attn_Wk"], dtype=f),
            "wv1": np.ascontiguousarray(inputs["L1_attn_Wv"], dtype=f),
            "wo1": np.ascontiguousarray(inputs["L1_attn_Wo"], dtype=f),
            "bo_row": np.ascontiguousarray(
                np.asarray(inputs["L1_attn_bo"], dtype=f)[None, :]),
            "wf_rep": np.ascontiguousarray(
                np.concatenate([np.asarray(inputs["L1_attn_Wf"], dtype=f)] * 2,
                               axis=0)),
            "bf_row": np.ascontiguousarray(
                np.asarray(inputs["L1_attn_bf"], dtype=f)[None, :]),
            "bf_pp": np.ascontiguousarray(
                np.asarray(inputs["L1_attn_bf"], dtype=f)[:, None]),
            "wg1": np.ascontiguousarray(inputs["L1_ffn_wg1"], dtype=f),
            "bg1_pp": np.ascontiguousarray(
                np.asarray(inputs["L1_ffn_bg1"], dtype=f).reshape(2, 128).T),
            "wg2": np.ascontiguousarray(inputs["L1_ffn_wg2"], dtype=f),
            "w_in": np.ascontiguousarray(inputs["L1_ffn_w_in"], dtype=f),
            "w_out": np.ascontiguousarray(inputs["L1_ffn_w_out"], dtype=f),
        }
        in_maps.append(m)

    b = _get_nc()
    res = run_bass_kernel_spmd(b.nc, in_maps, core_ids=list(range(8)))
    if DEV_TAPS:
        kernel.last_results = res.results
    outp = np.empty((4, S, D), dtype=np.float32)
    for core in range(8):
        p, hh = core // 2, core % 2
        outp[p, 2048 * hh:2048 * hh + 2048] = res.results[core]["out"]
    return outp


# version 04
# speedup vs baseline: 1.1483x; 1.0281x over previous
import contextlib
import numpy as np
import ml_dtypes
import concourse.bass as bass
import concourse.mybir as mybir
from concourse.bass_utils import run_bass_kernel_spmd

F32 = mybir.dt.float32
BF16 = mybir.dt.bfloat16
U32 = mybir.dt.uint32
AF = mybir.ActivationFunctionType
ALU = mybir.AluOpType
AX = mybir.AxisListType.X
BF = ml_dtypes.bfloat16
ds = bass.ds

S, D = 4096, 1024
NT, NG = 32, 8
TOK, NTO, NGO = 2048, 16, 4
HD, HOWN, NPAIR = 64, 8, 4
H1, FDIM = 16, 64
W0 = 4096
NEXP, EH, G1 = 8, 2048, 256
EPS = 1e-5
NEGBIG = -30000.0

DEV_TAPS = ()
PHASE_LIMIT = 99


class Build:
    def __init__(self):
        nc = bass.Bass("TRN2", target_bir_lowering=False)
        self.nc = nc
        self.cnt = {}
        self.sems = {}
        for name in ("dy", "dw", "tm", "sc", "vc", "gp", "cc",
                     "px0", "px1", "pt0", "pt1", "sag", "pa0", "pa1",
                     "p3x", "p4x", "p4w", "p5x", "p5w", "p5r", "p5k",
                     "p6x", "p6g", "p6s", "p6l0", "p6l1", "p6w",
                     "p6o0", "p6o1", "p6o2", "p6o3", "pcp", "p0w", "pnb", "p6w0", "p6w1"):
            self.sems[name] = nc.alloc_semaphore(name)
            self.cnt[name] = 0
        self.dma_owner = {}
        self.vdirty = set()
        self.sdirty = set()
        self.last = {}
        self.taps = {}

    def inc(self, inst, name, k=1):
        inst.then_inc(self.sems[name], k)
        self.cnt[name] += k
        return self.cnt[name]

    def mm(self, *a, **kw):
        inst = self.nc.tensor.matmul(*a, **kw)
        self.last["tm"] = inst
        return inst

    def tr(self, *a, **kw):
        inst = self.nc.tensor.transpose(*a, **kw)
        self.last["tm"] = inst
        return inst

    def incl(self, name):
        return self.inc(self.last[name], name)

    def n(self, name):
        return self.cnt[name]

    def wt(self, eng, name, val=None):
        if val is None:
            val = self.cnt[name]
        if val > 0:
            eng.wait_ge(self.sems[name], val)

    def ldy(self, out, in_, sem="dy"):
        self.dma_owner[sem] = "sync"
        return self.inc(self.nc.sync.dma_start(out=out, in_=in_), sem, 16)

    def ldw(self, out, in_, accum=False, sem="dw"):
        kw = {"accum_op": ALU.add} if accum else {}
        self.dma_owner[sem] = "gpsimd"
        return self.inc(self.nc.gpsimd.dma_start(out=out, in_=in_, **kw), sem, 16)

    def V(self, emit, reads=(), writes=()):
        rn = {ap.tensor.name for ap in reads}
        if rn & self.vdirty:
            self.nc.vector.drain()
            self.vdirty = set()
        inst = emit(self.nc.vector)
        self.vdirty |= {ap.tensor.name for ap in writes}
        self.last["vc"] = inst
        return inst

    def SC(self, emit, reads=(), writes=()):
        rn = {ap.tensor.name for ap in reads}
        if rn & self.sdirty:
            self.nc.scalar.drain()
            self.sdirty = set()
        inst = emit(self.nc.scalar)
        self.sdirty |= {ap.tensor.name for ap in writes}
        self.last["sc"] = inst
        return inst

    def finish(self):
        for sem, owner in self.dma_owner.items():
            eng = self.nc.sync if owner == "sync" else self.nc.gpsimd
            self.wt(eng, sem)
        self.nc.all_engine_barrier()
        self.vdirty = set()
        self.sdirty = set()

    def tap(self, name, shape, dtype=F32):
        if name in DEV_TAPS:
            t = self.nc.dram_tensor("tap_" + name, list(shape), dtype,
                                    kind="ExternalOutput")
            self.taps[name] = t.ap()
            return self.taps[name]
        return None


def _ln_tile(b, out_sb, x_ap, W, sc):
    nc = b.nc
    b.V(lambda v: v.reduce_sum(sc["ssum"][:, 0:1], x_ap, AX),
        reads=(x_ap,), writes=(sc["ssum"][:, 0:1],))
    b.SC(lambda s: s.activation(sc["sqscr"][:, 0:W], x_ap, AF.Square,
                                accum_out=sc["sqs"][:, 0:1]),
         reads=(x_ap,), writes=(sc["sqscr"][:, 0:W], sc["sqs"][:, 0:1]))
    b.incl("sc")
    b.wt(nc.vector, "sc")
    b.V(lambda v: v.tensor_scalar(sc["mean"][:, 0:1], sc["ssum"][:, 0:1],
                                  1.0 / W, None, ALU.mult),
        reads=(sc["ssum"][:, 0:1],), writes=(sc["mean"][:, 0:1],))
    b.V(lambda v: v.tensor_scalar(sc["msq"][:, 0:1], sc["sqs"][:, 0:1],
                                  1.0 / W, None, ALU.mult),
        reads=(sc["sqs"][:, 0:1],), writes=(sc["msq"][:, 0:1],))
    b.V(lambda v: v.tensor_mul(sc["mm"][:, 0:1], sc["mean"][:, 0:1],
                               sc["mean"][:, 0:1]),
        reads=(sc["mean"][:, 0:1],), writes=(sc["mm"][:, 0:1],))
    b.V(lambda v: v.tensor_sub(sc["var"][:, 0:1], sc["msq"][:, 0:1],
                               sc["mm"][:, 0:1]),
        reads=(sc["msq"][:, 0:1], sc["mm"][:, 0:1]), writes=(sc["var"][:, 0:1],))
    b.V(lambda v: v.tensor_scalar(sc["vare"][:, 0:1], sc["var"][:, 0:1],
                                  EPS, None, ALU.add),
        reads=(sc["var"][:, 0:1],), writes=(sc["vare"][:, 0:1],))
    b.V(lambda v: v.reciprocal(sc["rv"][:, 0:1], sc["vare"][:, 0:1]),
        reads=(sc["vare"][:, 0:1],), writes=(sc["rv"][:, 0:1],))
    b.incl("vc")
    b.wt(nc.scalar, "vc")
    b.SC(lambda s: s.activation(sc["rstd"][:, 0:1], sc["rv"][:, 0:1], AF.Sqrt),
         reads=(sc["rv"][:, 0:1],), writes=(sc["rstd"][:, 0:1],))
    b.incl("sc")
    b.wt(nc.vector, "sc")
    b.V(lambda v: v.scalar_tensor_tensor(sc["nmean"][:, 0:1], sc["mean"][:, 0:1],
                                         -1.0, sc["rstd"][:, 0:1],
                                         ALU.mult, ALU.mult),
        reads=(sc["mean"][:, 0:1],), writes=(sc["nmean"][:, 0:1],))
    b.V(lambda v: v.tensor_scalar(out_sb, x_ap, sc["rstd"][:, 0:1],
                                  sc["nmean"][:, 0:1], ALU.mult, ALU.add),
        reads=(x_ap, sc["nmean"][:, 0:1]), writes=(out_sb,))
    return b.n("vc"), b.n("sc")


def _make_consts():
    idb = np.eye(128, dtype=BF)
    idf = np.eye(128, dtype=np.float32)
    tri = np.where(np.arange(128)[None, :] > np.arange(128)[:, None],
                   np.float32(NEGBIG), np.float32(0.0)).astype(BF)
    ones_b = np.ones((128, 128), dtype=BF)
    ones_f = np.ones((1, 128), dtype=np.float32)
    sel2 = np.zeros((128, 2), dtype=BF)
    sel2[0:64, 0] = 1
    sel2[64:128, 1] = 1
    inv = (1.0 / (np.arange(S, dtype=np.float64) + 1.0)).astype(BF)[None, :]
    triinv = np.zeros((NG, 4, 128, 512), dtype=BF)
    for g in range(NG):
        qg = 512 * g + np.arange(512)
        for c in range(4):
            kg = 128 * (4 * g + c) + np.arange(128)
            m = (kg[:, None] <= qg[None, :]).astype(np.float64)
            triinv[g, c] = (m / (qg[None, :] + 1.0)).astype(BF)
    return idb, idf, tri, ones_b, ones_f, sel2, inv, triinv


def _strided3(ap_base, pstep, a, b_, c, d):
    return bass.AP(ap_base.tensor, ap_base.offset, [[pstep, 128], [a, b_], [c, d]])


def build_nc():
    b = Build()
    nc = b.nc
    st = contextlib.ExitStack()

    def inp(name, shape, dtype=F32):
        return nc.dram_tensor(name, list(shape), dtype, kind="ExternalInput").ap()

    x_seq = inp("x_seq", [S, D])
    x_own = inp("x_own", [TOK, D])
    tok_base = inp("tok_base", [1, 1], U32)
    qkw_own = inp("qkw_own", [D, 2 * 512])
    vw_own = inp("vw_own", [D, 512])
    qb_pp = inp("qb_pp", [128, 4])
    kb_pp = inp("kb_pp", [128, 4])
    cprojw = inp("cprojw", [D, D])
    gain_r = inp("gain_r", [64, 8])
    gs128i = inp("gain_s128", [128, 8])
    gain_c = inp("gain_c", [64, 8])
    w1 = inp("w1", [D, W0])
    b1_pp = inp("b1_pp", [128, 32])
    w2 = inp("w2", [W0, D])
    b2_row = inp("b2_row", [1, D])
    wq1 = inp("wq1", [D, D])
    wk1 = inp("wk1", [D, D])
    wv1 = inp("wv1", [D, D])
    wo1 = inp("wo1", [D, D])
    bo_row = inp("bo_row", [1, D])
    wf_rep = inp("wf_rep", [128, FDIM])
    bf_row = inp("bf_row", [1, FDIM])
    bf_pp = inp("bf_pp", [64, 1])
    wg1 = inp("wg1", [D, G1])
    bg1_pp = inp("bg1_pp", [128, 2])
    wg2 = inp("wg2", [G1, NEXP])
    w_in = inp("w_in", [NEXP, D, EH])
    w_out = inp("w_out", [NEXP, EH, D])

    out = nc.dram_tensor("out", [TOK, D], F32, kind="ExternalOutput").ap()

    idb_h, idf_h, tri_h, onesb_h, onesf_h, sel2_h, inv_h, triinv_h = _make_consts()
    c_idb = nc.inline_tensor(idb_h, name="c_idb").ap()
    c_idf = nc.inline_tensor(idf_h, name="c_idf").ap()
    c_tri = nc.inline_tensor(tri_h, name="c_tri").ap()
    c_onesb = nc.inline_tensor(onesb_h, name="c_onesb").ap()
    c_onesf = nc.inline_tensor(onesf_h, name="c_onesf").ap()
    c_sel2 = nc.inline_tensor(sel2_h, name="c_sel2").ap()
    c_inv = nc.inline_tensor(inv_h, name="c_inv").ap()
    c_triinv = nc.inline_tensor(triinv_h, name="c_triinv").ap()

    ag_in = nc.dram_tensor("ag_in", [512, S], BF16)
    ag_out = nc.dram_tensor("ag_out", [2, 512, S], BF16)
    ar_in = nc.dram_tensor("ar_in", [64, H1, 65], F32)
    ar_out = nc.dram_tensor("ar_out", [64, H1, 65], F32)
    stream_a = nc.dram_tensor("stream_a", [TOK, D], F32)
    stream_b = nc.dram_tensor("stream_b", [TOK, D], F32)
    stream_c = nc.dram_tensor("stream_c", [TOK, D], F32)
    lnx6 = nc.dram_tensor("lnx6", [128, NTO * 1024], BF16)
    oacc_d = nc.dram_tensor("oacc_d", [TOK, D], F32)

    def sb(name, shape, dtype):
        return st.enter_context(nc.sbuf_tensor(name, list(shape), dtype))

    idb = sb("idb", [128, 128], BF16)
    idf = sb("idf", [128, 128], F32)
    tri = sb("tri", [128, 128], BF16)
    onesb = sb("onesb", [128, 128], BF16)
    onesf = sb("onesf", [1, 128], F32)
    sel2 = sb("sel2", [128, 2], BF16)
    negb_rep = sb("negb_rep", [128, 8], F32)
    grep = sb("grep", [64, 8], F32)
    gs128 = sb("gs128", [128, 8], F32)
    craw = sb("craw", [64, 8], F32)
    cnrep = sb("cnrep", [64, 8], F32)
    qbp = sb("qbp", [128, 4], F32)
    kbp = sb("kbp", [128, 4], F32)
    b1p = sb("b1p", [128, 32], F32)
    wfr = sb("wfr", [128, FDIM], BF16)
    bfr = sb("bfr", [1, FDIM], BF16)
    bfp = sb("bfp", [64, 1], F32)
    bg1p = sb("bg1p", [128, 2], F32)
    tokb = sb("tokb", [1, 1], U32)
    lnsc = {k: sb("lnsc_" + k, [128, 1], F32)
            for k in ("ssum", "sqs", "mean", "msq", "mm", "var", "vare", "rv",
                      "rstd", "nmean")}
    lnsc["sqscr"] = sb("lnsc_sqscr", [128, 1024], BF16)
    q2all = sb("q2all", [2, 32], F32)
    k2all = sb("k2all", [2, 32], F32)
    nb2 = sb("nb2", [2, 4], F32)
    nb2b = sb("nb2b", [2, 4], F32)
    nbr1 = sb("nbr1", [1, 4], F32)
    tiny = sb("tiny", [2, 8], F32)

    b.ldy(idb[:, :], c_idb)
    b.ldy(idf[:, :], c_idf)
    b.ldy(tri[:, :], c_tri)
    b.ldy(onesb[:, :], c_onesb)
    b.ldy(onesf[:, :], c_onesf)
    b.ldy(sel2[:, :], c_sel2)
    b.ldy(grep[:, :], gain_r)
    b.ldy(gs128[:, :], gs128i)
    b.ldy(craw[:, :], gain_c)
    b.ldy(qbp[:, :], qb_pp)
    b.ldy(kbp[:, :], kb_pp)
    b.ldy(b1p[:, :], b1_pp)
    b.ldy(bfp[:, :], bf_pp)
    b.ldy(bg1p[:, :], bg1_pp)
    b.ldy(tokb[:, :], tok_base)
    b.ldw(wfr[:, :], wf_rep)
    b.ldw(bfr[:, :], bf_row)
    setup_dy = b.n("dy")
    b.wt(nc.vector, "dy", setup_dy)
    b.V(lambda v: v.tensor_scalar(cnrep[:, :], craw[:, :], -1.0, None, ALU.mult),
        reads=(craw[:, 0:1],), writes=(cnrep[:, 0:1],))
    b.incl("vc")

    _sid = {}
    _sid["ph0"] = nc.enter_named_scope("ph0", False)[0]
    p01 = contextlib.ExitStack()

    def sb1(name, shape, dtype):
        return p01.enter_context(nc.sbuf_tensor(name, list(shape), dtype))

    invp = sb1("invp", [1, S], BF16)
    carryT = sb1("carryT", [1, NG * 512], BF16)
    qfm = sb1("qfm", [128, 4 * S], BF16)
    kfm = sb1("kfm", [128, 4 * S], BF16)
    vext = sb1("vext", [128, NT * 520], BF16)
    probs = sb1("probs", [128, 4096], BF16)
    skipI = sb1("skipI", [128, 8 * 128], BF16)
    triinv_s = sb1("triinv_s", [128, 2 * 2048], BF16)
    recip_sb = sb1("recip_sb", [1, 1024], F32)
    pvs_sb = sb1("pvs_sb", [128, 1024], F32)
    r_sb = sb1("r_sb", [128, 1024], F32)
    ta_sb = sb1("ta_sb", [64, 512], F32)
    tb_sb = sb1("tb_sb", [64, 512], F32)
    outpair = sb1("outpair", [128, 512], BF16)
    gs_sb = sb1("gs_sb", [64, 8], F32)
    cacc = [sb1("cacc0", [64, 8], F32), sb1("cacc1", [64, 8], F32)]
    ct_sb = sb1("ct_sb", [8, 64], F32)

    p0t = contextlib.ExitStack()

    def sb0(name, shape, dtype):
        return p0t.enter_context(nc.sbuf_tensor(name, list(shape), dtype))

    qkwb = sb0("qkwb", [128, 8 * 1024], BF16)
    vwb = sb0("vwb", [128, 8 * 512], BF16)

    xa = sb0("xa", [128, 2 * 1024], F32)
    lnt = sb0("lnt", [128, 2 * 1024], F32)
    lnxfm = sb0("lnxfm", [128, 4096], BF16)
    sqscr2 = sb0("sqscr2", [128, 512], BF16)

    b.ldy(invp[:, :], c_inv)
    b.ldw(qkwb[:, :].rearrange("p (c n) -> p c n", c=8),
          qkw_own.rearrange("(c p) n -> p c n", p=128), sem="p0w")
    b.ldw(vwb[:, :].rearrange("p (c n) -> p c n", c=8),
          vw_own.rearrange("(c p) n -> p c n", p=128), sem="p0w")
    dw_w0 = b.n("p0w")
    b.V(lambda v: v.memset(carryT[0:1, 0:512], 0.0), writes=(carryT[0:1, 0:1],))
    vxv = vext[:, :].rearrange("p (t h o) -> p t h o", t=NT, h=HOWN)
    b.V(lambda v: v.memset(vxv[:, :, :, 64:65], 1.0), writes=(vext[:, 0:1],))
    vready0 = b.incl("vc")

    with (nc.psum_tensor("trps", [128, 1024], F32) as trps,
          nc.psum_tensor("mmA", [128, 512], F32) as mmA,
          nc.psum_tensor("mmB", [128, 512], F32) as mmB,
          nc.psum_tensor("vps", [128, 512], F32) as vps,
          nc.psum_tensor("q2ps", [2, 512], F32) as q2ps,
          nc.psum_tensor("gsps", [65, 8], F32) as gsps,
          nc.psum_tensor("ctps", [8, 64], F32) as ctps):
        lnv, lnsct, trv = {}, {}, {}
        last_sel2_tm = 0
        b.wt(nc.tensor, "dy", setup_dy)
        for t in range(NT):
            par = t % 2
            gpar = (t // 4) % 2
            if t >= 2:
                b.wt(nc.sync, "vc", trv[t - 2])
                b.wt(nc.sync, "sc", lnsct[t - 2])
            ldn = b.ldy(xa[:, 1024 * par:1024 * par + 1024],
                        x_seq[128 * t:128 * t + 128, :], sem=f"px{par}")
            b.wt(nc.vector, f"px{par}", ldn)
            b.wt(nc.scalar, f"px{par}", ldn)
            xap = xa[:, 1024 * par:1024 * par + 1024]
            lnap = lnt[:, 1024 * par:1024 * par + 1024]
            _v, _s = _ln_tile(b, lnap, xap, D, lnsc)
            lnv[t] = b.incl("vc")
            lnsct[t] = _s
            b.wt(nc.tensor, "vc", lnv[t])
            if t >= 1:
                b.wt(nc.tensor, "vc", trv[t - 1])
            for c in range(8):
                b.tr(trps[:, 128 * c:128 * c + 128],
                                    lnap[:, 128 * c:128 * c + 128], idf[:, :])
            tmn = b.incl("tm")
            b.wt(nc.vector, "tm", tmn)
            dst = lnxfm[:, 128 * (t % 4):]
            dstv = bass.AP(dst.tensor, dst.offset,
                           [[dst.ap[0][0], 128], [512, 8], [1, 128]])
            srcv = trps[:, :].rearrange("p (c n) -> p c n", c=8)
            b.V(lambda v, dv=dstv, sv=srcv: v.tensor_copy(dv, sv),
                writes=(lnxfm[:, 0:1],))
            trv[t] = b.incl("vc")

            if t % 4 == 3:
                gf = t // 4
                rhs_base = 0
                b.wt(nc.tensor, "vc", trv[t])
                if gf == 0:
                    b.wt(nc.tensor, "p0w", dw_w0)
                    b.wt(nc.tensor, "vc", vready0)
                for which, wofs, dst_fm, biasp, scl in (
                        ("q", 0, qfm, qbp, 0.125), ("k", 512, kfm, kbp, None)):
                    for pc in range(4):
                        ps = mmA if pc % 2 == 0 else mmB
                        b.wt(nc.tensor, "vc")
                        for k in range(8):
                            b.mm(
                                ps[:, :],
                                qkwb[:, 1024 * k + wofs + 128 * pc:
                                     1024 * k + wofs + 128 * pc + 128],
                                lnxfm[:, rhs_base + 512 * k:
                                      rhs_base + 512 * k + 512],
                                start=(k == 0), stop=(k == 7))
                        tmn = b.incl("tm")
                        b.wt(nc.vector, "tm", tmn)
                        dstap = dst_fm[:, 4096 * pc + 512 * gf:
                                       4096 * pc + 512 * gf + 512]
                        if scl is None:
                            b.V(lambda v, o=dstap, p=ps, bp=biasp, c=pc:
                                v.tensor_scalar(o, p[:, :], bp[:, c:c + 1],
                                                None, ALU.add),
                                writes=(dst_fm[:, 0:1],))
                        else:
                            b.V(lambda v, o=dstap, p=ps, bp=biasp, c=pc, s_=scl:
                                v.tensor_scalar(o, p[:, :], bp[:, c:c + 1],
                                                s_, ALU.add, ALU.mult),
                                writes=(dst_fm[:, 0:1],))
                        vn = b.incl("vc")
                        b.wt(nc.scalar, "vc", vn)
                        b.wt(nc.scalar, "tm", last_sel2_tm)
                        b.SC(lambda s, i=dstap: s.activation(
                            sqscr2[:, 0:512], i, AF.Square),
                            writes=(sqscr2[:, 0:1],))
                        scn = b.incl("sc")
                        b.wt(nc.tensor, "sc", scn)
                        b.mm(q2ps[:, :], sel2[:, :],
                                         sqscr2[:, 0:512], start=True, stop=True)
                        last_sel2_tm = tmn = b.incl("tm")
                        b.wt(nc.vector, "tm", tmn)
                        dcol = (q2all if which == "q" else k2all)
                        b.V(lambda v, d=dcol, col=8 * pc + gf: v.reduce_max(
                            d[:, col:col + 1], q2ps[:, :], AX),
                            writes=(dcol[:, 0:1],))
                        b.incl("vc")
                for tt in range(4 * gf, 4 * gf + 4):
                    b.wt(nc.tensor, "vc")
                    for k in range(8):
                        b.mm(
                            vps[:, :],
                            lnxfm[:, rhs_base + 512 * k + 128 * (tt % 4):
                                  rhs_base + 512 * k + 128 * (tt % 4) + 128],
                            vwb[:, 512 * k:512 * k + 512],
                            start=(k == 0), stop=(k == 7))
                    tmn = b.incl("tm")
                    b.wt(nc.vector, "tm", tmn)
                    dst = vext[:, 520 * tt:]
                    dstv = bass.AP(dst.tensor, dst.offset,
                                   [[dst.ap[0][0], 128], [65, 8], [1, 64]])
                    srcv = vps[:, :].rearrange("p (h o) -> p h o", h=8)
                    b.V(lambda v, dv=dstv, sv=srcv: v.tensor_copy(dv, sv),
                        writes=(vext[:, 0:1],))
                    vpse = b.incl("vc")
                b.wt(nc.tensor, "vc", vpse)
                for lh in range(8):
                    for ti in range(4):
                        tt2 = 4 * gf + ti
                        b.mm(gsps[:, lh:lh + 1],
                             vext[:, 520 * tt2 + 65 * lh:
                                  520 * tt2 + 65 * lh + 65],
                             onesb[:, 0:1],
                             start=(ti == 0), stop=(ti == 3))
                tmn = b.incl("tm")
                b.wt(nc.vector, "tm", tmn)
                b.V(lambda v: v.tensor_copy(gs_sb[:, :], gsps[0:64, :]),
                    writes=(gs_sb[:, 0:1],))
                vn = b.incl("vc")
                src_c = cacc[gf % 2]
                dst_c = cacc[1 - gf % 2]
                if gf > 0:
                    b.wt(nc.tensor, "vc", vn)
                    b.tr(ctps[:, :], src_c[:, 0:8],
                                        idf[0:64, 0:64])
                    tmn = b.incl("tm")
                    b.wt(nc.vector, "tm", tmn)
                    b.wt(nc.vector, "dw")
                    b.V(lambda v: v.tensor_copy(ct_sb[:, :], ctps[0:8, 0:64]),
                        writes=(ct_sb[0:1, 0:1],))
                    vn2 = b.incl("vc")
                    b.wt(nc.gpsimd, "vc", vn2)
                    b.ldw(carryT[0:1, 512 * gf:512 * gf + 512], ct_sb[:, :])
                    b.V(lambda v, d=dst_c, s_=src_c: v.tensor_add(
                        d[:, :], s_[:, :], gs_sb[:, :]),
                        reads=(gs_sb[:, 0:1],), writes=(dst_c[:, 0:1],))
                else:
                    b.V(lambda v, d=dst_c: v.tensor_copy(d[:, :], gs_sb[:, :]),
                        reads=(gs_sb[:, 0:1],), writes=(dst_c[:, 0:1],))
                b.incl("vc")

        for c in range(4):
            for d_, a_ in ((q2all, 0), (k2all, 1)):
                src = bass.AP(d_, 8 * c, [[32, 2], [1, 8]])
                b.V(lambda v, s_=src, aa=a_, cc=c: v.reduce_max(
                    tiny[:, 4 * aa + cc:4 * aa + cc + 1], s_, AX),
                    reads=(d_[:, 0:1],), writes=(tiny[:, 0:1],))
        b.V(lambda v: v.tensor_mul(nb2[:, :], tiny[:, 0:4], tiny[:, 4:8]),
            reads=(tiny[:, 0:1],), writes=(nb2[:, 0:1],))
        vn = b.incl("vc")
        b.wt(nc.scalar, "vc", vn)
        b.SC(lambda s: s.activation(nb2b[:, :], nb2[:, :], AF.Sqrt),
             reads=(nb2[:, 0:1],), writes=(nb2b[:, 0:1],))
        scn = b.incl("sc")
        b.wt(nc.vector, "sc", scn)
        b.V(lambda v: v.tensor_scalar(nb2[:, :], nb2b[:, :], -1.0, None,
                                      ALU.mult),
            reads=(nb2b[:, 0:1],), writes=(nb2[:, 0:1],))
        vn = b.incl("vc")
        b.wt(nc.sync, "vc", vn)
        nbl = b.ldy(nbr1[:, :], nb2[1:2, :], sem="pnb")
        b.wt(nc.tensor, "pnb", nbl)
        b.wt(nc.tensor, "vc", vn)
        b.mm(mmA[:, 0:4], onesf[0:1, :], nb2[0:1, :],
                         start=True, stop=True)
        b.mm(mmA[:, 4:8], onesf[0:1, :], nbr1[0:1, :],
                         start=True, stop=True)
        tmn = b.incl("tm")
        b.wt(nc.vector, "tm", tmn)
        dstv = bass.AP(negb_rep, 0, [[8, 128], [1, 2], [2, 4]])
        srcv = mmA[:, 0:8].rearrange("p (r c) -> p r c", r=2)
        b.V(lambda v: v.tensor_copy(dstv, srcv), writes=(negb_rep[:, 0:1],))
        negb_v = b.incl("vc")

        for nm, t_, w in (("p0_qfm", qfm, 4 * S), ("p0_kfm", kfm, 4 * S),
                          ("p0_vext", vext, NT * 520)):
            tp = b.tap(nm, [128, w], BF16)
            if tp is not None:
                b.wt(nc.sync, "vc")
                b.ldy(tp, t_[:, :])
        tp = b.tap("p0_negb", [128, 8])
        if tp is not None:
            b.wt(nc.sync, "vc")
            b.ldy(tp, negb_rep[:, :])
        tp = b.tap("p0_carryT", [1, NG * 512], BF16)
        if tp is not None:
            b.wt(nc.sync, "dw")
            b.ldy(tp, carryT[:, :])

    p0t.close()
    if PHASE_LIMIT < 1:
        p01.close()
        b.finish()
        st.close()
        return b

    nc.leave_named_scope("ph0", _sid["ph0"], False)
    _sid["ph1"] = nc.enter_named_scope("ph1", False)[0]
    b.wt(nc.vector, "dy", setup_dy)
    for lh in range(8):
        b.V(lambda v, l=lh: v.tensor_scalar(
            skipI[:, 128 * l:128 * l + 128], idb[:, :], gs128[:, l:l + 1],
            None, ALU.mult),
            writes=(skipI[:, 0:1],))
    skipIv = b.incl("vc")
    with (nc.psum_tensor("SA", [128, 1024], F32) as SA,
          nc.psum_tensor("SB", [128, 1024], F32) as SB,
          nc.psum_tensor("pvA", [65, 512], F32) as pvA,
          nc.psum_tensor("pvB", [65, 512], F32) as pvB,
          nc.psum_tensor("cmr", [64, 1024], F32) as cmr):
        b.wt(nc.tensor, "dw")
        b.wt(nc.tensor, "vc", skipIv)
        trild = {0: b.ldy(triinv_s[:, 0:2048].rearrange("p (c q) -> p c q", c=4),
                          c_triinv[0].rearrange("c k q -> k c q"),
                          sem="pt0")}
        prev_rel_sc = 0
        cm_free_vc = 0
        prev_ep_tm = 0
        last_ag = 0
        for g in range(NG):
            nj = 4 * g + 4
            gpar = g % 2
            if g + 1 < NG:
                b.wt(nc.sync, "tm", prev_ep_tm if g else 0)
                trild[g + 1] = b.ldy(
                    triinv_s[:, 2048 * (1 - gpar):2048 * (1 - gpar) + 2048]
                    .rearrange("p (c q) -> p c q", c=4),
                    c_triinv[g + 1].rearrange("c k q -> k c q"),
                    sem=f"pt{(g + 1) % 2}")
            for pc in range(NPAIR):
                lhA, lhB = 2 * pc, 2 * pc + 1
                exp_cnt, pv_cnt = {}, {}
                nb = nj // 2
                for bi in range(nb):
                    jj0 = 2 * bi
                    if bi == 0:
                        b.wt(nc.tensor, "sc", prev_rel_sc)
                        if g == 0 and pc == 0:
                            b.wt(nc.tensor, "pt0", trild[0])
                    else:
                        b.wt(nc.tensor, "sc", exp_cnt[bi - 1])
                    for j in (jj0, jj0 + 1):
                        c2 = j - 4 * g
                        qoff = 128 * c2 if c2 > 0 else 0
                        co = 512 * (j % 2)
                        diag = c2 >= 0
                        for (S_, r0) in ((SA, 0), (SB, 64)):
                            b.mm(
                                S_[:, co + qoff:co + 512],
                                kfm[:, 4096 * pc + 128 * j:
                                    4096 * pc + 128 * j + 128][r0:r0 + 64, :],
                                qfm[:, 4096 * pc + 512 * g + qoff:
                                    4096 * pc + 512 * g + 512][r0:r0 + 64, :],
                                start=True, stop=not diag,
                                tile_position=(r0, 0))
                            if diag:
                                b.mm(
                                    S_[:, co + qoff:co + qoff + 128],
                                    tri[:, :], idb[:, :],
                                    start=False, stop=True,
                                    tile_position=(0, 0))
                    s_tm = b.incl("tm")
                    b.wt(nc.scalar, "tm", s_tm)
                    if bi >= 2:
                        b.wt(nc.scalar, "tm", pv_cnt[bi - 2])
                    for (S_, lh, po) in ((SA, lhA, 0), (SB, lhB, 1024)):
                        b.SC(lambda s, ss=S_, l=lh,
                             p=po + 2048 * (bi % 2):
                             s.activation(probs[:, p:p + 1024],
                                          ss[:, 0:1024], AF.Exp,
                                          bias=negb_rep[:, l:l + 1]),
                             writes=(probs[:, 0:1],))
                    exp_cnt[bi] = b.incl("sc")
                    b.wt(nc.tensor, "sc", exp_cnt[bi])
                    for j in (jj0, jj0 + 1):
                        c2 = j - 4 * g
                        qoff = 128 * c2 if c2 > 0 else 0
                        co = 512 * (j % 2)
                        for (pv, lh, po) in ((pvA, lhA, 0), (pvB, lhB, 1024)):
                            b.mm(
                                pv[:, qoff:512],
                                vext[:, 520 * j + 65 * lh:
                                     520 * j + 65 * lh + 65],
                                probs[:, po + 2048 * (bi % 2) + co + qoff:
                                      po + 2048 * (bi % 2) + co + 512],
                                start=(j == 0), stop=(j == nj - 1))
                    pv_cnt[bi] = b.incl("tm")
                sweep_tm = pv_cnt[nb - 1]

                b.wt(nc.scalar, "tm", sweep_tm)
                b.wt(nc.scalar, "vc", cm_free_vc)
                b.SC(lambda s: s.copy(pvs_sb[0:64, 0:512], pvA[0:64, :]),
                     writes=(pvs_sb[0:1, 0:1],))
                b.SC(lambda s: s.copy(pvs_sb[0:64, 512:1024], pvB[0:64, :]),
                     writes=(pvs_sb[0:1, 0:1],))
                b.incl("sc")
                b.wt(nc.vector, "tm", sweep_tm)
                b.V(lambda v: v.reciprocal(recip_sb[0:1, 0:512],
                                           pvA[64:65, :]),
                    writes=(recip_sb[0:1, 0:1],))
                b.V(lambda v: v.reciprocal(recip_sb[0:1, 512:1024],
                                           pvB[64:65, :]),
                    writes=(recip_sb[0:1, 0:1],))
                rec_vc = b.incl("vc")
                b.wt(nc.tensor, "vc", rec_vc)
                b.mm(pvA[0:64, :], onesf[0:1, 0:64], recip_sb[0:1, 0:512],
                     start=True, stop=True)
                b.mm(pvB[0:64, :], onesf[0:1, 0:64], recip_sb[0:1, 512:1024],
                     start=True, stop=True)
                rtm = b.incl("tm")
                b.wt(nc.scalar, "tm", rtm)
                b.SC(lambda s: s.copy(r_sb[0:64, 0:512], pvA[0:64, :]),
                     writes=(r_sb[0:1, 0:1],))
                b.SC(lambda s: s.copy(r_sb[0:64, 512:1024], pvB[0:64, :]),
                     writes=(r_sb[0:1, 0:1],))
                prev_rel_sc = b.incl("sc")

                b.wt(nc.vector, "sag", last_ag)
                for (lh, hp, r0) in ((lhA, 0, 0), (lhB, 1, 64)):
                    b.wt(nc.tensor, "vc", cm_free_vc)
                    for c2 in range(4):
                        j2 = 4 * g + c2
                        b.mm(
                            cmr[0:64, 128 * c2:128 * c2 + 128],
                            vext[:, 520 * j2 + 65 * lh:520 * j2 + 65 * lh + 64],
                            skipI[:, 128 * lh:128 * lh + 128],
                            start=(c2 == 0), stop=(c2 == 3))
                    if g > 0:
                        b.wt(nc.tensor, f"pt{g % 2}", trild[g])
                        b.mm(
                            cmr[0:64, 512:1024],
                            carryT[0:1, 512 * g + 64 * lh:
                                   512 * g + 64 * lh + 64],
                            invp[0:1, 512 * g:512 * g + 512],
                            start=True, stop=False)
                    for c2 in range(4):
                        j2 = 4 * g + c2
                        b.mm(
                            cmr[0:64, 512:1024],
                            vext[:, 520 * j2 + 65 * lh:520 * j2 + 65 * lh + 64],
                            triinv_s[:, 2048 * gpar + 512 * c2:
                                     2048 * gpar + 512 * c2 + 512],
                            start=(c2 == 0 and g == 0), stop=(c2 == 3))
                    cm_tm = b.incl("tm")
                    b.wt(nc.vector, "sc", prev_rel_sc)
                    b.V(lambda v, hp_=hp: v.tensor_mul(
                        ta_sb[0:64, :], pvs_sb[0:64, 512 * hp_:512 * hp_ + 512],
                        r_sb[0:64, 512 * hp_:512 * hp_ + 512]),
                        reads=(pvs_sb[0:1, 0:1], r_sb[0:1, 0:1]),
                        writes=(ta_sb[0:1, 0:1],))
                    b.wt(nc.vector, "tm", cm_tm)
                    b.V(lambda v, l=lh: v.scalar_tensor_tensor(
                        tb_sb[0:64, :], ta_sb[0:64, :], grep[0:64, l:l + 1],
                        cmr[0:64, 0:512], ALU.mult, ALU.add),
                        reads=(ta_sb[0:1, 0:1],), writes=(tb_sb[0:1, 0:1],))
                    b.V(lambda v, l=lh, r=r0: v.scalar_tensor_tensor(
                        outpair[r:r + 64, :], cmr[0:64, 512:1024],
                        cnrep[0:64, l:l + 1], tb_sb[0:64, :],
                        ALU.mult, ALU.add),
                        reads=(tb_sb[0:1, 0:1],), writes=(outpair[0:1, 0:1],))
                    cm_free_vc = b.incl("vc")
                prev_ep_tm = b.n("tm")
                b.wt(nc.sync, "vc", cm_free_vc)
                last_ag = b.ldy(
                    ag_in.ap()[128 * pc:128 * pc + 128, 512 * g:512 * g + 512],
                    outpair[:, :], sem="sag")

        tp = b.tap("p1_ag_in", [512, S], BF16)
        if tp is not None:
            b.wt(nc.gpsimd, "sag")
            b.ldw(tp, ag_in.ap())

    p01.close()
    b.finish()
    if PHASE_LIMIT < 2:
        st.close()
        return b

    nc.leave_named_scope("ph1", _sid["ph1"], False)
    _sid["ph3"] = nc.enter_named_scope("ph3", False)[0]
    b.wt(nc.gpsimd, "sag")
    cci = nc.gpsimd.collective_compute(
        "AllGather", ALU.bypass,
        replica_groups=[[0, 1], [2, 3], [4, 5], [6, 7]],
        ins=[ag_in.ap().opt()], outs=[ag_out.ap().opt()])
    b.inc(cci, "cc", 1)

    w4s = contextlib.ExitStack()
    w1b = w4s.enter_context(nc.sbuf_tensor("w1b", [128, 8 * W0], BF16))
    w2b = w4s.enter_context(nc.sbuf_tensor("w2b", [128, 32 * 1024], BF16))
    b2r = w4s.enter_context(nc.sbuf_tensor("b2r", [1, D], BF16))
    b.ldw(b2r[:, :], b2_row, sem="p4w")
    for half in range(2):
        b.ldw(w1b[:, 4 * W0 * half:4 * W0 * half + 4 * W0]
              .rearrange("p (c n) -> p c n", c=4),
              w1.rearrange("(c p) n -> p c n", p=128)[:, 4 * half:4 * half + 4, :],
              sem="p4w")
        b.ldw(w2b[:, 16384 * half:16384 * half + 16384]
              .rearrange("p (c n) -> p c n", c=16),
              w2.rearrange("(c p) n -> p c n", p=128)[:, 16 * half:16 * half + 16, :],
              sem="p4w")
    dw4 = b.n("p4w")

    p3 = contextlib.ExitStack()

    def sb3(name, shape, dtype):
        return p3.enter_context(nc.sbuf_tensor(name, list(shape), dtype))

    cpw = sb3("cpw", [128, 8 * 1024], BF16)
    afm = sb3("afm", [128, 2 * 4096], BF16)
    xg3 = sb3("xg3", [128, 4 * 1024], F32)
    ysb3 = sb3("ysb3", [128, 1024], F32)
    ln3 = sb3("ln3", [128, 1024], F32)
    b.ldw(cpw[:, :].rearrange("p (c n) -> p c n", c=8),
          cprojw.rearrange("(c p) n -> p c n", p=128), sem="pcp")
    dw3 = b.n("pcp")
    tokv = nc.sync.value_load(tokb[0:1, 0:1])
    b.wt(nc.sync, "cc", b.n("cc"))
    with nc.psum_tensor("yp3", [128, 1024], F32) as yp3:
        grp_tm = {}
        for go in range(NGO):
            gpar = go % 2
            if go >= 2:
                b.wt(nc.sync, "tm", grp_tm[go - 2])
            src_ap = bass.AP(ag_out.ap().tensor, tokv + 512 * go,
                             [[S, 128], [512 * S, 2], [128 * S, 4], [1, 512]])
            dsta = afm[:, 4096 * gpar:4096 * gpar + 4096]
            dst_ap = bass.AP(dsta.tensor, dsta.offset,
                             [[dsta.ap[0][0], 128], [2048, 2], [512, 4],
                              [1, 512]])
            b.ldy(dst_ap, src_ap, sem=f"pa{gpar}")
            if go >= 1:
                b.wt(nc.sync, "vc")
                b.wt(nc.sync, "sc")
            for tt in range(4):
                b.ldy(xg3[:, 1024 * tt:1024 * tt + 1024],
                      x_own[512 * go + 128 * tt:512 * go + 128 * tt + 128, :],
                      sem="p3x")
            ldn = b.n("p3x")
            for tt in range(4):
                b.wt(nc.tensor, f"pa{gpar}")
                if go == 0 and tt == 0:
                    b.wt(nc.tensor, "pcp", dw3)
                b.wt(nc.tensor, "vc")
                for k in range(8):
                    for n2 in range(2):
                        b.mm(
                            yp3[:, 512 * n2:512 * n2 + 512],
                            afm[:, 4096 * gpar + 512 * k + 128 * tt:
                                4096 * gpar + 512 * k + 128 * tt + 128],
                            cpw[:, 1024 * k + 512 * n2:
                                1024 * k + 512 * n2 + 512],
                            start=(k == 0), stop=(k == 7))
                grp_tm[go] = tmn = b.incl("tm")
                b.wt(nc.vector, "tm", tmn)
                b.wt(nc.vector, "p3x", ldn)
                b.V(lambda v: v.tensor_copy(ysb3[:, :], yp3[:, :]),
                    writes=(ysb3[:, 0:1],))
                vn = b.incl("vc")
                b.wt(nc.scalar, "vc", vn)
                _ln_tile(b, ln3[:, :], ysb3[:, :], D, lnsc)
                xt = xg3[:, 1024 * tt:1024 * tt + 1024]
                b.V(lambda v, x_=xt: v.tensor_add(ysb3[:, :], ln3[:, :], x_),
                    reads=(ln3[:, 0:1], ysb3[:, 0:1]), writes=(ysb3[:, 0:1],))
                vn = b.incl("vc")
                b.wt(nc.sync, "vc", vn)
                b.ldy(stream_a.ap()[512 * go + 128 * tt:
                                    512 * go + 128 * tt + 128, :], ysb3[:, :])
    p3.close()
    b.finish()
    tp = b.tap("p3_stream", [TOK, D])
    if tp is not None:
        b.ldw(tp, stream_a.ap())
        b.wt(nc.gpsimd, "dw")
    if PHASE_LIMIT < 3:
        st.close()
        return b

    nc.leave_named_scope("ph3", _sid["ph3"], False)
    _sid["ph4"] = nc.enter_named_scope("ph4", False)[0]
    p4 = contextlib.ExitStack()

    def sb4(name, shape, dtype):
        return p4.enter_context(nc.sbuf_tensor(name, list(shape), dtype))

    lnxf4 = sb4("lnxf4", [128, 8 * 512], BF16)
    hfm = sb4("hfm", [128, 32 * 512], BF16)
    xg4 = sb4("xg4", [128, 4 * 1024], F32)
    ln4 = sb4("ln4", [128, 1024], F32)
    ysb4 = sb4("ysb4", [128, 1024], F32)
    with (nc.psum_tensor("trp4", [128, 1024], F32) as trp4,
          nc.psum_tensor("hp4a", [128, 512], F32) as hp4a,
          nc.psum_tensor("hp4b", [128, 512], F32) as hp4b,
          nc.psum_tensor("yp4", [128, 1024], F32) as yp4):
        for go in range(NGO):
            if go >= 1:
                b.wt(nc.sync, "vc")
                b.wt(nc.sync, "sc")
            for tt in range(4):
                b.ldy(xg4[:, 1024 * tt:1024 * tt + 1024],
                      stream_a.ap()[512 * go + 128 * tt:
                                    512 * go + 128 * tt + 128, :], sem="p4x")
            ldn = b.n("p4x")
            b.wt(nc.vector, "p4x", ldn)
            b.wt(nc.scalar, "p4x", ldn)
            for tt in range(4):
                xap = xg4[:, 1024 * tt:1024 * tt + 1024]
                _ln_tile(b, ln4[:, :], xap, D, lnsc)
                lnn = b.incl("vc")
                b.wt(nc.tensor, "vc", lnn)
                for c in range(8):
                    b.tr(trp4[:, 128 * c:128 * c + 128],
                                        ln4[:, 128 * c:128 * c + 128],
                                        idf[:, :])
                tmn = b.incl("tm")
                b.wt(nc.vector, "tm", tmn)
                dst = lnxf4[:, 128 * tt:]
                dstv = bass.AP(dst.tensor, dst.offset,
                               [[dst.ap[0][0], 128], [512, 8], [1, 128]])
                srcv = trp4[:, :].rearrange("p (c n) -> p c n", c=8)
                b.V(lambda v, dv=dstv, sv=srcv: v.tensor_copy(dv, sv),
                    writes=(lnxf4[:, 0:1],))
                b.incl("vc")
            lnxv = b.n("vc")
            b.wt(nc.tensor, "vc", lnxv)
            if go == 0:
                b.wt(nc.tensor, "p4w", dw4)
            hv = {}
            for hc in range(32):
                ps = hp4a if hc % 2 == 0 else hp4b
                if hc >= 2:
                    b.wt(nc.tensor, "sc", hv[hc - 2])
                for k in range(8):
                    b.mm(
                        ps[:, :],
                        w1b[:, W0 * k + 128 * hc:W0 * k + 128 * hc + 128],
                        lnxf4[:, 512 * k:512 * k + 512],
                        start=(k == 0), stop=(k == 7))
                tmn = b.incl("tm")
                b.wt(nc.scalar, "tm", tmn)
                b.SC(lambda s, p=ps, h=hc: s.activation(
                    hfm[:, 512 * h:512 * h + 512], p[:, :], AF.Relu,
                    bias=b1p[:, h:h + 1]),
                    writes=(hfm[:, 0:1],))
                hv[hc] = b.incl("sc")
            hdone = hv[31]
            for tt in range(4):
                b.wt(nc.tensor, "sc", hdone)
                b.wt(nc.tensor, "vc")
                for hc in range(32):
                    for n2 in range(2):
                        b.mm(
                            yp4[:, 512 * n2:512 * n2 + 512],
                            hfm[:, 512 * hc + 128 * tt:
                                512 * hc + 128 * tt + 128],
                            w2b[:, 1024 * hc + 512 * n2:
                                1024 * hc + 512 * n2 + 512],
                            start=(hc == 0), stop=False)
                for n2 in range(2):
                    b.mm(
                        yp4[:, 512 * n2:512 * n2 + 512], onesb[0:1, :],
                        b2r[0:1, 512 * n2:512 * n2 + 512],
                        start=False, stop=True)
                tmn = b.incl("tm")
                b.wt(nc.vector, "tm", tmn)
                b.wt(nc.scalar, "tm", tmn)
                b.V(lambda v: v.tensor_copy(ysb4[:, :], yp4[:, :]),
                    writes=(ysb4[:, 0:1],))
                vn = b.incl("vc")
                b.wt(nc.scalar, "vc", vn)
                _ln_tile(b, ln4[:, :], ysb4[:, :], D, lnsc)
                xt = xg4[:, 1024 * tt:1024 * tt + 1024]
                b.V(lambda v, x_=xt: v.tensor_add(ysb4[:, :], ln4[:, :], x_),
                    reads=(ln4[:, 0:1], ysb4[:, 0:1]), writes=(ysb4[:, 0:1],))
                vn = b.incl("vc")
                b.wt(nc.sync, "vc", vn)
                b.ldy(stream_b.ap()[512 * go + 128 * tt:
                                    512 * go + 128 * tt + 128, :], ysb4[:, :])
    p4.close()
    w4s.close()
    b.finish()
    tp = b.tap("p4_stream", [TOK, D])
    if tp is not None:
        b.ldw(tp, stream_b.ap())
        b.wt(nc.gpsimd, "dw")
    if PHASE_LIMIT < 4:
        st.close()
        return b

    nc.leave_named_scope("ph4", _sid["ph4"], False)
    _sid["ph5"] = nc.enter_named_scope("ph5", False)[0]
    p5 = contextlib.ExitStack()

    def sb5(name, shape, dtype):
        return p5.enter_context(nc.sbuf_tensor(name, list(shape), dtype))

    lnx_all = sb5("lnx_all", [128, NTO * 1024], BF16)
    wkb = sb5("wkb", [128, 8 * 1024], BF16)
    wvb = sb5("wvb", [128, 8 * 1024], BF16)
    wqb = sb5("wqb", [128, 8 * 1024], BF16)
    wob = sb5("wob", [128, 8 * 1024], BF16)
    kfm1 = sb5("kfm1", [128, 8 * 512], BF16)
    kft = sb5("kft", [128, 4 * 1024], BF16)
    vx1 = sb5("vx1", [128, 4 * 1040], BF16)
    qfsb = sb5("qfsb", [64, 16 * 512], BF16)
    kvf = sb5("kvf", [128, 520], F32)
    kvb = sb5("kvb", [64, 16 * 65], BF16)
    atm = sb5("atm", [128, 4 * 1024], F32)
    a1fm = sb5("a1fm", [128, 8 * 512], BF16)
    xg5 = sb5("xg5", [128, 4 * 1024], F32)
    ln5 = sb5("ln5", [128, 1024], F32)
    ysb5 = sb5("ysb5", [128, 1024], F32)
    bor = sb5("bor", [1, D], BF16)
    nrec = sb5("nrec", [128, 16], F32)
    nmax = sb5("nmax", [128, 16], F32)
    kvacc = [sb5("kvacc0", [128, 520], F32), sb5("kvacc1", [128, 520], F32)]
    b.ldw(bor[:, :], bo_row, sem="p5w")
    b.ldw(wkb[:, :].rearrange("p (c n) -> p c n", c=8),
          wk1.rearrange("(c p) n -> p c n", p=128), sem="p5w")
    b.ldw(wvb[:, :].rearrange("p (c n) -> p c n", c=8),
          wv1.rearrange("(c p) n -> p c n", p=128), sem="p5w")
    b.ldw(wqb[:, :].rearrange("p (c n) -> p c n", c=8),
          wq1.rearrange("(c p) n -> p c n", p=128), sem="p5w")
    b.ldw(wob[:, :].rearrange("p (c n) -> p c n", c=8),
          wo1.rearrange("(c p) n -> p c n", p=128), sem="p5w")
    dw5 = b.n("p5w")
    vx1v = vx1[:, :].rearrange("p (t h o) -> p t h o", t=4, h=H1)
    b.V(lambda v: v.memset(vx1v[:, :, :, 64:65], 1.0), writes=(vx1[:, 0:1],))
    memv = b.incl("vc")
    with (nc.psum_tensor("trp5", [128, 1024], F32) as trp5,
          nc.psum_tensor("pp5a", [128, 512], F32) as pp5a,
          nc.psum_tensor("pp5b", [128, 512], F32) as pp5b,
          nc.psum_tensor("kv0", [128, 260], F32) as kv0,
          nc.psum_tensor("kv1", [128, 260], F32) as kv1,
          nc.psum_tensor("kv2", [128, 260], F32) as kv2,
          nc.psum_tensor("kv3", [128, 260], F32) as kv3):
        kvps = [kv0, kv1, kv2, kv3]
        for go in range(NGO):
            if go >= 1:
                b.wt(nc.sync, "vc")
                b.wt(nc.sync, "sc")
            for tt in range(4):
                b.ldy(xg5[:, 1024 * tt:1024 * tt + 1024],
                      stream_b.ap()[512 * go + 128 * tt:
                                    512 * go + 128 * tt + 128, :], sem="p5x")
            ldn = b.n("p5x")
            b.wt(nc.vector, "p5x", ldn)
            b.wt(nc.scalar, "p5x", ldn)
            if go == 0:
                b.wt(nc.tensor, "vc", memv)
            for tt in range(4):
                xap = xg5[:, 1024 * tt:1024 * tt + 1024]
                _ln_tile(b, ln5[:, :], xap, D, lnsc)
                lnn = b.incl("vc")
                b.wt(nc.tensor, "vc", lnn)
                for c in range(8):
                    b.tr(trp5[:, 128 * c:128 * c + 128],
                                        ln5[:, 128 * c:128 * c + 128],
                                        idf[:, :])
                tmn = b.incl("tm")
                b.wt(nc.vector, "tm", tmn)
                dst = lnx_all[:, 4096 * go + 128 * tt:]
                dstv = bass.AP(dst.tensor, dst.offset,
                               [[dst.ap[0][0], 128], [512, 8], [1, 128]])
                srcv = trp5[:, :].rearrange("p (c n) -> p c n", c=8)
                b.V(lambda v, dv=dstv, sv=srcv: v.tensor_copy(dv, sv),
                    writes=(lnx_all[:, 0:1],))
                b.incl("vc")
            lnxv = b.n("vc")
            base5 = 4096 * go
            b.wt(nc.tensor, "vc", lnxv)
            if go == 0:
                b.wt(nc.tensor, "p5w", dw5)
            for c in range(8):
                ps = pp5a if c % 2 == 0 else pp5b
                b.wt(nc.tensor, "vc")
                for k in range(8):
                    b.mm(
                        ps[:, :],
                        wkb[:, 1024 * k + 128 * c:1024 * k + 128 * c + 128],
                        lnx_all[:, base5 + 512 * k:base5 + 512 * k + 512],
                        start=(k == 0), stop=(k == 7))
                tmn = b.incl("tm")
                b.wt(nc.vector, "tm", tmn)
                b.V(lambda v, p=ps, c_=c: v.tensor_copy(
                    kfm1[:, 512 * c_:512 * c_ + 512], p[:, :]),
                    writes=(kfm1[:, 0:1],))
                b.incl("vc")
            for tt in range(4):
                b.wt(nc.tensor, "vc")
                for k in range(8):
                    for n2 in range(2):
                        ps = pp5a if n2 == 0 else pp5b
                        b.mm(
                            ps[:, :],
                            lnx_all[:, base5 + 512 * k + 128 * tt:
                                    base5 + 512 * k + 128 * tt + 128],
                            wvb[:, 1024 * k + 512 * n2:
                                1024 * k + 512 * n2 + 512],
                            start=(k == 0), stop=(k == 7))
                tmn = b.incl("tm")
                b.wt(nc.vector, "tm", tmn)
                for n2 in range(2):
                    ps = pp5a if n2 == 0 else pp5b
                    dst = vx1[:, 1040 * tt + 65 * 8 * n2:]
                    dstv = bass.AP(dst.tensor, dst.offset,
                                   [[dst.ap[0][0], 128], [65, 8], [1, 64]])
                    srcv = ps[:, :].rearrange("p (h o) -> p h o", h=8)
                    b.V(lambda v, dv=dstv, sv=srcv: v.tensor_copy(dv, sv),
                        writes=(vx1[:, 0:1],))
                b.incl("vc")
            vdone = b.n("vc")
            for tt in range(4):
                b.wt(nc.tensor, "vc", vdone)
                b.wt(nc.tensor, "sc")
                for h in range(H1):
                    ps = pp5a if h < 8 else pp5b
                    hc, r0 = h // 2, 64 * (h % 2)
                    b.mm(
                        ps[:, 64 * (h % 8):64 * (h % 8) + 64],
                        kfm1[:, 512 * hc + 128 * tt:
                             512 * hc + 128 * tt + 128][r0:r0 + 64, :],
                        wfr[r0:r0 + 64, :], start=True, stop=False,
                        tile_position=(r0, 0))
                    b.mm(
                        ps[:, 64 * (h % 8):64 * (h % 8) + 64],
                        onesb[0:1, :], bfr[0:1, :], start=False, stop=True,
                        tile_position=(0, 0))
                tmn = b.incl("tm")
                b.wt(nc.scalar, "tm", tmn)
                for (ps, off) in ((pp5a, 0), (pp5b, 512)):
                    b.SC(lambda s, p=ps, o=off, t_=tt: s.activation(
                        kft[:, 1024 * t_ + o:1024 * t_ + o + 512], p[:, :],
                        AF.Relu),
                        writes=(kft[:, 0:1],))
                b.incl("sc")
            kfdone = b.n("sc")
            b.wt(nc.tensor, "sc", kfdone)
            b.wt(nc.tensor, "vc")
            for pr in range(8):
                tn = kvps[pr // 4]
                coff = 65 * (pr % 4)
                for hh2 in range(2):
                    h = 2 * pr + hh2
                    r0 = 64 * hh2
                    for ti in range(4):
                        b.mm(
                            tn[r0:r0 + 64, coff:coff + 65],
                            kft[:, 1024 * ti + 64 * h:1024 * ti + 64 * h + 64],
                            vx1[:, 1040 * ti + 65 * h:1040 * ti + 65 * h + 65],
                            start=(ti == 0), stop=(ti == 3),
                            tile_position=(0, r0))
            kvtm = b.incl("tm")
            b.wt(nc.vector, "tm", kvtm)
            dstk = kvacc[go % 2]
            srck = kvacc[(go + 1) % 2]
            if go == 0:
                for i in range(2):
                    b.V(lambda v, i_=i, d=dstk: v.tensor_copy(
                        d[:, 260 * i_:260 * i_ + 260], kvps[i_][:, 0:260]),
                        writes=(dstk[:, 0:1],))
            else:
                for i in range(2):
                    b.V(lambda v, i_=i, d=dstk, s_=srck: v.tensor_add(
                        d[:, 260 * i_:260 * i_ + 260],
                        s_[:, 260 * i_:260 * i_ + 260], kvps[i_][:, 0:260]),
                        reads=(srck[:, 0:1],), writes=(dstk[:, 0:1],))
            b.incl("vc")
        vn = b.n("vc")
        b.wt(nc.sync, "vc", vn)
        kvv = kvacc[1][:, :].rearrange("p (pr o) -> p pr o", pr=8)
        arv = ar_in.ap().rearrange("f (pr two) o -> two f pr o", two=2)
        b.ldy(arv[0], kvv[0:64, :, :], sem="p5r")
        b.ldy(arv[1], kvv[64:128, :, :], sem="p5r")
        b.wt(nc.gpsimd, "p5r")
        cci = nc.gpsimd.collective_compute(
            "AllReduce", ALU.add,
            replica_groups=[[0, 1], [2, 3], [4, 5], [6, 7]],
            ins=[ar_in.ap().opt()], outs=[ar_out.ap().opt()])
        b.inc(cci, "cc", 1)
        b.wt(nc.gpsimd, "cc", b.n("cc"))
        b.ldw(kvb[:, :].rearrange("p (h o) -> p h o", h=H1), ar_out.ap(),
              sem="p5k")
        kvldn = b.n("p5k")

        for go in range(NGO):
            b.wt(nc.sync, "vc")
            b.wt(nc.sync, "sc")
            for tt in range(4):
                b.ldy(xg5[:, 1024 * tt:1024 * tt + 1024],
                      stream_b.ap()[512 * go + 128 * tt:
                                    512 * go + 128 * tt + 128, :], sem="p5x")
            ldn = b.n("p5x")
            base5 = 4096 * go
            for c in range(8):
                ps = pp5a if c % 2 == 0 else pp5b
                b.wt(nc.tensor, "vc")
                if go == 0 and c == 0:
                    b.wt(nc.tensor, "p5k", kvldn)
                    b.wt(nc.tensor, "sc")
                for k in range(8):
                    b.mm(
                        ps[:, :],
                        wqb[:, 1024 * k + 128 * c:1024 * k + 128 * c + 128],
                        lnx_all[:, base5 + 512 * k:base5 + 512 * k + 512],
                        start=(k == 0), stop=(k == 7))
                tmn = b.incl("tm")
                b.wt(nc.vector, "tm", tmn)
                b.V(lambda v, p=ps, c_=c: v.tensor_copy(
                    kfm1[:, 512 * c_:512 * c_ + 512], p[:, :]),
                    writes=(kfm1[:, 0:1],))
                b.incl("vc")
            qv = b.n("vc")
            b.wt(nc.tensor, "vc", qv)
            for h in range(H1):
                hc, r0 = h // 2, 64 * (h % 2)
                ps = pp5a if h % 2 == 0 else pp5b
                b.wt(nc.tensor, "sc")
                b.mm(
                    ps[0:64, :], wfr[r0:r0 + 64, :],
                    kfm1[:, 512 * hc:512 * hc + 512][r0:r0 + 64, :],
                    start=True, stop=True, tile_position=(r0, 0))
                tmn = b.incl("tm")
                b.wt(nc.scalar, "tm", tmn)
                b.SC(lambda s, p=ps, h_=h: s.activation(
                    qfsb[0:64, 512 * h_:512 * h_ + 512], p[0:64, :], AF.Relu,
                    bias=bfp[0:64, 0:1]),
                    writes=(qfsb[0:1, 0:1],))
                b.incl("sc")
            qfdone = b.n("sc")
            b.wt(nc.tensor, "sc", qfdone)
            for tt in range(4):
                b.wt(nc.tensor, "vc")
                for h in range(H1):
                    b.mm(
                        kvps[h // 4][:, 65 * (h % 4):65 * (h % 4) + 65],
                        qfsb[0:64, 512 * h + 128 * tt:512 * h + 128 * tt + 128],
                        kvb[0:64, 65 * h:65 * h + 65],
                        start=True, stop=True)
                tmn = b.incl("tm")
                b.wt(nc.vector, "tm", tmn)
                for i in range(4):
                    pv_ = kvps[i]
                    src = bass.AP(pv_, 64, [[260, 128], [65, 4]])
                    b.V(lambda v, s_=src, i_=i: v.tensor_scalar(
                        nmax[:, 4 * i_:4 * i_ + 4], s_, 1e-6, None, ALU.max),
                        writes=(nmax[:, 0:1],))
                b.V(lambda v: v.reciprocal(nrec[:, :], nmax[:, :]),
                    reads=(nmax[:, 0:1],), writes=(nrec[:, 0:1],))
                for h in range(H1):
                    pv_ = kvps[h // 4]
                    b.V(lambda v, p=pv_, h_=h, t_=tt: v.tensor_scalar(
                        atm[:, 1024 * t_ + 64 * h_:1024 * t_ + 64 * h_ + 64],
                        p[:, 65 * (h_ % 4):65 * (h_ % 4) + 64],
                        nrec[:, h_:h_ + 1], None, ALU.mult),
                        reads=(nrec[:, 0:1],), writes=(atm[:, 0:1],))
                b.incl("vc")
            for tt in range(4):
                b.wt(nc.tensor, "vc")
                for c in range(8):
                    b.tr(trp5[:, 128 * c:128 * c + 128],
                         atm[:, 1024 * tt + 128 * c:1024 * tt + 128 * c + 128],
                         idf[:, :])
                tmn = b.incl("tm")
                b.wt(nc.vector, "tm", tmn)
                dst = a1fm[:, 128 * tt:]
                dstv = bass.AP(dst.tensor, dst.offset,
                               [[dst.ap[0][0], 128], [512, 8], [1, 128]])
                srcv = trp5[:, :].rearrange("p (c n) -> p c n", c=8)
                b.V(lambda v, dv=dstv, sv=srcv: v.tensor_copy(dv, sv),
                    writes=(a1fm[:, 0:1],))
                b.incl("vc")
            a1v = b.n("vc")
            b.wt(nc.tensor, "vc", a1v)
            b.wt(nc.vector, "p5x", ldn)
            b.wt(nc.scalar, "p5x", ldn)
            for tt in range(4):
                b.wt(nc.tensor, "vc")
                for k in range(8):
                    for n2 in range(2):
                        b.mm(
                            trp5[:, 512 * n2:512 * n2 + 512],
                            a1fm[:, 512 * k + 128 * tt:
                                 512 * k + 128 * tt + 128],
                            wob[:, 1024 * k + 512 * n2:
                                1024 * k + 512 * n2 + 512],
                            start=(k == 0), stop=False)
                for n2 in range(2):
                    b.mm(
                        trp5[:, 512 * n2:512 * n2 + 512], onesb[0:1, :],
                        bor[0:1, 512 * n2:512 * n2 + 512],
                        start=False, stop=True)
                tmn = b.incl("tm")
                b.wt(nc.vector, "tm", tmn)
                b.wt(nc.scalar, "tm", tmn)
                b.V(lambda v: v.tensor_copy(ysb5[:, :], trp5[:, :]),
                    writes=(ysb5[:, 0:1],))
                vn = b.incl("vc")
                b.wt(nc.scalar, "vc", vn)
                _ln_tile(b, ln5[:, :], ysb5[:, :], D, lnsc)
                xt = xg5[:, 1024 * tt:1024 * tt + 1024]
                b.V(lambda v, x_=xt: v.tensor_add(ysb5[:, :], ln5[:, :], x_),
                    reads=(ln5[:, 0:1], ysb5[:, 0:1]), writes=(ysb5[:, 0:1],))
                vn = b.incl("vc")
                b.wt(nc.sync, "vc", vn)
                b.ldy(stream_c.ap()[512 * go + 128 * tt:
                                    512 * go + 128 * tt + 128, :], ysb5[:, :])
    p5.close()
    b.finish()
    tp = b.tap("p5_stream", [TOK, D])
    if tp is not None:
        b.ldw(tp, stream_c.ap())
        b.wt(nc.gpsimd, "dw")
    if PHASE_LIMIT < 5:
        st.close()
        return b

    nc.leave_named_scope("ph5", _sid["ph5"], False)
    _sid["ph6"] = nc.enter_named_scope("ph6", False)[0]
    p6 = contextlib.ExitStack()

    def sb6(name, shape, dtype):
        return p6.enter_context(nc.sbuf_tensor(name, list(shape), dtype))

    gates = sb6("gates", [128, NTO * NEXP], F32)
    xg6 = sb6("xg6", [128, 4 * 1024], F32)
    ln6 = sb6("ln6", [128, 1024], F32)
    ysb6 = sb6("ysb6", [128, 1024], F32)

    p6a = contextlib.ExitStack()

    def sb6a(name, shape, dtype):
        return p6a.enter_context(nc.sbuf_tensor(name, list(shape), dtype))

    lnf32 = sb6a("lnf32", [128, 8 * 512], F32)
    lnxb6 = sb6a("lnxb6", [128, 8 * 512], BF16)
    wg1f = sb6a("wg1f", [128, 8 * G1], F32)
    wg2f = sb6a("wg2f", [128, 2 * NEXP], F32)
    gsb = sb6a("gsb", [128, 1024], F32)
    gm1 = sb6a("gm1", [128, 1], F32)
    gm2 = sb6a("gm2", [128, 1], F32)
    gtmp = sb6a("gtmp", [128, NEXP], F32)
    gtmp2 = sb6a("gtmp2", [128, NEXP], F32)
    gsum = sb6a("gsum", [128, 1], F32)
    gprob = sb6a("gprob", [128, NEXP], F32)
    b.ldy(wg1f[:, :].rearrange("p (c n) -> p c n", c=8),
          wg1.rearrange("(c p) n -> p c n", p=128), sem="p6g")
    b.ldy(wg2f[:, :].rearrange("p (c n) -> p c n", c=2),
          wg2.rearrange("(c p) n -> p c n", p=128), sem="p6g")
    with (nc.psum_tensor("trp6", [128, 1024], F32) as trp6,
          nc.psum_tensor("gp6", [128, 1024], F32) as gp6):
        for go in range(NGO):
            b.wt(nc.sync, "vc")
            b.wt(nc.sync, "sc")
            for tt in range(4):
                b.ldy(xg6[:, 1024 * tt:1024 * tt + 1024],
                      stream_c.ap()[512 * go + 128 * tt:
                                    512 * go + 128 * tt + 128, :], sem="p6x")
            ldn = b.n("p6x")
            b.wt(nc.vector, "p6x", ldn)
            b.wt(nc.scalar, "p6x", ldn)
            for tt in range(4):
                xap = xg6[:, 1024 * tt:1024 * tt + 1024]
                _ln_tile(b, ln6[:, :], xap, D, lnsc)
                lnn = b.incl("vc")
                b.wt(nc.tensor, "vc", lnn)
                if go > 0 or tt > 0:
                    b.wt(nc.tensor, "sc")
                for c in range(8):
                    b.tr(trp6[:, 128 * c:128 * c + 128],
                         ln6[:, 128 * c:128 * c + 128], idf[:, :])
                tmn = b.incl("tm")
                b.wt(nc.vector, "tm", tmn)
                b.wt(nc.scalar, "tm", tmn)
                dst = lnxb6[:, 128 * tt:]
                dstv = bass.AP(dst.tensor, dst.offset,
                               [[dst.ap[0][0], 128], [512, 8], [1, 128]])
                srcv = trp6[:, :].rearrange("p (c n) -> p c n", c=8)
                b.V(lambda v, dv=dstv, sv=srcv: v.tensor_copy(dv, sv),
                    writes=(lnxb6[:, 0:1],))
                evv = b.incl("vc")
                b.wt(nc.scalar, "vc", evv)
                dstf = lnf32[:, 128 * tt:]
                dstfv = bass.AP(dstf.tensor, dstf.offset,
                                [[dstf.ap[0][0], 128], [512, 8], [1, 128]])
                b.SC(lambda s, dv=dstfv, sv=srcv: s.copy(dv, sv),
                     writes=(lnf32[:, 0:1],))
                b.incl("sc")
            lnxv, lnxs = b.n("vc"), b.n("sc")
            b.wt(nc.sync, "vc", lnxv)
            b.ldy(lnx6.ap()[:, 4096 * go:4096 * go + 4096], lnxb6[:, :],
                  sem="p6s")
            b.wt(nc.tensor, "sc", lnxs)
            b.wt(nc.tensor, "p6g", b.n("p6g"))
            b.wt(nc.tensor, "vc")
            for m in range(2):
                for k in range(8):
                    b.mm(
                        gp6[:, 512 * m:512 * m + 512],
                        wg1f[:, G1 * k + 128 * m:G1 * k + 128 * m + 128],
                        lnf32[:, 512 * k:512 * k + 512],
                        start=(k == 0), stop=(k == 7))
            tmn = b.incl("tm")
            b.wt(nc.scalar, "tm", tmn)
            for m in range(2):
                b.SC(lambda s, m_=m: s.activation(
                    gsb[:, 512 * m_:512 * m_ + 512],
                    gp6[:, 512 * m_:512 * m_ + 512], AF.Gelu,
                    bias=bg1p[:, m_:m_ + 1]),
                    writes=(gsb[:, 0:1],))
            gsn = b.incl("sc")
            b.wt(nc.tensor, "sc", gsn)
            for tt in range(4):
                b.wt(nc.tensor, "vc")
                for m in range(2):
                    b.mm(
                        gp6[:, 0:NEXP],
                        gsb[:, 512 * m + 128 * tt:512 * m + 128 * tt + 128],
                        wg2f[:, NEXP * m:NEXP * m + NEXP],
                        start=(m == 0), stop=(m == 1))
                tmn = b.incl("tm")
                b.wt(nc.vector, "tm", tmn)
                b.V(lambda v: v.reduce_max(gm1[:, 0:1], gp6[:, 0:NEXP], AX,
                                           negate=True),
                    writes=(gm1[:, 0:1],))
                vn = b.incl("vc")
                b.wt(nc.scalar, "vc", vn)
                b.SC(lambda s: s.activation(gprob[:, :], gp6[:, 0:NEXP],
                                            AF.Exp, bias=gm1[:, 0:1],
                                            accum_out=gsum[:, 0:1]),
                     writes=(gprob[:, 0:1], gsum[:, 0:1]))
                scn = b.incl("sc")
                b.wt(nc.vector, "sc", scn)
                b.V(lambda v: v.reciprocal(gm2[:, 0:1], gsum[:, 0:1]),
                    writes=(gm2[:, 0:1],))
                b.V(lambda v: v.tensor_scalar(gtmp[:, :], gprob[:, :],
                                              gm2[:, 0:1], None, ALU.mult),
                    reads=(gm2[:, 0:1],), writes=(gtmp[:, 0:1],))
                b.V(lambda v: v.reduce_max(gm1[:, 0:1], gtmp[:, :], AX),
                    reads=(gtmp[:, 0:1],), writes=(gm1[:, 0:1],))
                b.V(lambda v: v.tensor_scalar(gtmp2[:, :], gtmp[:, :],
                                              gm1[:, 0:1], None, ALU.is_lt),
                    reads=(gm1[:, 0:1],), writes=(gtmp2[:, 0:1],))
                b.V(lambda v: v.tensor_mul(gprob[:, :], gtmp2[:, :],
                                           gtmp[:, :]),
                    reads=(gtmp2[:, 0:1],), writes=(gprob[:, 0:1],))
                b.V(lambda v: v.reduce_max(gm2[:, 0:1], gprob[:, :], AX),
                    reads=(gprob[:, 0:1],), writes=(gm2[:, 0:1],))
                b.V(lambda v: v.tensor_scalar(gtmp2[:, :], gtmp[:, :],
                                              gm2[:, 0:1], None, ALU.is_ge),
                    reads=(gm2[:, 0:1],), writes=(gtmp2[:, 0:1],))
                b.V(lambda v, t_=tt, g_=go: v.tensor_mul(
                    gates[:, NEXP * (4 * g_ + t_):NEXP * (4 * g_ + t_) + NEXP],
                    gtmp2[:, :], gtmp[:, :]),
                    reads=(gtmp2[:, 0:1],), writes=(gates[:, 0:1],))
                b.incl("vc")
    gatesv = b.n("vc")
    lnx6_dy = b.n("p6s")
    tp = b.tap("p6_gates", [128, NTO * NEXP])
    if tp is not None:
        b.wt(nc.sync, "vc", gatesv)
        b.ldy(tp, gates[:, :])
    b.wt(nc.sync, "vc", gatesv)
    b.wt(nc.gpsimd, "vc", gatesv)
    p6a.close()

    p6b = contextlib.ExitStack()

    def sb6b(name, shape, dtype):
        return p6b.enter_context(nc.sbuf_tensor(name, list(shape), dtype))

    winb = [sb6b(f"winb{i}", [128, 8 * EH], BF16) for i in range(2)]
    woutb = [sb6b(f"woutb{i}", [128, 16 * 1024], BF16) for i in range(2)]
    hfm6 = sb6b("hfm6", [128, 16 * 512], BF16)
    lnxg = sb6b("lnxg", [128, 2 * 8 * 512], BF16)
    otmp = sb6b("otmp", [128, 4 * 512], F32)

    def load_w(e):
        sl = e % 2
        b.ldw(winb[sl][:, :].rearrange("p (c n) -> p c n", c=8),
              w_in[e].rearrange("(c p) n -> p c n", p=128), sem=f"p6w{sl}")
        for half in range(2):
            b.ldw(woutb[sl][:, 8192 * half:8192 * half + 8192]
                  .rearrange("p (c n) -> p c n", c=8),
                  w_out[e].rearrange("(c p) n -> p c n",
                                     p=128)[:, 8 * half:8 * half + 8, :],
                  sem=f"p6w{sl}")
        return b.n(f"p6w{sl}")

    def load_lnx(idx):
        go, sl = idx % 4, idx % 2
        b.ldy(lnxg[:, 4096 * sl:4096 * sl + 4096],
              lnx6.ap()[:, 4096 * go:4096 * go + 4096], sem=f"p6l{sl}")
        return b.n(f"p6l{sl}")

    with (nc.psum_tensor("hp6a", [128, 512], F32) as hp6a,
          nc.psum_tensor("hp6b", [128, 512], F32) as hp6b,
          nc.psum_tensor("op6a", [128, 512], F32) as op6a,
          nc.psum_tensor("op6b", [128, 512], F32) as op6b):
        otmp_dw = {}
        wldn, lnld = {}, {}
        h_tm = {}
        exp_end_tm = {}
        if PHASE_LIMIT >= 6:
            wldn[0] = load_w(0)
            b.wt(nc.sync, "p6s", lnx6_dy)
            lnld[0] = load_lnx(0)
            lnld[1] = load_lnx(1)
        for e in range(NEXP if PHASE_LIMIT >= 6 else 0):
            if e + 1 < NEXP:
                if e >= 1:
                    b.wt(nc.gpsimd, "tm", exp_end_tm[e - 1])
                wldn[e + 1] = load_w(e + 1)
            for go in range(NGO):
                idx = 4 * e + go
                b.wt(nc.tensor, f"p6l{idx % 2}", lnld[idx])
                if go == 0:
                    b.wt(nc.tensor, f"p6w{e % 2}", wldn[e])
                hv = {}
                for hc in range(16):
                    ps = hp6a if hc % 2 == 0 else hp6b
                    if hc >= 2:
                        b.wt(nc.tensor, "sc", hv[hc - 2])
                    for k in range(8):
                        b.mm(
                            ps[:, :],
                            winb[e % 2][:, EH * k + 128 * hc:
                                        EH * k + 128 * hc + 128],
                            lnxg[:, 4096 * (idx % 2) + 512 * k:
                                 4096 * (idx % 2) + 512 * k + 512],
                            start=(k == 0), stop=(k == 7))
                    tmn = b.incl("tm")
                    b.wt(nc.scalar, "tm", tmn)
                    b.SC(lambda s, p=ps, h=hc: s.activation(
                        hfm6[:, 512 * h:512 * h + 512], p[:, :], AF.Gelu),
                        writes=(hfm6[:, 0:1],))
                    hv[hc] = b.incl("sc")
                h_tm[idx] = b.n("tm")
                if idx + 2 < 4 * NEXP:
                    b.wt(nc.sync, "tm", h_tm[idx])
                    lnld[idx + 2] = load_lnx(idx + 2)
                hdone = hv[15]
                b.wt(nc.tensor, "sc", hdone)
                if e == 0 and go == 0:
                    b.wt(nc.tensor, "vc", gatesv)
                for tt in range(4):
                    b.wt(nc.tensor, "vc")
                    for hc in range(16):
                        for n2 in range(2):
                            ps = op6a if n2 == 0 else op6b
                            b.mm(
                                ps[:, :],
                                hfm6[:, 512 * hc + 128 * tt:
                                     512 * hc + 128 * tt + 128],
                                woutb[e % 2][:, 1024 * hc + 512 * n2:
                                             1024 * hc + 512 * n2 + 512],
                                start=(hc == 0), stop=(hc == 15))
                    tmn = b.incl("tm")
                    gt = 4 * go + tt
                    b.wt(nc.vector, "tm", tmn)
                    for n2 in range(2):
                        ps = op6a if n2 == 0 else op6b
                        slot = (2 * gt + n2) % 4
                        if slot in otmp_dw:
                            b.wt(nc.vector, f"p6o{slot}", otmp_dw[slot])
                        b.V(lambda v, p=ps, sl=slot, g_=gt, e_=e:
                            v.tensor_scalar(
                                otmp[:, 512 * sl:512 * sl + 512], p[:, :],
                                gates[:, NEXP * g_ + e_:NEXP * g_ + e_ + 1],
                                None, ALU.mult),
                            writes=(otmp[:, 0:1],))
                        vn = b.incl("vc")
                        b.wt(nc.gpsimd, "vc", vn)
                        otmp_dw[slot] = b.ldw(
                            oacc_d.ap()[128 * gt:128 * gt + 128,
                                        512 * n2:512 * n2 + 512],
                            otmp[:, 512 * slot:512 * slot + 512],
                            accum=(e > 0), sem=f"p6o{slot}")
            exp_end_tm[e] = b.n("tm")
    b.wt(nc.vector, "tm")
    b.wt(nc.scalar, "tm")
    p6b.close()

    for sl in range(4):
        b.wt(nc.sync, f"p6o{sl}")
    tp = b.tap("p6_oacc", [TOK, D])
    if tp is not None:
        b.ldw(tp, oacc_d.ap())
    for tt in range(NTO if PHASE_LIMIT >= 7 else 0):
        b.wt(nc.sync, "vc")
        b.wt(nc.sync, "sc")
        par = tt % 2
        b.ldy(xg6[:, 2048 * par:2048 * par + 1024],
              stream_c.ap()[128 * tt:128 * tt + 128, :], sem="p6x")
        b.ldy(xg6[:, 2048 * par + 1024:2048 * par + 2048],
              oacc_d.ap()[128 * tt:128 * tt + 128, :], sem="p6x")
        ldn = b.n("p6x")
        b.wt(nc.vector, "p6x", ldn)
        b.wt(nc.scalar, "p6x", ldn)
        oap = xg6[:, 2048 * par + 1024:2048 * par + 2048]
        _ln_tile(b, ln6[:, :], oap, D, lnsc)
        xt = xg6[:, 2048 * par:2048 * par + 1024]
        b.V(lambda v, x_=xt: v.tensor_add(ysb6[:, :], ln6[:, :], x_),
            reads=(ln6[:, 0:1],), writes=(ysb6[:, 0:1],))
        vn = b.incl("vc")
        b.wt(nc.sync, "vc", vn)
        b.ldy(out[128 * tt:128 * tt + 128, :], ysb6[:, :])
    p6.close()
    b.wt(nc.sync, "dy")
    b.wt(nc.gpsimd, "dw")
    nc.leave_named_scope("ph6", _sid["ph6"], False)
    st.close()
    return b


_NC_CACHE = {}


def _get_nc():
    key = (PHASE_LIMIT, tuple(DEV_TAPS))
    if key not in _NC_CACHE:
        _NC_CACHE[key] = build_nc()
    return _NC_CACHE[key]


def kernel(**inputs):
    f = np.float32
    x = np.ascontiguousarray(inputs["L0_attn_x"], dtype=f)
    qk_w = np.asarray(inputs["L0_attn_qk_w"], dtype=f)
    qk_b = np.asarray(inputs["L0_attn_qk_b"], dtype=f)
    v_w = np.asarray(inputs["L0_attn_v_w"], dtype=f)
    gains = [np.asarray(inputs["L0_attn_resid_gain"], dtype=f).reshape(16),
             np.asarray(inputs["L0_attn_skip_gain"], dtype=f).reshape(16),
             np.asarray(inputs["L0_attn_centre_gain"], dtype=f).reshape(16)]
    in_maps = []
    for core in range(8):
        p, hh = core // 2, core % 2
        hs = slice(512 * hh, 512 * hh + 512)
        m = {
            "x_seq": np.ascontiguousarray(x[p]),
            "x_own": np.ascontiguousarray(x[p, 2048 * hh:2048 * hh + 2048]),
            "tok_base": np.array([[2048 * hh]], dtype=np.uint32),
            "qkw_own": np.ascontiguousarray(
                np.concatenate([qk_w[:, 0:1024][:, hs],
                                qk_w[:, 1024:2048][:, hs]], axis=1)),
            "vw_own": np.ascontiguousarray(v_w[:, hs]),
            "qb_pp": np.ascontiguousarray(qk_b[0:1024][hs].reshape(4, 128).T),
            "kb_pp": np.ascontiguousarray(qk_b[1024:2048][hs].reshape(4, 128).T),
            "cprojw": np.ascontiguousarray(inputs["L0_attn_cproj_w"], dtype=f),
            "gain_r": np.ascontiguousarray(
                np.broadcast_to(gains[0][8 * hh:8 * hh + 8][None, :], (64, 8))),
            "gain_s128": np.ascontiguousarray(
                np.broadcast_to(gains[1][8 * hh:8 * hh + 8][None, :], (128, 8))),
            "gain_c": np.ascontiguousarray(
                np.broadcast_to(gains[2][8 * hh:8 * hh + 8][None, :], (64, 8))),
            "w1": np.ascontiguousarray(inputs["L0_ffn_w1"], dtype=f),
            "b1_pp": np.ascontiguousarray(
                np.asarray(inputs["L0_ffn_b1"], dtype=f).reshape(32, 128).T),
            "w2": np.ascontiguousarray(inputs["L0_ffn_w2"], dtype=f),
            "b2_row": np.ascontiguousarray(
                np.asarray(inputs["L0_ffn_b2"], dtype=f)[None, :]),
            "wq1": np.ascontiguousarray(inputs["L1_attn_Wq"], dtype=f),
            "wk1": np.ascontiguousarray(inputs["L1_attn_Wk"], dtype=f),
            "wv1": np.ascontiguousarray(inputs["L1_attn_Wv"], dtype=f),
            "wo1": np.ascontiguousarray(inputs["L1_attn_Wo"], dtype=f),
            "bo_row": np.ascontiguousarray(
                np.asarray(inputs["L1_attn_bo"], dtype=f)[None, :]),
            "wf_rep": np.ascontiguousarray(
                np.concatenate([np.asarray(inputs["L1_attn_Wf"], dtype=f)] * 2,
                               axis=0)),
            "bf_row": np.ascontiguousarray(
                np.asarray(inputs["L1_attn_bf"], dtype=f)[None, :]),
            "bf_pp": np.ascontiguousarray(
                np.asarray(inputs["L1_attn_bf"], dtype=f)[:, None]),
            "wg1": np.ascontiguousarray(inputs["L1_ffn_wg1"], dtype=f),
            "bg1_pp": np.ascontiguousarray(
                np.asarray(inputs["L1_ffn_bg1"], dtype=f).reshape(2, 128).T),
            "wg2": np.ascontiguousarray(inputs["L1_ffn_wg2"], dtype=f),
            "w_in": np.ascontiguousarray(inputs["L1_ffn_w_in"], dtype=f),
            "w_out": np.ascontiguousarray(inputs["L1_ffn_w_out"], dtype=f),
        }
        in_maps.append(m)

    b = _get_nc()
    res = run_bass_kernel_spmd(b.nc, in_maps, core_ids=list(range(8)))
    if DEV_TAPS:
        kernel.last_results = res.results
    outp = np.empty((4, S, D), dtype=np.float32)
    for core in range(8):
        p, hh = core // 2, core % 2
        outp[p, 2048 * hh:2048 * hh + 2048] = res.results[core]["out"]
    return outp


# version 05
# speedup vs baseline: 1.1547x; 1.0055x over previous
import contextlib
import numpy as np
import ml_dtypes
import concourse.bass as bass
import concourse.mybir as mybir
from concourse.bass_utils import run_bass_kernel_spmd

F32 = mybir.dt.float32
BF16 = mybir.dt.bfloat16
U32 = mybir.dt.uint32
AF = mybir.ActivationFunctionType
ALU = mybir.AluOpType
AX = mybir.AxisListType.X
BF = ml_dtypes.bfloat16
ds = bass.ds

S, D = 4096, 1024
NT, NG = 32, 8
TOK, NTO, NGO = 2048, 16, 4
HD, HOWN, NPAIR = 64, 8, 4
H1, FDIM = 16, 64
W0 = 4096
NEXP, EH, G1 = 8, 2048, 256
EPS = 1e-5
NEGBIG = -30000.0

DEV_TAPS = ()
PHASE_LIMIT = 99


class Build:
    def __init__(self):
        nc = bass.Bass("TRN2", target_bir_lowering=False)
        self.nc = nc
        self.cnt = {}
        self.sems = {}
        for name in ("dy", "dw", "tm", "sc", "vc", "gp", "cc",
                     "px0", "px1", "pt0", "pt1", "sag", "pa0", "pa1",
                     "p3x", "p4x", "p4w", "p5x", "p5w", "p5r", "p5k",
                     "p6x", "p6g", "p6s", "p6l0", "p6l1", "p6w",
                     "p6o0", "p6o1", "p6o2", "p6o3", "pcp", "p0w", "pnb", "p6w0", "p6w1"):
            self.sems[name] = nc.alloc_semaphore(name)
            self.cnt[name] = 0
        self.dma_owner = {}
        self.vdirty = set()
        self.sdirty = set()
        self.last = {}
        self.taps = {}

    def inc(self, inst, name, k=1):
        inst.then_inc(self.sems[name], k)
        self.cnt[name] += k
        return self.cnt[name]

    def mm(self, *a, **kw):
        inst = self.nc.tensor.matmul(*a, **kw)
        self.last["tm"] = inst
        return inst

    def tr(self, *a, **kw):
        inst = self.nc.tensor.transpose(*a, **kw)
        self.last["tm"] = inst
        return inst

    def incl(self, name):
        return self.inc(self.last[name], name)

    def n(self, name):
        return self.cnt[name]

    def wt(self, eng, name, val=None):
        if val is None:
            val = self.cnt[name]
        if val > 0:
            eng.wait_ge(self.sems[name], val)

    def ldy(self, out, in_, sem="dy"):
        self.dma_owner[sem] = "sync"
        return self.inc(self.nc.sync.dma_start(out=out, in_=in_), sem, 16)

    def ldw(self, out, in_, accum=False, sem="dw"):
        kw = {"accum_op": ALU.add} if accum else {}
        self.dma_owner[sem] = "gpsimd"
        return self.inc(self.nc.gpsimd.dma_start(out=out, in_=in_, **kw), sem, 16)

    def V(self, emit, reads=(), writes=()):
        rn = {ap.tensor.name for ap in reads}
        if rn & self.vdirty:
            self.nc.vector.drain()
            self.vdirty = set()
        inst = emit(self.nc.vector)
        self.vdirty |= {ap.tensor.name for ap in writes}
        self.last["vc"] = inst
        return inst

    def SC(self, emit, reads=(), writes=()):
        rn = {ap.tensor.name for ap in reads}
        if rn & self.sdirty:
            self.nc.scalar.drain()
            self.sdirty = set()
        inst = emit(self.nc.scalar)
        self.sdirty |= {ap.tensor.name for ap in writes}
        self.last["sc"] = inst
        return inst

    def finish(self):
        for sem, owner in self.dma_owner.items():
            eng = self.nc.sync if owner == "sync" else self.nc.gpsimd
            self.wt(eng, sem)
        self.nc.all_engine_barrier()
        self.vdirty = set()
        self.sdirty = set()

    def tap(self, name, shape, dtype=F32):
        if name in DEV_TAPS:
            t = self.nc.dram_tensor("tap_" + name, list(shape), dtype,
                                    kind="ExternalOutput")
            self.taps[name] = t.ap()
            return self.taps[name]
        return None


def _ln_tile(b, out_sb, x_ap, W, sc):
    nc = b.nc
    b.V(lambda v: v.reduce_sum(sc["ssum"][:, 0:1], x_ap, AX),
        reads=(x_ap,), writes=(sc["ssum"][:, 0:1],))
    b.SC(lambda s: s.activation(sc["sqscr"][:, 0:W], x_ap, AF.Square,
                                accum_out=sc["sqs"][:, 0:1]),
         reads=(x_ap,), writes=(sc["sqscr"][:, 0:W], sc["sqs"][:, 0:1]))
    b.incl("sc")
    b.wt(nc.vector, "sc")
    b.V(lambda v: v.tensor_scalar(sc["mean"][:, 0:1], sc["ssum"][:, 0:1],
                                  1.0 / W, None, ALU.mult),
        reads=(sc["ssum"][:, 0:1],), writes=(sc["mean"][:, 0:1],))
    b.V(lambda v: v.tensor_scalar(sc["msq"][:, 0:1], sc["sqs"][:, 0:1],
                                  1.0 / W, None, ALU.mult),
        reads=(sc["sqs"][:, 0:1],), writes=(sc["msq"][:, 0:1],))
    b.V(lambda v: v.tensor_mul(sc["mm"][:, 0:1], sc["mean"][:, 0:1],
                               sc["mean"][:, 0:1]),
        reads=(sc["mean"][:, 0:1],), writes=(sc["mm"][:, 0:1],))
    b.V(lambda v: v.tensor_sub(sc["var"][:, 0:1], sc["msq"][:, 0:1],
                               sc["mm"][:, 0:1]),
        reads=(sc["msq"][:, 0:1], sc["mm"][:, 0:1]), writes=(sc["var"][:, 0:1],))
    b.V(lambda v: v.tensor_scalar(sc["vare"][:, 0:1], sc["var"][:, 0:1],
                                  EPS, None, ALU.add),
        reads=(sc["var"][:, 0:1],), writes=(sc["vare"][:, 0:1],))
    b.V(lambda v: v.reciprocal(sc["rv"][:, 0:1], sc["vare"][:, 0:1]),
        reads=(sc["vare"][:, 0:1],), writes=(sc["rv"][:, 0:1],))
    b.incl("vc")
    b.wt(nc.scalar, "vc")
    b.SC(lambda s: s.activation(sc["rstd"][:, 0:1], sc["rv"][:, 0:1], AF.Sqrt),
         reads=(sc["rv"][:, 0:1],), writes=(sc["rstd"][:, 0:1],))
    b.incl("sc")
    b.wt(nc.vector, "sc")
    b.V(lambda v: v.scalar_tensor_tensor(sc["nmean"][:, 0:1], sc["mean"][:, 0:1],
                                         -1.0, sc["rstd"][:, 0:1],
                                         ALU.mult, ALU.mult),
        reads=(sc["mean"][:, 0:1],), writes=(sc["nmean"][:, 0:1],))
    b.V(lambda v: v.tensor_scalar(out_sb, x_ap, sc["rstd"][:, 0:1],
                                  sc["nmean"][:, 0:1], ALU.mult, ALU.add),
        reads=(x_ap, sc["nmean"][:, 0:1]), writes=(out_sb,))
    return b.n("vc"), b.n("sc")


def _make_consts():
    idb = np.eye(128, dtype=BF)
    idf = np.eye(128, dtype=np.float32)
    tri = np.where(np.arange(128)[None, :] > np.arange(128)[:, None],
                   np.float32(NEGBIG), np.float32(0.0)).astype(BF)
    ones_b = np.ones((128, 128), dtype=BF)
    ones_f = np.ones((1, 128), dtype=np.float32)
    sel2 = np.zeros((128, 2), dtype=BF)
    sel2[0:64, 0] = 1
    sel2[64:128, 1] = 1
    inv = (1.0 / (np.arange(S, dtype=np.float64) + 1.0)).astype(BF)[None, :]
    triinv = np.zeros((NG, 4, 128, 512), dtype=BF)
    for g in range(NG):
        qg = 512 * g + np.arange(512)
        for c in range(4):
            kg = 128 * (4 * g + c) + np.arange(128)
            m = (kg[:, None] <= qg[None, :]).astype(np.float64)
            triinv[g, c] = (m / (qg[None, :] + 1.0)).astype(BF)
    return idb, idf, tri, ones_b, ones_f, sel2, inv, triinv


def _strided3(ap_base, pstep, a, b_, c, d):
    return bass.AP(ap_base.tensor, ap_base.offset, [[pstep, 128], [a, b_], [c, d]])


def build_nc():
    b = Build()
    nc = b.nc
    st = contextlib.ExitStack()

    def inp(name, shape, dtype=F32):
        return nc.dram_tensor(name, list(shape), dtype, kind="ExternalInput").ap()

    x_seq = inp("x_seq", [S, D])
    x_own = inp("x_own", [TOK, D])
    tok_base = inp("tok_base", [1, 1], U32)
    qkw_own = inp("qkw_own", [D, 2 * 512])
    vw_own = inp("vw_own", [D, 512])
    qb_pp = inp("qb_pp", [128, 4])
    kb_pp = inp("kb_pp", [128, 4])
    cprojw = inp("cprojw", [D, D])
    gain_r = inp("gain_r", [64, 8])
    gs128i = inp("gain_s128", [128, 8])
    gain_c = inp("gain_c", [64, 8])
    w1 = inp("w1", [D, W0])
    b1_pp = inp("b1_pp", [128, 32])
    w2 = inp("w2", [W0, D])
    b2_row = inp("b2_row", [1, D])
    wq1 = inp("wq1", [D, D])
    wk1 = inp("wk1", [D, D])
    wv1 = inp("wv1", [D, D])
    wo1 = inp("wo1", [D, D])
    bo_row = inp("bo_row", [1, D])
    wf_rep = inp("wf_rep", [128, FDIM])
    bf_row = inp("bf_row", [1, FDIM])
    bf_pp = inp("bf_pp", [64, 1])
    wg1 = inp("wg1", [D, G1])
    bg1_pp = inp("bg1_pp", [128, 2])
    wg2 = inp("wg2", [G1, NEXP])
    w_in = inp("w_in", [NEXP, D, EH])
    w_out = inp("w_out", [NEXP, EH, D])

    out = nc.dram_tensor("out", [TOK, D], F32, kind="ExternalOutput").ap()

    idb_h, idf_h, tri_h, onesb_h, onesf_h, sel2_h, inv_h, triinv_h = _make_consts()
    c_idb = nc.inline_tensor(idb_h, name="c_idb").ap()
    c_idf = nc.inline_tensor(idf_h, name="c_idf").ap()
    c_tri = nc.inline_tensor(tri_h, name="c_tri").ap()
    c_onesb = nc.inline_tensor(onesb_h, name="c_onesb").ap()
    c_onesf = nc.inline_tensor(onesf_h, name="c_onesf").ap()
    c_sel2 = nc.inline_tensor(sel2_h, name="c_sel2").ap()
    c_inv = nc.inline_tensor(inv_h, name="c_inv").ap()
    c_triinv = nc.inline_tensor(triinv_h, name="c_triinv").ap()

    ag_in = nc.dram_tensor("ag_in", [512, S], BF16)
    ag_out = nc.dram_tensor("ag_out", [2, 512, S], BF16)
    ar_in = nc.dram_tensor("ar_in", [64, H1, 65], F32)
    ar_out = nc.dram_tensor("ar_out", [64, H1, 65], F32)
    stream_a = nc.dram_tensor("stream_a", [TOK, D], F32)
    stream_b = nc.dram_tensor("stream_b", [TOK, D], F32)
    stream_c = nc.dram_tensor("stream_c", [TOK, D], F32)
    lnx6 = nc.dram_tensor("lnx6", [128, NTO * 1024], BF16)
    oacc_d = nc.dram_tensor("oacc_d", [TOK, D], F32)

    def sb(name, shape, dtype):
        return st.enter_context(nc.sbuf_tensor(name, list(shape), dtype))

    idb = sb("idb", [128, 128], BF16)
    idf = sb("idf", [128, 128], F32)
    tri = sb("tri", [128, 128], BF16)
    onesb = sb("onesb", [128, 128], BF16)
    onesf = sb("onesf", [1, 128], F32)
    sel2 = sb("sel2", [128, 2], BF16)
    negb_rep = sb("negb_rep", [128, 8], F32)
    grep = sb("grep", [64, 8], F32)
    gs128 = sb("gs128", [128, 8], F32)
    craw = sb("craw", [64, 8], F32)
    cnrep = sb("cnrep", [64, 8], F32)
    qbp = sb("qbp", [128, 4], F32)
    kbp = sb("kbp", [128, 4], F32)
    b1p = sb("b1p", [128, 32], F32)
    wfr = sb("wfr", [128, FDIM], BF16)
    bfr = sb("bfr", [1, FDIM], BF16)
    bfp = sb("bfp", [64, 1], F32)
    bg1p = sb("bg1p", [128, 2], F32)
    tokb = sb("tokb", [1, 1], U32)
    lnsc = {k: sb("lnsc_" + k, [128, 1], F32)
            for k in ("ssum", "sqs", "mean", "msq", "mm", "var", "vare", "rv",
                      "rstd", "nmean")}
    lnsc["sqscr"] = sb("lnsc_sqscr", [128, 1024], BF16)
    q2all = sb("q2all", [2, 32], F32)
    k2all = sb("k2all", [2, 32], F32)
    nb2 = sb("nb2", [2, 4], F32)
    nb2b = sb("nb2b", [2, 4], F32)
    nbr1 = sb("nbr1", [1, 4], F32)
    tiny = sb("tiny", [2, 8], F32)

    b.ldy(idb[:, :], c_idb)
    b.ldy(idf[:, :], c_idf)
    b.ldy(tri[:, :], c_tri)
    b.ldy(onesb[:, :], c_onesb)
    b.ldy(onesf[:, :], c_onesf)
    b.ldy(sel2[:, :], c_sel2)
    b.ldy(grep[:, :], gain_r)
    b.ldy(gs128[:, :], gs128i)
    b.ldy(craw[:, :], gain_c)
    b.ldy(qbp[:, :], qb_pp)
    b.ldy(kbp[:, :], kb_pp)
    b.ldy(b1p[:, :], b1_pp)
    b.ldy(bfp[:, :], bf_pp)
    b.ldy(bg1p[:, :], bg1_pp)
    b.ldy(tokb[:, :], tok_base)
    b.ldw(wfr[:, :], wf_rep)
    b.ldw(bfr[:, :], bf_row)
    setup_dy = b.n("dy")
    b.wt(nc.vector, "dy", setup_dy)
    b.V(lambda v: v.tensor_scalar(cnrep[:, :], craw[:, :], -1.0, None, ALU.mult),
        reads=(craw[:, 0:1],), writes=(cnrep[:, 0:1],))
    b.incl("vc")

    _sid = {}
    _sid["ph0"] = nc.enter_named_scope("ph0", False)[0]
    p01 = contextlib.ExitStack()

    def sb1(name, shape, dtype):
        return p01.enter_context(nc.sbuf_tensor(name, list(shape), dtype))

    invp = sb1("invp", [1, S], BF16)
    carryT = sb1("carryT", [1, NG * 512], BF16)
    qfm = sb1("qfm", [128, 4 * S], BF16)
    kfm = sb1("kfm", [128, 4 * S], BF16)
    vext = sb1("vext", [128, NT * 520], BF16)
    probs = sb1("probs", [128, 2048], BF16)
    skipI = sb1("skipI", [128, 8 * 128], BF16)
    triinv_s = sb1("triinv_s", [128, 2 * 2048], BF16)
    recip_sb = sb1("recip_sb", [1, 1024], F32)
    pvs_sb = sb1("pvs_sb", [128, 1024], F32)
    r_sb = sb1("r_sb", [128, 1024], F32)
    ta_sb = sb1("ta_sb", [64, 512], F32)
    tb_sb = sb1("tb_sb", [64, 512], F32)
    outpair = sb1("outpair", [128, 512], BF16)
    gs_sb = sb1("gs_sb", [64, 8], F32)
    cacc = [sb1("cacc0", [64, 8], F32), sb1("cacc1", [64, 8], F32)]
    ct_sb = sb1("ct_sb", [8, 64], F32)

    p0t = contextlib.ExitStack()

    def sb0(name, shape, dtype):
        return p0t.enter_context(nc.sbuf_tensor(name, list(shape), dtype))

    qkwb = sb0("qkwb", [128, 8 * 1024], BF16)
    vwb = sb0("vwb", [128, 8 * 512], BF16)

    xa = sb0("xa", [128, 2 * 1024], F32)
    lnt = sb0("lnt", [128, 2 * 1024], F32)
    lnxfm = sb0("lnxfm", [128, 4096], BF16)
    sqscr2 = sb0("sqscr2", [128, 512], BF16)

    b.ldy(invp[:, :], c_inv)
    b.ldw(qkwb[:, :].rearrange("p (c n) -> p c n", c=8),
          qkw_own.rearrange("(c p) n -> p c n", p=128), sem="p0w")
    b.ldw(vwb[:, :].rearrange("p (c n) -> p c n", c=8),
          vw_own.rearrange("(c p) n -> p c n", p=128), sem="p0w")
    dw_w0 = b.n("p0w")
    b.V(lambda v: v.memset(carryT[0:1, 0:512], 0.0), writes=(carryT[0:1, 0:1],))
    vxv = vext[:, :].rearrange("p (t h o) -> p t h o", t=NT, h=HOWN)
    b.V(lambda v: v.memset(vxv[:, :, :, 64:65], 1.0), writes=(vext[:, 0:1],))
    vready0 = b.incl("vc")

    with (nc.psum_tensor("trps", [128, 1024], F32) as trps,
          nc.psum_tensor("mmA", [128, 512], F32) as mmA,
          nc.psum_tensor("mmB", [128, 512], F32) as mmB,
          nc.psum_tensor("vps", [128, 512], F32) as vps,
          nc.psum_tensor("q2ps", [2, 512], F32) as q2ps,
          nc.psum_tensor("gsps", [65, 8], F32) as gsps,
          nc.psum_tensor("ctps", [8, 64], F32) as ctps):
        lnv, lnsct, trv = {}, {}, {}
        last_sel2_tm = 0
        b.wt(nc.tensor, "dy", setup_dy)
        for t in range(NT):
            par = t % 2
            gpar = (t // 4) % 2
            if t >= 2:
                b.wt(nc.sync, "vc", trv[t - 2])
                b.wt(nc.sync, "sc", lnsct[t - 2])
            ldn = b.ldy(xa[:, 1024 * par:1024 * par + 1024],
                        x_seq[128 * t:128 * t + 128, :], sem=f"px{par}")
            b.wt(nc.vector, f"px{par}", ldn)
            b.wt(nc.scalar, f"px{par}", ldn)
            xap = xa[:, 1024 * par:1024 * par + 1024]
            lnap = lnt[:, 1024 * par:1024 * par + 1024]
            _v, _s = _ln_tile(b, lnap, xap, D, lnsc)
            lnv[t] = b.incl("vc")
            lnsct[t] = _s
            b.wt(nc.tensor, "vc", lnv[t])
            if t >= 1:
                b.wt(nc.tensor, "vc", trv[t - 1])
            for c in range(8):
                b.tr(trps[:, 128 * c:128 * c + 128],
                                    lnap[:, 128 * c:128 * c + 128], idf[:, :])
            tmn = b.incl("tm")
            b.wt(nc.vector, "tm", tmn)
            dst = lnxfm[:, 128 * (t % 4):]
            dstv = bass.AP(dst.tensor, dst.offset,
                           [[dst.ap[0][0], 128], [512, 8], [1, 128]])
            srcv = trps[:, :].rearrange("p (c n) -> p c n", c=8)
            b.V(lambda v, dv=dstv, sv=srcv: v.tensor_copy(dv, sv),
                writes=(lnxfm[:, 0:1],))
            trv[t] = b.incl("vc")

            if t % 4 == 3:
                gf = t // 4
                rhs_base = 0
                b.wt(nc.tensor, "vc", trv[t])
                if gf == 0:
                    b.wt(nc.tensor, "p0w", dw_w0)
                    b.wt(nc.tensor, "vc", vready0)
                for which, wofs, dst_fm, biasp, scl in (
                        ("q", 0, qfm, qbp, 0.125), ("k", 512, kfm, kbp, None)):
                    for pc in range(4):
                        ps = mmA if pc % 2 == 0 else mmB
                        b.wt(nc.tensor, "vc")
                        for k in range(8):
                            b.mm(
                                ps[:, :],
                                qkwb[:, 1024 * k + wofs + 128 * pc:
                                     1024 * k + wofs + 128 * pc + 128],
                                lnxfm[:, rhs_base + 512 * k:
                                      rhs_base + 512 * k + 512],
                                start=(k == 0), stop=(k == 7))
                        tmn = b.incl("tm")
                        b.wt(nc.vector, "tm", tmn)
                        dstap = dst_fm[:, 4096 * pc + 512 * gf:
                                       4096 * pc + 512 * gf + 512]
                        if scl is None:
                            b.V(lambda v, o=dstap, p=ps, bp=biasp, c=pc:
                                v.tensor_scalar(o, p[:, :], bp[:, c:c + 1],
                                                None, ALU.add),
                                writes=(dst_fm[:, 0:1],))
                        else:
                            b.V(lambda v, o=dstap, p=ps, bp=biasp, c=pc, s_=scl:
                                v.tensor_scalar(o, p[:, :], bp[:, c:c + 1],
                                                s_, ALU.add, ALU.mult),
                                writes=(dst_fm[:, 0:1],))
                        vn = b.incl("vc")
                        b.wt(nc.scalar, "vc", vn)
                        b.wt(nc.scalar, "tm", last_sel2_tm)
                        b.SC(lambda s, i=dstap: s.activation(
                            sqscr2[:, 0:512], i, AF.Square),
                            writes=(sqscr2[:, 0:1],))
                        scn = b.incl("sc")
                        b.wt(nc.tensor, "sc", scn)
                        b.mm(q2ps[:, :], sel2[:, :],
                                         sqscr2[:, 0:512], start=True, stop=True)
                        last_sel2_tm = tmn = b.incl("tm")
                        b.wt(nc.vector, "tm", tmn)
                        dcol = (q2all if which == "q" else k2all)
                        b.V(lambda v, d=dcol, col=8 * pc + gf: v.reduce_max(
                            d[:, col:col + 1], q2ps[:, :], AX),
                            writes=(dcol[:, 0:1],))
                        b.incl("vc")
                for tt in range(4 * gf, 4 * gf + 4):
                    b.wt(nc.tensor, "vc")
                    for k in range(8):
                        b.mm(
                            vps[:, :],
                            lnxfm[:, rhs_base + 512 * k + 128 * (tt % 4):
                                  rhs_base + 512 * k + 128 * (tt % 4) + 128],
                            vwb[:, 512 * k:512 * k + 512],
                            start=(k == 0), stop=(k == 7))
                    tmn = b.incl("tm")
                    b.wt(nc.vector, "tm", tmn)
                    dst = vext[:, 520 * tt:]
                    dstv = bass.AP(dst.tensor, dst.offset,
                                   [[dst.ap[0][0], 128], [65, 8], [1, 64]])
                    srcv = vps[:, :].rearrange("p (h o) -> p h o", h=8)
                    b.V(lambda v, dv=dstv, sv=srcv: v.tensor_copy(dv, sv),
                        writes=(vext[:, 0:1],))
                    vpse = b.incl("vc")
                b.wt(nc.tensor, "vc", vpse)
                for lh in range(8):
                    for ti in range(4):
                        tt2 = 4 * gf + ti
                        b.mm(gsps[:, lh:lh + 1],
                             vext[:, 520 * tt2 + 65 * lh:
                                  520 * tt2 + 65 * lh + 65],
                             onesb[:, 0:1],
                             start=(ti == 0), stop=(ti == 3))
                tmn = b.incl("tm")
                b.wt(nc.vector, "tm", tmn)
                b.V(lambda v: v.tensor_copy(gs_sb[:, :], gsps[0:64, :]),
                    writes=(gs_sb[:, 0:1],))
                vn = b.incl("vc")
                src_c = cacc[gf % 2]
                dst_c = cacc[1 - gf % 2]
                if gf > 0:
                    b.wt(nc.tensor, "vc", vn)
                    b.tr(ctps[:, :], src_c[:, 0:8],
                                        idf[0:64, 0:64])
                    tmn = b.incl("tm")
                    b.wt(nc.vector, "tm", tmn)
                    b.wt(nc.vector, "dw")
                    b.V(lambda v: v.tensor_copy(ct_sb[:, :], ctps[0:8, 0:64]),
                        writes=(ct_sb[0:1, 0:1],))
                    vn2 = b.incl("vc")
                    b.wt(nc.gpsimd, "vc", vn2)
                    b.ldw(carryT[0:1, 512 * gf:512 * gf + 512], ct_sb[:, :])
                    b.V(lambda v, d=dst_c, s_=src_c: v.tensor_add(
                        d[:, :], s_[:, :], gs_sb[:, :]),
                        reads=(gs_sb[:, 0:1],), writes=(dst_c[:, 0:1],))
                else:
                    b.V(lambda v, d=dst_c: v.tensor_copy(d[:, :], gs_sb[:, :]),
                        reads=(gs_sb[:, 0:1],), writes=(dst_c[:, 0:1],))
                b.incl("vc")

        for c in range(4):
            for d_, a_ in ((q2all, 0), (k2all, 1)):
                src = bass.AP(d_, 8 * c, [[32, 2], [1, 8]])
                b.V(lambda v, s_=src, aa=a_, cc=c: v.reduce_max(
                    tiny[:, 4 * aa + cc:4 * aa + cc + 1], s_, AX),
                    reads=(d_[:, 0:1],), writes=(tiny[:, 0:1],))
        b.V(lambda v: v.tensor_mul(nb2[:, :], tiny[:, 0:4], tiny[:, 4:8]),
            reads=(tiny[:, 0:1],), writes=(nb2[:, 0:1],))
        vn = b.incl("vc")
        b.wt(nc.scalar, "vc", vn)
        b.SC(lambda s: s.activation(nb2b[:, :], nb2[:, :], AF.Sqrt),
             reads=(nb2[:, 0:1],), writes=(nb2b[:, 0:1],))
        scn = b.incl("sc")
        b.wt(nc.vector, "sc", scn)
        b.V(lambda v: v.tensor_scalar(nb2[:, :], nb2b[:, :], -1.0, None,
                                      ALU.mult),
            reads=(nb2b[:, 0:1],), writes=(nb2[:, 0:1],))
        vn = b.incl("vc")
        b.wt(nc.sync, "vc", vn)
        nbl = b.ldy(nbr1[:, :], nb2[1:2, :], sem="pnb")
        b.wt(nc.tensor, "pnb", nbl)
        b.wt(nc.tensor, "vc", vn)
        b.mm(mmA[:, 0:4], onesf[0:1, :], nb2[0:1, :],
                         start=True, stop=True)
        b.mm(mmA[:, 4:8], onesf[0:1, :], nbr1[0:1, :],
                         start=True, stop=True)
        tmn = b.incl("tm")
        b.wt(nc.vector, "tm", tmn)
        dstv = bass.AP(negb_rep, 0, [[8, 128], [1, 2], [2, 4]])
        srcv = mmA[:, 0:8].rearrange("p (r c) -> p r c", r=2)
        b.V(lambda v: v.tensor_copy(dstv, srcv), writes=(negb_rep[:, 0:1],))
        negb_v = b.incl("vc")

        for nm, t_, w in (("p0_qfm", qfm, 4 * S), ("p0_kfm", kfm, 4 * S),
                          ("p0_vext", vext, NT * 520)):
            tp = b.tap(nm, [128, w], BF16)
            if tp is not None:
                b.wt(nc.sync, "vc")
                b.ldy(tp, t_[:, :])
        tp = b.tap("p0_negb", [128, 8])
        if tp is not None:
            b.wt(nc.sync, "vc")
            b.ldy(tp, negb_rep[:, :])
        tp = b.tap("p0_carryT", [1, NG * 512], BF16)
        if tp is not None:
            b.wt(nc.sync, "dw")
            b.ldy(tp, carryT[:, :])

    p0t.close()
    if PHASE_LIMIT < 1:
        p01.close()
        b.finish()
        st.close()
        return b

    nc.leave_named_scope("ph0", _sid["ph0"], False)
    _sid["ph1"] = nc.enter_named_scope("ph1", False)[0]
    b.wt(nc.vector, "dy", setup_dy)
    for lh in range(8):
        b.V(lambda v, l=lh: v.tensor_scalar(
            skipI[:, 128 * l:128 * l + 128], idb[:, :], gs128[:, l:l + 1],
            None, ALU.mult),
            writes=(skipI[:, 0:1],))
    skipIv = b.incl("vc")
    with (nc.psum_tensor("S0", [128, 1024], F32) as S0,
          nc.psum_tensor("S1", [128, 1024], F32) as S1,
          nc.psum_tensor("pvA", [65, 512], F32) as pvA,
          nc.psum_tensor("pvB", [65, 512], F32) as pvB,
          nc.psum_tensor("cmr", [64, 1024], F32) as cmr):
        b.wt(nc.tensor, "dw")
        b.wt(nc.tensor, "vc", skipIv)
        trild = {0: b.ldy(triinv_s[:, 0:2048].rearrange("p (c q) -> p c q", c=4),
                          c_triinv[0].rearrange("c k q -> k c q"),
                          sem="pt0")}
        prev_rel_sc = 0
        cm_free_vc = 0
        prev_ep_tm = 0
        last_ag = 0
        for g in range(NG):
            nj = 4 * g + 4
            gpar = g % 2
            if g + 1 < NG:
                b.wt(nc.sync, "tm", prev_ep_tm if g else 0)
                trild[g + 1] = b.ldy(
                    triinv_s[:, 2048 * (1 - gpar):2048 * (1 - gpar) + 2048]
                    .rearrange("p (c q) -> p c q", c=4),
                    c_triinv[g + 1].rearrange("c k q -> k c q"),
                    sem=f"pt{(g + 1) % 2}")
            for pc in range(NPAIR):
                lhA, lhB = 2 * pc, 2 * pc + 1
                exp_cnt, pv_cnt = {}, {}
                for j in range(nj):
                    c2 = j - 4 * g
                    qoff = 128 * c2 if c2 > 0 else 0
                    diag = c2 >= 0
                    S_ = S0 if j % 2 == 0 else S1
                    if j >= 2:
                        b.wt(nc.tensor, "sc", exp_cnt[j - 2])
                    elif j == 0:
                        b.wt(nc.tensor, "sc", prev_rel_sc)
                        if g == 0 and pc == 0:
                            b.wt(nc.tensor, "pt0", trild[0])
                    for (co, r0) in ((0, 0), (512, 64)):
                        b.mm(
                            S_[:, co + qoff:co + 512],
                            kfm[:, 4096 * pc + 128 * j:
                                4096 * pc + 128 * j + 128][r0:r0 + 64, :],
                            qfm[:, 4096 * pc + 512 * g + qoff:
                                4096 * pc + 512 * g + 512][r0:r0 + 64, :],
                            start=True, stop=not diag,
                            tile_position=(r0, 0))
                        if diag:
                            b.mm(
                                S_[:, co + qoff:co + qoff + 128],
                                tri[:, :], idb[:, :],
                                start=False, stop=True,
                                tile_position=(0, 0))
                    s_tm = b.incl("tm")
                    b.wt(nc.scalar, "tm", s_tm)
                    if j >= 2:
                        b.wt(nc.scalar, "tm", pv_cnt[j - 2])
                    for (lh, co) in ((lhA, 0), (lhB, 512)):
                        b.SC(lambda s, ss=S_, l=lh,
                             p=1024 * (j % 2) + co, c_=co, q=qoff:
                             s.activation(probs[:, p + q:p + 512],
                                          ss[:, c_ + q:c_ + 512], AF.Exp,
                                          bias=negb_rep[:, l:l + 1]),
                             writes=(probs[:, 0:1],))
                    exp_cnt[j] = b.incl("sc")
                    b.wt(nc.tensor, "sc", exp_cnt[j])
                    for (pv, lh, co) in ((pvA, lhA, 0), (pvB, lhB, 512)):
                        b.mm(
                            pv[:, qoff:512],
                            vext[:, 520 * j + 65 * lh:
                                 520 * j + 65 * lh + 65],
                            probs[:, 1024 * (j % 2) + co + qoff:
                                  1024 * (j % 2) + co + 512],
                            start=(j == 0), stop=(j == nj - 1))
                    pv_cnt[j] = b.incl("tm")
                sweep_tm = pv_cnt[nj - 1]

                b.wt(nc.scalar, "tm", sweep_tm)
                b.wt(nc.scalar, "vc", cm_free_vc)
                b.SC(lambda s: s.copy(pvs_sb[0:64, 0:512], pvA[0:64, :]),
                     writes=(pvs_sb[0:1, 0:1],))
                b.SC(lambda s: s.copy(pvs_sb[0:64, 512:1024], pvB[0:64, :]),
                     writes=(pvs_sb[0:1, 0:1],))
                b.incl("sc")
                b.wt(nc.vector, "tm", sweep_tm)
                b.V(lambda v: v.reciprocal(recip_sb[0:1, 0:512],
                                           pvA[64:65, :]),
                    writes=(recip_sb[0:1, 0:1],))
                b.V(lambda v: v.reciprocal(recip_sb[0:1, 512:1024],
                                           pvB[64:65, :]),
                    writes=(recip_sb[0:1, 0:1],))
                rec_vc = b.incl("vc")
                b.wt(nc.tensor, "vc", rec_vc)
                b.mm(pvA[0:64, :], onesf[0:1, 0:64], recip_sb[0:1, 0:512],
                     start=True, stop=True)
                b.mm(pvB[0:64, :], onesf[0:1, 0:64], recip_sb[0:1, 512:1024],
                     start=True, stop=True)
                rtm = b.incl("tm")
                b.wt(nc.scalar, "tm", rtm)
                b.SC(lambda s: s.copy(r_sb[0:64, 0:512], pvA[0:64, :]),
                     writes=(r_sb[0:1, 0:1],))
                b.SC(lambda s: s.copy(r_sb[0:64, 512:1024], pvB[0:64, :]),
                     writes=(r_sb[0:1, 0:1],))
                prev_rel_sc = b.incl("sc")

                b.wt(nc.vector, "sag", last_ag)
                for (lh, hp, r0) in ((lhA, 0, 0), (lhB, 1, 64)):
                    b.wt(nc.tensor, "vc", cm_free_vc)
                    for c2 in range(4):
                        j2 = 4 * g + c2
                        b.mm(
                            cmr[0:64, 128 * c2:128 * c2 + 128],
                            vext[:, 520 * j2 + 65 * lh:520 * j2 + 65 * lh + 64],
                            skipI[:, 128 * lh:128 * lh + 128],
                            start=(c2 == 0), stop=(c2 == 3))
                    if g > 0:
                        b.wt(nc.tensor, f"pt{g % 2}", trild[g])
                        b.mm(
                            cmr[0:64, 512:1024],
                            carryT[0:1, 512 * g + 64 * lh:
                                   512 * g + 64 * lh + 64],
                            invp[0:1, 512 * g:512 * g + 512],
                            start=True, stop=False)
                    for c2 in range(4):
                        j2 = 4 * g + c2
                        b.mm(
                            cmr[0:64, 512:1024],
                            vext[:, 520 * j2 + 65 * lh:520 * j2 + 65 * lh + 64],
                            triinv_s[:, 2048 * gpar + 512 * c2:
                                     2048 * gpar + 512 * c2 + 512],
                            start=(c2 == 0 and g == 0), stop=(c2 == 3))
                    cm_tm = b.incl("tm")
                    b.wt(nc.vector, "sc", prev_rel_sc)
                    b.V(lambda v, hp_=hp: v.tensor_mul(
                        ta_sb[0:64, :], pvs_sb[0:64, 512 * hp_:512 * hp_ + 512],
                        r_sb[0:64, 512 * hp_:512 * hp_ + 512]),
                        reads=(pvs_sb[0:1, 0:1], r_sb[0:1, 0:1]),
                        writes=(ta_sb[0:1, 0:1],))
                    b.wt(nc.vector, "tm", cm_tm)
                    b.V(lambda v, l=lh: v.scalar_tensor_tensor(
                        tb_sb[0:64, :], ta_sb[0:64, :], grep[0:64, l:l + 1],
                        cmr[0:64, 0:512], ALU.mult, ALU.add),
                        reads=(ta_sb[0:1, 0:1],), writes=(tb_sb[0:1, 0:1],))
                    b.V(lambda v, l=lh, r=r0: v.scalar_tensor_tensor(
                        outpair[r:r + 64, :], cmr[0:64, 512:1024],
                        cnrep[0:64, l:l + 1], tb_sb[0:64, :],
                        ALU.mult, ALU.add),
                        reads=(tb_sb[0:1, 0:1],), writes=(outpair[0:1, 0:1],))
                    cm_free_vc = b.incl("vc")
                prev_ep_tm = b.n("tm")
                b.wt(nc.sync, "vc", cm_free_vc)
                last_ag = b.ldy(
                    ag_in.ap()[128 * pc:128 * pc + 128, 512 * g:512 * g + 512],
                    outpair[:, :], sem="sag")

        tp = b.tap("p1_ag_in", [512, S], BF16)
        if tp is not None:
            b.wt(nc.gpsimd, "sag")
            b.ldw(tp, ag_in.ap())

    p01.close()
    b.finish()
    if PHASE_LIMIT < 2:
        st.close()
        return b

    nc.leave_named_scope("ph1", _sid["ph1"], False)
    _sid["ph3"] = nc.enter_named_scope("ph3", False)[0]
    b.wt(nc.gpsimd, "sag")
    cci = nc.gpsimd.collective_compute(
        "AllGather", ALU.bypass,
        replica_groups=[[0, 1], [2, 3], [4, 5], [6, 7]],
        ins=[ag_in.ap().opt()], outs=[ag_out.ap().opt()])
    b.inc(cci, "cc", 1)

    w4s = contextlib.ExitStack()
    w1b = w4s.enter_context(nc.sbuf_tensor("w1b", [128, 8 * W0], BF16))
    w2b = w4s.enter_context(nc.sbuf_tensor("w2b", [128, 32 * 1024], BF16))
    b2r = w4s.enter_context(nc.sbuf_tensor("b2r", [1, D], BF16))
    b.ldw(b2r[:, :], b2_row, sem="p4w")
    for half in range(2):
        b.ldw(w1b[:, 4 * W0 * half:4 * W0 * half + 4 * W0]
              .rearrange("p (c n) -> p c n", c=4),
              w1.rearrange("(c p) n -> p c n", p=128)[:, 4 * half:4 * half + 4, :],
              sem="p4w")
        b.ldw(w2b[:, 16384 * half:16384 * half + 16384]
              .rearrange("p (c n) -> p c n", c=16),
              w2.rearrange("(c p) n -> p c n", p=128)[:, 16 * half:16 * half + 16, :],
              sem="p4w")
    dw4 = b.n("p4w")

    p3 = contextlib.ExitStack()

    def sb3(name, shape, dtype):
        return p3.enter_context(nc.sbuf_tensor(name, list(shape), dtype))

    cpw = sb3("cpw", [128, 8 * 1024], BF16)
    afm = sb3("afm", [128, 2 * 4096], BF16)
    xg3 = sb3("xg3", [128, 4 * 1024], F32)
    ysb3 = sb3("ysb3", [128, 1024], F32)
    ln3 = sb3("ln3", [128, 1024], F32)
    b.ldw(cpw[:, :].rearrange("p (c n) -> p c n", c=8),
          cprojw.rearrange("(c p) n -> p c n", p=128), sem="pcp")
    dw3 = b.n("pcp")
    tokv = nc.sync.value_load(tokb[0:1, 0:1])
    b.wt(nc.sync, "cc", b.n("cc"))
    with nc.psum_tensor("yp3", [128, 1024], F32) as yp3:
        grp_tm = {}
        for go in range(NGO):
            gpar = go % 2
            if go >= 2:
                b.wt(nc.sync, "tm", grp_tm[go - 2])
            src_ap = bass.AP(ag_out.ap().tensor, tokv + 512 * go,
                             [[S, 128], [512 * S, 2], [128 * S, 4], [1, 512]])
            dsta = afm[:, 4096 * gpar:4096 * gpar + 4096]
            dst_ap = bass.AP(dsta.tensor, dsta.offset,
                             [[dsta.ap[0][0], 128], [2048, 2], [512, 4],
                              [1, 512]])
            b.ldy(dst_ap, src_ap, sem=f"pa{gpar}")
            if go >= 1:
                b.wt(nc.sync, "vc")
                b.wt(nc.sync, "sc")
            for tt in range(4):
                b.ldy(xg3[:, 1024 * tt:1024 * tt + 1024],
                      x_own[512 * go + 128 * tt:512 * go + 128 * tt + 128, :],
                      sem="p3x")
            ldn = b.n("p3x")
            for tt in range(4):
                b.wt(nc.tensor, f"pa{gpar}")
                if go == 0 and tt == 0:
                    b.wt(nc.tensor, "pcp", dw3)
                b.wt(nc.tensor, "vc")
                for k in range(8):
                    for n2 in range(2):
                        b.mm(
                            yp3[:, 512 * n2:512 * n2 + 512],
                            afm[:, 4096 * gpar + 512 * k + 128 * tt:
                                4096 * gpar + 512 * k + 128 * tt + 128],
                            cpw[:, 1024 * k + 512 * n2:
                                1024 * k + 512 * n2 + 512],
                            start=(k == 0), stop=(k == 7))
                grp_tm[go] = tmn = b.incl("tm")
                b.wt(nc.vector, "tm", tmn)
                b.wt(nc.vector, "p3x", ldn)
                b.V(lambda v: v.tensor_copy(ysb3[:, :], yp3[:, :]),
                    writes=(ysb3[:, 0:1],))
                vn = b.incl("vc")
                b.wt(nc.scalar, "vc", vn)
                _ln_tile(b, ln3[:, :], ysb3[:, :], D, lnsc)
                xt = xg3[:, 1024 * tt:1024 * tt + 1024]
                b.V(lambda v, x_=xt: v.tensor_add(ysb3[:, :], ln3[:, :], x_),
                    reads=(ln3[:, 0:1], ysb3[:, 0:1]), writes=(ysb3[:, 0:1],))
                vn = b.incl("vc")
                b.wt(nc.sync, "vc", vn)
                b.ldy(stream_a.ap()[512 * go + 128 * tt:
                                    512 * go + 128 * tt + 128, :], ysb3[:, :])
    p3.close()
    b.finish()
    tp = b.tap("p3_stream", [TOK, D])
    if tp is not None:
        b.ldw(tp, stream_a.ap())
        b.wt(nc.gpsimd, "dw")
    if PHASE_LIMIT < 3:
        st.close()
        return b

    nc.leave_named_scope("ph3", _sid["ph3"], False)
    _sid["ph4"] = nc.enter_named_scope("ph4", False)[0]
    p4 = contextlib.ExitStack()

    def sb4(name, shape, dtype):
        return p4.enter_context(nc.sbuf_tensor(name, list(shape), dtype))

    lnxf4 = sb4("lnxf4", [128, 8 * 512], BF16)
    hfm = sb4("hfm", [128, 32 * 512], BF16)
    xg4 = sb4("xg4", [128, 4 * 1024], F32)
    ln4 = sb4("ln4", [128, 1024], F32)
    ysb4 = sb4("ysb4", [128, 1024], F32)
    with (nc.psum_tensor("trp4", [128, 1024], F32) as trp4,
          nc.psum_tensor("hp4a", [128, 512], F32) as hp4a,
          nc.psum_tensor("hp4b", [128, 512], F32) as hp4b,
          nc.psum_tensor("yp4", [128, 1024], F32) as yp4):
        for go in range(NGO):
            if go >= 1:
                b.wt(nc.sync, "vc")
                b.wt(nc.sync, "sc")
            for tt in range(4):
                b.ldy(xg4[:, 1024 * tt:1024 * tt + 1024],
                      stream_a.ap()[512 * go + 128 * tt:
                                    512 * go + 128 * tt + 128, :], sem="p4x")
            ldn = b.n("p4x")
            b.wt(nc.vector, "p4x", ldn)
            b.wt(nc.scalar, "p4x", ldn)
            for tt in range(4):
                xap = xg4[:, 1024 * tt:1024 * tt + 1024]
                _ln_tile(b, ln4[:, :], xap, D, lnsc)
                lnn = b.incl("vc")
                b.wt(nc.tensor, "vc", lnn)
                for c in range(8):
                    b.tr(trp4[:, 128 * c:128 * c + 128],
                                        ln4[:, 128 * c:128 * c + 128],
                                        idf[:, :])
                tmn = b.incl("tm")
                b.wt(nc.vector, "tm", tmn)
                dst = lnxf4[:, 128 * tt:]
                dstv = bass.AP(dst.tensor, dst.offset,
                               [[dst.ap[0][0], 128], [512, 8], [1, 128]])
                srcv = trp4[:, :].rearrange("p (c n) -> p c n", c=8)
                b.V(lambda v, dv=dstv, sv=srcv: v.tensor_copy(dv, sv),
                    writes=(lnxf4[:, 0:1],))
                b.incl("vc")
            lnxv = b.n("vc")
            b.wt(nc.tensor, "vc", lnxv)
            if go == 0:
                b.wt(nc.tensor, "p4w", dw4)
            hv = {}
            for hc in range(32):
                ps = hp4a if hc % 2 == 0 else hp4b
                if hc >= 2:
                    b.wt(nc.tensor, "sc", hv[hc - 2])
                for k in range(8):
                    b.mm(
                        ps[:, :],
                        w1b[:, W0 * k + 128 * hc:W0 * k + 128 * hc + 128],
                        lnxf4[:, 512 * k:512 * k + 512],
                        start=(k == 0), stop=(k == 7))
                tmn = b.incl("tm")
                b.wt(nc.scalar, "tm", tmn)
                b.SC(lambda s, p=ps, h=hc: s.activation(
                    hfm[:, 512 * h:512 * h + 512], p[:, :], AF.Relu,
                    bias=b1p[:, h:h + 1]),
                    writes=(hfm[:, 0:1],))
                hv[hc] = b.incl("sc")
            hdone = hv[31]
            for tt in range(4):
                b.wt(nc.tensor, "sc", hdone)
                b.wt(nc.tensor, "vc")
                for hc in range(32):
                    for n2 in range(2):
                        b.mm(
                            yp4[:, 512 * n2:512 * n2 + 512],
                            hfm[:, 512 * hc + 128 * tt:
                                512 * hc + 128 * tt + 128],
                            w2b[:, 1024 * hc + 512 * n2:
                                1024 * hc + 512 * n2 + 512],
                            start=(hc == 0), stop=False)
                for n2 in range(2):
                    b.mm(
                        yp4[:, 512 * n2:512 * n2 + 512], onesb[0:1, :],
                        b2r[0:1, 512 * n2:512 * n2 + 512],
                        start=False, stop=True)
                tmn = b.incl("tm")
                b.wt(nc.vector, "tm", tmn)
                b.wt(nc.scalar, "tm", tmn)
                b.V(lambda v: v.tensor_copy(ysb4[:, :], yp4[:, :]),
                    writes=(ysb4[:, 0:1],))
                vn = b.incl("vc")
                b.wt(nc.scalar, "vc", vn)
                _ln_tile(b, ln4[:, :], ysb4[:, :], D, lnsc)
                xt = xg4[:, 1024 * tt:1024 * tt + 1024]
                b.V(lambda v, x_=xt: v.tensor_add(ysb4[:, :], ln4[:, :], x_),
                    reads=(ln4[:, 0:1], ysb4[:, 0:1]), writes=(ysb4[:, 0:1],))
                vn = b.incl("vc")
                b.wt(nc.sync, "vc", vn)
                b.ldy(stream_b.ap()[512 * go + 128 * tt:
                                    512 * go + 128 * tt + 128, :], ysb4[:, :])
    p4.close()
    w4s.close()
    b.finish()
    tp = b.tap("p4_stream", [TOK, D])
    if tp is not None:
        b.ldw(tp, stream_b.ap())
        b.wt(nc.gpsimd, "dw")
    if PHASE_LIMIT < 4:
        st.close()
        return b

    nc.leave_named_scope("ph4", _sid["ph4"], False)
    _sid["ph5"] = nc.enter_named_scope("ph5", False)[0]
    p5 = contextlib.ExitStack()

    def sb5(name, shape, dtype):
        return p5.enter_context(nc.sbuf_tensor(name, list(shape), dtype))

    lnx_all = sb5("lnx_all", [128, NTO * 1024], BF16)
    wkb = sb5("wkb", [128, 8 * 1024], BF16)
    wvb = sb5("wvb", [128, 8 * 1024], BF16)
    wqb = sb5("wqb", [128, 8 * 1024], BF16)
    wob = sb5("wob", [128, 8 * 1024], BF16)
    kfm1 = sb5("kfm1", [128, 8 * 512], BF16)
    kft = sb5("kft", [128, 4 * 1024], BF16)
    vx1 = sb5("vx1", [128, 4 * 1040], BF16)
    qfsb = sb5("qfsb", [64, 16 * 512], BF16)
    kvf = sb5("kvf", [128, 520], F32)
    kvb = sb5("kvb", [64, 16 * 65], BF16)
    atm = sb5("atm", [128, 4 * 1024], F32)
    a1fm = sb5("a1fm", [128, 8 * 512], BF16)
    xg5 = sb5("xg5", [128, 4 * 1024], F32)
    ln5 = sb5("ln5", [128, 1024], F32)
    ysb5 = sb5("ysb5", [128, 1024], F32)
    bor = sb5("bor", [1, D], BF16)
    nrec = sb5("nrec", [128, 16], F32)
    nmax = sb5("nmax", [128, 16], F32)
    kvacc = [sb5("kvacc0", [128, 520], F32), sb5("kvacc1", [128, 520], F32)]
    b.ldw(bor[:, :], bo_row, sem="p5w")
    b.ldw(wkb[:, :].rearrange("p (c n) -> p c n", c=8),
          wk1.rearrange("(c p) n -> p c n", p=128), sem="p5w")
    b.ldw(wvb[:, :].rearrange("p (c n) -> p c n", c=8),
          wv1.rearrange("(c p) n -> p c n", p=128), sem="p5w")
    b.ldw(wqb[:, :].rearrange("p (c n) -> p c n", c=8),
          wq1.rearrange("(c p) n -> p c n", p=128), sem="p5w")
    b.ldw(wob[:, :].rearrange("p (c n) -> p c n", c=8),
          wo1.rearrange("(c p) n -> p c n", p=128), sem="p5w")
    dw5 = b.n("p5w")
    vx1v = vx1[:, :].rearrange("p (t h o) -> p t h o", t=4, h=H1)
    b.V(lambda v: v.memset(vx1v[:, :, :, 64:65], 1.0), writes=(vx1[:, 0:1],))
    memv = b.incl("vc")
    with (nc.psum_tensor("trp5", [128, 1024], F32) as trp5,
          nc.psum_tensor("pp5a", [128, 512], F32) as pp5a,
          nc.psum_tensor("pp5b", [128, 512], F32) as pp5b,
          nc.psum_tensor("kv0", [128, 260], F32) as kv0,
          nc.psum_tensor("kv1", [128, 260], F32) as kv1,
          nc.psum_tensor("kv2", [128, 260], F32) as kv2,
          nc.psum_tensor("kv3", [128, 260], F32) as kv3):
        kvps = [kv0, kv1, kv2, kv3]
        for go in range(NGO):
            if go >= 1:
                b.wt(nc.sync, "vc")
                b.wt(nc.sync, "sc")
            for tt in range(4):
                b.ldy(xg5[:, 1024 * tt:1024 * tt + 1024],
                      stream_b.ap()[512 * go + 128 * tt:
                                    512 * go + 128 * tt + 128, :], sem="p5x")
            ldn = b.n("p5x")
            b.wt(nc.vector, "p5x", ldn)
            b.wt(nc.scalar, "p5x", ldn)
            if go == 0:
                b.wt(nc.tensor, "vc", memv)
            for tt in range(4):
                xap = xg5[:, 1024 * tt:1024 * tt + 1024]
                _ln_tile(b, ln5[:, :], xap, D, lnsc)
                lnn = b.incl("vc")
                b.wt(nc.tensor, "vc", lnn)
                for c in range(8):
                    b.tr(trp5[:, 128 * c:128 * c + 128],
                                        ln5[:, 128 * c:128 * c + 128],
                                        idf[:, :])
                tmn = b.incl("tm")
                b.wt(nc.vector, "tm", tmn)
                dst = lnx_all[:, 4096 * go + 128 * tt:]
                dstv = bass.AP(dst.tensor, dst.offset,
                               [[dst.ap[0][0], 128], [512, 8], [1, 128]])
                srcv = trp5[:, :].rearrange("p (c n) -> p c n", c=8)
                b.V(lambda v, dv=dstv, sv=srcv: v.tensor_copy(dv, sv),
                    writes=(lnx_all[:, 0:1],))
                b.incl("vc")
            lnxv = b.n("vc")
            base5 = 4096 * go
            b.wt(nc.tensor, "vc", lnxv)
            if go == 0:
                b.wt(nc.tensor, "p5w", dw5)
            for c in range(8):
                ps = pp5a if c % 2 == 0 else pp5b
                b.wt(nc.tensor, "vc")
                for k in range(8):
                    b.mm(
                        ps[:, :],
                        wkb[:, 1024 * k + 128 * c:1024 * k + 128 * c + 128],
                        lnx_all[:, base5 + 512 * k:base5 + 512 * k + 512],
                        start=(k == 0), stop=(k == 7))
                tmn = b.incl("tm")
                b.wt(nc.vector, "tm", tmn)
                b.V(lambda v, p=ps, c_=c: v.tensor_copy(
                    kfm1[:, 512 * c_:512 * c_ + 512], p[:, :]),
                    writes=(kfm1[:, 0:1],))
                b.incl("vc")
            for tt in range(4):
                b.wt(nc.tensor, "vc")
                for k in range(8):
                    for n2 in range(2):
                        ps = pp5a if n2 == 0 else pp5b
                        b.mm(
                            ps[:, :],
                            lnx_all[:, base5 + 512 * k + 128 * tt:
                                    base5 + 512 * k + 128 * tt + 128],
                            wvb[:, 1024 * k + 512 * n2:
                                1024 * k + 512 * n2 + 512],
                            start=(k == 0), stop=(k == 7))
                tmn = b.incl("tm")
                b.wt(nc.vector, "tm", tmn)
                for n2 in range(2):
                    ps = pp5a if n2 == 0 else pp5b
                    dst = vx1[:, 1040 * tt + 65 * 8 * n2:]
                    dstv = bass.AP(dst.tensor, dst.offset,
                                   [[dst.ap[0][0], 128], [65, 8], [1, 64]])
                    srcv = ps[:, :].rearrange("p (h o) -> p h o", h=8)
                    b.V(lambda v, dv=dstv, sv=srcv: v.tensor_copy(dv, sv),
                        writes=(vx1[:, 0:1],))
                b.incl("vc")
            vdone = b.n("vc")
            for tt in range(4):
                b.wt(nc.tensor, "vc", vdone)
                b.wt(nc.tensor, "sc")
                for h in range(H1):
                    ps = pp5a if h < 8 else pp5b
                    hc, r0 = h // 2, 64 * (h % 2)
                    b.mm(
                        ps[:, 64 * (h % 8):64 * (h % 8) + 64],
                        kfm1[:, 512 * hc + 128 * tt:
                             512 * hc + 128 * tt + 128][r0:r0 + 64, :],
                        wfr[r0:r0 + 64, :], start=True, stop=False,
                        tile_position=(r0, 0))
                    b.mm(
                        ps[:, 64 * (h % 8):64 * (h % 8) + 64],
                        onesb[0:1, :], bfr[0:1, :], start=False, stop=True,
                        tile_position=(0, 0))
                tmn = b.incl("tm")
                b.wt(nc.scalar, "tm", tmn)
                for (ps, off) in ((pp5a, 0), (pp5b, 512)):
                    b.SC(lambda s, p=ps, o=off, t_=tt: s.activation(
                        kft[:, 1024 * t_ + o:1024 * t_ + o + 512], p[:, :],
                        AF.Relu),
                        writes=(kft[:, 0:1],))
                b.incl("sc")
            kfdone = b.n("sc")
            b.wt(nc.tensor, "sc", kfdone)
            b.wt(nc.tensor, "vc")
            for pr in range(8):
                tn = kvps[pr // 4]
                coff = 65 * (pr % 4)
                for hh2 in range(2):
                    h = 2 * pr + hh2
                    r0 = 64 * hh2
                    for ti in range(4):
                        b.mm(
                            tn[r0:r0 + 64, coff:coff + 65],
                            kft[:, 1024 * ti + 64 * h:1024 * ti + 64 * h + 64],
                            vx1[:, 1040 * ti + 65 * h:1040 * ti + 65 * h + 65],
                            start=(ti == 0), stop=(ti == 3),
                            tile_position=(0, r0))
            kvtm = b.incl("tm")
            b.wt(nc.vector, "tm", kvtm)
            dstk = kvacc[go % 2]
            srck = kvacc[(go + 1) % 2]
            if go == 0:
                for i in range(2):
                    b.V(lambda v, i_=i, d=dstk: v.tensor_copy(
                        d[:, 260 * i_:260 * i_ + 260], kvps[i_][:, 0:260]),
                        writes=(dstk[:, 0:1],))
            else:
                for i in range(2):
                    b.V(lambda v, i_=i, d=dstk, s_=srck: v.tensor_add(
                        d[:, 260 * i_:260 * i_ + 260],
                        s_[:, 260 * i_:260 * i_ + 260], kvps[i_][:, 0:260]),
                        reads=(srck[:, 0:1],), writes=(dstk[:, 0:1],))
            b.incl("vc")
        vn = b.n("vc")
        b.wt(nc.sync, "vc", vn)
        kvv = kvacc[1][:, :].rearrange("p (pr o) -> p pr o", pr=8)
        arv = ar_in.ap().rearrange("f (pr two) o -> two f pr o", two=2)
        b.ldy(arv[0], kvv[0:64, :, :], sem="p5r")
        b.ldy(arv[1], kvv[64:128, :, :], sem="p5r")
        b.wt(nc.gpsimd, "p5r")
        cci = nc.gpsimd.collective_compute(
            "AllReduce", ALU.add,
            replica_groups=[[0, 1], [2, 3], [4, 5], [6, 7]],
            ins=[ar_in.ap().opt()], outs=[ar_out.ap().opt()])
        b.inc(cci, "cc", 1)
        b.wt(nc.gpsimd, "cc", b.n("cc"))
        b.ldw(kvb[:, :].rearrange("p (h o) -> p h o", h=H1), ar_out.ap(),
              sem="p5k")
        kvldn = b.n("p5k")

        for go in range(NGO):
            b.wt(nc.sync, "vc")
            b.wt(nc.sync, "sc")
            for tt in range(4):
                b.ldy(xg5[:, 1024 * tt:1024 * tt + 1024],
                      stream_b.ap()[512 * go + 128 * tt:
                                    512 * go + 128 * tt + 128, :], sem="p5x")
            ldn = b.n("p5x")
            base5 = 4096 * go
            for c in range(8):
                ps = pp5a if c % 2 == 0 else pp5b
                b.wt(nc.tensor, "vc")
                if go == 0 and c == 0:
                    b.wt(nc.tensor, "p5k", kvldn)
                    b.wt(nc.tensor, "sc")
                for k in range(8):
                    b.mm(
                        ps[:, :],
                        wqb[:, 1024 * k + 128 * c:1024 * k + 128 * c + 128],
                        lnx_all[:, base5 + 512 * k:base5 + 512 * k + 512],
                        start=(k == 0), stop=(k == 7))
                tmn = b.incl("tm")
                b.wt(nc.vector, "tm", tmn)
                b.V(lambda v, p=ps, c_=c: v.tensor_copy(
                    kfm1[:, 512 * c_:512 * c_ + 512], p[:, :]),
                    writes=(kfm1[:, 0:1],))
                b.incl("vc")
            qv = b.n("vc")
            b.wt(nc.tensor, "vc", qv)
            for h in range(H1):
                hc, r0 = h // 2, 64 * (h % 2)
                ps = pp5a if h % 2 == 0 else pp5b
                b.wt(nc.tensor, "sc")
                b.mm(
                    ps[0:64, :], wfr[r0:r0 + 64, :],
                    kfm1[:, 512 * hc:512 * hc + 512][r0:r0 + 64, :],
                    start=True, stop=True, tile_position=(r0, 0))
                tmn = b.incl("tm")
                b.wt(nc.scalar, "tm", tmn)
                b.SC(lambda s, p=ps, h_=h: s.activation(
                    qfsb[0:64, 512 * h_:512 * h_ + 512], p[0:64, :], AF.Relu,
                    bias=bfp[0:64, 0:1]),
                    writes=(qfsb[0:1, 0:1],))
                b.incl("sc")
            qfdone = b.n("sc")
            b.wt(nc.tensor, "sc", qfdone)
            for tt in range(4):
                b.wt(nc.tensor, "vc")
                for h in range(H1):
                    b.mm(
                        kvps[h // 4][:, 65 * (h % 4):65 * (h % 4) + 65],
                        qfsb[0:64, 512 * h + 128 * tt:512 * h + 128 * tt + 128],
                        kvb[0:64, 65 * h:65 * h + 65],
                        start=True, stop=True)
                tmn = b.incl("tm")
                b.wt(nc.vector, "tm", tmn)
                for i in range(4):
                    pv_ = kvps[i]
                    src = bass.AP(pv_, 64, [[260, 128], [65, 4]])
                    b.V(lambda v, s_=src, i_=i: v.tensor_scalar(
                        nmax[:, 4 * i_:4 * i_ + 4], s_, 1e-6, None, ALU.max),
                        writes=(nmax[:, 0:1],))
                b.V(lambda v: v.reciprocal(nrec[:, :], nmax[:, :]),
                    reads=(nmax[:, 0:1],), writes=(nrec[:, 0:1],))
                for h in range(H1):
                    pv_ = kvps[h // 4]
                    b.V(lambda v, p=pv_, h_=h, t_=tt: v.tensor_scalar(
                        atm[:, 1024 * t_ + 64 * h_:1024 * t_ + 64 * h_ + 64],
                        p[:, 65 * (h_ % 4):65 * (h_ % 4) + 64],
                        nrec[:, h_:h_ + 1], None, ALU.mult),
                        reads=(nrec[:, 0:1],), writes=(atm[:, 0:1],))
                b.incl("vc")
            for tt in range(4):
                b.wt(nc.tensor, "vc")
                for c in range(8):
                    b.tr(trp5[:, 128 * c:128 * c + 128],
                         atm[:, 1024 * tt + 128 * c:1024 * tt + 128 * c + 128],
                         idf[:, :])
                tmn = b.incl("tm")
                b.wt(nc.vector, "tm", tmn)
                dst = a1fm[:, 128 * tt:]
                dstv = bass.AP(dst.tensor, dst.offset,
                               [[dst.ap[0][0], 128], [512, 8], [1, 128]])
                srcv = trp5[:, :].rearrange("p (c n) -> p c n", c=8)
                b.V(lambda v, dv=dstv, sv=srcv: v.tensor_copy(dv, sv),
                    writes=(a1fm[:, 0:1],))
                b.incl("vc")
            a1v = b.n("vc")
            b.wt(nc.tensor, "vc", a1v)
            b.wt(nc.vector, "p5x", ldn)
            b.wt(nc.scalar, "p5x", ldn)
            for tt in range(4):
                b.wt(nc.tensor, "vc")
                for k in range(8):
                    for n2 in range(2):
                        b.mm(
                            trp5[:, 512 * n2:512 * n2 + 512],
                            a1fm[:, 512 * k + 128 * tt:
                                 512 * k + 128 * tt + 128],
                            wob[:, 1024 * k + 512 * n2:
                                1024 * k + 512 * n2 + 512],
                            start=(k == 0), stop=False)
                for n2 in range(2):
                    b.mm(
                        trp5[:, 512 * n2:512 * n2 + 512], onesb[0:1, :],
                        bor[0:1, 512 * n2:512 * n2 + 512],
                        start=False, stop=True)
                tmn = b.incl("tm")
                b.wt(nc.vector, "tm", tmn)
                b.wt(nc.scalar, "tm", tmn)
                b.V(lambda v: v.tensor_copy(ysb5[:, :], trp5[:, :]),
                    writes=(ysb5[:, 0:1],))
                vn = b.incl("vc")
                b.wt(nc.scalar, "vc", vn)
                _ln_tile(b, ln5[:, :], ysb5[:, :], D, lnsc)
                xt = xg5[:, 1024 * tt:1024 * tt + 1024]
                b.V(lambda v, x_=xt: v.tensor_add(ysb5[:, :], ln5[:, :], x_),
                    reads=(ln5[:, 0:1], ysb5[:, 0:1]), writes=(ysb5[:, 0:1],))
                vn = b.incl("vc")
                b.wt(nc.sync, "vc", vn)
                b.ldy(stream_c.ap()[512 * go + 128 * tt:
                                    512 * go + 128 * tt + 128, :], ysb5[:, :])
    p5.close()
    b.finish()
    tp = b.tap("p5_stream", [TOK, D])
    if tp is not None:
        b.ldw(tp, stream_c.ap())
        b.wt(nc.gpsimd, "dw")
    if PHASE_LIMIT < 5:
        st.close()
        return b

    nc.leave_named_scope("ph5", _sid["ph5"], False)
    _sid["ph6"] = nc.enter_named_scope("ph6", False)[0]
    p6 = contextlib.ExitStack()

    def sb6(name, shape, dtype):
        return p6.enter_context(nc.sbuf_tensor(name, list(shape), dtype))

    gates = sb6("gates", [128, NTO * NEXP], F32)
    xg6 = sb6("xg6", [128, 4 * 1024], F32)
    ln6 = sb6("ln6", [128, 1024], F32)
    ysb6 = sb6("ysb6", [128, 1024], F32)

    p6a = contextlib.ExitStack()

    def sb6a(name, shape, dtype):
        return p6a.enter_context(nc.sbuf_tensor(name, list(shape), dtype))

    lnf32 = sb6a("lnf32", [128, 8 * 512], F32)
    lnxb6 = sb6a("lnxb6", [128, 8 * 512], BF16)
    wg1f = sb6a("wg1f", [128, 8 * G1], F32)
    wg2f = sb6a("wg2f", [128, 2 * NEXP], F32)
    gsb = sb6a("gsb", [128, 1024], F32)
    gm1 = sb6a("gm1", [128, 1], F32)
    gm2 = sb6a("gm2", [128, 1], F32)
    gtmp = sb6a("gtmp", [128, NEXP], F32)
    gtmp2 = sb6a("gtmp2", [128, NEXP], F32)
    gsum = sb6a("gsum", [128, 1], F32)
    gprob = sb6a("gprob", [128, NEXP], F32)
    b.ldy(wg1f[:, :].rearrange("p (c n) -> p c n", c=8),
          wg1.rearrange("(c p) n -> p c n", p=128), sem="p6g")
    b.ldy(wg2f[:, :].rearrange("p (c n) -> p c n", c=2),
          wg2.rearrange("(c p) n -> p c n", p=128), sem="p6g")
    with (nc.psum_tensor("trp6", [128, 1024], F32) as trp6,
          nc.psum_tensor("gp6", [128, 1024], F32) as gp6):
        for go in range(NGO):
            b.wt(nc.sync, "vc")
            b.wt(nc.sync, "sc")
            for tt in range(4):
                b.ldy(xg6[:, 1024 * tt:1024 * tt + 1024],
                      stream_c.ap()[512 * go + 128 * tt:
                                    512 * go + 128 * tt + 128, :], sem="p6x")
            ldn = b.n("p6x")
            b.wt(nc.vector, "p6x", ldn)
            b.wt(nc.scalar, "p6x", ldn)
            for tt in range(4):
                xap = xg6[:, 1024 * tt:1024 * tt + 1024]
                _ln_tile(b, ln6[:, :], xap, D, lnsc)
                lnn = b.incl("vc")
                b.wt(nc.tensor, "vc", lnn)
                if go > 0 or tt > 0:
                    b.wt(nc.tensor, "sc")
                for c in range(8):
                    b.tr(trp6[:, 128 * c:128 * c + 128],
                         ln6[:, 128 * c:128 * c + 128], idf[:, :])
                tmn = b.incl("tm")
                b.wt(nc.vector, "tm", tmn)
                b.wt(nc.scalar, "tm", tmn)
                dst = lnxb6[:, 128 * tt:]
                dstv = bass.AP(dst.tensor, dst.offset,
                               [[dst.ap[0][0], 128], [512, 8], [1, 128]])
                srcv = trp6[:, :].rearrange("p (c n) -> p c n", c=8)
                b.V(lambda v, dv=dstv, sv=srcv: v.tensor_copy(dv, sv),
                    writes=(lnxb6[:, 0:1],))
                evv = b.incl("vc")
                b.wt(nc.scalar, "vc", evv)
                dstf = lnf32[:, 128 * tt:]
                dstfv = bass.AP(dstf.tensor, dstf.offset,
                                [[dstf.ap[0][0], 128], [512, 8], [1, 128]])
                b.SC(lambda s, dv=dstfv, sv=srcv: s.copy(dv, sv),
                     writes=(lnf32[:, 0:1],))
                b.incl("sc")
            lnxv, lnxs = b.n("vc"), b.n("sc")
            b.wt(nc.sync, "vc", lnxv)
            b.ldy(lnx6.ap()[:, 4096 * go:4096 * go + 4096], lnxb6[:, :],
                  sem="p6s")
            b.wt(nc.tensor, "sc", lnxs)
            b.wt(nc.tensor, "p6g", b.n("p6g"))
            b.wt(nc.tensor, "vc")
            for m in range(2):
                for k in range(8):
                    b.mm(
                        gp6[:, 512 * m:512 * m + 512],
                        wg1f[:, G1 * k + 128 * m:G1 * k + 128 * m + 128],
                        lnf32[:, 512 * k:512 * k + 512],
                        start=(k == 0), stop=(k == 7))
            tmn = b.incl("tm")
            b.wt(nc.scalar, "tm", tmn)
            for m in range(2):
                b.SC(lambda s, m_=m: s.activation(
                    gsb[:, 512 * m_:512 * m_ + 512],
                    gp6[:, 512 * m_:512 * m_ + 512], AF.Gelu,
                    bias=bg1p[:, m_:m_ + 1]),
                    writes=(gsb[:, 0:1],))
            gsn = b.incl("sc")
            b.wt(nc.tensor, "sc", gsn)
            for tt in range(4):
                b.wt(nc.tensor, "vc")
                for m in range(2):
                    b.mm(
                        gp6[:, 0:NEXP],
                        gsb[:, 512 * m + 128 * tt:512 * m + 128 * tt + 128],
                        wg2f[:, NEXP * m:NEXP * m + NEXP],
                        start=(m == 0), stop=(m == 1))
                tmn = b.incl("tm")
                b.wt(nc.vector, "tm", tmn)
                b.V(lambda v: v.reduce_max(gm1[:, 0:1], gp6[:, 0:NEXP], AX,
                                           negate=True),
                    writes=(gm1[:, 0:1],))
                vn = b.incl("vc")
                b.wt(nc.scalar, "vc", vn)
                b.SC(lambda s: s.activation(gprob[:, :], gp6[:, 0:NEXP],
                                            AF.Exp, bias=gm1[:, 0:1],
                                            accum_out=gsum[:, 0:1]),
                     writes=(gprob[:, 0:1], gsum[:, 0:1]))
                scn = b.incl("sc")
                b.wt(nc.vector, "sc", scn)
                b.V(lambda v: v.reciprocal(gm2[:, 0:1], gsum[:, 0:1]),
                    writes=(gm2[:, 0:1],))
                b.V(lambda v: v.tensor_scalar(gtmp[:, :], gprob[:, :],
                                              gm2[:, 0:1], None, ALU.mult),
                    reads=(gm2[:, 0:1],), writes=(gtmp[:, 0:1],))
                b.V(lambda v: v.reduce_max(gm1[:, 0:1], gtmp[:, :], AX),
                    reads=(gtmp[:, 0:1],), writes=(gm1[:, 0:1],))
                b.V(lambda v: v.tensor_scalar(gtmp2[:, :], gtmp[:, :],
                                              gm1[:, 0:1], None, ALU.is_lt),
                    reads=(gm1[:, 0:1],), writes=(gtmp2[:, 0:1],))
                b.V(lambda v: v.tensor_mul(gprob[:, :], gtmp2[:, :],
                                           gtmp[:, :]),
                    reads=(gtmp2[:, 0:1],), writes=(gprob[:, 0:1],))
                b.V(lambda v: v.reduce_max(gm2[:, 0:1], gprob[:, :], AX),
                    reads=(gprob[:, 0:1],), writes=(gm2[:, 0:1],))
                b.V(lambda v: v.tensor_scalar(gtmp2[:, :], gtmp[:, :],
                                              gm2[:, 0:1], None, ALU.is_ge),
                    reads=(gm2[:, 0:1],), writes=(gtmp2[:, 0:1],))
                b.V(lambda v, t_=tt, g_=go: v.tensor_mul(
                    gates[:, NEXP * (4 * g_ + t_):NEXP * (4 * g_ + t_) + NEXP],
                    gtmp2[:, :], gtmp[:, :]),
                    reads=(gtmp2[:, 0:1],), writes=(gates[:, 0:1],))
                b.incl("vc")
    gatesv = b.n("vc")
    lnx6_dy = b.n("p6s")
    tp = b.tap("p6_gates", [128, NTO * NEXP])
    if tp is not None:
        b.wt(nc.sync, "vc", gatesv)
        b.ldy(tp, gates[:, :])
    b.wt(nc.sync, "vc", gatesv)
    b.wt(nc.gpsimd, "vc", gatesv)
    p6a.close()

    p6b = contextlib.ExitStack()

    def sb6b(name, shape, dtype):
        return p6b.enter_context(nc.sbuf_tensor(name, list(shape), dtype))

    winb = [sb6b(f"winb{i}", [128, 8 * EH], BF16) for i in range(2)]
    woutb = [sb6b(f"woutb{i}", [128, 16 * 1024], BF16) for i in range(2)]
    hfm6 = sb6b("hfm6", [128, 16 * 512], BF16)
    lnxg = sb6b("lnxg", [128, 2 * 8 * 512], BF16)
    otmp = sb6b("otmp", [128, 4 * 512], F32)

    def load_w(e):
        sl = e % 2
        b.ldw(winb[sl][:, :].rearrange("p (c n) -> p c n", c=8),
              w_in[e].rearrange("(c p) n -> p c n", p=128), sem=f"p6w{sl}")
        for half in range(2):
            b.ldw(woutb[sl][:, 8192 * half:8192 * half + 8192]
                  .rearrange("p (c n) -> p c n", c=8),
                  w_out[e].rearrange("(c p) n -> p c n",
                                     p=128)[:, 8 * half:8 * half + 8, :],
                  sem=f"p6w{sl}")
        return b.n(f"p6w{sl}")

    def load_lnx(idx):
        go, sl = idx % 4, idx % 2
        b.ldy(lnxg[:, 4096 * sl:4096 * sl + 4096],
              lnx6.ap()[:, 4096 * go:4096 * go + 4096], sem=f"p6l{sl}")
        return b.n(f"p6l{sl}")

    with (nc.psum_tensor("hp6a", [128, 512], F32) as hp6a,
          nc.psum_tensor("hp6b", [128, 512], F32) as hp6b,
          nc.psum_tensor("op6a", [128, 512], F32) as op6a,
          nc.psum_tensor("op6b", [128, 512], F32) as op6b):
        otmp_dw = {}
        wldn, lnld = {}, {}
        h_tm = {}
        exp_end_tm = {}
        if PHASE_LIMIT >= 6:
            wldn[0] = load_w(0)
            b.wt(nc.sync, "p6s", lnx6_dy)
            lnld[0] = load_lnx(0)
            lnld[1] = load_lnx(1)
        for e in range(NEXP if PHASE_LIMIT >= 6 else 0):
            if e + 1 < NEXP:
                if e >= 1:
                    b.wt(nc.gpsimd, "tm", exp_end_tm[e - 1])
                wldn[e + 1] = load_w(e + 1)
            for go in range(NGO):
                idx = 4 * e + go
                b.wt(nc.tensor, f"p6l{idx % 2}", lnld[idx])
                if go == 0:
                    b.wt(nc.tensor, f"p6w{e % 2}", wldn[e])
                hv = {}
                for hc in range(16):
                    ps = hp6a if hc % 2 == 0 else hp6b
                    if hc >= 2:
                        b.wt(nc.tensor, "sc", hv[hc - 2])
                    for k in range(8):
                        b.mm(
                            ps[:, :],
                            winb[e % 2][:, EH * k + 128 * hc:
                                        EH * k + 128 * hc + 128],
                            lnxg[:, 4096 * (idx % 2) + 512 * k:
                                 4096 * (idx % 2) + 512 * k + 512],
                            start=(k == 0), stop=(k == 7))
                    tmn = b.incl("tm")
                    b.wt(nc.scalar, "tm", tmn)
                    b.SC(lambda s, p=ps, h=hc: s.activation(
                        hfm6[:, 512 * h:512 * h + 512], p[:, :], AF.Gelu),
                        writes=(hfm6[:, 0:1],))
                    hv[hc] = b.incl("sc")
                h_tm[idx] = b.n("tm")
                if idx + 2 < 4 * NEXP:
                    b.wt(nc.sync, "tm", h_tm[idx])
                    lnld[idx + 2] = load_lnx(idx + 2)
                hdone = hv[15]
                b.wt(nc.tensor, "sc", hdone)
                if e == 0 and go == 0:
                    b.wt(nc.tensor, "vc", gatesv)
                for tt in range(4):
                    b.wt(nc.tensor, "vc")
                    for hc in range(16):
                        for n2 in range(2):
                            ps = op6a if n2 == 0 else op6b
                            b.mm(
                                ps[:, :],
                                hfm6[:, 512 * hc + 128 * tt:
                                     512 * hc + 128 * tt + 128],
                                woutb[e % 2][:, 1024 * hc + 512 * n2:
                                             1024 * hc + 512 * n2 + 512],
                                start=(hc == 0), stop=(hc == 15))
                    tmn = b.incl("tm")
                    gt = 4 * go + tt
                    b.wt(nc.vector, "tm", tmn)
                    for n2 in range(2):
                        ps = op6a if n2 == 0 else op6b
                        slot = (2 * gt + n2) % 4
                        if slot in otmp_dw:
                            b.wt(nc.vector, f"p6o{slot}", otmp_dw[slot])
                        b.V(lambda v, p=ps, sl=slot, g_=gt, e_=e:
                            v.tensor_scalar(
                                otmp[:, 512 * sl:512 * sl + 512], p[:, :],
                                gates[:, NEXP * g_ + e_:NEXP * g_ + e_ + 1],
                                None, ALU.mult),
                            writes=(otmp[:, 0:1],))
                        vn = b.incl("vc")
                        b.wt(nc.gpsimd, "vc", vn)
                        otmp_dw[slot] = b.ldw(
                            oacc_d.ap()[128 * gt:128 * gt + 128,
                                        512 * n2:512 * n2 + 512],
                            otmp[:, 512 * slot:512 * slot + 512],
                            accum=(e > 0), sem=f"p6o{slot}")
            exp_end_tm[e] = b.n("tm")
    b.wt(nc.vector, "tm")
    b.wt(nc.scalar, "tm")
    p6b.close()

    for sl in range(4):
        b.wt(nc.sync, f"p6o{sl}")
    tp = b.tap("p6_oacc", [TOK, D])
    if tp is not None:
        b.ldw(tp, oacc_d.ap())
    for tt in range(NTO if PHASE_LIMIT >= 7 else 0):
        b.wt(nc.sync, "vc")
        b.wt(nc.sync, "sc")
        par = tt % 2
        b.ldy(xg6[:, 2048 * par:2048 * par + 1024],
              stream_c.ap()[128 * tt:128 * tt + 128, :], sem="p6x")
        b.ldy(xg6[:, 2048 * par + 1024:2048 * par + 2048],
              oacc_d.ap()[128 * tt:128 * tt + 128, :], sem="p6x")
        ldn = b.n("p6x")
        b.wt(nc.vector, "p6x", ldn)
        b.wt(nc.scalar, "p6x", ldn)
        oap = xg6[:, 2048 * par + 1024:2048 * par + 2048]
        _ln_tile(b, ln6[:, :], oap, D, lnsc)
        xt = xg6[:, 2048 * par:2048 * par + 1024]
        b.V(lambda v, x_=xt: v.tensor_add(ysb6[:, :], ln6[:, :], x_),
            reads=(ln6[:, 0:1],), writes=(ysb6[:, 0:1],))
        vn = b.incl("vc")
        b.wt(nc.sync, "vc", vn)
        b.ldy(out[128 * tt:128 * tt + 128, :], ysb6[:, :])
    p6.close()
    b.wt(nc.sync, "dy")
    b.wt(nc.gpsimd, "dw")
    nc.leave_named_scope("ph6", _sid["ph6"], False)
    st.close()
    return b


_NC_CACHE = {}


def _get_nc():
    key = (PHASE_LIMIT, tuple(DEV_TAPS))
    if key not in _NC_CACHE:
        _NC_CACHE[key] = build_nc()
    return _NC_CACHE[key]


def kernel(**inputs):
    f = np.float32
    x = np.ascontiguousarray(inputs["L0_attn_x"], dtype=f)
    qk_w = np.asarray(inputs["L0_attn_qk_w"], dtype=f)
    qk_b = np.asarray(inputs["L0_attn_qk_b"], dtype=f)
    v_w = np.asarray(inputs["L0_attn_v_w"], dtype=f)
    gains = [np.asarray(inputs["L0_attn_resid_gain"], dtype=f).reshape(16),
             np.asarray(inputs["L0_attn_skip_gain"], dtype=f).reshape(16),
             np.asarray(inputs["L0_attn_centre_gain"], dtype=f).reshape(16)]
    in_maps = []
    for core in range(8):
        p, hh = core // 2, core % 2
        hs = slice(512 * hh, 512 * hh + 512)
        m = {
            "x_seq": np.ascontiguousarray(x[p]),
            "x_own": np.ascontiguousarray(x[p, 2048 * hh:2048 * hh + 2048]),
            "tok_base": np.array([[2048 * hh]], dtype=np.uint32),
            "qkw_own": np.ascontiguousarray(
                np.concatenate([qk_w[:, 0:1024][:, hs],
                                qk_w[:, 1024:2048][:, hs]], axis=1)),
            "vw_own": np.ascontiguousarray(v_w[:, hs]),
            "qb_pp": np.ascontiguousarray(qk_b[0:1024][hs].reshape(4, 128).T),
            "kb_pp": np.ascontiguousarray(qk_b[1024:2048][hs].reshape(4, 128).T),
            "cprojw": np.ascontiguousarray(inputs["L0_attn_cproj_w"], dtype=f),
            "gain_r": np.ascontiguousarray(
                np.broadcast_to(gains[0][8 * hh:8 * hh + 8][None, :], (64, 8))),
            "gain_s128": np.ascontiguousarray(
                np.broadcast_to(gains[1][8 * hh:8 * hh + 8][None, :], (128, 8))),
            "gain_c": np.ascontiguousarray(
                np.broadcast_to(gains[2][8 * hh:8 * hh + 8][None, :], (64, 8))),
            "w1": np.ascontiguousarray(inputs["L0_ffn_w1"], dtype=f),
            "b1_pp": np.ascontiguousarray(
                np.asarray(inputs["L0_ffn_b1"], dtype=f).reshape(32, 128).T),
            "w2": np.ascontiguousarray(inputs["L0_ffn_w2"], dtype=f),
            "b2_row": np.ascontiguousarray(
                np.asarray(inputs["L0_ffn_b2"], dtype=f)[None, :]),
            "wq1": np.ascontiguousarray(inputs["L1_attn_Wq"], dtype=f),
            "wk1": np.ascontiguousarray(inputs["L1_attn_Wk"], dtype=f),
            "wv1": np.ascontiguousarray(inputs["L1_attn_Wv"], dtype=f),
            "wo1": np.ascontiguousarray(inputs["L1_attn_Wo"], dtype=f),
            "bo_row": np.ascontiguousarray(
                np.asarray(inputs["L1_attn_bo"], dtype=f)[None, :]),
            "wf_rep": np.ascontiguousarray(
                np.concatenate([np.asarray(inputs["L1_attn_Wf"], dtype=f)] * 2,
                               axis=0)),
            "bf_row": np.ascontiguousarray(
                np.asarray(inputs["L1_attn_bf"], dtype=f)[None, :]),
            "bf_pp": np.ascontiguousarray(
                np.asarray(inputs["L1_attn_bf"], dtype=f)[:, None]),
            "wg1": np.ascontiguousarray(inputs["L1_ffn_wg1"], dtype=f),
            "bg1_pp": np.ascontiguousarray(
                np.asarray(inputs["L1_ffn_bg1"], dtype=f).reshape(2, 128).T),
            "wg2": np.ascontiguousarray(inputs["L1_ffn_wg2"], dtype=f),
            "w_in": np.ascontiguousarray(inputs["L1_ffn_w_in"], dtype=f),
            "w_out": np.ascontiguousarray(inputs["L1_ffn_w_out"], dtype=f),
        }
        in_maps.append(m)

    b = _get_nc()
    res = run_bass_kernel_spmd(b.nc, in_maps, core_ids=list(range(8)))
    if DEV_TAPS:
        kernel.last_results = res.results
    outp = np.empty((4, S, D), dtype=np.float32)
    for core in range(8):
        p, hh = core // 2, core % 2
        outp[p, 2048 * hh:2048 * hh + 2048] = res.results[core]["out"]
    return outp


# version 06
# speedup vs baseline: 1.3073x; 1.1322x over previous
import contextlib
import numpy as np
import ml_dtypes
import concourse.bass as bass
import concourse.mybir as mybir
from concourse.bass_utils import run_bass_kernel_spmd

F32 = mybir.dt.float32
BF16 = mybir.dt.bfloat16
U32 = mybir.dt.uint32
AF = mybir.ActivationFunctionType
ALU = mybir.AluOpType
AX = mybir.AxisListType.X
BF = ml_dtypes.bfloat16
ds = bass.ds

S, D = 4096, 1024
NT, NG = 32, 8
TOK, NTO, NGO = 2048, 16, 4
HD, HOWN, NPAIR = 64, 8, 4
H1, FDIM = 16, 64
W0 = 4096
NEXP, EH, G1 = 8, 2048, 256
EPS = 1e-5
NEGBIG = -30000.0

DEV_TAPS = ()
PHASE_LIMIT = 99


class Build:
    def __init__(self):
        nc = bass.Bass("TRN2", target_bir_lowering=False)
        self.nc = nc
        self.cnt = {}
        self.sems = {}
        for name in ("dy", "dw", "tm", "sc", "vc", "gp", "cc",
                     "px0", "px1", "pt0", "pt1", "sag", "pa0", "pa1",
                     "p3x", "p4x", "p4w", "p5x", "p5w", "p5r", "p5k",
                     "p6x", "p6g", "p6s", "p6l0", "p6l1", "p6w",
                     "p6o0", "p6o1", "p6o2", "p6o3", "pcp", "p0w", "pnb", "p6w0", "p6w1"):
            self.sems[name] = nc.alloc_semaphore(name)
            self.cnt[name] = 0
        self.dma_owner = {}
        self.vdirty = set()
        self.sdirty = set()
        self.last = {}
        self.taps = {}

    def inc(self, inst, name, k=1):
        inst.then_inc(self.sems[name], k)
        self.cnt[name] += k
        return self.cnt[name]

    def mm(self, *a, **kw):
        inst = self.nc.tensor.matmul(*a, **kw)
        self.last["tm"] = inst
        return inst

    def tr(self, *a, **kw):
        inst = self.nc.tensor.transpose(*a, **kw)
        self.last["tm"] = inst
        return inst

    def incl(self, name):
        return self.inc(self.last[name], name)

    def n(self, name):
        return self.cnt[name]

    def wt(self, eng, name, val=None):
        if val is None:
            val = self.cnt[name]
        if val > 0:
            eng.wait_ge(self.sems[name], val)

    def ldy(self, out, in_, sem="dy"):
        self.dma_owner[sem] = "sync"
        return self.inc(self.nc.sync.dma_start(out=out, in_=in_), sem, 16)

    def ldw(self, out, in_, accum=False, sem="dw"):
        kw = {"accum_op": ALU.add} if accum else {}
        self.dma_owner[sem] = "gpsimd"
        return self.inc(self.nc.gpsimd.dma_start(out=out, in_=in_, **kw), sem, 16)

    def V(self, emit, reads=(), writes=()):
        rn = {ap.tensor.name for ap in reads}
        if rn & self.vdirty:
            self.nc.vector.drain()
            self.vdirty = set()
        inst = emit(self.nc.vector)
        self.vdirty |= {ap.tensor.name for ap in writes}
        self.last["vc"] = inst
        return inst

    def SC(self, emit, reads=(), writes=()):
        rn = {ap.tensor.name for ap in reads}
        if rn & self.sdirty:
            self.nc.scalar.drain()
            self.sdirty = set()
        inst = emit(self.nc.scalar)
        self.sdirty |= {ap.tensor.name for ap in writes}
        self.last["sc"] = inst
        return inst

    def finish(self):
        for sem, owner in self.dma_owner.items():
            eng = self.nc.sync if owner == "sync" else self.nc.gpsimd
            self.wt(eng, sem)
        self.nc.all_engine_barrier()
        self.vdirty = set()
        self.sdirty = set()

    def tap(self, name, shape, dtype=F32):
        if name in DEV_TAPS:
            t = self.nc.dram_tensor("tap_" + name, list(shape), dtype,
                                    kind="ExternalOutput")
            self.taps[name] = t.ap()
            return self.taps[name]
        return None


def _ln_tile(b, out_sb, x_ap, W, sc):
    nc = b.nc
    b.V(lambda v: v.reduce_sum(sc["ssum"][:, 0:1], x_ap, AX),
        reads=(x_ap,), writes=(sc["ssum"][:, 0:1],))
    b.SC(lambda s: s.activation(sc["sqscr"][:, 0:W], x_ap, AF.Square,
                                accum_out=sc["sqs"][:, 0:1]),
         reads=(x_ap,), writes=(sc["sqscr"][:, 0:W], sc["sqs"][:, 0:1]))
    b.incl("sc")
    b.wt(nc.vector, "sc")
    b.V(lambda v: v.tensor_scalar(sc["mean"][:, 0:1], sc["ssum"][:, 0:1],
                                  1.0 / W, None, ALU.mult),
        reads=(sc["ssum"][:, 0:1],), writes=(sc["mean"][:, 0:1],))
    b.V(lambda v: v.tensor_scalar(sc["msq"][:, 0:1], sc["sqs"][:, 0:1],
                                  1.0 / W, None, ALU.mult),
        reads=(sc["sqs"][:, 0:1],), writes=(sc["msq"][:, 0:1],))
    b.V(lambda v: v.tensor_mul(sc["mm"][:, 0:1], sc["mean"][:, 0:1],
                               sc["mean"][:, 0:1]),
        reads=(sc["mean"][:, 0:1],), writes=(sc["mm"][:, 0:1],))
    b.V(lambda v: v.tensor_sub(sc["var"][:, 0:1], sc["msq"][:, 0:1],
                               sc["mm"][:, 0:1]),
        reads=(sc["msq"][:, 0:1], sc["mm"][:, 0:1]), writes=(sc["var"][:, 0:1],))
    b.V(lambda v: v.tensor_scalar(sc["vare"][:, 0:1], sc["var"][:, 0:1],
                                  EPS, None, ALU.add),
        reads=(sc["var"][:, 0:1],), writes=(sc["vare"][:, 0:1],))
    b.V(lambda v: v.reciprocal(sc["rv"][:, 0:1], sc["vare"][:, 0:1]),
        reads=(sc["vare"][:, 0:1],), writes=(sc["rv"][:, 0:1],))
    b.incl("vc")
    b.wt(nc.scalar, "vc")
    b.SC(lambda s: s.activation(sc["rstd"][:, 0:1], sc["rv"][:, 0:1], AF.Sqrt),
         reads=(sc["rv"][:, 0:1],), writes=(sc["rstd"][:, 0:1],))
    b.incl("sc")
    b.wt(nc.vector, "sc")
    b.V(lambda v: v.scalar_tensor_tensor(sc["nmean"][:, 0:1], sc["mean"][:, 0:1],
                                         -1.0, sc["rstd"][:, 0:1],
                                         ALU.mult, ALU.mult),
        reads=(sc["mean"][:, 0:1],), writes=(sc["nmean"][:, 0:1],))
    b.V(lambda v: v.tensor_scalar(out_sb, x_ap, sc["rstd"][:, 0:1],
                                  sc["nmean"][:, 0:1], ALU.mult, ALU.add),
        reads=(x_ap, sc["nmean"][:, 0:1]), writes=(out_sb,))
    return b.n("vc"), b.n("sc")


def _make_consts():
    idb = np.eye(128, dtype=BF)
    idf = np.eye(128, dtype=np.float32)
    tri = np.where(np.arange(128)[None, :] > np.arange(128)[:, None],
                   np.float32(NEGBIG), np.float32(0.0)).astype(BF)
    ones_b = np.ones((128, 128), dtype=BF)
    ones_f = np.ones((1, 128), dtype=np.float32)
    sel2 = np.zeros((128, 2), dtype=BF)
    sel2[0:64, 0] = 1
    sel2[64:128, 1] = 1
    inv = (1.0 / (np.arange(S, dtype=np.float64) + 1.0)).astype(BF)[None, :]
    triinv = np.zeros((NG, 4, 128, 512), dtype=BF)
    for g in range(NG):
        qg = 512 * g + np.arange(512)
        for c in range(4):
            kg = 128 * (4 * g + c) + np.arange(128)
            m = (kg[:, None] <= qg[None, :]).astype(np.float64)
            triinv[g, c] = (m / (qg[None, :] + 1.0)).astype(BF)
    return idb, idf, tri, ones_b, ones_f, sel2, inv, triinv


def _strided3(ap_base, pstep, a, b_, c, d):
    return bass.AP(ap_base.tensor, ap_base.offset, [[pstep, 128], [a, b_], [c, d]])


def build_nc():
    b = Build()
    nc = b.nc
    st = contextlib.ExitStack()

    def inp(name, shape, dtype=F32):
        return nc.dram_tensor(name, list(shape), dtype, kind="ExternalInput").ap()

    x_seq = inp("x_seq", [S, D])
    x_own = inp("x_own", [TOK, D])
    tok_base = inp("tok_base", [1, 1], U32)
    qkw_own = inp("qkw_own", [D, 2 * 512])
    vw_own = inp("vw_own", [D, 512])
    qb_pp = inp("qb_pp", [128, 4])
    kb_pp = inp("kb_pp", [128, 4])
    cprojw = inp("cprojw", [D, D])
    gain_r = inp("gain_r", [64, 8])
    gs128i = inp("gain_s128", [128, 8])
    gain_c = inp("gain_c", [64, 8])
    w1 = inp("w1", [D, W0])
    b1_pp = inp("b1_pp", [128, 32])
    w2 = inp("w2", [W0, D])
    b2_row = inp("b2_row", [1, D])
    wq1 = inp("wq1", [D, D])
    wk1 = inp("wk1", [D, D])
    wv1 = inp("wv1", [D, D])
    wo1 = inp("wo1", [D, D])
    bo_row = inp("bo_row", [1, D])
    wf_rep = inp("wf_rep", [128, FDIM])
    bf_row = inp("bf_row", [1, FDIM])
    bf_pp = inp("bf_pp", [64, 1])
    wg1 = inp("wg1", [D, G1])
    bg1_pp = inp("bg1_pp", [128, 2])
    wg2 = inp("wg2", [G1, NEXP])
    w_in = inp("w_in", [NEXP, D, EH])
    w_out = inp("w_out", [NEXP, EH, D])

    out = nc.dram_tensor("out", [TOK, D], F32, kind="ExternalOutput").ap()

    idb_h, idf_h, tri_h, onesb_h, onesf_h, sel2_h, inv_h, triinv_h = _make_consts()
    c_idb = nc.inline_tensor(idb_h, name="c_idb").ap()
    c_idf = nc.inline_tensor(idf_h, name="c_idf").ap()
    c_tri = nc.inline_tensor(tri_h, name="c_tri").ap()
    c_onesb = nc.inline_tensor(onesb_h, name="c_onesb").ap()
    c_onesf = nc.inline_tensor(onesf_h, name="c_onesf").ap()
    c_sel2 = nc.inline_tensor(sel2_h, name="c_sel2").ap()
    c_inv = nc.inline_tensor(inv_h, name="c_inv").ap()
    c_triinv = nc.inline_tensor(triinv_h, name="c_triinv").ap()

    ag_in = nc.dram_tensor("ag_in", [512, S], BF16)
    ag_out = nc.dram_tensor("ag_out", [2, 512, S], BF16)
    ar_in = nc.dram_tensor("ar_in", [64, H1, 65], F32)
    ar_out = nc.dram_tensor("ar_out", [64, H1, 65], F32)
    stream_a = nc.dram_tensor("stream_a", [TOK, D], F32)
    stream_b = nc.dram_tensor("stream_b", [TOK, D], F32)
    stream_c = nc.dram_tensor("stream_c", [TOK, D], F32)
    lnx6 = nc.dram_tensor("lnx6", [128, NTO * 1024], BF16)
    oacc_d = nc.dram_tensor("oacc_d", [TOK, D], F32)

    def sb(name, shape, dtype):
        return st.enter_context(nc.sbuf_tensor(name, list(shape), dtype))

    idb = sb("idb", [128, 128], BF16)
    idf = sb("idf", [128, 128], F32)
    tri = sb("tri", [128, 128], BF16)
    onesb = sb("onesb", [128, 128], BF16)
    onesf = sb("onesf", [1, 128], F32)
    sel2 = sb("sel2", [128, 2], BF16)
    negb_rep = sb("negb_rep", [128, 8], F32)
    grep = sb("grep", [64, 8], F32)
    gs128 = sb("gs128", [128, 8], F32)
    craw = sb("craw", [64, 8], F32)
    cnrep = sb("cnrep", [64, 8], F32)
    qbp = sb("qbp", [128, 4], F32)
    kbp = sb("kbp", [128, 4], F32)
    b1p = sb("b1p", [128, 32], F32)
    wfr = sb("wfr", [128, FDIM], BF16)
    bfr = sb("bfr", [1, FDIM], BF16)
    bfp = sb("bfp", [64, 1], F32)
    bg1p = sb("bg1p", [128, 2], F32)
    tokb = sb("tokb", [1, 1], U32)
    lnsc = {k: sb("lnsc_" + k, [128, 1], F32)
            for k in ("ssum", "sqs", "mean", "msq", "mm", "var", "vare", "rv",
                      "rstd", "nmean")}
    lnsc["sqscr"] = sb("lnsc_sqscr", [128, 1024], BF16)
    q2all = sb("q2all", [2, 32], F32)
    k2all = sb("k2all", [2, 32], F32)
    nb2 = sb("nb2", [2, 4], F32)
    nb2b = sb("nb2b", [2, 4], F32)
    nbr1 = sb("nbr1", [1, 4], F32)
    tiny = sb("tiny", [2, 8], F32)

    b.ldy(idb[:, :], c_idb)
    b.ldy(idf[:, :], c_idf)
    b.ldy(tri[:, :], c_tri)
    b.ldy(onesb[:, :], c_onesb)
    b.ldy(onesf[:, :], c_onesf)
    b.ldy(sel2[:, :], c_sel2)
    b.ldy(grep[:, :], gain_r)
    b.ldy(gs128[:, :], gs128i)
    b.ldy(craw[:, :], gain_c)
    b.ldy(qbp[:, :], qb_pp)
    b.ldy(kbp[:, :], kb_pp)
    b.ldy(b1p[:, :], b1_pp)
    b.ldy(bfp[:, :], bf_pp)
    b.ldy(bg1p[:, :], bg1_pp)
    b.ldy(tokb[:, :], tok_base)
    b.ldw(wfr[:, :], wf_rep)
    b.ldw(bfr[:, :], bf_row)
    setup_dy = b.n("dy")
    b.wt(nc.vector, "dy", setup_dy)
    b.V(lambda v: v.tensor_scalar(cnrep[:, :], craw[:, :], -1.0, None, ALU.mult),
        reads=(craw[:, 0:1],), writes=(cnrep[:, 0:1],))
    b.incl("vc")

    _sid = {}
    _sid["ph0"] = nc.enter_named_scope("ph0", False)[0]
    p01 = contextlib.ExitStack()

    def sb1(name, shape, dtype):
        return p01.enter_context(nc.sbuf_tensor(name, list(shape), dtype))

    invp = sb1("invp", [1, S], BF16)
    carryT = sb1("carryT", [1, NG * 512], BF16)
    qfm = sb1("qfm", [128, 4 * S], BF16)
    kfm = sb1("kfm", [128, 4 * S], BF16)
    vext = sb1("vext", [128, NT * 520], BF16)
    probs = sb1("probs", [128, 2048], BF16)
    skipI = sb1("skipI", [128, 8 * 128], BF16)
    triinv_s = sb1("triinv_s", [128, 2 * 2048], BF16)
    recip_sb = sb1("recip_sb", [1, 1024], F32)
    pvs_sb = sb1("pvs_sb", [128, 1024], F32)
    r_sb = sb1("r_sb", [128, 1024], F32)
    ta_sb = sb1("ta_sb", [64, 512], F32)
    tb_sb = sb1("tb_sb", [64, 512], F32)
    outpair = sb1("outpair", [128, 512], BF16)
    gs_sb = sb1("gs_sb", [64, 8], F32)
    cacc = [sb1("cacc0", [64, 8], F32), sb1("cacc1", [64, 8], F32)]
    ct_sb = sb1("ct_sb", [8, 64], F32)

    p0t = contextlib.ExitStack()

    def sb0(name, shape, dtype):
        return p0t.enter_context(nc.sbuf_tensor(name, list(shape), dtype))

    qkwb = sb0("qkwb", [128, 8 * 1024], BF16)
    vwb = sb0("vwb", [128, 8 * 512], BF16)

    xa = sb0("xa", [128, 2 * 1024], F32)
    lnt = sb0("lnt", [128, 2 * 1024], F32)
    lnxfm = sb0("lnxfm", [128, 4096], BF16)
    sqscr2 = sb0("sqscr2", [128, 512], BF16)

    b.ldy(invp[:, :], c_inv)
    b.ldw(qkwb[:, :].rearrange("p (c n) -> p c n", c=8),
          qkw_own.rearrange("(c p) n -> p c n", p=128), sem="p0w")
    b.ldw(vwb[:, :].rearrange("p (c n) -> p c n", c=8),
          vw_own.rearrange("(c p) n -> p c n", p=128), sem="p0w")
    dw_w0 = b.n("p0w")
    b.V(lambda v: v.memset(carryT[0:1, 0:512], 0.0), writes=(carryT[0:1, 0:1],))
    vxv = vext[:, :].rearrange("p (t h o) -> p t h o", t=NT, h=HOWN)
    b.V(lambda v: v.memset(vxv[:, :, :, 64:65], 1.0), writes=(vext[:, 0:1],))
    vready0 = b.incl("vc")

    with (nc.psum_tensor("trps", [128, 1024], F32) as trps,
          nc.psum_tensor("mmA", [128, 512], F32) as mmA,
          nc.psum_tensor("mmB", [128, 512], F32) as mmB,
          nc.psum_tensor("vps", [128, 512], F32) as vps,
          nc.psum_tensor("q2ps", [2, 512], F32) as q2ps,
          nc.psum_tensor("gsps", [65, 8], F32) as gsps,
          nc.psum_tensor("ctps", [8, 64], F32) as ctps):
        lnv, lnsct, trv = {}, {}, {}
        last_sel2_tm = 0
        b.wt(nc.tensor, "dy", setup_dy)
        for t in range(NT):
            par = t % 2
            gpar = (t // 4) % 2
            if t >= 2:
                b.wt(nc.sync, "vc", trv[t - 2])
                b.wt(nc.sync, "sc", lnsct[t - 2])
            ldn = b.ldy(xa[:, 1024 * par:1024 * par + 1024],
                        x_seq[128 * t:128 * t + 128, :], sem=f"px{par}")
            b.wt(nc.vector, f"px{par}", ldn)
            b.wt(nc.scalar, f"px{par}", ldn)
            xap = xa[:, 1024 * par:1024 * par + 1024]
            lnap = lnt[:, 1024 * par:1024 * par + 1024]
            _v, _s = _ln_tile(b, lnap, xap, D, lnsc)
            lnv[t] = b.incl("vc")
            lnsct[t] = _s
            b.wt(nc.tensor, "vc", lnv[t])
            if t >= 1:
                b.wt(nc.tensor, "vc", trv[t - 1])
            for c in range(8):
                b.tr(trps[:, 128 * c:128 * c + 128],
                                    lnap[:, 128 * c:128 * c + 128], idf[:, :])
            tmn = b.incl("tm")
            b.wt(nc.vector, "tm", tmn)
            dst = lnxfm[:, 128 * (t % 4):]
            dstv = bass.AP(dst.tensor, dst.offset,
                           [[dst.ap[0][0], 128], [512, 8], [1, 128]])
            srcv = trps[:, :].rearrange("p (c n) -> p c n", c=8)
            b.V(lambda v, dv=dstv, sv=srcv: v.tensor_copy(dv, sv),
                writes=(lnxfm[:, 0:1],))
            trv[t] = b.incl("vc")

            if t % 4 == 3:
                gf = t // 4
                rhs_base = 0
                b.wt(nc.tensor, "vc", trv[t])
                if gf == 0:
                    b.wt(nc.tensor, "p0w", dw_w0)
                    b.wt(nc.tensor, "vc", vready0)
                for which, wofs, dst_fm, biasp, scl in (
                        ("q", 0, qfm, qbp, 0.125), ("k", 512, kfm, kbp, None)):
                    for pc in range(4):
                        ps = mmA if pc % 2 == 0 else mmB
                        b.wt(nc.tensor, "vc")
                        for k in range(8):
                            b.mm(
                                ps[:, :],
                                qkwb[:, 1024 * k + wofs + 128 * pc:
                                     1024 * k + wofs + 128 * pc + 128],
                                lnxfm[:, rhs_base + 512 * k:
                                      rhs_base + 512 * k + 512],
                                start=(k == 0), stop=(k == 7))
                        tmn = b.incl("tm")
                        b.wt(nc.vector, "tm", tmn)
                        dstap = dst_fm[:, 4096 * pc + 512 * gf:
                                       4096 * pc + 512 * gf + 512]
                        if scl is None:
                            b.V(lambda v, o=dstap, p=ps, bp=biasp, c=pc:
                                v.tensor_scalar(o, p[:, :], bp[:, c:c + 1],
                                                None, ALU.add),
                                writes=(dst_fm[:, 0:1],))
                        else:
                            b.V(lambda v, o=dstap, p=ps, bp=biasp, c=pc, s_=scl:
                                v.tensor_scalar(o, p[:, :], bp[:, c:c + 1],
                                                s_, ALU.add, ALU.mult),
                                writes=(dst_fm[:, 0:1],))
                        vn = b.incl("vc")
                        b.wt(nc.scalar, "vc", vn)
                        b.wt(nc.scalar, "tm", last_sel2_tm)
                        b.SC(lambda s, i=dstap: s.activation(
                            sqscr2[:, 0:512], i, AF.Square),
                            writes=(sqscr2[:, 0:1],))
                        scn = b.incl("sc")
                        b.wt(nc.tensor, "sc", scn)
                        b.mm(q2ps[:, :], sel2[:, :],
                                         sqscr2[:, 0:512], start=True, stop=True)
                        last_sel2_tm = tmn = b.incl("tm")
                        b.wt(nc.vector, "tm", tmn)
                        dcol = (q2all if which == "q" else k2all)
                        b.V(lambda v, d=dcol, col=8 * pc + gf: v.reduce_max(
                            d[:, col:col + 1], q2ps[:, :], AX),
                            writes=(dcol[:, 0:1],))
                        b.incl("vc")
                for tt in range(4 * gf, 4 * gf + 4):
                    b.wt(nc.tensor, "vc")
                    for k in range(8):
                        b.mm(
                            vps[:, :],
                            lnxfm[:, rhs_base + 512 * k + 128 * (tt % 4):
                                  rhs_base + 512 * k + 128 * (tt % 4) + 128],
                            vwb[:, 512 * k:512 * k + 512],
                            start=(k == 0), stop=(k == 7))
                    tmn = b.incl("tm")
                    b.wt(nc.vector, "tm", tmn)
                    dst = vext[:, 520 * tt:]
                    dstv = bass.AP(dst.tensor, dst.offset,
                                   [[dst.ap[0][0], 128], [65, 8], [1, 64]])
                    srcv = vps[:, :].rearrange("p (h o) -> p h o", h=8)
                    b.V(lambda v, dv=dstv, sv=srcv: v.tensor_copy(dv, sv),
                        writes=(vext[:, 0:1],))
                    vpse = b.incl("vc")
                b.wt(nc.tensor, "vc", vpse)
                for lh in range(8):
                    for ti in range(4):
                        tt2 = 4 * gf + ti
                        b.mm(gsps[:, lh:lh + 1],
                             vext[:, 520 * tt2 + 65 * lh:
                                  520 * tt2 + 65 * lh + 65],
                             onesb[:, 0:1],
                             start=(ti == 0), stop=(ti == 3))
                tmn = b.incl("tm")
                b.wt(nc.vector, "tm", tmn)
                b.V(lambda v: v.tensor_copy(gs_sb[:, :], gsps[0:64, :]),
                    writes=(gs_sb[:, 0:1],))
                vn = b.incl("vc")
                src_c = cacc[gf % 2]
                dst_c = cacc[1 - gf % 2]
                if gf > 0:
                    b.wt(nc.tensor, "vc", vn)
                    b.tr(ctps[:, :], src_c[:, 0:8],
                                        idf[0:64, 0:64])
                    tmn = b.incl("tm")
                    b.wt(nc.vector, "tm", tmn)
                    b.wt(nc.vector, "dw")
                    b.V(lambda v: v.tensor_copy(ct_sb[:, :], ctps[0:8, 0:64]),
                        writes=(ct_sb[0:1, 0:1],))
                    vn2 = b.incl("vc")
                    b.wt(nc.gpsimd, "vc", vn2)
                    b.ldw(carryT[0:1, 512 * gf:512 * gf + 512], ct_sb[:, :])
                    b.V(lambda v, d=dst_c, s_=src_c: v.tensor_add(
                        d[:, :], s_[:, :], gs_sb[:, :]),
                        reads=(gs_sb[:, 0:1],), writes=(dst_c[:, 0:1],))
                else:
                    b.V(lambda v, d=dst_c: v.tensor_copy(d[:, :], gs_sb[:, :]),
                        reads=(gs_sb[:, 0:1],), writes=(dst_c[:, 0:1],))
                b.incl("vc")

        for c in range(4):
            for d_, a_ in ((q2all, 0), (k2all, 1)):
                src = bass.AP(d_, 8 * c, [[32, 2], [1, 8]])
                b.V(lambda v, s_=src, aa=a_, cc=c: v.reduce_max(
                    tiny[:, 4 * aa + cc:4 * aa + cc + 1], s_, AX),
                    reads=(d_[:, 0:1],), writes=(tiny[:, 0:1],))
        b.V(lambda v: v.tensor_mul(nb2[:, :], tiny[:, 0:4], tiny[:, 4:8]),
            reads=(tiny[:, 0:1],), writes=(nb2[:, 0:1],))
        vn = b.incl("vc")
        b.wt(nc.scalar, "vc", vn)
        b.SC(lambda s: s.activation(nb2b[:, :], nb2[:, :], AF.Sqrt),
             reads=(nb2[:, 0:1],), writes=(nb2b[:, 0:1],))
        scn = b.incl("sc")
        b.wt(nc.vector, "sc", scn)
        b.V(lambda v: v.tensor_scalar(nb2[:, :], nb2b[:, :], -1.0, None,
                                      ALU.mult),
            reads=(nb2b[:, 0:1],), writes=(nb2[:, 0:1],))
        vn = b.incl("vc")
        b.wt(nc.sync, "vc", vn)
        nbl = b.ldy(nbr1[:, :], nb2[1:2, :], sem="pnb")
        b.wt(nc.tensor, "pnb", nbl)
        b.wt(nc.tensor, "vc", vn)
        b.mm(mmA[:, 0:4], onesf[0:1, :], nb2[0:1, :],
                         start=True, stop=True)
        b.mm(mmA[:, 4:8], onesf[0:1, :], nbr1[0:1, :],
                         start=True, stop=True)
        tmn = b.incl("tm")
        b.wt(nc.vector, "tm", tmn)
        dstv = bass.AP(negb_rep, 0, [[8, 128], [1, 2], [2, 4]])
        srcv = mmA[:, 0:8].rearrange("p (r c) -> p r c", r=2)
        b.V(lambda v: v.tensor_copy(dstv, srcv), writes=(negb_rep[:, 0:1],))
        negb_v = b.incl("vc")

        for nm, t_, w in (("p0_qfm", qfm, 4 * S), ("p0_kfm", kfm, 4 * S),
                          ("p0_vext", vext, NT * 520)):
            tp = b.tap(nm, [128, w], BF16)
            if tp is not None:
                b.wt(nc.sync, "vc")
                b.ldy(tp, t_[:, :])
        tp = b.tap("p0_negb", [128, 8])
        if tp is not None:
            b.wt(nc.sync, "vc")
            b.ldy(tp, negb_rep[:, :])
        tp = b.tap("p0_carryT", [1, NG * 512], BF16)
        if tp is not None:
            b.wt(nc.sync, "dw")
            b.ldy(tp, carryT[:, :])

    p0t.close()
    if PHASE_LIMIT < 1:
        p01.close()
        b.finish()
        st.close()
        return b

    nc.leave_named_scope("ph0", _sid["ph0"], False)
    _sid["ph1"] = nc.enter_named_scope("ph1", False)[0]
    b.wt(nc.vector, "dy", setup_dy)
    for lh in range(8):
        b.V(lambda v, l=lh: v.tensor_scalar(
            skipI[:, 128 * l:128 * l + 128], idb[:, :], gs128[:, l:l + 1],
            None, ALU.mult),
            writes=(skipI[:, 0:1],))
    skipIv = b.incl("vc")
    with (nc.psum_tensor("S0", [128, 1024], F32) as S0,
          nc.psum_tensor("S1", [128, 1024], F32) as S1,
          nc.psum_tensor("pvA", [65, 512], F32) as pvA,
          nc.psum_tensor("pvB", [65, 512], F32) as pvB,
          nc.psum_tensor("cmr", [64, 1024], F32) as cmr):
        b.wt(nc.tensor, "dw")
        b.wt(nc.tensor, "vc", skipIv)
        trild = {0: b.ldy(triinv_s[:, 0:2048].rearrange("p (c q) -> p c q", c=4),
                          c_triinv[0].rearrange("c k q -> k c q"),
                          sem="pt0")}
        prev_rel_sc = 0
        cm_free_vc = 0
        prev_ep_tm = 0
        last_ag = 0
        for g in range(NG):
            nj = 4 * g + 4
            gpar = g % 2
            if g + 1 < NG:
                b.wt(nc.sync, "tm", prev_ep_tm if g else 0)
                trild[g + 1] = b.ldy(
                    triinv_s[:, 2048 * (1 - gpar):2048 * (1 - gpar) + 2048]
                    .rearrange("p (c q) -> p c q", c=4),
                    c_triinv[g + 1].rearrange("c k q -> k c q"),
                    sem=f"pt{(g + 1) % 2}")
            for pc in range(NPAIR):
                lhA, lhB = 2 * pc, 2 * pc + 1
                exp_cnt, pv_cnt = {}, {}
                def emit_S(j):
                    c2 = j - 4 * g
                    qoff = 128 * c2 if c2 > 0 else 0
                    diag = c2 >= 0
                    S_ = S0 if j % 2 == 0 else S1
                    if j >= 2:
                        b.wt(nc.tensor, "sc", exp_cnt[j - 2])
                    elif j == 0:
                        b.wt(nc.tensor, "sc", prev_rel_sc)
                        if g == 0 and pc == 0:
                            b.wt(nc.tensor, "pt0", trild[0])
                    for (co, r0) in ((0, 0), (512, 64)):
                        b.mm(
                            S_[:, co + qoff:co + 512],
                            kfm[:, 4096 * pc + 128 * j:
                                4096 * pc + 128 * j + 128][r0:r0 + 64, :],
                            qfm[:, 4096 * pc + 512 * g + qoff:
                                4096 * pc + 512 * g + 512][r0:r0 + 64, :],
                            start=True, stop=not diag,
                            tile_position=(r0, 0))
                        if diag:
                            b.mm(
                                S_[:, co + qoff:co + qoff + 128],
                                tri[:, :], idb[:, :],
                                start=False, stop=True,
                                tile_position=(0, 0))
                    return b.incl("tm")

                def emit_exp(j, s_tm):
                    c2 = j - 4 * g
                    qoff = 128 * c2 if c2 > 0 else 0
                    S_ = S0 if j % 2 == 0 else S1
                    b.wt(nc.scalar, "tm", s_tm)
                    if j >= 2:
                        b.wt(nc.scalar, "tm", pv_cnt[j - 2])
                    for (lh, co) in ((lhA, 0), (lhB, 512)):
                        b.SC(lambda s, ss=S_, l=lh,
                             p=1024 * (j % 2) + co, c_=co, q=qoff:
                             s.activation(probs[:, p + q:p + 512],
                                          ss[:, c_ + q:c_ + 512], AF.Exp,
                                          bias=negb_rep[:, l:l + 1]),
                             writes=(probs[:, 0:1],))
                    exp_cnt[j] = b.incl("sc")

                def emit_PV(j):
                    c2 = j - 4 * g
                    qoff = 128 * c2 if c2 > 0 else 0
                    b.wt(nc.tensor, "sc", exp_cnt[j])
                    for (pv, lh, co) in ((pvA, lhA, 0), (pvB, lhB, 512)):
                        b.mm(
                            pv[:, qoff:512],
                            vext[:, 520 * j + 65 * lh:
                                 520 * j + 65 * lh + 65],
                            probs[:, 1024 * (j % 2) + co + qoff:
                                  1024 * (j % 2) + co + 512],
                            start=(j == 0), stop=(j == nj - 1))
                    pv_cnt[j] = b.incl("tm")

                stms = {}
                for j in range(nj):
                    stms[j] = emit_S(j)
                    emit_exp(j, stms[j])
                    if j >= 1:
                        emit_PV(j - 1)
                emit_PV(nj - 1)
                sweep_tm = pv_cnt[nj - 1]

                b.wt(nc.scalar, "tm", sweep_tm)
                b.wt(nc.scalar, "vc", cm_free_vc)
                b.SC(lambda s: s.copy(pvs_sb[0:64, 0:512], pvA[0:64, :]),
                     writes=(pvs_sb[0:1, 0:1],))
                b.SC(lambda s: s.copy(pvs_sb[0:64, 512:1024], pvB[0:64, :]),
                     writes=(pvs_sb[0:1, 0:1],))
                b.incl("sc")
                b.wt(nc.vector, "tm", sweep_tm)
                b.V(lambda v: v.reciprocal(recip_sb[0:1, 0:512],
                                           pvA[64:65, :]),
                    writes=(recip_sb[0:1, 0:1],))
                b.V(lambda v: v.reciprocal(recip_sb[0:1, 512:1024],
                                           pvB[64:65, :]),
                    writes=(recip_sb[0:1, 0:1],))
                rec_vc = b.incl("vc")
                b.wt(nc.tensor, "vc", rec_vc)
                b.mm(pvA[0:64, :], onesf[0:1, 0:64], recip_sb[0:1, 0:512],
                     start=True, stop=True)
                b.mm(pvB[0:64, :], onesf[0:1, 0:64], recip_sb[0:1, 512:1024],
                     start=True, stop=True)
                rtm = b.incl("tm")
                b.wt(nc.scalar, "tm", rtm)
                b.SC(lambda s: s.copy(r_sb[0:64, 0:512], pvA[0:64, :]),
                     writes=(r_sb[0:1, 0:1],))
                b.SC(lambda s: s.copy(r_sb[0:64, 512:1024], pvB[0:64, :]),
                     writes=(r_sb[0:1, 0:1],))
                prev_rel_sc = b.incl("sc")

                b.wt(nc.vector, "sag", last_ag)
                for (lh, hp, r0) in ((lhA, 0, 0), (lhB, 1, 64)):
                    b.wt(nc.tensor, "vc", cm_free_vc)
                    for c2 in range(4):
                        j2 = 4 * g + c2
                        b.mm(
                            cmr[0:64, 128 * c2:128 * c2 + 128],
                            vext[:, 520 * j2 + 65 * lh:520 * j2 + 65 * lh + 64],
                            skipI[:, 128 * lh:128 * lh + 128],
                            start=(c2 == 0), stop=(c2 == 3))
                    if g > 0:
                        b.wt(nc.tensor, f"pt{g % 2}", trild[g])
                        b.mm(
                            cmr[0:64, 512:1024],
                            carryT[0:1, 512 * g + 64 * lh:
                                   512 * g + 64 * lh + 64],
                            invp[0:1, 512 * g:512 * g + 512],
                            start=True, stop=False)
                    for c2 in range(4):
                        j2 = 4 * g + c2
                        b.mm(
                            cmr[0:64, 512:1024],
                            vext[:, 520 * j2 + 65 * lh:520 * j2 + 65 * lh + 64],
                            triinv_s[:, 2048 * gpar + 512 * c2:
                                     2048 * gpar + 512 * c2 + 512],
                            start=(c2 == 0 and g == 0), stop=(c2 == 3))
                    cm_tm = b.incl("tm")
                    b.wt(nc.vector, "sc", prev_rel_sc)
                    b.V(lambda v, hp_=hp: v.tensor_mul(
                        ta_sb[0:64, :], pvs_sb[0:64, 512 * hp_:512 * hp_ + 512],
                        r_sb[0:64, 512 * hp_:512 * hp_ + 512]),
                        reads=(pvs_sb[0:1, 0:1], r_sb[0:1, 0:1]),
                        writes=(ta_sb[0:1, 0:1],))
                    b.wt(nc.vector, "tm", cm_tm)
                    b.V(lambda v, l=lh: v.scalar_tensor_tensor(
                        tb_sb[0:64, :], ta_sb[0:64, :], grep[0:64, l:l + 1],
                        cmr[0:64, 0:512], ALU.mult, ALU.add),
                        reads=(ta_sb[0:1, 0:1],), writes=(tb_sb[0:1, 0:1],))
                    b.V(lambda v, l=lh, r=r0: v.scalar_tensor_tensor(
                        outpair[r:r + 64, :], cmr[0:64, 512:1024],
                        cnrep[0:64, l:l + 1], tb_sb[0:64, :],
                        ALU.mult, ALU.add),
                        reads=(tb_sb[0:1, 0:1],), writes=(outpair[0:1, 0:1],))
                    cm_free_vc = b.incl("vc")
                prev_ep_tm = b.n("tm")
                b.wt(nc.sync, "vc", cm_free_vc)
                last_ag = b.ldy(
                    ag_in.ap()[128 * pc:128 * pc + 128, 512 * g:512 * g + 512],
                    outpair[:, :], sem="sag")

        tp = b.tap("p1_ag_in", [512, S], BF16)
        if tp is not None:
            b.wt(nc.gpsimd, "sag")
            b.ldw(tp, ag_in.ap())

    p01.close()
    b.finish()
    if PHASE_LIMIT < 2:
        st.close()
        return b

    nc.leave_named_scope("ph1", _sid["ph1"], False)
    _sid["ph3"] = nc.enter_named_scope("ph3", False)[0]
    b.wt(nc.gpsimd, "sag")
    cci = nc.gpsimd.collective_compute(
        "AllGather", ALU.bypass,
        replica_groups=[[0, 1], [2, 3], [4, 5], [6, 7]],
        ins=[ag_in.ap().opt()], outs=[ag_out.ap().opt()])
    b.inc(cci, "cc", 1)

    w4s = contextlib.ExitStack()
    w1b = w4s.enter_context(nc.sbuf_tensor("w1b", [128, 8 * W0], BF16))
    w2b = w4s.enter_context(nc.sbuf_tensor("w2b", [128, 32 * 1024], BF16))
    b2r = w4s.enter_context(nc.sbuf_tensor("b2r", [1, D], BF16))
    b.ldw(b2r[:, :], b2_row, sem="p4w")
    for half in range(2):
        b.ldw(w1b[:, 4 * W0 * half:4 * W0 * half + 4 * W0]
              .rearrange("p (c n) -> p c n", c=4),
              w1.rearrange("(c p) n -> p c n", p=128)[:, 4 * half:4 * half + 4, :],
              sem="p4w")
        b.ldw(w2b[:, 16384 * half:16384 * half + 16384]
              .rearrange("p (c n) -> p c n", c=16),
              w2.rearrange("(c p) n -> p c n", p=128)[:, 16 * half:16 * half + 16, :],
              sem="p4w")
    dw4 = b.n("p4w")

    p3 = contextlib.ExitStack()

    def sb3(name, shape, dtype):
        return p3.enter_context(nc.sbuf_tensor(name, list(shape), dtype))

    cpw = sb3("cpw", [128, 8 * 1024], BF16)
    afm = sb3("afm", [128, 2 * 4096], BF16)
    xg3 = sb3("xg3", [128, 4 * 1024], F32)
    ysb3 = sb3("ysb3", [128, 1024], F32)
    ln3 = sb3("ln3", [128, 1024], F32)
    b.ldw(cpw[:, :].rearrange("p (c n) -> p c n", c=8),
          cprojw.rearrange("(c p) n -> p c n", p=128), sem="pcp")
    dw3 = b.n("pcp")
    tokv = nc.sync.value_load(tokb[0:1, 0:1])
    b.wt(nc.sync, "cc", b.n("cc"))
    with nc.psum_tensor("yp3", [128, 1024], F32) as yp3:
        grp_tm = {}
        for go in range(NGO):
            gpar = go % 2
            if go >= 2:
                b.wt(nc.sync, "tm", grp_tm[go - 2])
            src_ap = bass.AP(ag_out.ap().tensor, tokv + 512 * go,
                             [[S, 128], [512 * S, 2], [128 * S, 4], [1, 512]])
            dsta = afm[:, 4096 * gpar:4096 * gpar + 4096]
            dst_ap = bass.AP(dsta.tensor, dsta.offset,
                             [[dsta.ap[0][0], 128], [2048, 2], [512, 4],
                              [1, 512]])
            b.ldy(dst_ap, src_ap, sem=f"pa{gpar}")
            if go >= 1:
                b.wt(nc.sync, "vc")
                b.wt(nc.sync, "sc")
            for tt in range(4):
                b.ldy(xg3[:, 1024 * tt:1024 * tt + 1024],
                      x_own[512 * go + 128 * tt:512 * go + 128 * tt + 128, :],
                      sem="p3x")
            ldn = b.n("p3x")
            for tt in range(4):
                b.wt(nc.tensor, f"pa{gpar}")
                if go == 0 and tt == 0:
                    b.wt(nc.tensor, "pcp", dw3)
                b.wt(nc.tensor, "vc")
                for k in range(8):
                    for n2 in range(2):
                        b.mm(
                            yp3[:, 512 * n2:512 * n2 + 512],
                            afm[:, 4096 * gpar + 512 * k + 128 * tt:
                                4096 * gpar + 512 * k + 128 * tt + 128],
                            cpw[:, 1024 * k + 512 * n2:
                                1024 * k + 512 * n2 + 512],
                            start=(k == 0), stop=(k == 7))
                grp_tm[go] = tmn = b.incl("tm")
                b.wt(nc.vector, "tm", tmn)
                b.wt(nc.vector, "p3x", ldn)
                b.V(lambda v: v.tensor_copy(ysb3[:, :], yp3[:, :]),
                    writes=(ysb3[:, 0:1],))
                vn = b.incl("vc")
                b.wt(nc.scalar, "vc", vn)
                _ln_tile(b, ln3[:, :], ysb3[:, :], D, lnsc)
                xt = xg3[:, 1024 * tt:1024 * tt + 1024]
                b.V(lambda v, x_=xt: v.tensor_add(ysb3[:, :], ln3[:, :], x_),
                    reads=(ln3[:, 0:1], ysb3[:, 0:1]), writes=(ysb3[:, 0:1],))
                vn = b.incl("vc")
                b.wt(nc.sync, "vc", vn)
                b.ldy(stream_a.ap()[512 * go + 128 * tt:
                                    512 * go + 128 * tt + 128, :], ysb3[:, :])
    p3.close()
    b.finish()
    tp = b.tap("p3_stream", [TOK, D])
    if tp is not None:
        b.ldw(tp, stream_a.ap())
        b.wt(nc.gpsimd, "dw")
    if PHASE_LIMIT < 3:
        st.close()
        return b

    nc.leave_named_scope("ph3", _sid["ph3"], False)
    _sid["ph4"] = nc.enter_named_scope("ph4", False)[0]
    p4 = contextlib.ExitStack()

    def sb4(name, shape, dtype):
        return p4.enter_context(nc.sbuf_tensor(name, list(shape), dtype))

    lnxf4 = sb4("lnxf4", [128, 8 * 512], BF16)
    hfm = sb4("hfm", [128, 32 * 512], BF16)
    xg4 = sb4("xg4", [128, 4 * 1024], F32)
    ln4 = sb4("ln4", [128, 1024], F32)
    ysb4 = sb4("ysb4", [128, 1024], F32)
    with (nc.psum_tensor("trp4", [128, 1024], F32) as trp4,
          nc.psum_tensor("hp4a", [128, 512], F32) as hp4a,
          nc.psum_tensor("hp4b", [128, 512], F32) as hp4b,
          nc.psum_tensor("yp4", [128, 1024], F32) as yp4):
        for go in range(NGO):
            if go >= 1:
                b.wt(nc.sync, "vc")
                b.wt(nc.sync, "sc")
            for tt in range(4):
                b.ldy(xg4[:, 1024 * tt:1024 * tt + 1024],
                      stream_a.ap()[512 * go + 128 * tt:
                                    512 * go + 128 * tt + 128, :], sem="p4x")
            ldn = b.n("p4x")
            b.wt(nc.vector, "p4x", ldn)
            b.wt(nc.scalar, "p4x", ldn)
            for tt in range(4):
                xap = xg4[:, 1024 * tt:1024 * tt + 1024]
                _ln_tile(b, ln4[:, :], xap, D, lnsc)
                lnn = b.incl("vc")
                b.wt(nc.tensor, "vc", lnn)
                for c in range(8):
                    b.tr(trp4[:, 128 * c:128 * c + 128],
                                        ln4[:, 128 * c:128 * c + 128],
                                        idf[:, :])
                tmn = b.incl("tm")
                b.wt(nc.vector, "tm", tmn)
                dst = lnxf4[:, 128 * tt:]
                dstv = bass.AP(dst.tensor, dst.offset,
                               [[dst.ap[0][0], 128], [512, 8], [1, 128]])
                srcv = trp4[:, :].rearrange("p (c n) -> p c n", c=8)
                b.V(lambda v, dv=dstv, sv=srcv: v.tensor_copy(dv, sv),
                    writes=(lnxf4[:, 0:1],))
                b.incl("vc")
            lnxv = b.n("vc")
            b.wt(nc.tensor, "vc", lnxv)
            if go == 0:
                b.wt(nc.tensor, "p4w", dw4)
            hv = {}
            for hc in range(32):
                ps = hp4a if hc % 2 == 0 else hp4b
                if hc >= 2:
                    b.wt(nc.tensor, "sc", hv[hc - 2])
                for k in range(8):
                    b.mm(
                        ps[:, :],
                        w1b[:, W0 * k + 128 * hc:W0 * k + 128 * hc + 128],
                        lnxf4[:, 512 * k:512 * k + 512],
                        start=(k == 0), stop=(k == 7))
                tmn = b.incl("tm")
                b.wt(nc.scalar, "tm", tmn)
                b.SC(lambda s, p=ps, h=hc: s.activation(
                    hfm[:, 512 * h:512 * h + 512], p[:, :], AF.Relu,
                    bias=b1p[:, h:h + 1]),
                    writes=(hfm[:, 0:1],))
                hv[hc] = b.incl("sc")
            hdone = hv[31]
            for tt in range(4):
                b.wt(nc.tensor, "sc", hdone)
                b.wt(nc.tensor, "vc")
                for hc in range(32):
                    for n2 in range(2):
                        b.mm(
                            yp4[:, 512 * n2:512 * n2 + 512],
                            hfm[:, 512 * hc + 128 * tt:
                                512 * hc + 128 * tt + 128],
                            w2b[:, 1024 * hc + 512 * n2:
                                1024 * hc + 512 * n2 + 512],
                            start=(hc == 0), stop=False)
                for n2 in range(2):
                    b.mm(
                        yp4[:, 512 * n2:512 * n2 + 512], onesb[0:1, :],
                        b2r[0:1, 512 * n2:512 * n2 + 512],
                        start=False, stop=True)
                tmn = b.incl("tm")
                b.wt(nc.vector, "tm", tmn)
                b.wt(nc.scalar, "tm", tmn)
                b.V(lambda v: v.tensor_copy(ysb4[:, :], yp4[:, :]),
                    writes=(ysb4[:, 0:1],))
                vn = b.incl("vc")
                b.wt(nc.scalar, "vc", vn)
                _ln_tile(b, ln4[:, :], ysb4[:, :], D, lnsc)
                xt = xg4[:, 1024 * tt:1024 * tt + 1024]
                b.V(lambda v, x_=xt: v.tensor_add(ysb4[:, :], ln4[:, :], x_),
                    reads=(ln4[:, 0:1], ysb4[:, 0:1]), writes=(ysb4[:, 0:1],))
                vn = b.incl("vc")
                b.wt(nc.sync, "vc", vn)
                b.ldy(stream_b.ap()[512 * go + 128 * tt:
                                    512 * go + 128 * tt + 128, :], ysb4[:, :])
    p4.close()
    w4s.close()
    b.finish()
    tp = b.tap("p4_stream", [TOK, D])
    if tp is not None:
        b.ldw(tp, stream_b.ap())
        b.wt(nc.gpsimd, "dw")
    if PHASE_LIMIT < 4:
        st.close()
        return b

    nc.leave_named_scope("ph4", _sid["ph4"], False)
    _sid["ph5"] = nc.enter_named_scope("ph5", False)[0]
    p5 = contextlib.ExitStack()

    def sb5(name, shape, dtype):
        return p5.enter_context(nc.sbuf_tensor(name, list(shape), dtype))

    lnx_all = sb5("lnx_all", [128, NTO * 1024], BF16)
    wkb = sb5("wkb", [128, 8 * 1024], BF16)
    wvb = sb5("wvb", [128, 8 * 1024], BF16)
    wqb = sb5("wqb", [128, 8 * 1024], BF16)
    wob = sb5("wob", [128, 8 * 1024], BF16)
    kfm1 = sb5("kfm1", [128, 8 * 512], BF16)
    kft = sb5("kft", [128, 4 * 1024], BF16)
    vx1 = sb5("vx1", [128, 4 * 1040], BF16)
    qfsb = sb5("qfsb", [64, 16 * 512], BF16)
    kvf = sb5("kvf", [128, 520], F32)
    kvb = sb5("kvb", [64, 16 * 65], BF16)
    atm = sb5("atm", [128, 4 * 1024], F32)
    a1fm = sb5("a1fm", [128, 8 * 512], BF16)
    xg5 = sb5("xg5", [128, 4 * 1024], F32)
    ln5 = sb5("ln5", [128, 1024], F32)
    ysb5 = sb5("ysb5", [128, 1024], F32)
    bor = sb5("bor", [1, D], BF16)
    nrec = sb5("nrec", [128, 16], F32)
    nmax = sb5("nmax", [128, 16], F32)
    kvacc = [sb5("kvacc0", [128, 520], F32), sb5("kvacc1", [128, 520], F32)]
    b.ldw(bor[:, :], bo_row, sem="p5w")
    b.ldw(wkb[:, :].rearrange("p (c n) -> p c n", c=8),
          wk1.rearrange("(c p) n -> p c n", p=128), sem="p5w")
    b.ldw(wvb[:, :].rearrange("p (c n) -> p c n", c=8),
          wv1.rearrange("(c p) n -> p c n", p=128), sem="p5w")
    b.ldw(wqb[:, :].rearrange("p (c n) -> p c n", c=8),
          wq1.rearrange("(c p) n -> p c n", p=128), sem="p5w")
    b.ldw(wob[:, :].rearrange("p (c n) -> p c n", c=8),
          wo1.rearrange("(c p) n -> p c n", p=128), sem="p5w")
    dw5 = b.n("p5w")
    vx1v = vx1[:, :].rearrange("p (t h o) -> p t h o", t=4, h=H1)
    b.V(lambda v: v.memset(vx1v[:, :, :, 64:65], 1.0), writes=(vx1[:, 0:1],))
    memv = b.incl("vc")
    with (nc.psum_tensor("trp5", [128, 1024], F32) as trp5,
          nc.psum_tensor("pp5a", [128, 512], F32) as pp5a,
          nc.psum_tensor("pp5b", [128, 512], F32) as pp5b,
          nc.psum_tensor("kv0", [128, 260], F32) as kv0,
          nc.psum_tensor("kv1", [128, 260], F32) as kv1,
          nc.psum_tensor("kv2", [128, 260], F32) as kv2,
          nc.psum_tensor("kv3", [128, 260], F32) as kv3):
        kvps = [kv0, kv1, kv2, kv3]
        for go in range(NGO):
            if go >= 1:
                b.wt(nc.sync, "vc")
                b.wt(nc.sync, "sc")
            for tt in range(4):
                b.ldy(xg5[:, 1024 * tt:1024 * tt + 1024],
                      stream_b.ap()[512 * go + 128 * tt:
                                    512 * go + 128 * tt + 128, :], sem="p5x")
            ldn = b.n("p5x")
            b.wt(nc.vector, "p5x", ldn)
            b.wt(nc.scalar, "p5x", ldn)
            if go == 0:
                b.wt(nc.tensor, "vc", memv)
            for tt in range(4):
                xap = xg5[:, 1024 * tt:1024 * tt + 1024]
                _ln_tile(b, ln5[:, :], xap, D, lnsc)
                lnn = b.incl("vc")
                b.wt(nc.tensor, "vc", lnn)
                for c in range(8):
                    b.tr(trp5[:, 128 * c:128 * c + 128],
                                        ln5[:, 128 * c:128 * c + 128],
                                        idf[:, :])
                tmn = b.incl("tm")
                b.wt(nc.vector, "tm", tmn)
                dst = lnx_all[:, 4096 * go + 128 * tt:]
                dstv = bass.AP(dst.tensor, dst.offset,
                               [[dst.ap[0][0], 128], [512, 8], [1, 128]])
                srcv = trp5[:, :].rearrange("p (c n) -> p c n", c=8)
                b.V(lambda v, dv=dstv, sv=srcv: v.tensor_copy(dv, sv),
                    writes=(lnx_all[:, 0:1],))
                b.incl("vc")
            lnxv = b.n("vc")
            base5 = 4096 * go
            b.wt(nc.tensor, "vc", lnxv)
            if go == 0:
                b.wt(nc.tensor, "p5w", dw5)
            for c in range(8):
                ps = pp5a if c % 2 == 0 else pp5b
                b.wt(nc.tensor, "vc")
                for k in range(8):
                    b.mm(
                        ps[:, :],
                        wkb[:, 1024 * k + 128 * c:1024 * k + 128 * c + 128],
                        lnx_all[:, base5 + 512 * k:base5 + 512 * k + 512],
                        start=(k == 0), stop=(k == 7))
                tmn = b.incl("tm")
                b.wt(nc.vector, "tm", tmn)
                b.V(lambda v, p=ps, c_=c: v.tensor_copy(
                    kfm1[:, 512 * c_:512 * c_ + 512], p[:, :]),
                    writes=(kfm1[:, 0:1],))
                b.incl("vc")
            for tt in range(4):
                b.wt(nc.tensor, "vc")
                for k in range(8):
                    for n2 in range(2):
                        ps = pp5a if n2 == 0 else pp5b
                        b.mm(
                            ps[:, :],
                            lnx_all[:, base5 + 512 * k + 128 * tt:
                                    base5 + 512 * k + 128 * tt + 128],
                            wvb[:, 1024 * k + 512 * n2:
                                1024 * k + 512 * n2 + 512],
                            start=(k == 0), stop=(k == 7))
                tmn = b.incl("tm")
                b.wt(nc.vector, "tm", tmn)
                for n2 in range(2):
                    ps = pp5a if n2 == 0 else pp5b
                    dst = vx1[:, 1040 * tt + 65 * 8 * n2:]
                    dstv = bass.AP(dst.tensor, dst.offset,
                                   [[dst.ap[0][0], 128], [65, 8], [1, 64]])
                    srcv = ps[:, :].rearrange("p (h o) -> p h o", h=8)
                    b.V(lambda v, dv=dstv, sv=srcv: v.tensor_copy(dv, sv),
                        writes=(vx1[:, 0:1],))
                b.incl("vc")
            vdone = b.n("vc")
            for tt in range(4):
                b.wt(nc.tensor, "vc", vdone)
                b.wt(nc.tensor, "sc")
                for h in range(H1):
                    ps = pp5a if h < 8 else pp5b
                    hc, r0 = h // 2, 64 * (h % 2)
                    b.mm(
                        ps[:, 64 * (h % 8):64 * (h % 8) + 64],
                        kfm1[:, 512 * hc + 128 * tt:
                             512 * hc + 128 * tt + 128][r0:r0 + 64, :],
                        wfr[r0:r0 + 64, :], start=True, stop=False,
                        tile_position=(r0, 0))
                    b.mm(
                        ps[:, 64 * (h % 8):64 * (h % 8) + 64],
                        onesb[0:1, :], bfr[0:1, :], start=False, stop=True,
                        tile_position=(0, 0))
                tmn = b.incl("tm")
                b.wt(nc.scalar, "tm", tmn)
                for (ps, off) in ((pp5a, 0), (pp5b, 512)):
                    b.SC(lambda s, p=ps, o=off, t_=tt: s.activation(
                        kft[:, 1024 * t_ + o:1024 * t_ + o + 512], p[:, :],
                        AF.Relu),
                        writes=(kft[:, 0:1],))
                b.incl("sc")
            kfdone = b.n("sc")
            b.wt(nc.tensor, "sc", kfdone)
            b.wt(nc.tensor, "vc")
            for pr in range(8):
                tn = kvps[pr // 4]
                coff = 65 * (pr % 4)
                for hh2 in range(2):
                    h = 2 * pr + hh2
                    r0 = 64 * hh2
                    for ti in range(4):
                        b.mm(
                            tn[r0:r0 + 64, coff:coff + 65],
                            kft[:, 1024 * ti + 64 * h:1024 * ti + 64 * h + 64],
                            vx1[:, 1040 * ti + 65 * h:1040 * ti + 65 * h + 65],
                            start=(ti == 0), stop=(ti == 3),
                            tile_position=(0, r0))
            kvtm = b.incl("tm")
            b.wt(nc.vector, "tm", kvtm)
            dstk = kvacc[go % 2]
            srck = kvacc[(go + 1) % 2]
            if go == 0:
                for i in range(2):
                    b.V(lambda v, i_=i, d=dstk: v.tensor_copy(
                        d[:, 260 * i_:260 * i_ + 260], kvps[i_][:, 0:260]),
                        writes=(dstk[:, 0:1],))
            else:
                for i in range(2):
                    b.V(lambda v, i_=i, d=dstk, s_=srck: v.tensor_add(
                        d[:, 260 * i_:260 * i_ + 260],
                        s_[:, 260 * i_:260 * i_ + 260], kvps[i_][:, 0:260]),
                        reads=(srck[:, 0:1],), writes=(dstk[:, 0:1],))
            b.incl("vc")
        vn = b.n("vc")
        b.wt(nc.sync, "vc", vn)
        kvv = kvacc[1][:, :].rearrange("p (pr o) -> p pr o", pr=8)
        arv = ar_in.ap().rearrange("f (pr two) o -> two f pr o", two=2)
        b.ldy(arv[0], kvv[0:64, :, :], sem="p5r")
        b.ldy(arv[1], kvv[64:128, :, :], sem="p5r")
        b.wt(nc.gpsimd, "p5r")
        cci = nc.gpsimd.collective_compute(
            "AllReduce", ALU.add,
            replica_groups=[[0, 1], [2, 3], [4, 5], [6, 7]],
            ins=[ar_in.ap().opt()], outs=[ar_out.ap().opt()])
        b.inc(cci, "cc", 1)
        b.wt(nc.gpsimd, "cc", b.n("cc"))
        b.ldw(kvb[:, :].rearrange("p (h o) -> p h o", h=H1), ar_out.ap(),
              sem="p5k")
        kvldn = b.n("p5k")

        for go in range(NGO):
            b.wt(nc.sync, "vc")
            b.wt(nc.sync, "sc")
            for tt in range(4):
                b.ldy(xg5[:, 1024 * tt:1024 * tt + 1024],
                      stream_b.ap()[512 * go + 128 * tt:
                                    512 * go + 128 * tt + 128, :], sem="p5x")
            ldn = b.n("p5x")
            base5 = 4096 * go
            for c in range(8):
                ps = pp5a if c % 2 == 0 else pp5b
                b.wt(nc.tensor, "vc")
                if go == 0 and c == 0:
                    b.wt(nc.tensor, "p5k", kvldn)
                    b.wt(nc.tensor, "sc")
                for k in range(8):
                    b.mm(
                        ps[:, :],
                        wqb[:, 1024 * k + 128 * c:1024 * k + 128 * c + 128],
                        lnx_all[:, base5 + 512 * k:base5 + 512 * k + 512],
                        start=(k == 0), stop=(k == 7))
                tmn = b.incl("tm")
                b.wt(nc.vector, "tm", tmn)
                b.V(lambda v, p=ps, c_=c: v.tensor_copy(
                    kfm1[:, 512 * c_:512 * c_ + 512], p[:, :]),
                    writes=(kfm1[:, 0:1],))
                b.incl("vc")
            qv = b.n("vc")
            b.wt(nc.tensor, "vc", qv)
            for h in range(H1):
                hc, r0 = h // 2, 64 * (h % 2)
                ps = pp5a if h % 2 == 0 else pp5b
                b.wt(nc.tensor, "sc")
                b.mm(
                    ps[0:64, :], wfr[r0:r0 + 64, :],
                    kfm1[:, 512 * hc:512 * hc + 512][r0:r0 + 64, :],
                    start=True, stop=True, tile_position=(r0, 0))
                tmn = b.incl("tm")
                b.wt(nc.scalar, "tm", tmn)
                b.SC(lambda s, p=ps, h_=h: s.activation(
                    qfsb[0:64, 512 * h_:512 * h_ + 512], p[0:64, :], AF.Relu,
                    bias=bfp[0:64, 0:1]),
                    writes=(qfsb[0:1, 0:1],))
                b.incl("sc")
            qfdone = b.n("sc")
            b.wt(nc.tensor, "sc", qfdone)
            for tt in range(4):
                b.wt(nc.tensor, "vc")
                for h in range(H1):
                    b.mm(
                        kvps[h // 4][:, 65 * (h % 4):65 * (h % 4) + 65],
                        qfsb[0:64, 512 * h + 128 * tt:512 * h + 128 * tt + 128],
                        kvb[0:64, 65 * h:65 * h + 65],
                        start=True, stop=True)
                tmn = b.incl("tm")
                b.wt(nc.vector, "tm", tmn)
                for i in range(4):
                    pv_ = kvps[i]
                    src = bass.AP(pv_, 64, [[260, 128], [65, 4]])
                    b.V(lambda v, s_=src, i_=i: v.tensor_scalar(
                        nmax[:, 4 * i_:4 * i_ + 4], s_, 1e-6, None, ALU.max),
                        writes=(nmax[:, 0:1],))
                b.V(lambda v: v.reciprocal(nrec[:, :], nmax[:, :]),
                    reads=(nmax[:, 0:1],), writes=(nrec[:, 0:1],))
                for h in range(H1):
                    pv_ = kvps[h // 4]
                    b.V(lambda v, p=pv_, h_=h, t_=tt: v.tensor_scalar(
                        atm[:, 1024 * t_ + 64 * h_:1024 * t_ + 64 * h_ + 64],
                        p[:, 65 * (h_ % 4):65 * (h_ % 4) + 64],
                        nrec[:, h_:h_ + 1], None, ALU.mult),
                        reads=(nrec[:, 0:1],), writes=(atm[:, 0:1],))
                b.incl("vc")
            for tt in range(4):
                b.wt(nc.tensor, "vc")
                for c in range(8):
                    b.tr(trp5[:, 128 * c:128 * c + 128],
                         atm[:, 1024 * tt + 128 * c:1024 * tt + 128 * c + 128],
                         idf[:, :])
                tmn = b.incl("tm")
                b.wt(nc.vector, "tm", tmn)
                dst = a1fm[:, 128 * tt:]
                dstv = bass.AP(dst.tensor, dst.offset,
                               [[dst.ap[0][0], 128], [512, 8], [1, 128]])
                srcv = trp5[:, :].rearrange("p (c n) -> p c n", c=8)
                b.V(lambda v, dv=dstv, sv=srcv: v.tensor_copy(dv, sv),
                    writes=(a1fm[:, 0:1],))
                b.incl("vc")
            a1v = b.n("vc")
            b.wt(nc.tensor, "vc", a1v)
            b.wt(nc.vector, "p5x", ldn)
            b.wt(nc.scalar, "p5x", ldn)
            for tt in range(4):
                b.wt(nc.tensor, "vc")
                for k in range(8):
                    for n2 in range(2):
                        b.mm(
                            trp5[:, 512 * n2:512 * n2 + 512],
                            a1fm[:, 512 * k + 128 * tt:
                                 512 * k + 128 * tt + 128],
                            wob[:, 1024 * k + 512 * n2:
                                1024 * k + 512 * n2 + 512],
                            start=(k == 0), stop=False)
                for n2 in range(2):
                    b.mm(
                        trp5[:, 512 * n2:512 * n2 + 512], onesb[0:1, :],
                        bor[0:1, 512 * n2:512 * n2 + 512],
                        start=False, stop=True)
                tmn = b.incl("tm")
                b.wt(nc.vector, "tm", tmn)
                b.wt(nc.scalar, "tm", tmn)
                b.V(lambda v: v.tensor_copy(ysb5[:, :], trp5[:, :]),
                    writes=(ysb5[:, 0:1],))
                vn = b.incl("vc")
                b.wt(nc.scalar, "vc", vn)
                _ln_tile(b, ln5[:, :], ysb5[:, :], D, lnsc)
                xt = xg5[:, 1024 * tt:1024 * tt + 1024]
                b.V(lambda v, x_=xt: v.tensor_add(ysb5[:, :], ln5[:, :], x_),
                    reads=(ln5[:, 0:1], ysb5[:, 0:1]), writes=(ysb5[:, 0:1],))
                vn = b.incl("vc")
                b.wt(nc.sync, "vc", vn)
                b.ldy(stream_c.ap()[512 * go + 128 * tt:
                                    512 * go + 128 * tt + 128, :], ysb5[:, :])
    p5.close()
    b.finish()
    tp = b.tap("p5_stream", [TOK, D])
    if tp is not None:
        b.ldw(tp, stream_c.ap())
        b.wt(nc.gpsimd, "dw")
    if PHASE_LIMIT < 5:
        st.close()
        return b

    nc.leave_named_scope("ph5", _sid["ph5"], False)
    _sid["ph6"] = nc.enter_named_scope("ph6", False)[0]
    p6 = contextlib.ExitStack()

    def sb6(name, shape, dtype):
        return p6.enter_context(nc.sbuf_tensor(name, list(shape), dtype))

    gates = sb6("gates", [128, NTO * NEXP], F32)
    xg6 = sb6("xg6", [128, 4 * 1024], F32)
    ln6 = sb6("ln6", [128, 1024], F32)
    ysb6 = sb6("ysb6", [128, 1024], F32)

    p6a = contextlib.ExitStack()

    def sb6a(name, shape, dtype):
        return p6a.enter_context(nc.sbuf_tensor(name, list(shape), dtype))

    lnf32 = sb6a("lnf32", [128, 8 * 512], F32)
    lnxb6 = sb6a("lnxb6", [128, 8 * 512], BF16)
    wg1f = sb6a("wg1f", [128, 8 * G1], F32)
    wg2f = sb6a("wg2f", [128, 2 * NEXP], F32)
    gsb = sb6a("gsb", [128, 1024], F32)
    gm1 = sb6a("gm1", [128, 1], F32)
    gm2 = sb6a("gm2", [128, 1], F32)
    gtmp = sb6a("gtmp", [128, NEXP], F32)
    gtmp2 = sb6a("gtmp2", [128, NEXP], F32)
    gsum = sb6a("gsum", [128, 1], F32)
    gprob = sb6a("gprob", [128, NEXP], F32)
    b.ldy(wg1f[:, :].rearrange("p (c n) -> p c n", c=8),
          wg1.rearrange("(c p) n -> p c n", p=128), sem="p6g")
    b.ldy(wg2f[:, :].rearrange("p (c n) -> p c n", c=2),
          wg2.rearrange("(c p) n -> p c n", p=128), sem="p6g")
    with (nc.psum_tensor("trp6", [128, 1024], F32) as trp6,
          nc.psum_tensor("gp6", [128, 1024], F32) as gp6):
        for go in range(NGO):
            b.wt(nc.sync, "vc")
            b.wt(nc.sync, "sc")
            for tt in range(4):
                b.ldy(xg6[:, 1024 * tt:1024 * tt + 1024],
                      stream_c.ap()[512 * go + 128 * tt:
                                    512 * go + 128 * tt + 128, :], sem="p6x")
            ldn = b.n("p6x")
            b.wt(nc.vector, "p6x", ldn)
            b.wt(nc.scalar, "p6x", ldn)
            for tt in range(4):
                xap = xg6[:, 1024 * tt:1024 * tt + 1024]
                _ln_tile(b, ln6[:, :], xap, D, lnsc)
                lnn = b.incl("vc")
                b.wt(nc.tensor, "vc", lnn)
                if go > 0 or tt > 0:
                    b.wt(nc.tensor, "sc")
                for c in range(8):
                    b.tr(trp6[:, 128 * c:128 * c + 128],
                         ln6[:, 128 * c:128 * c + 128], idf[:, :])
                tmn = b.incl("tm")
                b.wt(nc.vector, "tm", tmn)
                b.wt(nc.scalar, "tm", tmn)
                dst = lnxb6[:, 128 * tt:]
                dstv = bass.AP(dst.tensor, dst.offset,
                               [[dst.ap[0][0], 128], [512, 8], [1, 128]])
                srcv = trp6[:, :].rearrange("p (c n) -> p c n", c=8)
                b.V(lambda v, dv=dstv, sv=srcv: v.tensor_copy(dv, sv),
                    writes=(lnxb6[:, 0:1],))
                evv = b.incl("vc")
                b.wt(nc.scalar, "vc", evv)
                dstf = lnf32[:, 128 * tt:]
                dstfv = bass.AP(dstf.tensor, dstf.offset,
                                [[dstf.ap[0][0], 128], [512, 8], [1, 128]])
                b.SC(lambda s, dv=dstfv, sv=srcv: s.copy(dv, sv),
                     writes=(lnf32[:, 0:1],))
                b.incl("sc")
            lnxv, lnxs = b.n("vc"), b.n("sc")
            b.wt(nc.sync, "vc", lnxv)
            b.ldy(lnx6.ap()[:, 4096 * go:4096 * go + 4096], lnxb6[:, :],
                  sem="p6s")
            b.wt(nc.tensor, "sc", lnxs)
            b.wt(nc.tensor, "p6g", b.n("p6g"))
            b.wt(nc.tensor, "vc")
            for m in range(2):
                for k in range(8):
                    b.mm(
                        gp6[:, 512 * m:512 * m + 512],
                        wg1f[:, G1 * k + 128 * m:G1 * k + 128 * m + 128],
                        lnf32[:, 512 * k:512 * k + 512],
                        start=(k == 0), stop=(k == 7))
            tmn = b.incl("tm")
            b.wt(nc.scalar, "tm", tmn)
            for m in range(2):
                b.SC(lambda s, m_=m: s.activation(
                    gsb[:, 512 * m_:512 * m_ + 512],
                    gp6[:, 512 * m_:512 * m_ + 512], AF.Gelu,
                    bias=bg1p[:, m_:m_ + 1]),
                    writes=(gsb[:, 0:1],))
            gsn = b.incl("sc")
            b.wt(nc.tensor, "sc", gsn)
            for tt in range(4):
                b.wt(nc.tensor, "vc")
                for m in range(2):
                    b.mm(
                        gp6[:, 0:NEXP],
                        gsb[:, 512 * m + 128 * tt:512 * m + 128 * tt + 128],
                        wg2f[:, NEXP * m:NEXP * m + NEXP],
                        start=(m == 0), stop=(m == 1))
                tmn = b.incl("tm")
                b.wt(nc.vector, "tm", tmn)
                b.V(lambda v: v.reduce_max(gm1[:, 0:1], gp6[:, 0:NEXP], AX,
                                           negate=True),
                    writes=(gm1[:, 0:1],))
                vn = b.incl("vc")
                b.wt(nc.scalar, "vc", vn)
                b.SC(lambda s: s.activation(gprob[:, :], gp6[:, 0:NEXP],
                                            AF.Exp, bias=gm1[:, 0:1],
                                            accum_out=gsum[:, 0:1]),
                     writes=(gprob[:, 0:1], gsum[:, 0:1]))
                scn = b.incl("sc")
                b.wt(nc.vector, "sc", scn)
                b.V(lambda v: v.reciprocal(gm2[:, 0:1], gsum[:, 0:1]),
                    writes=(gm2[:, 0:1],))
                b.V(lambda v: v.tensor_scalar(gtmp[:, :], gprob[:, :],
                                              gm2[:, 0:1], None, ALU.mult),
                    reads=(gm2[:, 0:1],), writes=(gtmp[:, 0:1],))
                b.V(lambda v: v.reduce_max(gm1[:, 0:1], gtmp[:, :], AX),
                    reads=(gtmp[:, 0:1],), writes=(gm1[:, 0:1],))
                b.V(lambda v: v.tensor_scalar(gtmp2[:, :], gtmp[:, :],
                                              gm1[:, 0:1], None, ALU.is_lt),
                    reads=(gm1[:, 0:1],), writes=(gtmp2[:, 0:1],))
                b.V(lambda v: v.tensor_mul(gprob[:, :], gtmp2[:, :],
                                           gtmp[:, :]),
                    reads=(gtmp2[:, 0:1],), writes=(gprob[:, 0:1],))
                b.V(lambda v: v.reduce_max(gm2[:, 0:1], gprob[:, :], AX),
                    reads=(gprob[:, 0:1],), writes=(gm2[:, 0:1],))
                b.V(lambda v: v.tensor_scalar(gtmp2[:, :], gtmp[:, :],
                                              gm2[:, 0:1], None, ALU.is_ge),
                    reads=(gm2[:, 0:1],), writes=(gtmp2[:, 0:1],))
                b.V(lambda v, t_=tt, g_=go: v.tensor_mul(
                    gates[:, NEXP * (4 * g_ + t_):NEXP * (4 * g_ + t_) + NEXP],
                    gtmp2[:, :], gtmp[:, :]),
                    reads=(gtmp2[:, 0:1],), writes=(gates[:, 0:1],))
                b.incl("vc")
    gatesv = b.n("vc")
    lnx6_dy = b.n("p6s")
    tp = b.tap("p6_gates", [128, NTO * NEXP])
    if tp is not None:
        b.wt(nc.sync, "vc", gatesv)
        b.ldy(tp, gates[:, :])
    b.wt(nc.sync, "vc", gatesv)
    b.wt(nc.gpsimd, "vc", gatesv)
    p6a.close()

    p6b = contextlib.ExitStack()

    def sb6b(name, shape, dtype):
        return p6b.enter_context(nc.sbuf_tensor(name, list(shape), dtype))

    winb = [sb6b(f"winb{i}", [128, 8 * EH], BF16) for i in range(2)]
    woutb = [sb6b(f"woutb{i}", [128, 16 * 1024], BF16) for i in range(2)]
    hfm6 = sb6b("hfm6", [128, 16 * 512], BF16)
    lnxg = sb6b("lnxg", [128, 2 * 8 * 512], BF16)
    otmp = sb6b("otmp", [128, 4 * 512], F32)

    def load_w(e):
        sl = e % 2
        b.ldw(winb[sl][:, :].rearrange("p (c n) -> p c n", c=8),
              w_in[e].rearrange("(c p) n -> p c n", p=128), sem=f"p6w{sl}")
        for half in range(2):
            b.ldw(woutb[sl][:, 8192 * half:8192 * half + 8192]
                  .rearrange("p (c n) -> p c n", c=8),
                  w_out[e].rearrange("(c p) n -> p c n",
                                     p=128)[:, 8 * half:8 * half + 8, :],
                  sem=f"p6w{sl}")
        return b.n(f"p6w{sl}")

    def load_lnx(idx):
        go, sl = idx % 4, idx % 2
        b.ldy(lnxg[:, 4096 * sl:4096 * sl + 4096],
              lnx6.ap()[:, 4096 * go:4096 * go + 4096], sem=f"p6l{sl}")
        return b.n(f"p6l{sl}")

    with (nc.psum_tensor("hp6a", [128, 512], F32) as hp6a,
          nc.psum_tensor("hp6b", [128, 512], F32) as hp6b,
          nc.psum_tensor("op6a", [128, 512], F32) as op6a,
          nc.psum_tensor("op6b", [128, 512], F32) as op6b):
        otmp_dw = {}
        wldn, lnld = {}, {}
        h_tm = {}
        exp_end_tm = {}
        if PHASE_LIMIT >= 6:
            wldn[0] = load_w(0)
            b.wt(nc.sync, "p6s", lnx6_dy)
            lnld[0] = load_lnx(0)
            lnld[1] = load_lnx(1)
        for e in range(NEXP if PHASE_LIMIT >= 6 else 0):
            if e + 1 < NEXP:
                if e >= 1:
                    b.wt(nc.gpsimd, "tm", exp_end_tm[e - 1])
                wldn[e + 1] = load_w(e + 1)
            for go in range(NGO):
                idx = 4 * e + go
                b.wt(nc.tensor, f"p6l{idx % 2}", lnld[idx])
                if go == 0:
                    b.wt(nc.tensor, f"p6w{e % 2}", wldn[e])
                hv = {}
                for hc in range(16):
                    ps = hp6a if hc % 2 == 0 else hp6b
                    if hc >= 2:
                        b.wt(nc.tensor, "sc", hv[hc - 2])
                    for k in range(8):
                        b.mm(
                            ps[:, :],
                            winb[e % 2][:, EH * k + 128 * hc:
                                        EH * k + 128 * hc + 128],
                            lnxg[:, 4096 * (idx % 2) + 512 * k:
                                 4096 * (idx % 2) + 512 * k + 512],
                            start=(k == 0), stop=(k == 7))
                    tmn = b.incl("tm")
                    b.wt(nc.scalar, "tm", tmn)
                    b.SC(lambda s, p=ps, h=hc: s.activation(
                        hfm6[:, 512 * h:512 * h + 512], p[:, :], AF.Gelu),
                        writes=(hfm6[:, 0:1],))
                    hv[hc] = b.incl("sc")
                h_tm[idx] = b.n("tm")
                if idx + 2 < 4 * NEXP:
                    b.wt(nc.sync, "tm", h_tm[idx])
                    lnld[idx + 2] = load_lnx(idx + 2)
                hdone = hv[15]
                b.wt(nc.tensor, "sc", hdone)
                if e == 0 and go == 0:
                    b.wt(nc.tensor, "vc", gatesv)
                for tt in range(4):
                    b.wt(nc.tensor, "vc")
                    for hc in range(16):
                        for n2 in range(2):
                            ps = op6a if n2 == 0 else op6b
                            b.mm(
                                ps[:, :],
                                hfm6[:, 512 * hc + 128 * tt:
                                     512 * hc + 128 * tt + 128],
                                woutb[e % 2][:, 1024 * hc + 512 * n2:
                                             1024 * hc + 512 * n2 + 512],
                                start=(hc == 0), stop=(hc == 15))
                    tmn = b.incl("tm")
                    gt = 4 * go + tt
                    b.wt(nc.vector, "tm", tmn)
                    for n2 in range(2):
                        ps = op6a if n2 == 0 else op6b
                        slot = (2 * gt + n2) % 4
                        if slot in otmp_dw:
                            b.wt(nc.vector, f"p6o{slot}", otmp_dw[slot])
                        b.V(lambda v, p=ps, sl=slot, g_=gt, e_=e:
                            v.tensor_scalar(
                                otmp[:, 512 * sl:512 * sl + 512], p[:, :],
                                gates[:, NEXP * g_ + e_:NEXP * g_ + e_ + 1],
                                None, ALU.mult),
                            writes=(otmp[:, 0:1],))
                        vn = b.incl("vc")
                        b.wt(nc.gpsimd, "vc", vn)
                        otmp_dw[slot] = b.ldw(
                            oacc_d.ap()[128 * gt:128 * gt + 128,
                                        512 * n2:512 * n2 + 512],
                            otmp[:, 512 * slot:512 * slot + 512],
                            accum=(e > 0), sem=f"p6o{slot}")
            exp_end_tm[e] = b.n("tm")
    b.wt(nc.vector, "tm")
    b.wt(nc.scalar, "tm")
    p6b.close()

    for sl in range(4):
        b.wt(nc.sync, f"p6o{sl}")
    tp = b.tap("p6_oacc", [TOK, D])
    if tp is not None:
        b.ldw(tp, oacc_d.ap())
    for tt in range(NTO if PHASE_LIMIT >= 7 else 0):
        b.wt(nc.sync, "vc")
        b.wt(nc.sync, "sc")
        par = tt % 2
        b.ldy(xg6[:, 2048 * par:2048 * par + 1024],
              stream_c.ap()[128 * tt:128 * tt + 128, :], sem="p6x")
        b.ldy(xg6[:, 2048 * par + 1024:2048 * par + 2048],
              oacc_d.ap()[128 * tt:128 * tt + 128, :], sem="p6x")
        ldn = b.n("p6x")
        b.wt(nc.vector, "p6x", ldn)
        b.wt(nc.scalar, "p6x", ldn)
        oap = xg6[:, 2048 * par + 1024:2048 * par + 2048]
        _ln_tile(b, ln6[:, :], oap, D, lnsc)
        xt = xg6[:, 2048 * par:2048 * par + 1024]
        b.V(lambda v, x_=xt: v.tensor_add(ysb6[:, :], ln6[:, :], x_),
            reads=(ln6[:, 0:1],), writes=(ysb6[:, 0:1],))
        vn = b.incl("vc")
        b.wt(nc.sync, "vc", vn)
        b.ldy(out[128 * tt:128 * tt + 128, :], ysb6[:, :])
    p6.close()
    b.wt(nc.sync, "dy")
    b.wt(nc.gpsimd, "dw")
    nc.leave_named_scope("ph6", _sid["ph6"], False)
    st.close()
    return b


_NC_CACHE = {}


def _get_nc():
    key = (PHASE_LIMIT, tuple(DEV_TAPS))
    if key not in _NC_CACHE:
        _NC_CACHE[key] = build_nc()
    return _NC_CACHE[key]


def kernel(**inputs):
    f = np.float32
    x = np.ascontiguousarray(inputs["L0_attn_x"], dtype=f)
    qk_w = np.asarray(inputs["L0_attn_qk_w"], dtype=f)
    qk_b = np.asarray(inputs["L0_attn_qk_b"], dtype=f)
    v_w = np.asarray(inputs["L0_attn_v_w"], dtype=f)
    gains = [np.asarray(inputs["L0_attn_resid_gain"], dtype=f).reshape(16),
             np.asarray(inputs["L0_attn_skip_gain"], dtype=f).reshape(16),
             np.asarray(inputs["L0_attn_centre_gain"], dtype=f).reshape(16)]
    in_maps = []
    for core in range(8):
        p, hh = core // 2, core % 2
        hs = slice(512 * hh, 512 * hh + 512)
        m = {
            "x_seq": np.ascontiguousarray(x[p]),
            "x_own": np.ascontiguousarray(x[p, 2048 * hh:2048 * hh + 2048]),
            "tok_base": np.array([[2048 * hh]], dtype=np.uint32),
            "qkw_own": np.ascontiguousarray(
                np.concatenate([qk_w[:, 0:1024][:, hs],
                                qk_w[:, 1024:2048][:, hs]], axis=1)),
            "vw_own": np.ascontiguousarray(v_w[:, hs]),
            "qb_pp": np.ascontiguousarray(qk_b[0:1024][hs].reshape(4, 128).T),
            "kb_pp": np.ascontiguousarray(qk_b[1024:2048][hs].reshape(4, 128).T),
            "cprojw": np.ascontiguousarray(inputs["L0_attn_cproj_w"], dtype=f),
            "gain_r": np.ascontiguousarray(
                np.broadcast_to(gains[0][8 * hh:8 * hh + 8][None, :], (64, 8))),
            "gain_s128": np.ascontiguousarray(
                np.broadcast_to(gains[1][8 * hh:8 * hh + 8][None, :], (128, 8))),
            "gain_c": np.ascontiguousarray(
                np.broadcast_to(gains[2][8 * hh:8 * hh + 8][None, :], (64, 8))),
            "w1": np.ascontiguousarray(inputs["L0_ffn_w1"], dtype=f),
            "b1_pp": np.ascontiguousarray(
                np.asarray(inputs["L0_ffn_b1"], dtype=f).reshape(32, 128).T),
            "w2": np.ascontiguousarray(inputs["L0_ffn_w2"], dtype=f),
            "b2_row": np.ascontiguousarray(
                np.asarray(inputs["L0_ffn_b2"], dtype=f)[None, :]),
            "wq1": np.ascontiguousarray(inputs["L1_attn_Wq"], dtype=f),
            "wk1": np.ascontiguousarray(inputs["L1_attn_Wk"], dtype=f),
            "wv1": np.ascontiguousarray(inputs["L1_attn_Wv"], dtype=f),
            "wo1": np.ascontiguousarray(inputs["L1_attn_Wo"], dtype=f),
            "bo_row": np.ascontiguousarray(
                np.asarray(inputs["L1_attn_bo"], dtype=f)[None, :]),
            "wf_rep": np.ascontiguousarray(
                np.concatenate([np.asarray(inputs["L1_attn_Wf"], dtype=f)] * 2,
                               axis=0)),
            "bf_row": np.ascontiguousarray(
                np.asarray(inputs["L1_attn_bf"], dtype=f)[None, :]),
            "bf_pp": np.ascontiguousarray(
                np.asarray(inputs["L1_attn_bf"], dtype=f)[:, None]),
            "wg1": np.ascontiguousarray(inputs["L1_ffn_wg1"], dtype=f),
            "bg1_pp": np.ascontiguousarray(
                np.asarray(inputs["L1_ffn_bg1"], dtype=f).reshape(2, 128).T),
            "wg2": np.ascontiguousarray(inputs["L1_ffn_wg2"], dtype=f),
            "w_in": np.ascontiguousarray(inputs["L1_ffn_w_in"], dtype=f),
            "w_out": np.ascontiguousarray(inputs["L1_ffn_w_out"], dtype=f),
        }
        in_maps.append(m)

    b = _get_nc()
    res = run_bass_kernel_spmd(b.nc, in_maps, core_ids=list(range(8)))
    if DEV_TAPS:
        kernel.last_results = res.results
    outp = np.empty((4, S, D), dtype=np.float32)
    for core in range(8):
        p, hh = core // 2, core % 2
        outp[p, 2048 * hh:2048 * hh + 2048] = res.results[core]["out"]
    return outp
